# version 01
# baseline (speedup 1.0000x reference)
import os
import math
import numpy as np
import concourse.bass as bass
import concourse.mybir as mybir
from concourse.bass_utils import run_bass_kernel_spmd
from contextlib import ExitStack

F32 = mybir.dt.float32
BF16 = mybir.dt.bfloat16
I32 = mybir.dt.int32
AF = mybir.ActivationFunctionType
ALU = mybir.AluOpType
AXX = mybir.AxisListType.X

D = 2048
DH = 16
TOK = 8192
L = 4096
G = 16
GW = 512
NH = 16
KVH = 4
NST = 16
R = 64
NE = 8
FF = 4096
EPS = 1e-5
PI = 3.14159265358979
LOG_THETA = math.log(1.0e6)
INV2PI = 1.0 / (2.0 * PI)
TWOPI = 2.0 * PI

DEBUG = os.environ.get("KDBG", "0") == "1"
NPHASE = int(os.environ.get("KPHASES", "9"))

PE_G = 78
ACT_G = 77
DVE_G = 130
POOL_G = 52
SP_G = 45
SX_G = 16

PRO_POOL = 19
PRO_GLD = 13 * 16
PRO_SP = 5 * 16
PRO_PE = 36
PRO_ACT = 38
PRO_DVE = 245


def pe_at(g, off):
    return PRO_PE + g * PE_G + off


def act_at(g, off):
    return PRO_ACT + g * ACT_G + off


def dve_at(g, off):
    return PRO_DVE + g * DVE_G + off


def pool_at(g, off):
    return PRO_POOL + g * POOL_G + off


def sx_at(g, off):
    return 16 * (g * SX_G + off)


def sp_at(g, off):
    return PRO_SP + 16 * (g * SP_G + off)


class Ctr:
    def __init__(self, nc, stack, name):
        self.h = stack.enter_context(nc.semaphore(name))
        self.v = 0
        self.name = name

    def inc(self, instr, n=1):
        instr.then_inc(self.h, n)
        self.v += n
        return self.v


class Env(dict):
    __getattr__ = dict.__getitem__
    __setattr__ = dict.__setitem__


def I(ctr, instr, n=1):
    instr.then_inc(ctr.h, n)
    ctr.v += n
    return instr


def W(eng, ctr, v=None):
    eng.wait_ge(ctr.h, ctr.v if v is None else v)



def build_nc():
    nc = bass.Bass("TRN2", target_bir_lowering=False)
    e = Env()
    e.nc = nc

    e.xT = nc.dram_tensor("xT", [D, TOK], F32, kind="ExternalInput").ap()
    e.wxpT = nc.dram_tensor("wxpT", [D, 96], F32, kind="ExternalInput").ap()
    e.wdtT = nc.dram_tensor("wdtT", [R, D], F32, kind="ExternalInput").ap()
    e.bdt_r = nc.dram_tensor("bdt_r", [DH, 128], F32, kind="ExternalInput").ap()
    e.Alog = nc.dram_tensor("Alog", [D, NST], F32, kind="ExternalInput").ap()
    e.D_r = nc.dram_tensor("D_r", [DH, 128], F32, kind="ExternalInput").ap()
    e.n1w_r = nc.dram_tensor("n1w_r", [DH, 128], F32, kind="ExternalInput").ap()
    e.n2w_r = nc.dram_tensor("n2w_r", [DH, 128], F32, kind="ExternalInput").ap()
    e.wqh = nc.dram_tensor("wqh", [NH, 128, DH, 128], F32, kind="ExternalInput").ap()
    e.wkh = nc.dram_tensor("wkh", [KVH, 128, DH, 128], F32, kind="ExternalInput").ap()
    e.wv = nc.dram_tensor("wv", [D, 512], F32, kind="ExternalInput").ap()
    e.woh = nc.dram_tensor("woh", [DH, 128, DH, 128], F32, kind="ExternalInput").ap()
    e.gwt = nc.dram_tensor("gwt", [D, NE], F32, kind="ExternalInput").ap()
    e.w1h = nc.dram_tensor("w1h", [NE, 128, DH, FF], F32, kind="ExternalInput").ap()
    e.w3h = nc.dram_tensor("w3h", [NE, 128, DH, FF], F32, kind="ExternalInput").ap()
    e.w2h = nc.dram_tensor("w2h", [NE, 128, 32, D], F32, kind="ExternalInput").ap()
    e.cmeta = nc.dram_tensor("cmeta", [128, 32], F32, kind="ExternalInput").ap()

    e.d_delta = nc.dram_tensor("d_delta", [D, TOK], F32, kind="Internal").ap()
    e.d_u = nc.dram_tensor("d_u", [D, TOK], F32, kind="Internal").ap()
    e.d_xn1 = nc.dram_tensor("d_xn1", [D, TOK], BF16, kind="Internal").ap()
    e.d_k = nc.dram_tensor("d_k", [KVH, 128, TOK], BF16, kind="Internal").ap()
    e.d_v = nc.dram_tensor("d_v", [TOK, 512], BF16, kind="Internal").ap()
    e.d_bc = nc.dram_tensor("d_bc", [32, TOK], F32, kind="Internal").ap()
    e.d_rows = nc.dram_tensor("d_rows", [2, TOK], F32, kind="Internal").ap()
    e.d_h2 = nc.dram_tensor("d_h2", [D, 1024], F32, kind="Internal").ap()
    e.d_y = nc.dram_tensor("d_y", [D, 1024], F32, kind="Internal").ap()
    e.d_moe = nc.dram_tensor("d_moe", [D, 1024], F32, kind="Internal").ap()
    e.d_dR = nc.dram_tensor("d_dR", [D, 18 * 256], F32, kind="Internal").ap()
    e.d_uR = nc.dram_tensor("d_uR", [D, 18 * 256], F32, kind="Internal").ap()
    e.d_bcR = nc.dram_tensor("d_bcR", [32, 18 * 256], F32, kind="Internal").ap()
    e.d_xn2 = nc.dram_tensor("d_xn2", [D, 1024], BF16, kind="Internal").ap()
    e.d_w = nc.dram_tensor("d_w", [NE, 1024], F32, kind="Internal").ap()
    e.out = nc.dram_tensor("out", [D, 1024], F32, kind="ExternalOutput").ap()

    e.dbg = {}
    if DEBUG:
        for nm, shape in [
            ("dbg_delta", [D, 1024]), ("dbg_u", [D, 1024]), ("dbg_bc", [32, 1024]),
            ("dbg_xn1", [D, 1024]), ("dbg_k", [KVH * 128, 1024]), ("dbg_v", [1024, 512]),
            ("dbg_rows", [2, TOK]), ("dbg_cos", [64, 1024]), ("dbg_sin", [64, 1024]),
            ("dbg_y", [D, 1024]), ("dbg_hloc", [D, NST]), ("dbg_hinit", [D, 2 * NST]),
            ("dbg_av", [D, 1024]), ("dbg_q", [D, 1024]),
            ("dbg_h2", [D, 1024]), ("dbg_xn2", [D, 1024]), ("dbg_w", [NE, 1024]),
            ("dbg_moe", [D, 1024]),
        ]:
            e.dbg[nm] = nc.dram_tensor(nm, shape, F32, kind="ExternalOutput").ap()

    with ExitStack() as ctx:
        def sb(name, shape, dt):
            return ctx.enter_context(nc.sbuf_tensor(name, shape, dt))
        e.sb = sb

        e.ones_bf = sb("ones_bf", [128, 1], BF16)
        e.ones1f = sb("ones1f", [1, 128], F32)
        e.ones16 = sb("ones16", [16, 16], F32)
        e.id16 = sb("id16", [16, 16], F32)
        e.Apos = sb("Apos", [128, DH * NST], F32)
        e.bdt_c = sb("bdt_c", [128, DH], F32)
        e.D_c = sb("D_c", [128, DH], F32)
        e.n1w_c = sb("n1w_c", [128, DH], F32)
        e.n2w_c = sb("n2w_c", [128, DH], F32)
        e.invf = sb("invf", [1, 64], F32)
        e.iot = sb("iot", [1, 1024], F32)
        e.kmax2 = sb("kmax2", [1, 8], F32)
        e.cst = sb("cst", [128, 2], F32)
        nc.const_aps.register(F32, EPS, e.cst[:, 0:1])
        nc.const_aps.register(F32, 1.0, e.cst[:, 1:2])
        e.scr = [sb(f"scr{i}", [128, GW], F32) for i in range(6)]
        e.iscr = sb("iscr", [128, GW], I32)

        e.ropectx = ExitStack()
        e.cosr = e.ropectx.enter_context(nc.sbuf_tensor("cosr", [64, TOK], BF16))
        e.sinr = e.ropectx.enter_context(nc.sbuf_tensor("sinr", [64, TOK], BF16))

        p0ctx = ExitStack()

        e.p0ctx = p0ctx

        def sb0(name, shape, dt):
            return p0ctx.enter_context(nc.sbuf_tensor(name, shape, dt))
        e.xbuf = sb0("xbuf", [128, DH * GW], F32)
        e.xnb = sb0("xnb", [128, DH * GW], BF16)
        e.xn1b = sb0("xn1b", [128, DH * GW], BF16)
        e.sqb = [sb0(f"sqb{i}", [128, GW], BF16) for i in range(2)]
        e.draw_b = sb0("draw_b", [64, GW], BF16)
        e.wxp_sb = sb0("wxp_sb", [128, DH * 96], BF16)
        e.wdt_sb = sb0("wdt_sb", [64, D], BF16)
        e.wk_sb = sb0("wk_sb", [128, KVH * DH * 128], BF16)
        e.wv_sb = sb0("wv_sb", [128, DH * 512], BF16)
        e.kvb = [sb0(f"kvb{i}", [128, GW], BF16) for i in range(2)]
        e.vb = [sb0(f"vb{i}", [128, GW], BF16) for i in range(2)]
        e.dsc = [sb0(f"dsc{i}", [128, GW], F32) for i in range(2)]
        e.usc = [sb0(f"usc{i}", [128, GW], F32) for i in range(2)]
        e.rowsB = sb0("rowsB", [1, 4 * GW], F32)
        e.mrow = e.rowsB[0:1, 0:GW]
        e.mrow2 = e.rowsB[0:1, GW:2 * GW]
        e.rrow = e.rowsB[0:1, 2 * GW:3 * GW]
        e.r1row = e.rowsB[0:1, 3 * GW:4 * GW]

        e.B = [ctx.enter_context(nc.psum_tensor(f"bank{i}", [128, 512], F32))
               for i in range(8)]

        for nm in ["sX", "sSP", "sPE", "sACT", "sDVE", "sPOOL", "sGLD", "sST",
                   "sX1", "sBC", "sRW", "sMS", "sQS", "sKV", "sWQ0", "sWQ1",
                   "sXB", "sWO0", "sWO1", "sH2", "sXN0", "sXN1",
                   "sMR", "sMW0", "sMW1", "sFT0", "sFT1", "sOU0", "sOU1"]:
            e[nm] = Ctr(nc, ctx, nm)
        e.sDS = [Ctr(nc, ctx, f"sDS{i}") for i in range(2)]
        e.sUS = [Ctr(nc, ctx, f"sUS{i}") for i in range(2)]
        e.sKB = [Ctr(nc, ctx, f"sKB{i}") for i in range(2)]
        e.sVB = [Ctr(nc, ctx, f"sVB{i}") for i in range(2)]

        emit_program(e)
        verify_sem_totals(e)

    return nc, e


def verify_sem_totals(e):
    import collections
    nc = e.nc
    tot = collections.Counter()
    for ins in nc.all_instructions():
        si = ins.sync_info
        if si is None:
            continue
        for u in si.on_update:
            if u.update_mode in ("sem-inc", "sem-add-imm"):
                tot[u.ant_name] += u.update_value
    names = ["sX", "sSP", "sPE", "sACT", "sDVE", "sPOOL", "sGLD", "sST", "sX1", "sBC", "sRW",
             "sQS", "sKV", "sWQ0", "sWQ1",
             "sXB", "sWO0", "sWO1", "sH2", "sXN0", "sXN1",
             "sMR", "sMW0", "sMW1", "sFT0", "sFT1", "sOU0", "sOU1",
             "sDS0", "sDS1", "sUS0", "sUS1", "sKB0", "sKB1", "sVB0", "sVB1"]
    objs = {n: e[n] for n in names[:28]}
    for fam, lst in [("sDS", e.sDS), ("sUS", e.sUS), ("sKB", e.sKB), ("sVB", e.sVB)]:
        for i, c in enumerate(lst):
            objs[f"{fam}{i}"] = c
    excess = e.get("sw_excess", {})
    for name in names:
        want = objs[name].v + excess.get(name, 0)
        got = tot.get(name, 0)
        if got != want:
            raise AssertionError(f"sem {name}: emitted {got} != bookkept {want}")


def emit_program(e):
    nc = e.nc
    with nc.Block() as blk:
        blk.gpsimd(lambda eng: p0_gpsimd(e, eng))
        blk.scalar(lambda eng: p0_scalar(e, eng))
        blk.tensor(lambda eng: p0_tensor(e, eng))
        blk.vector(lambda eng: p0_vector(e, eng))
        blk.sync(lambda eng: p0_sync(e, eng))
    e.p0ctx.close()

    if NPHASE >= 2:
        emit_mamba(e)
    if NPHASE >= 3:
        emit_attention(e)
        if NPHASE >= 4:
            emit_h2(e)
        e.avctx.close()
        if DEBUG:
            with nc.Block() as blkr:
                blkr.gpsimd(lambda g: rope_debug(e, g))
        e.ropectx.close()
        if NPHASE >= 5:
            emit_moe(e)
            if NPHASE >= 6:
                emit_final(e)
            e.moectx.close()

    if DEBUG:
        with nc.Block() as blk2:
            blk2.gpsimd(lambda eng: p0_debug_g(e, eng))
            blk2.sync(lambda eng: p0_debug_s(e, eng))
    elif NPHASE < 6:
        with nc.Block() as blk3:
            blk3.sync(lambda eng: stub_out(e, eng))


def stub_out(e, s):
    s.dma_start(out=e.out[0:128, 0:512], in_=e.scr[0][:, 0:512]).then_inc(e.sST.h, 16)
    e.sST.v += 16
    s.wait_ge(e.sST.h, e.sST.v)



def p0_gpsimd(e, g):
    I(e.sPOOL, g.iota(e.invf[:, :], pattern=[[1, 64]], base=0, channel_multiplier=0,
                      allow_small_or_imprecise_dtypes=True))
    I(e.sPOOL, g.memset(e.ones16[:, :], 1.0))
    W(g, e.sPOOL)
    I(e.sPOOL, g.affine_select(out=e.id16[:, :], in_=e.ones16[:, :], pattern=[[-1, 16]],
                               compare_op=ALU.is_equal, fill=0.0, base=0,
                               channel_multiplier=1))
    for ch in range(16):
        if ch >= 2:
            g.wait_ge(e.sPE.h, 4 + 2 * (ch - 1))
        I(e.sPOOL, g.iota(e.iot[:, (ch % 2) * 512:(ch % 2) * 512 + 512], pattern=[[1, 512]],
                          base=(ch % 8) * 512, channel_multiplier=0,
                          allow_small_or_imprecise_dtypes=True))
    assert e.sPOOL.v == PRO_POOL, e.sPOOL.v
    for half in range(2):
        I(e.sGLD, g.dma_start(out=e.wxp_sb[:, half * 8 * 96:(half + 1) * 8 * 96].rearrange(
                        "p (dh r) -> p dh r", dh=8),
                    in_=e.wxpT.rearrange("(dh dl) r -> dl dh r", dl=128)[:, half * 8:(half + 1) * 8, :]), 16)
    I(e.sGLD, g.dma_start(out=e.wdt_sb[:, :], in_=e.wdtT), 16)
    for kvh in range(KVH):
        for half in range(2):
            I(e.sGLD, g.dma_start(
                out=e.wk_sb[:, (kvh * DH + half * 8) * 128:(kvh * DH + (half + 1) * 8) * 128
                            ].rearrange("p (dh x) -> p dh x", dh=8),
                in_=e.wkh[kvh, :, half * 8:(half + 1) * 8, :]), 16)
    for half in range(2):
        I(e.sGLD, g.dma_start(out=e.wv_sb[:, half * 8 * 512:(half + 1) * 8 * 512].rearrange(
                        "p (dh x) -> p dh x", dh=8),
                    in_=e.wv.rearrange("(dh dl) x -> dl dh x", dl=128)[:, half * 8:(half + 1) * 8, :]), 16)
    assert e.sGLD.v == PRO_GLD, e.sGLD.v

    for gi in range(G):
        for dh in range(DH):
            g.wait_ge(e.sX.h, sx_at(gi, 16))
            if dh >= 2:
                g.wait_ge(e.sPE.h, pe_at(gi, dh - 1))
            elif gi > 0:
                g.wait_ge(e.sPE.h, pe_at(gi - 1, PE_G))
            I(e.sPOOL, g.tensor_tensor(out=e.sqb[dh % 2][:, :], in0=e.xbuf[:, dh * GW:(dh + 1) * GW],
                                       in1=e.xbuf[:, dh * GW:(dh + 1) * GW], op=ALU.mult))
        for dh in range(DH):
            g.wait_ge(e.sDVE.h, dve_at(gi, 21 + 3 * dh))
            g.wait_ge(e.sPE.h, pe_at(gi, 16 + dh) if dh >= 2 else pe_at(gi, 16))
            I(e.sPOOL, g.tensor_tensor(out=e.sqb[dh % 2][:, :], in0=e.xbuf[:, dh * GW:(dh + 1) * GW],
                                       in1=e.xbuf[:, dh * GW:(dh + 1) * GW], op=ALU.mult))
        for dh in range(DH):
            g.wait_ge(e.sDVE.h, dve_at(gi, 83 + dh))
            if dh >= 2:
                g.wait_ge(e.sUS[dh % 2].h, 16 * (gi * 8 + (dh - 2) // 2 + 1))
            elif gi > 0:
                g.wait_ge(e.sUS[dh % 2].h, 16 * (gi * 8))
            I(e.sPOOL, g.tensor_tensor(out=e.usc[dh % 2][:, :], in0=e.dsc[dh % 2][:, :],
                                       in1=e.xbuf[:, dh * GW:(dh + 1) * GW], op=ALU.mult))
        for kvh in range(KVH):
            g.wait_ge(e.sDVE.h, dve_at(gi, 104 + 6 * kvh))
            if kvh >= 2:
                g.wait_ge(e.sPE.h, pe_at(gi, 74 + kvh - 1))
            I(e.sPOOL, g.tensor_tensor(out=e.sqb[kvh % 2][:, :], in0=e.kvb[kvh % 2][:, :],
                                       in1=e.kvb[kvh % 2][:, :], op=ALU.mult))
        assert e.sPOOL.v == pool_at(gi, POOL_G), (gi, e.sPOOL.v)


def p0_scalar(e, s):
    s.wait_ge(e.sSP.h, 16 * 5)
    I(e.sACT, s.activation(out=e.Apos[:, :], in_=e.scr[0][:, 0:DH * NST], func=AF.Exp))
    s.wait_ge(e.sPOOL.h, 1)
    I(e.sACT, s.activation(out=e.invf[:, :], in_=e.invf[:, :], func=AF.Exp,
                           scale=-LOG_THETA / 64.0))
    for i, dst in enumerate([e.bdt_c, e.D_c, e.n1w_c, e.n2w_c]):
        s.wait_ge(e.sPE.h, i + 1)
        I(e.sACT, s.copy(out=dst[:, :], in_=e.B[0][:, i * 16:(i + 1) * 16]))
    for ch in range(16):
        s.wait_ge(e.sDVE.h, 5 + 15 * ch + 7)
        I(e.sACT, s.activation(out=e.sinr[:, ch * 512:(ch + 1) * 512], in_=e.scr[5][0:64, :],
                               func=AF.Sin))
        s.wait_ge(e.sDVE.h, 5 + 15 * ch + 15)
        I(e.sACT, s.activation(out=e.cosr[:, ch * 512:(ch + 1) * 512], in_=e.scr[4][0:64, :],
                               func=AF.Sin))
    assert e.sACT.v == PRO_ACT, e.sACT.v

    for gi in range(G):
        s.wait_ge(e.sPE.h, pe_at(gi, 16))
        if gi > 0:
            s.wait_ge(e.sRW.h, 32 * gi)
            W(s, e.sACT)
        I(e.sACT, s.mul(out=e.mrow[:, :], in_=e.B[0][0:1, :], mul=1.0 / D))
        I(e.sACT, s.mul(out=e.mrow2[:, :], in_=e.B[1][0:1, :], mul=1.0 / D))
        s.wait_ge(e.sDVE.h, dve_at(gi, 18))
        I(e.sACT, s.activation(out=e.scr[2][0:1, :], in_=e.mrow2[:, :], func=AF.Ln, bias=EPS))
        W(s, e.sACT)
        I(e.sACT, s.activation(out=e.rrow[:, :], in_=e.scr[2][0:1, :], func=AF.Exp, scale=-0.5))
        s.wait_ge(e.sPE.h, pe_at(gi, 33))
        W(s, e.sACT)
        I(e.sACT, s.mul(out=e.scr[2][0:1, :], in_=e.B[1][0:1, :], mul=1.0 / D))
        W(s, e.sACT)
        I(e.sACT, s.activation(out=e.scr[2][0:1, :], in_=e.scr[2][0:1, :], func=AF.Ln, bias=EPS))
        W(s, e.sACT)
        I(e.sACT, s.activation(out=e.r1row[:, :], in_=e.scr[2][0:1, :], func=AF.Exp, scale=-0.5))
        s.wait_ge(e.sPE.h, pe_at(gi, 50))
        I(e.sACT, s.copy(out=e.draw_b[:, :], in_=e.B[4][0:64, :]))
        if gi > 0:
            s.wait_ge(e.sBC.h, 16 * gi)
        I(e.sACT, s.copy(out=e.scr[3][0:32, :], in_=e.B[5][0:32, :]))
        for dh in range(DH):
            bank = e.B[6 + dh % 2]
            s.wait_ge(e.sPE.h, pe_at(gi, 50 + dh + 1))
            if dh >= 1:
                s.wait_ge(e.sDVE.h, dve_at(gi, 82 + dh))
            elif gi > 0:
                s.wait_ge(e.sDVE.h, dve_at(gi - 1, 98))
            W(s, e.sACT)
            bcol = e.bdt_c[:, dh:dh + 1]
            I(e.sACT, s.activation(out=e.scr[0][:, :], in_=bank[:, :], func=AF.Abs, bias=bcol))
            W(s, e.sACT)
            I(e.sACT, s.activation(out=e.scr[1][:, :], in_=e.scr[0][:, :], func=AF.Exp, scale=-1.0))
            W(s, e.sACT)
            I(e.sACT, s.activation(out=e.scr[1][:, :], in_=e.scr[1][:, :], func=AF.Ln, bias=1.0))
            W(s, e.sACT)
            I(e.sACT, s.activation(out=e.scr[0][:, :], in_=bank[:, :], func=AF.Relu, bias=bcol))
        for tc in range(4):
            s.wait_ge(e.sPE.h, pe_at(gi, 70 + tc + 1))
            if tc >= 2:
                s.wait_ge(e.sVB[tc % 2].h, 16 * (gi * 2 + 1))
            elif gi > 0:
                s.wait_ge(e.sVB[tc % 2].h, 16 * (gi * 2))
            I(e.sACT, s.copy(out=e.vb[tc % 2][:, :], in_=e.B[2 + tc % 2][:, :]))
        assert e.sACT.v == act_at(gi, ACT_G), (gi, e.sACT.v)


def p0_tensor(e, t):
    t.wait_ge(e.sSP.h, 16 * 5)
    t.wait_ge(e.sPOOL.h, 3)
    for i in range(4):
        if i > 0:
            t.wait_ge(e.sACT.h, 2 + i)
        I(e.sPE, t.transpose(e.B[0][:, i * 16:(i + 1) * 16], e.scr[1][0:16, i * 128:(i + 1) * 128],
                             e.id16[:, :]))
    for ch in range(16):
        t.wait_ge(e.sACT.h, 2)
        t.wait_ge(e.sPOOL.h, 3 + ch + 1)
        if ch >= 1:
            t.wait_ge(e.sDVE.h, 5 + 15 * (ch - 1) + 8)
        sl = e.iot[:, (ch % 2) * 512:(ch % 2) * 512 + 512]
        I(e.sPE, t.matmul(e.B[1][0:64, :], e.invf[:, :], sl, start=True, stop=True))
        I(e.sPE, t.matmul(e.B[2][0:64, :], e.invf[:, :], sl, start=True, stop=True))
    assert e.sPE.v == PRO_PE, e.sPE.v

    for gi in range(G):
        if gi > 0:
            t.wait_ge(e.sACT.h, act_at(gi - 1, ACT_G))
            t.wait_ge(e.sDVE.h, dve_at(gi - 1, DVE_G))
        else:
            t.wait_ge(e.sDVE.h, PRO_DVE)
            t.wait_ge(e.sGLD.h, PRO_GLD)
        for dh in range(DH):
            t.wait_ge(e.sDVE.h, dve_at(gi, dh + 1))
            t.wait_ge(e.sPOOL.h, pool_at(gi, dh + 1))
            t.matmul(e.B[0][0:1, :], e.ones_bf[:, :], e.xn1b[:, dh * GW:(dh + 1) * GW],
                     start=(dh == 0), stop=(dh == DH - 1))
            I(e.sPE, t.matmul(e.B[1][0:1, :], e.ones_bf[:, :], e.sqb[dh % 2][:, :],
                              start=(dh == 0), stop=(dh == DH - 1)))
        t.wait_ge(e.sACT.h, act_at(gi, 4))
        t.matmul(e.B[2][:, :], e.ones1f[:, :], e.mrow[:, :], start=True, stop=True)
        I(e.sPE, t.matmul(e.B[3][:, :], e.ones1f[:, :], e.rrow[:, :], start=True, stop=True))
        for dh in range(DH):
            t.wait_ge(e.sPOOL.h, pool_at(gi, 16 + dh + 1))
            I(e.sPE, t.matmul(e.B[1][0:1, :], e.ones_bf[:, :], e.sqb[dh % 2][:, :],
                              start=(dh == 0), stop=(dh == DH - 1)))
        t.wait_ge(e.sACT.h, act_at(gi, 7))
        t.wait_ge(e.sDVE.h, dve_at(gi, 66))
        I(e.sPE, t.matmul(e.B[3][:, :], e.ones1f[:, :], e.r1row[:, :], start=True, stop=True))
        for dh in range(DH):
            t.wait_ge(e.sDVE.h, dve_at(gi, 21 + 3 * dh))
            if dh == 0 and gi > 0:
                t.wait_ge(e.sACT.h, act_at(gi - 1, 9))
            t.matmul(e.B[4][0:64, :], e.wxp_sb[:, dh * 96:dh * 96 + 64],
                     e.xnb[:, dh * GW:(dh + 1) * GW], start=(dh == 0), stop=(dh == DH - 1))
            I(e.sPE, t.matmul(e.B[5][0:32, :], e.wxp_sb[:, dh * 96 + 64:dh * 96 + 96],
                              e.xnb[:, dh * GW:(dh + 1) * GW], start=(dh == 0),
                              stop=(dh == DH - 1)))
        for dh in range(DH):
            t.wait_ge(e.sACT.h, act_at(gi, 8))
            if dh >= 2:
                t.wait_ge(e.sACT.h, act_at(gi, 13 + 4 * (dh - 2)))
            I(e.sPE, t.matmul(e.B[6 + dh % 2][:, :], e.wdt_sb[:, dh * 128:(dh + 1) * 128],
                              e.draw_b[:, :], start=True, stop=True))
        t.wait_ge(e.sGLD.h, PRO_GLD)
        for kvh in range(KVH):
            t.wait_ge(e.sDVE.h, dve_at(gi, 82))
            if kvh >= 2:
                t.wait_ge(e.sDVE.h, dve_at(gi, 104 + 6 * (kvh - 2)))
            for dh in range(DH):
                ins = t.matmul(e.B[kvh % 2][:, :],
                               e.wk_sb[:, (kvh * DH + dh) * 128:(kvh * DH + dh + 1) * 128],
                               e.xn1b[:, dh * GW:(dh + 1) * GW],
                               start=(dh == 0), stop=(dh == DH - 1))
                if dh == DH - 1:
                    I(e.sPE, ins)
        for tc in range(4):
            if tc >= 2:
                t.wait_ge(e.sACT.h, act_at(gi, 72 + tc))
            for dh in range(DH):
                ins = t.matmul(e.B[2 + tc % 2][:, :],
                               e.xn1b[:, dh * GW + tc * 128:dh * GW + (tc + 1) * 128],
                               e.wv_sb[:, dh * 512:(dh + 1) * 512],
                               start=(dh == 0), stop=(dh == DH - 1))
                if dh == DH - 1:
                    I(e.sPE, ins)
        for kvh in range(KVH):
            t.wait_ge(e.sPOOL.h, pool_at(gi, 48 + kvh + 1))
            if kvh == 0:
                t.wait_ge(e.sACT.h, act_at(gi, 8))
            else:
                t.wait_ge(e.sDVE.h, dve_at(gi, 122 + 2 * kvh))
            I(e.sPE, t.matmul(e.B[4][0:1, :], e.ones_bf[:, :], e.sqb[kvh % 2][:, :],
                              start=True, stop=True))
        assert e.sPE.v == pe_at(gi, PE_G), (gi, e.sPE.v)


def p0_vector(e, v):
    I(e.sDVE, v.memset(e.ones_bf[:, :], 1.0))
    I(e.sDVE, v.memset(e.ones1f[:, :], 1.0))
    I(e.sDVE, v.memset(e.cst[:, 0:1], EPS))
    I(e.sDVE, v.memset(e.cst[:, 1:2], 1.0))
    I(e.sDVE, v.memset(e.kmax2[:, :], 0.0))
    for ch in range(16):
        v.wait_ge(e.sPE.h, 4 + 2 * ch + 2)
        if ch >= 1:
            v.wait_ge(e.sACT.h, 6 + 2 * ch)
        b = e.B[1]
        b2 = e.B[2]

        def redchain(v, srcap, dst, tmp):
            W(v, e.sDVE)
            I(e.sDVE, v.tensor_scalar_mul(dst[0:64, :], srcap, INV2PI))
            W(v, e.sDVE)
            I(e.sDVE, v.tensor_copy(out=e.iscr[0:64, :], in_=dst[0:64, :]))
            W(v, e.sDVE)
            I(e.sDVE, v.tensor_copy(out=tmp[0:64, :], in_=e.iscr[0:64, :]))
            W(v, e.sDVE)
            I(e.sDVE, v.scalar_tensor_tensor(out=dst[0:64, :], in0=tmp[0:64, :], scalar=-TWOPI,
                                             in1=srcap, op0=ALU.mult, op1=ALU.add))
            W(v, e.sDVE)
            I(e.sDVE, v.tensor_scalar(out=tmp[0:64, :], in0=dst[0:64, :], scalar1=PI,
                                      scalar2=None, op0=ALU.is_gt, op1=ALU.bypass,
                                      accum_out=None))
            W(v, e.sDVE)
            I(e.sDVE, v.scalar_tensor_tensor(out=dst[0:64, :], in0=tmp[0:64, :], scalar=-TWOPI,
                                             in1=dst[0:64, :], op0=ALU.mult, op1=ALU.add))
            W(v, e.sDVE)
            I(e.sDVE, v.tensor_scalar(out=dst[0:64, :], in0=dst[0:64, :], scalar1=PI,
                                      scalar2=-PI, op0=ALU.min, op1=ALU.max, accum_out=None))

        redchain(v, b[0:64, :], e.scr[5], e.scr[3])
        W(v, e.sDVE)
        I(e.sDVE, v.tensor_scalar_add(e.scr[2][0:64, :], b2[0:64, :], PI / 2))
        redchain(v, e.scr[2][0:64, :], e.scr[4], e.scr[3])
        assert e.sDVE.v == 5 + 15 * (ch + 1), (ch, e.sDVE.v)
    assert e.sDVE.v == PRO_DVE, e.sDVE.v

    for gi in range(G):
        for dh in range(DH):
            v.wait_ge(e.sX.h, sx_at(gi, 16))
            if gi > 0 and dh == 0:
                v.wait_ge(e.sPE.h, pe_at(gi - 1, PE_G))
                v.wait_ge(e.sX1.h, 32 * gi)
            I(e.sDVE, v.tensor_copy(out=e.xn1b[:, dh * GW:(dh + 1) * GW],
                                    in_=e.xbuf[:, dh * GW:(dh + 1) * GW]))
        v.wait_ge(e.sACT.h, act_at(gi, 2))
        W(v, e.sDVE)
        I(e.sDVE, v.tensor_mul(e.scr[2][0:1, :], e.mrow[:, :], e.mrow[:, :]))
        W(v, e.sDVE)
        I(e.sDVE, v.tensor_sub(e.mrow2[:, :], e.mrow2[:, :], e.scr[2][0:1, :]))
        v.wait_ge(e.sPE.h, pe_at(gi, 17))
        for dh in range(DH):
            v.wait_ge(e.sPOOL.h, pool_at(gi, dh + 1))
            W(v, e.sDVE)
            I(e.sDVE, v.tensor_sub(e.xbuf[:, dh * GW:(dh + 1) * GW],
                                   e.xbuf[:, dh * GW:(dh + 1) * GW], e.B[2][:, :]))
            W(v, e.sDVE)
            I(e.sDVE, v.tensor_mul(e.xbuf[:, dh * GW:(dh + 1) * GW],
                                   e.xbuf[:, dh * GW:(dh + 1) * GW], e.B[3][:, :]))
            W(v, e.sDVE)
            I(e.sDVE, v.tensor_copy(out=e.xnb[:, dh * GW:(dh + 1) * GW],
                                    in_=e.xbuf[:, dh * GW:(dh + 1) * GW]))
        v.wait_ge(e.sPE.h, pe_at(gi, 34))
        W(v, e.sDVE)
        for dh in range(DH):
            I(e.sDVE, v.scalar_tensor_tensor(out=e.xn1b[:, dh * GW:(dh + 1) * GW],
                                             in0=e.xbuf[:, dh * GW:(dh + 1) * GW],
                                             scalar=e.n1w_c[:, dh:dh + 1], in1=e.B[3][:, :],
                                             op0=ALU.mult, op1=ALU.mult))
        for dh in range(DH):
            v.wait_ge(e.sACT.h, act_at(gi, 13 + 4 * dh))
            if dh >= 2:
                v.wait_ge(e.sDS[dh % 2].h, 16 * (gi * 8 + (dh - 2) // 2 + 1))
                v.wait_ge(e.sPOOL.h, pool_at(gi, 32 + dh - 1))
            elif gi > 0:
                v.wait_ge(e.sDS[dh % 2].h, 16 * (gi * 8))
                v.wait_ge(e.sPOOL.h, pool_at(gi - 1, 32 + DH - 1 + dh))
            I(e.sDVE, v.tensor_add(e.dsc[dh % 2][:, :], e.scr[0][:, :], e.scr[1][:, :]))
        for kvh in range(KVH):
            v.wait_ge(e.sPE.h, pe_at(gi, 66 + kvh + 1))
            v.wait_ge(e.sACT.h, PRO_ACT)
            if kvh >= 2:
                v.wait_ge(e.sKB[kvh % 2].h, 16 * (gi * 2 + 1))
                v.wait_ge(e.sPOOL.h, pool_at(gi, 48 + kvh - 1))
            elif gi > 0:
                v.wait_ge(e.sKB[kvh % 2].h, 16 * (gi * 2))
                v.wait_ge(e.sPOOL.h, pool_at(gi - 1, 48 + kvh + 3))
            b = e.B[kvh % 2]
            cs = e.cosr[:, gi * GW:(gi + 1) * GW]
            sn = e.sinr[:, gi * GW:(gi + 1) * GW]
            kb = e.kvb[kvh % 2]
            W(v, e.sDVE)
            I(e.sDVE, v.tensor_mul(e.scr[0][0:64, :], b[0:64, :], cs))
            I(e.sDVE, v.tensor_mul(e.scr[1][0:64, :], b[64:128, :], sn))
            W(v, e.sDVE)
            I(e.sDVE, v.tensor_sub(kb[0:64, :], e.scr[0][0:64, :], e.scr[1][0:64, :]))
            W(v, e.sDVE)
            I(e.sDVE, v.tensor_mul(e.scr[0][0:64, :], b[64:128, :], cs))
            I(e.sDVE, v.tensor_mul(e.scr[1][0:64, :], b[0:64, :], sn))
            W(v, e.sDVE)
            I(e.sDVE, v.tensor_add(kb[64:128, :], e.scr[0][0:64, :], e.scr[1][0:64, :]))
        for kvh in range(KVH):
            v.wait_ge(e.sPE.h, pe_at(gi, 74 + kvh + 1))
            W(v, e.sDVE)
            I(e.sDVE, v.reduce_max(e.scr[2][0:1, 0:1], e.B[4][0:1, :], axis=AXX))
            W(v, e.sDVE)
            idx = (gi // 8) * 4 + kvh
            I(e.sDVE, v.tensor_max(e.kmax2[:, idx:idx + 1], e.kmax2[:, idx:idx + 1],
                                   e.scr[2][0:1, 0:1]))
        assert e.sDVE.v == dve_at(gi, DVE_G), (gi, e.sDVE.v)


def p0_sync(e, s):
    s.dma_start(out=e.scr[0][:, 0:DH * NST].rearrange("p (dh n) -> p dh n", dh=DH),
                in_=e.Alog.rearrange("(dh dl) n -> dl dh n", dl=128)).then_inc(e.sSP.h, 16)
    for i, t_ in enumerate([e.bdt_r, e.D_r, e.n1w_r, e.n2w_r]):
        s.dma_start(out=e.scr[1][0:16, i * 128:(i + 1) * 128], in_=t_).then_inc(e.sSP.h, 16)
    e.sSP.v = PRO_SP

    for gi in range(G):
        if gi > 0:
            s.wait_ge(e.sPOOL.h, pool_at(gi - 1, 48))
            s.wait_ge(e.sDVE.h, dve_at(gi - 1, 82))
        for dh in range(DH):
            s.dma_start(out=e.xbuf[:, dh * GW:(dh + 1) * GW],
                        in_=e.xT[dh * 128:(dh + 1) * 128, gi * GW:(gi + 1) * GW]).then_inc(e.sX.h, 16)
        e.sX.v = sx_at(gi, SX_G)
        s.wait_ge(e.sDVE.h, dve_at(gi, 82))
        for half in range(2):
            I(e.sX1, s.dma_start(
                out=e.d_xn1.rearrange("(dh dl) t -> dl dh t", dl=128)[:, half * 8:(half + 1) * 8,
                                                                      gi * GW:(gi + 1) * GW],
                in_=e.xn1b[:, half * 8 * GW:(half + 1) * 8 * GW].rearrange(
                    "p (dh t) -> p dh t", dh=8)), 16)
        for dh in range(DH):
            s.wait_ge(e.sDVE.h, dve_at(gi, 83 + dh))
            I(e.sDS[dh % 2], s.dma_start(out=e.d_delta[dh * 128:(dh + 1) * 128,
                                                       gi * GW:(gi + 1) * GW],
                                         in_=e.dsc[dh % 2][:, :]), 16)
            s.wait_ge(e.sPOOL.h, pool_at(gi, 32 + dh + 1))
            I(e.sUS[dh % 2], s.dma_start(out=e.d_u[dh * 128:(dh + 1) * 128,
                                                   gi * GW:(gi + 1) * GW],
                                         in_=e.usc[dh % 2][:, :]), 16)
        for kvh in range(KVH):
            s.wait_ge(e.sDVE.h, dve_at(gi, 104 + 6 * kvh))
            I(e.sKB[kvh % 2], s.dma_start(out=e.d_k[kvh, :, gi * GW:(gi + 1) * GW],
                                          in_=e.kvb[kvh % 2][:, :]), 16)
        for tc in range(4):
            s.wait_ge(e.sACT.h, act_at(gi, 73 + tc + 1))
            I(e.sVB[tc % 2], s.dma_start(out=e.d_v[gi * GW + tc * 128:gi * GW + (tc + 1) * 128, :],
                                         in_=e.vb[tc % 2][:, :]), 16)
        s.wait_ge(e.sACT.h, act_at(gi, 9))
        I(e.sBC, s.dma_start(out=e.d_bc[:, gi * GW:(gi + 1) * GW], in_=e.scr[3][0:32, :]), 16)
        s.wait_ge(e.sACT.h, act_at(gi, 4))
        I(e.sRW, s.dma_start(out=e.d_rows[0:1, gi * GW:(gi + 1) * GW], in_=e.mrow), 16)
        I(e.sRW, s.dma_start(out=e.d_rows[1:2, gi * GW:(gi + 1) * GW], in_=e.rrow), 16)
    for c in [e.sX1, e.sBC, e.sRW] + e.sDS + e.sUS + e.sKB + e.sVB:
        s.wait_ge(c.h, c.v)


def rope_debug(e, g):
    g.dma_start(out=e.dbg["dbg_cos"], in_=e.cosr[:, 0:1024]).then_inc(e.sGLD.h, 16)
    g.dma_start(out=e.dbg["dbg_sin"], in_=e.sinr[:, 0:1024]).then_inc(e.sGLD.h, 16)
    e.sGLD.v += 2 * 16
    g.wait_ge(e.sGLD.h, e.sGLD.v)


def p0_debug_g(e, g):
    for half in range(2):
        g.dma_start(out=e.dbg["dbg_xn1"][half * 1024:(half + 1) * 1024, :],
                    in_=e.d_xn1[half * 1024:(half + 1) * 1024, 0:1024]).then_inc(e.sGLD.h, 16)
    g.dma_start(out=e.dbg["dbg_k"].rearrange("(a p) t -> a p t", a=KVH),
                in_=e.d_k[:, :, 0:1024]).then_inc(e.sGLD.h, 16)
    g.dma_start(out=e.dbg["dbg_v"], in_=e.d_v[0:1024, :]).then_inc(e.sGLD.h, 16)
    e.sGLD.v += 4 * 16


def p0_debug_s(e, s):
    s.dma_start(out=e.dbg["dbg_delta"], in_=e.d_delta[:, 0:1024]).then_inc(e.sST.h, 16)
    s.dma_start(out=e.dbg["dbg_u"], in_=e.d_u[:, 0:1024]).then_inc(e.sST.h, 16)
    s.dma_start(out=e.dbg["dbg_bc"], in_=e.d_bc[:, 0:1024]).then_inc(e.sST.h, 16)
    s.dma_start(out=e.dbg["dbg_rows"], in_=e.d_rows).then_inc(e.sST.h, 16)
    s.dma_start(out=e.out[0:128, 0:512], in_=e.scr[0][:, 0:512]).then_inc(e.sST.h, 16)
    e.sST.v += 5 * 16
    s.wait_ge(e.sST.h, e.sST.v)
    s.wait_ge(e.sGLD.h, e.sGLD.v)



def own_cols(c):
    return (512 * c, 4096 + 512 * (7 - c))


def prefix_units(c):
    out = [(0, b, h) for b in range(c) for h in range(2)] + \
          [(1, b, h) for b in range(7 - c) for h in range(2)]
    assert len(out) == 14
    return out


def emit_mamba(e):
    nc = e.nc
    mctx = ExitStack()

    def sb(name, shape, dt):
        return mctx.enter_context(nc.sbuf_tensor(name, shape, dt))
    if "sw_excess" not in e:
        e.sw_excess = {}
    e.Hloc = sb("Hloc", [128, 14 * NST * NST], F32)
    e.Pblk = sb("Pblk", [128, 14 * NST * NST], F32)
    e.Hinit = sb("Hinit", [128, 2 * NST * NST], F32)
    e.Hmid = sb("Hmid", [128, NST * NST], F32)
    e.cmeta_t = sb("cmeta_t", [128, 32], F32)
    e.mscr = [sb(f"mscr{i}", [128, 256], F32) for i in range(4)]
    e.zz = sb("zz", [128, 256], F32)

    with nc.Block() as b0:
        b0.vector(lambda v: I(e.sDVE, v.memset(e.zz[:, :], 0.0)))

    pctx = ExitStack()

    def sbp(name, shape, dt):
        return pctx.enter_context(nc.sbuf_tensor(name, shape, dt))
    sw_rearrange(e)
    e.pdt = sbp("pdt", [128, DH * 256], F32)
    e.put = sbp("put", [128, DH * 256], F32)
    e.pS = sbp("pS", [128, DH * 256], F32)
    e.pSe = sbp("pSe", [128, DH], F32)
    e.brows = sbp("brows", [1, 16 * 256], F32)
    for u in range(14):
        sw_stage_prefix(e, u)
        block_prefix(e, u)
    pctx.close()

    e.brows2 = sb("brows2", [1, 16 * 256], F32)
    e.crows2 = sb("crows2", [1, 16 * 256], F32)
    e.odt = sb("odt", [128, DH * 256], F32)
    e.out_t = sb("out_t", [128, DH * 256], F32)
    e.ybuf = sb("ybuf", [128, DH * 256], F32)
    sw_stage_own(e, 0)
    block_fold_own(e, 0, do_fold=True)
    for o in range(1, 4):
        sw_stage_own(e, o)
        block_own(e, o)
    sw_spill_y(e, 3)
    if DEBUG:
        with nc.Block() as blk:
            blk.sync(lambda s: mamba_debug(e, s))
    mctx.close()


def sw_rearrange(e):
    nc = e.nc
    seng = nc.sync
    if "pid_s" not in e:
        e.pid_s = seng.partition_id()
    nrange = 18
    for c in nc.Switch(engines=[seng], index=[e.pid_s], n=8):
        cols = [bat * L + blk * 512 + hf * 256 for (bat, blk, hf) in prefix_units(c)]
        a0, a1 = own_cols(c)
        cols += [a0, a0 + 256, a1, a1 + 256]
        for r, col in enumerate(cols):
            I(e.sMS, seng.dma_start(out=e.d_dR[:, r * 256:(r + 1) * 256],
                                    in_=e.d_delta[:, col:col + 256]), 16)
            e.sMS.v -= 16
            I(e.sMS, seng.dma_start(out=e.d_uR[:, r * 256:(r + 1) * 256],
                                    in_=e.d_u[:, col:col + 256]), 16)
            e.sMS.v -= 16
            I(e.sMS, seng.dma_start(out=e.d_bcR[:, r * 256:(r + 1) * 256],
                                    in_=e.d_bc[:, col:col + 256]), 16)
            e.sMS.v -= 16
    e.sMS.v += 16 * 3 * nrange
    e.sw_excess["sMS"] = e.sw_excess.get("sMS", 0) + 7 * 16 * 3 * nrange
    e.ms_rearr = e.sMS.v


def sw_stage_prefix(e, u):
    nc = e.nc
    with nc.Block() as blk:
        def f(s):
            col = u * 256
            s.wait_ge(e.sMS.h, e.ms_rearr)
            for ci in range(DH):
                I(e.sMS, s.dma_start(out=e.pdt[:, ci * 256:(ci + 1) * 256],
                                     in_=e.d_dR[ci * 128:(ci + 1) * 128, col:col + 256]), 16)
                I(e.sMS, s.dma_start(out=e.put[:, ci * 256:(ci + 1) * 256],
                                     in_=e.d_uR[ci * 128:(ci + 1) * 128, col:col + 256]), 16)
            for n in range(NST):
                I(e.sMS, s.dma_start(out=e.brows[0:1, n * 256:(n + 1) * 256],
                                     in_=e.d_bcR[n:n + 1, col:col + 256]), 16)
        blk.sync(f)


def sw_stage_own(e, o):
    nc = e.nc
    with nc.Block() as blk:
        def f(s):
            col = (14 + o) * 256
            s.wait_ge(e.sMS.h, e.ms_rearr)
            for ci in range(DH):
                I(e.sMS, s.dma_start(out=e.odt[:, ci * 256:(ci + 1) * 256],
                                     in_=e.d_dR[ci * 128:(ci + 1) * 128, col:col + 256]), 16)
                I(e.sMS, s.dma_start(out=e.out_t[:, ci * 256:(ci + 1) * 256],
                                     in_=e.d_uR[ci * 128:(ci + 1) * 128, col:col + 256]), 16)
            for n in range(NST):
                I(e.sMS, s.dma_start(out=e.brows2[0:1, n * 256:(n + 1) * 256],
                                     in_=e.d_bcR[n:n + 1, col:col + 256]), 16)
                I(e.sMS, s.dma_start(out=e.crows2[0:1, n * 256:(n + 1) * 256],
                                     in_=e.d_bcR[16 + n:17 + n, col:col + 256]), 16)
            if o == 0:
                I(e.sMS, s.dma_start(out=e.cmeta_t[:, :], in_=e.cmeta), 16)
        blk.sync(f)
    if o > 0:
        sw_spill_y(e, o - 1)


def sw_spill_y(e, o):
    nc = e.nc
    beta, half = o // 2, o % 2
    with nc.Block() as blk:
        def f(s):
            dst = e.d_y.rearrange("(ci dl) t -> dl ci t", dl=128)[
                :, :, beta * 512 + half * 256: beta * 512 + half * 256 + 256]
            I(e.sMS, s.dma_start(out=dst[:, 0:8, :],
                                 in_=e.ybuf[:, 0:8 * 256].rearrange("p (ci t) -> p ci t", ci=8)), 16)
            I(e.sMS, s.dma_start(out=dst[:, 8:16, :],
                                 in_=e.ybuf[:, 8 * 256:16 * 256].rearrange("p (ci t) -> p ci t", ci=8)), 16)
        blk.sync(f)


def block_prefix(e, u):
    nc = e.nc
    with nc.Block() as blk:
        wb = {"pe": e.sPE.v, "act": e.sACT.v, "dve": e.sDVE.v}

        def fv(v):
            v.wait_ge(e.sMS.h, e.sMS.v)
            for ci in range(DH):
                W(v, e.sDVE)
                I(e.sDVE, v.tensor_tensor_scan(out=e.pS[:, ci * 256:(ci + 1) * 256],
                                               data0=e.pdt[:, ci * 256:(ci + 1) * 256],
                                               data1=e.zz[:, :], initial=0.0,
                                               op0=ALU.add, op1=ALU.add))
                W(v, e.sDVE)
                I(e.sDVE, v.tensor_copy(out=e.pSe[:, ci:ci + 1],
                                        in_=e.pS[:, ci * 256 + 255:ci * 256 + 256]))
                W(v, e.sDVE)
                I(e.sDVE, v.tensor_scalar(out=e.mscr[3][:, ci * 16:(ci + 1) * 16],
                                          in0=e.Apos[:, ci * NST:(ci + 1) * NST],
                                          scalar1=e.pSe[:, ci:ci + 1], scalar2=-1.0,
                                          op0=ALU.mult, op1=ALU.mult, accum_out=None))
            for n in range(NST):
                for ci in range(DH):
                    W(v, e.sDVE)
                    I(e.sDVE, v.tensor_scalar(out=e.mscr[0][:, :],
                                              in0=e.pS[:, ci * 256:(ci + 1) * 256],
                                              scalar1=e.pSe[:, ci:ci + 1],
                                              scalar2=e.Apos[:, ci * NST + n:ci * NST + n + 1],
                                              op0=ALU.subtract, op1=ALU.mult, accum_out=None))
                    v.wait_ge(e.sACT.h, wb["act"] + 16 + n * DH + ci + 1)
                    W(v, e.sDVE)
                    I(e.sDVE, v.tensor_mul(e.mscr[1][:, :], e.mscr[2][:, :],
                                           e.put[:, ci * 256:(ci + 1) * 256]))
                    v.wait_ge(e.sPE.h, wb["pe"] + n + 1)
                    W(v, e.sDVE)
                    I(e.sDVE, v.scalar_tensor_tensor(
                        out=e.mscr[0][:, :], in0=e.mscr[1][:, :], scalar=1.0,
                        in1=e.B[5 + n % 2][0:128, 0:256], op0=ALU.mult, op1=ALU.mult,
                        accum_out=e.Hloc[:, u * 256 + ci * 16 + n:u * 256 + ci * 16 + n + 1]))

        def fs(s):
            for ci in range(DH):
                s.wait_ge(e.sDVE.h, wb["dve"] + 3 * (ci + 1))
                W(s, e.sACT)
                I(e.sACT, s.activation(out=e.Pblk[:, u * 256 + ci * 16:u * 256 + (ci + 1) * 16],
                                       in_=e.mscr[3][:, ci * 16:(ci + 1) * 16], func=AF.Exp))
            for n in range(NST):
                for ci in range(DH):
                    s.wait_ge(e.sDVE.h, wb["dve"] + 48 + (n * DH + ci) * 3 + 1)
                    W(s, e.sACT)
                    I(e.sACT, s.activation(out=e.mscr[2][:, :], in_=e.mscr[0][:, :], func=AF.Exp))

        def ft(t):
            t.wait_ge(e.sMS.h, e.sMS.v)
            for n in range(NST):
                if n >= 2:
                    t.wait_ge(e.sDVE.h, wb["dve"] + 48 + ((n - 2) * DH + DH - 1) * 3 + 3)
                I(e.sPE, t.matmul(e.B[5 + n % 2][0:128, 0:256], e.ones1f[:, :],
                                  e.brows[0:1, n * 256:(n + 1) * 256], start=True, stop=True))

        blk.tensor(ft)
        blk.scalar(fs)
        blk.vector(fv)


def block_fold_own(e, o, do_fold=False):
    nc = e.nc
    with nc.Block() as blk:
        wb = {"pe": e.sPE.v, "act": e.sACT.v, "dve": e.sDVE.v}

        def fv(v):
            v.wait_ge(e.sMS.h, e.sMS.v)
            if do_fold:
                for bf in range(2):
                    dst = e.Hinit[:, bf * 256:(bf + 1) * 256]
                    W(v, e.sDVE)
                    I(e.sDVE, v.memset(dst, 0.0))
                    for j in range(14):
                        sl = slice(j * 256, (j + 1) * 256)
                        mcol = e.cmeta_t[:, bf * 14 + j:bf * 14 + j + 1]
                        W(v, e.sDVE)
                        I(e.sDVE, v.tensor_scalar(out=e.mscr[0][:, 0:256], in0=e.Pblk[:, sl],
                                                  scalar1=1.0, scalar2=None, op0=ALU.subtract,
                                                  op1=ALU.bypass, accum_out=None))
                        W(v, e.sDVE)
                        I(e.sDVE, v.tensor_scalar(out=e.mscr[0][:, 0:256], in0=e.mscr[0][:, 0:256],
                                                  scalar1=mcol, scalar2=1.0, op0=ALU.mult,
                                                  op1=ALU.add, accum_out=None))
                        W(v, e.sDVE)
                        I(e.sDVE, v.tensor_scalar(out=e.mscr[1][:, 0:256], in0=e.Hloc[:, sl],
                                                  scalar1=mcol, scalar2=None, op0=ALU.mult,
                                                  op1=ALU.bypass, accum_out=None))
                        W(v, e.sDVE)
                        I(e.sDVE, v.tensor_mul(dst, dst, e.mscr[0][:, 0:256]))
                        W(v, e.sDVE)
                        I(e.sDVE, v.tensor_add(dst, dst, e.mscr[1][:, 0:256]))
            own_scan_vec(e, v, o, wb)

        def fs(s):
            own_scan_act(e, s, o, wb, extra_dve=2 * (1 + 14 * 5) if do_fold else 0)

        def ft(t):
            own_bc_mms(e, t, o, wb)

        blk.tensor(ft)
        blk.scalar(fs)
        blk.vector(fv)


def block_own(e, o):
    nc = e.nc
    with nc.Block() as blk:
        wb = {"pe": e.sPE.v, "act": e.sACT.v, "dve": e.sDVE.v}

        def fv(v):
            v.wait_ge(e.sMS.h, e.sMS.v)
            own_scan_vec(e, v, o, wb)

        def fs(s):
            own_scan_act(e, s, o, wb, extra_dve=0)

        def ft(t):
            own_bc_mms(e, t, o, wb)

        blk.tensor(ft)
        blk.scalar(fs)
        blk.vector(fv)


def own_bc_mms(e, t, o, wb):
    t.wait_ge(e.sMS.h, e.sMS.v)
    fold_dve = 2 * (1 + 14 * 5) if o == 0 else 0
    per = 6 if o % 2 == 0 else 5
    for n in range(NST):
        if n >= 2:
            t.wait_ge(e.sDVE.h, wb["dve"] + fold_dve + ((n - 2) * DH + DH) * per)
        I(e.sPE, t.matmul(e.B[4 + 2 * (n % 2)][0:128, 0:256], e.ones1f[:, :],
                          e.brows2[0:1, n * 256:(n + 1) * 256], start=True, stop=True))
        I(e.sPE, t.matmul(e.B[5 + 2 * (n % 2)][0:128, 0:256], e.ones1f[:, :],
                          e.crows2[0:1, n * 256:(n + 1) * 256], start=True, stop=True))


def own_scan_vec(e, v, o, wb):
    beta, half = o // 2, o % 2
    for n in range(NST):
        for ci in range(DH):
            hm = (half == 0)
            W(v, e.sDVE)
            I(e.sDVE, v.tensor_scalar(out=e.mscr[0][:, 0:256],
                                      in0=e.odt[:, ci * 256:(ci + 1) * 256],
                                      scalar1=e.Apos[:, ci * NST + n:ci * NST + n + 1],
                                      scalar2=-1.0, op0=ALU.mult, op1=ALU.mult,
                                      accum_out=None))
            v.wait_ge(e.sACT.h, wb["act"] + n * DH + ci + 1)
            v.wait_ge(e.sPE.h, wb["pe"] + 2 * (n + 1))
            W(v, e.sDVE)
            I(e.sDVE, v.tensor_mul(e.mscr[1][:, 0:256], e.out_t[:, ci * 256:(ci + 1) * 256],
                                   e.B[4 + 2 * (n % 2)][0:128, 0:256]))
            W(v, e.sDVE)
            init = (e.Hinit[:, beta * 256 + ci * 16 + n:beta * 256 + ci * 16 + n + 1]
                    if half == 0 else e.Hmid[:, ci * 16 + n:ci * 16 + n + 1])
            I(e.sDVE, v.tensor_tensor_scan(out=e.mscr[0][:, 0:256], data0=e.mscr[2][:, 0:256],
                                           data1=e.mscr[1][:, 0:256], initial=init,
                                           op0=ALU.mult, op1=ALU.add))
            if hm:
                W(v, e.sDVE)
                I(e.sDVE, v.tensor_copy(out=e.Hmid[:, ci * 16 + n:ci * 16 + n + 1],
                                        in_=e.mscr[0][:, 255:256]))
            W(v, e.sDVE)
            I(e.sDVE, v.tensor_mul(e.mscr[1][:, 0:256], e.mscr[0][:, 0:256],
                                   e.B[5 + 2 * (n % 2)][0:128, 0:256]))
            W(v, e.sDVE)
            if n == 0:
                I(e.sDVE, v.tensor_copy(out=e.ybuf[:, ci * 256:(ci + 1) * 256],
                                        in_=e.mscr[1][:, 0:256]))
            else:
                I(e.sDVE, v.tensor_add(e.ybuf[:, ci * 256:(ci + 1) * 256],
                                       e.ybuf[:, ci * 256:(ci + 1) * 256], e.mscr[1][:, 0:256]))


def own_scan_act(e, s, o, wb, extra_dve=0):
    beta, half = o // 2, o % 2
    per = 6 if half == 0 else 5
    for n in range(NST):
        for ci in range(DH):
            idx = n * DH + ci
            s.wait_ge(e.sDVE.h, wb["dve"] + extra_dve + idx * per + 1)
            W(s, e.sACT)
            I(e.sACT, s.activation(out=e.mscr[2][:, 0:256], in_=e.mscr[0][:, 0:256],
                                   func=AF.Exp))


def mamba_debug(e, s):
    s.wait_ge(e.sMS.h, e.sMS.v)
    I(e.sST, s.dma_start(out=e.dbg["dbg_y"], in_=e.d_y), 16)
    I(e.sST, s.dma_start(out=e.dbg["dbg_hloc"].rearrange("(ci dl) n -> dl ci n", dl=128),
                         in_=e.Hloc[:, 0:256].rearrange("p (ci n) -> p ci n", ci=DH)), 16)
    for b_ in range(2):
        I(e.sST, s.dma_start(
            out=e.dbg["dbg_hinit"][:, b_ * NST:(b_ + 1) * NST].rearrange(
                "(ci dl) n -> dl ci n", dl=128),
            in_=e.Hinit[:, b_ * 256:(b_ + 1) * 256].rearrange("p (ci n) -> p ci n", ci=DH)), 16)
        e.sST.v -= 16
    e.sST.v += 32
    s.wait_ge(e.sST.h, e.sST.v)



ATT_PAIRS = [(b, k) for b in range(2) for k in range(KVH)]


def att_E(c, beta):
    return 4 * (c + 1) if beta == 0 else 4 * (8 - c)


def act_pairh_base(c, pair, h):
    off = 2
    for p2 in range(pair):
        off += 4 * (att_E(c, ATT_PAIRS[p2][0]) + 4)
    off += h * (att_E(c, ATT_PAIRS[pair][0]) + 4)
    return off


class SwBook:

    def __init__(self, e, names):
        self.e = e
        self.names = names
        self.base = {}

    def enter_body(self):
        for n in self.names:
            c = self.e[n]
            if n not in self.base:
                self.base[n] = c.v
            c.v = self.base[n]

    def finish(self):
        for n in self.names:
            c = self.e[n]
            per = c.v - self.base[n]
            self.e.sw_excess[n] = self.e.sw_excess.get(n, 0) + 7 * per


def emit_attention(e):
    nc = e.nc
    if "sw_excess" not in e:
        e.sw_excess = {}
    e.avctx = ExitStack()
    e.av_sb = e.avctx.enter_context(
        nc.sbuf_tensor("av_sb", [128, NH * 1024], BF16))
    actx = ExitStack()

    def sb(name, shape, dt):
        return actx.enter_context(nc.sbuf_tensor(name, shape, dt))

    e.qin = sb("qin", [128, DH * 1024], BF16)
    e.q_sb = sb("q_sb", [128, NH * 1024], BF16)
    e.kbufA = sb("kbufA", [128, 32 * 128], BF16)
    e.vbufA = sb("vbufA", [128, 32 * 128], BF16)
    e.wq_sb = sb("wq_sb", [128, 2 * DH * 128], BF16)
    e.probs = [sb(f"probs{i}", [128, 512], BF16) for i in range(2)]
    e.avf = sb("avf", [128, 512], F32)
    e.q2b = sb("q2b", [128, 1024], BF16)
    e.bias_col = sb("bias_col", [128, 1], F32)
    e.srow = sb("srow", [1, 512], F32)
    e.trow = sb("trow", [1, 8], F32)

    teng, seng, veng, geng, speng = nc.tensor, nc.scalar, nc.vector, nc.gpsimd, nc.sync
    if "pid_t" not in e:
        e.pid_t = teng.partition_id()
        e.pid_sc = seng.partition_id()
        e.pid_v = veng.partition_id()
        e.pid_g2 = geng.partition_id()
        e.pid_sp2 = speng.partition_id()
    book = SwBook(e, ["sPE", "sACT", "sDVE", "sPOOL", "sQS", "sKV", "sWQ0", "sWQ1"])
    for c in nc.Switch(engines=[teng, seng, veng, geng, speng],
                       index=[e.pid_t, e.pid_sc, e.pid_v, e.pid_g2, e.pid_sp2], n=8):
        book.enter_body()
        e.ab = dict(book.base)
        attn_body(e, c, teng, seng, veng, geng, speng)
    book.finish()

    if DEBUG:
        with nc.Block() as blk:
            blk.gpsimd(lambda s: attn_debug(e, s))
    actx.close()


def attn_body(e, c, t, s, v, g, sp):
    A0, A1 = own_cols(c)
    ab = e.ab
    peq = lambda hq, k: ab["sPE"] + 2 * hq + k
    peqn = lambda hq: ab["sPE"] + 32 + hq + 1
    pebias = ab["sPE"] + 49
    dveq = lambda hq, ib: ab["sDVE"] + 12 * hq + 6 * (ib + 1)
    dveqn = lambda hq: ab["sDVE"] + 192 + 2 * (hq + 1)
    dveu = ab["sDVE"] + 226
    dveneg = ab["sDVE"] + 227
    dveav = lambda pair, h: ab["sDVE"] + 227 + pair * 4 + h + 1
    actexp = lambda pair, h, kc: ab["sACT"] + act_pairh_base(c, pair, h) + kc + 1
    actsum = lambda pair, h: (ab["sACT"] + act_pairh_base(c, pair, h)
                              + att_E(c, ATT_PAIRS[pair][0]) + 1)
    actrec = lambda pair, h: actsum(pair, h) + 2
    actavc = lambda pair, h: actsum(pair, h) + 3
    poolq2 = lambda hq: ab["sPOOL"] + hq + 1
    poolaff = lambda pair, h, j: ab["sPOOL"] + 16 + (pair * 4 + h) * 4 + j + 1

    pe_off = {}

    def emit_sumav(pair, h, kc, beta):
        E = att_E(c, beta)
        t.wait_ge(e.sACT.h, actexp(pair, h, kc))
        if kc >= E - 4:
            t.wait_ge(e.sPOOL.h, poolaff(pair, h, kc - (E - 4)))
        t.matmul(e.B[3][0:1, :], e.ones_bf[:, :], e.probs[kc % 2][:, :],
                 start=(kc == 0), stop=(kc == E - 1))
        ins = t.matmul(e.B[2][:, :], e.vbufA[:, kc * 128:(kc + 1) * 128],
                       e.probs[kc % 2][:, :], start=(kc == 0), stop=(kc == E - 1))
        I(e.sPE, ins)
        pe_off[("av", pair, h, kc)] = e.sPE.v

    def emit_tensor():
        t.wait_ge(e.sQS.h, ab["sQS"] + 16 * 4)
        for hq in range(NH):
            t.wait_ge(e["sWQ%d" % (hq % 2)].h, ab["sWQ%d" % (hq % 2)] + 32 * (hq // 2 + 1))
            for grp in range(2):
                if hq >= 1:
                    t.wait_ge(e.sDVE.h, dveq(hq - 1, grp))
                for dh in range(DH):
                    ins = t.matmul(
                        e.B[grp][:, :],
                        e.wq_sb[:, ((hq % 2) * DH + dh) * 128:((hq % 2) * DH + dh + 1) * 128],
                        e.qin[:, dh * 1024 + grp * 512:dh * 1024 + (grp + 1) * 512],
                        start=(dh == 0), stop=(dh == DH - 1))
                I(e.sPE, ins)
        for hq in range(NH):
            t.wait_ge(e.sPOOL.h, poolq2(hq))
            if hq >= 1:
                t.wait_ge(e.sDVE.h, dveqn(hq - 1))
            t.matmul(e.B[5][0:1, :], e.ones_bf[:, :], e.q2b[:, 0:512],
                     start=True, stop=False)
            I(e.sPE, t.matmul(e.B[5][0:1, :], e.ones_bf[:, :], e.q2b[:, 512:1024],
                              start=False, stop=True))
        t.wait_ge(e.sACT.h, ab["sACT"] + 2)
        I(e.sPE, t.matmul(e.B[4][:, 0:1], e.ones1f[:, :], e.trow[0:1, 3:4],
                          start=True, stop=True))
        t.wait_ge(e.sDVE.h, ab["sDVE"] + 192)
        for pair, (beta, kvh) in enumerate(ATT_PAIRS):
            E = att_E(c, beta)
            t.wait_ge(e.sKV.h, ab["sKV"] + 32 * (pair + 1))
            for h in range(4):
                hglob = kvh * 4 + h
                qsl = e.q_sb[:, hglob * 1024 + beta * 512:hglob * 1024 + (beta + 1) * 512]
                if h == 0 and pair == 0:
                    t.wait_ge(e.sDVE.h, dveneg)
                else:
                    ph, hh = (pair, h - 1) if h > 0 else (pair - 1, 3)
                    t.wait_ge(e.sACT.h, actavc(ph, hh))
                for kc in range(E):
                    if kc >= 2:
                        t.wait_ge(e.sACT.h, actexp(pair, h, kc - 2))
                    ins = t.matmul(e.B[kc % 2][:, :], e.kbufA[:, kc * 128:(kc + 1) * 128],
                                   qsl, start=True, stop=True)
                    I(e.sPE, ins)
                    pe_off[("score", pair, h, kc)] = e.sPE.v
                    if kc >= 1:
                        emit_sumav(pair, h, kc - 1, beta)
                emit_sumav(pair, h, E - 1, beta)
                t.wait_ge(e.sACT.h, actrec(pair, h))
                if pair == 0 and h == 0:
                    t.wait_ge(e.sDVE.h, dveneg)
                else:
                    ph, hh = (pair, h - 1) if h > 0 else (pair - 1, 3)
                    t.wait_ge(e.sDVE.h, dveav(ph, hh))
                I(e.sPE, t.matmul(e.B[4][:, :], e.ones1f[:, :], e.srow[0:1, :],
                                  start=True, stop=True))
                pe_off[("recip", pair, h)] = e.sPE.v

    emit_tensor()

    sp.wait_ge(e.sMS.h, e.sMS.v)
    sp.wait_ge(e.sDVE.h, ab["sDVE"])
    sp.wait_ge(e.sPE.h, ab["sPE"])
    sp.wait_ge(e.sACT.h, ab["sACT"])
    for half in range(2):
        for ib, a in enumerate((A0, A1)):
            I(e.sQS, sp.dma_start(
                out=e.qin[:, :].rearrange("p (dh tk) -> p dh tk", dh=DH)[
                    :, half * 8:(half + 1) * 8, ib * 512:(ib + 1) * 512],
                in_=e.d_xn1.rearrange("(dh dl) t -> dl dh t", dl=128)[
                    :, half * 8:(half + 1) * 8, a:a + 512]), 16)
    for pair, (beta, kvh) in enumerate(ATT_PAIRS):
        E = att_E(c, beta)
        kb = beta * L
        if pair >= 1:
            sp.wait_ge(e.sPE.h, pe_off[("recip", pair - 1, 3)])
        I(e.sKV, sp.dma_start(out=e.kbufA[:, 0:E * 128],
                              in_=e.d_k[kvh, :, kb:kb + E * 128]), 16)
        I(e.sKV, sp.dma_start(
            out=e.vbufA[:, 0:E * 128].rearrange("p (kc x) -> p kc x", kc=E),
            in_=e.d_v[kb:kb + E * 128, kvh * 128:(kvh + 1) * 128].rearrange(
                "(kc p) x -> p kc x", p=128)), 16)

    g.wait_ge(e.sMS.h, e.sMS.v)
    g.wait_ge(e.sDVE.h, ab["sDVE"])
    g.wait_ge(e.sPE.h, ab["sPE"])
    g.wait_ge(e.sACT.h, ab["sACT"])
    for hq in range(NH):
        if hq >= 2:
            g.wait_ge(e.sPE.h, peq(hq - 2, 2))
        for half in range(2):
            I(e["sWQ%d" % (hq % 2)], g.dma_start(
                out=e.wq_sb[:, ((hq % 2) * DH + half * 8) * 128:
                            ((hq % 2) * DH + (half + 1) * 8) * 128].rearrange(
                                "p (dh x) -> p dh x", dh=8),
                in_=e.wqh[hq, :, half * 8:(half + 1) * 8, :]), 16)
    for hq in range(NH):
        g.wait_ge(e.sDVE.h, dveq(hq, 1))
        if hq >= 1:
            g.wait_ge(e.sPE.h, peqn(hq - 1))
        I(e.sPOOL, g.tensor_tensor(out=e.q2b[:, :], in0=e.q_sb[:, hq * 1024:(hq + 1) * 1024],
                                   in1=e.q_sb[:, hq * 1024:(hq + 1) * 1024], op=ALU.mult))
    for pair, (beta, kvh) in enumerate(ATT_PAIRS):
        E = att_E(c, beta)
        qposl = 512 * c if beta == 0 else 512 * (7 - c)
        for h in range(4):
            for j in range(4):
                kc = E - 4 + j
                g.wait_ge(e.sACT.h, actexp(pair, h, kc))
                I(e.sPOOL, g.affine_select(
                    out=e.probs[kc % 2][:, :], in_=e.probs[kc % 2][:, :],
                    pattern=[[1, 512]], compare_op=ALU.is_ge, fill=0.0,
                    base=qposl - kc * 128, channel_multiplier=-1))

    s.wait_ge(e.sDVE.h, dveu)
    I(e.sACT, s.activation(out=e.trow[0:1, 2:3], in_=e.trow[0:1, 0:1], func=AF.Ln))
    W(s, e.sACT)
    I(e.sACT, s.activation(out=e.trow[0:1, 3:4], in_=e.trow[0:1, 2:3], func=AF.Exp,
                           scale=0.5))
    for pair, (beta, kvh) in enumerate(ATT_PAIRS):
        E = att_E(c, beta)
        for h in range(4):
            for kc in range(E):
                s.wait_ge(e.sPE.h, pe_off[("score", pair, h, kc)])
                if kc >= 2:
                    s.wait_ge(e.sPE.h, pe_off[("av", pair, h, kc - 2)])
                if pair == 0 and h == 0 and kc == 0:
                    s.wait_ge(e.sDVE.h, dveneg)
                I(e.sACT, s.activation(out=e.probs[kc % 2][:, :], in_=e.B[kc % 2][:, :],
                                       func=AF.Exp, scale=1.0 / math.sqrt(128.0),
                                       bias=e.bias_col[:, 0:1]))
            s.wait_ge(e.sPE.h, pe_off[("av", pair, h, E - 1)])
            I(e.sACT, s.copy(out=e.srow[:, :], in_=e.B[3][0:1, :]))
            W(s, e.sACT)
            I(e.sACT, s.activation(out=e.scr[2][0:1, 0:512], in_=e.srow[:, :], func=AF.Ln))
            W(s, e.sACT)
            I(e.sACT, s.activation(out=e.srow[:, :], in_=e.scr[2][0:1, 0:512], func=AF.Exp,
                                   scale=-1.0))
            if pair > 0 or h > 0:
                ph, hh = (pair, h - 1) if h > 0 else (pair - 1, 3)
                s.wait_ge(e.sDVE.h, dveav(ph, hh))
            I(e.sACT, s.copy(out=e.avf[:, :], in_=e.B[2][:, :]))

    for hq in range(NH):
        for ib in range(2):
            v.wait_ge(e.sPE.h, peq(hq, ib + 1))
            a = (A0, A1)[ib]
            b = e.B[ib]
            cs = e.cosr[:, a:a + 512]
            sn = e.sinr[:, a:a + 512]
            dst = e.q_sb[:, hq * 1024 + ib * 512:hq * 1024 + (ib + 1) * 512]
            W(v, e.sDVE)
            I(e.sDVE, v.tensor_mul(e.scr[0][0:64, :], b[0:64, :], cs))
            I(e.sDVE, v.tensor_mul(e.scr[1][0:64, :], b[64:128, :], sn))
            W(v, e.sDVE)
            I(e.sDVE, v.tensor_sub(dst[0:64, :], e.scr[0][0:64, :], e.scr[1][0:64, :]))
            W(v, e.sDVE)
            I(e.sDVE, v.tensor_mul(e.scr[0][0:64, :], b[64:128, :], cs))
            I(e.sDVE, v.tensor_mul(e.scr[1][0:64, :], b[0:64, :], sn))
            W(v, e.sDVE)
            I(e.sDVE, v.tensor_add(dst[64:128, :], e.scr[0][0:64, :], e.scr[1][0:64, :]))
    for hq in range(NH):
        v.wait_ge(e.sPE.h, peqn(hq))
        W(v, e.sDVE)
        I(e.sDVE, v.reduce_max(e.scr[2][0:1, 0:1], e.B[5][0:1, :], axis=AXX))
        W(v, e.sDVE)
        if hq == 0:
            I(e.sDVE, v.tensor_copy(out=e.trow[0:1, 4:5], in_=e.scr[2][0:1, 0:1]))
        else:
            I(e.sDVE, v.tensor_max(e.trow[0:1, 4:5], e.trow[0:1, 4:5], e.scr[2][0:1, 0:1]))
    W(v, e.sDVE)
    I(e.sDVE, v.reduce_max(e.scr[2][0:1, 0:1], e.kmax2[:, :], axis=AXX))
    W(v, e.sDVE)
    I(e.sDVE, v.scalar_tensor_tensor(out=e.trow[0:1, 0:1], in0=e.trow[0:1, 4:5],
                                     scalar=1.0 / 128.0, in1=e.scr[2][0:1, 0:1],
                                     op0=ALU.mult, op1=ALU.mult))
    v.wait_ge(e.sPE.h, pebias)
    W(v, e.sDVE)
    I(e.sDVE, v.tensor_scalar_mul(e.bias_col[:, :], e.B[4][:, 0:1], -1.0))
    for pair, (beta, kvh) in enumerate(ATT_PAIRS):
        for h in range(4):
            hglob = kvh * 4 + h
            v.wait_ge(e.sACT.h, actavc(pair, h))
            v.wait_ge(e.sPE.h, pe_off[("recip", pair, h)])
            W(v, e.sDVE)
            I(e.sDVE, v.tensor_mul(
                e.av_sb[:, hglob * 1024 + beta * 512:hglob * 1024 + (beta + 1) * 512],
                e.avf[:, :], e.B[4][:, :]))


def attn_debug(e, s):
    s.wait_ge(e.sDVE.h, e.sDVE.v)
    for half in range(2):
        I(e.sGLD, s.dma_start(
            out=e.dbg["dbg_av"].rearrange("(hq dl) t -> dl hq t", dl=128)[
                :, half * 8:(half + 1) * 8, :],
            in_=e.av_sb[:, half * 8 * 1024:(half + 1) * 8 * 1024].rearrange(
                "p (hq t) -> p hq t", hq=8)), 16)
        I(e.sGLD, s.dma_start(
            out=e.dbg["dbg_q"].rearrange("(hq dl) t -> dl hq t", dl=128)[
                :, half * 8:(half + 1) * 8, :],
            in_=e.q_sb[:, half * 8 * 1024:(half + 1) * 8 * 1024].rearrange(
                "p (hq t) -> p hq t", hq=8)), 16)
    s.wait_ge(e.sGLD.h, e.sGLD.v)



def emit_h2(e):
    nc = e.nc
    h2ctx = ExitStack()

    def sb(name, shape, dt):
        return h2ctx.enter_context(nc.sbuf_tensor(name, shape, dt))

    e.xbuf2 = sb("xbuf2", [128, DH * 1024], F32)
    e.wo_sb = sb("wo_sb", [128, 2 * DH * 128], BF16)
    e.rows_sb = sb("rows_sb", [1, 8 * 512], F32)
    e.mr_sb = sb("mr_sb", [128, 4 * 512], F32)
    e.gw_sb = sb("gw_sb", [128, DH * NE], F32)
    e.id128 = sb("id128", [128, 128], F32)
    e.xnt = [sb(f"xnt{i}", [128, 1024], BF16) for i in range(2)]
    e.wch = sb("wch", [128, 64], F32)
    e.logacc = sb("logacc", [128, 64], F32)
    e.wr_sb = sb("wr_sb", [8, 1024], F32)

    hb = {nm: e[nm].v for nm in
          ["sPE", "sACT", "sDVE", "sPOOL", "sXB", "sWO0", "sWO1", "sH2",
           "sXN0", "sXN1", "sSP"]}
    e.h2b = hb
    A = lambda ci, tg, k: hb["sDVE"] + (ci * 2 + tg) * 4 + k
    Bv = lambda ci, tg, k: hb["sDVE"] + 128 + ci * 7 + tg * 3 + k
    accL = lambda ci: hb["sDVE"] + 128 + ci * 7 + 7
    T = lambda ch, k: hb["sDVE"] + 240 + ch * 10 + k
    actmr = hb["sACT"] + 4
    actexp = lambda tg: hb["sACT"] + 6 + tg + 1
    actsig = lambda ch: hb["sACT"] + 8 + ch + 1
    actwr = hb["sACT"] + 18

    seng = nc.sync
    if "pid_s" not in e:
        e.pid_s = seng.partition_id()
    book = SwBook(e, ["sXB"])
    for c in nc.Switch(engines=[seng], index=[e.pid_s], n=8):
        book.enter_body()
        A0, A1 = own_cols(c)
        for nm in ["sDVE", "sPE", "sACT", "sPOOL"]:
            seng.wait_ge(e[nm].h, hb[nm])
        for half in range(2):
            for ib, a in enumerate((A0, A1)):
                I(e.sXB, seng.dma_start(
                    out=e.xbuf2[:, :].rearrange("p (ci t) -> p ci t", ci=DH)[
                        :, half * 8:(half + 1) * 8, ib * 512:(ib + 1) * 512],
                    in_=e.xT.rearrange("(ci dl) t -> dl ci t", dl=128)[
                        :, half * 8:(half + 1) * 8, a:a + 512]), 16)
        for r in range(2):
            for ib, a in enumerate((A0, A1)):
                I(e.sXB, seng.dma_start(
                    out=e.rows_sb[0:1, (r * 2 + ib) * 512:(r * 2 + ib + 1) * 512],
                    in_=e.d_rows[r:r + 1, a:a + 512]), 16)
    book.finish()
    sxb_full = hb["sXB"] + 128

    e.h2pe = {}

    with nc.Block() as blk:
        blk.tensor(lambda t: h2_tensor(e, t, hb, A, Bv, accL, T, actmr, actexp, sxb_full))
        blk.scalar(lambda s: h2_scalar(e, s, hb, T))
        blk.vector(lambda v: h2_vector(e, v, hb, A, Bv, T, actmr, actsig, sxb_full))
        blk.gpsimd(lambda g: h2_gpsimd(e, g, hb))
        blk.sync(lambda s: h2_sync(e, s, hb, A, Bv, actwr))

    if DEBUG:
        with nc.Block() as blk2:
            blk2.sync(lambda s: h2_debug_s(e, s))
            blk2.gpsimd(lambda g: h2_debug_g(e, g))
    h2ctx.close()


def h2_tensor(e, t, hb, A, Bv, accL, T, actmr, actexp, sxb_full):
    pe = e.h2pe
    t.wait_ge(e.sDVE.h, hb["sDVE"])
    t.wait_ge(e.sACT.h, hb["sACT"])
    t.wait_ge(e.sPOOL.h, hb["sPOOL"])
    t.wait_ge(e.sXB.h, sxb_full)
    for i in range(4):
        I(e.sPE, t.matmul(e.B[2 + i][:, :], e.ones1f[:, :],
                          e.rows_sb[0:1, i * 512:(i + 1) * 512],
                          start=True, stop=True))
        pe[("mr", i)] = e.sPE.v
    for ci in range(DH):
        for tg in range(2):
            j = ci * 2 + tg
            sw = e["sWO%d" % (ci % 2)]
            t.wait_ge(sw.h, hb["sWO%d" % (ci % 2)] + 32 * (ci // 2 + 1))
            if j >= 2:
                pc, pt = (j - 2) // 2, (j - 2) % 2
                t.wait_ge(e.sDVE.h, A(pc, pt, 3))
            for hq in range(DH):
                ins = t.matmul(
                    e.B[j % 2][:, :],
                    e.wo_sb[:, ((ci % 2) * DH + hq) * 128:((ci % 2) * DH + hq + 1) * 128],
                    e.av_sb[:, hq * 1024 + tg * 512:hq * 1024 + (tg + 1) * 512],
                    start=(hq == 0), stop=(hq == DH - 1))
            I(e.sPE, ins)
            pe[("wo", ci, tg)] = e.sPE.v
        if ci >= 1:
            for tg in range(2):
                t.wait_ge(e.sDVE.h, A(ci - 1, tg, 4))
                I(e.sPE, t.matmul(e.B[2 + tg][0:1, :], e.cst[:, 1:2],
                                  e.scr[2 + tg][:, :],
                                  start=(ci - 1 == 0), stop=False))
                pe[("v2", ci - 1, tg)] = e.sPE.v
    for tg in range(2):
        t.wait_ge(e.sDVE.h, A(15, tg, 4))
        I(e.sPE, t.matmul(e.B[2 + tg][0:1, :], e.cst[:, 1:2], e.scr[2 + tg][:, :],
                          start=False, stop=True))
        pe[("v2", 15, tg)] = e.sPE.v
    for tg in range(2):
        t.wait_ge(e.sACT.h, actexp(tg))
        t.wait_ge(e.sDVE.h, A(15, tg, 3))
        I(e.sPE, t.matmul(e.B[tg][:, :], e.ones1f[:, :],
                          e.rows_sb[0:1, (6 + tg) * 512:(7 + tg) * 512],
                          start=True, stop=True))
        pe[("rbc", tg)] = e.sPE.v
    t.wait_ge(e.sSP.h, hb["sSP"] + 16)
    for ci in range(DH):
        for ch in range(8):
            t.wait_ge(e.sDVE.h, Bv(ci, ch // 4, 2))
            if ci >= 1 and ch == 0:
                t.wait_ge(e.sDVE.h, accL(ci - 1))
            I(e.sPE, t.matmul(e.B[4][:, ch * 8:(ch + 1) * 8],
                              e.scr[ch // 4][:, (ch % 4) * 128:(ch % 4 + 1) * 128],
                              e.gw_sb[:, ci * 8:(ci + 1) * 8],
                              start=True, stop=True))
            pe[("gate", ci, ch)] = e.sPE.v
    t.wait_ge(e.sPOOL.h, hb["sPOOL"] + 2)
    for ch in range(8):
        t.wait_ge(e.sDVE.h, T(ch, 10))
        I(e.sPE, t.matmul(e.B[5 + ch // 4][0:8, (ch % 4) * 128:(ch % 4 + 1) * 128],
                          e.wch[:, ch * 8:(ch + 1) * 8], e.id128[:, :],
                          start=True, stop=True))
        pe[("tr", ch)] = e.sPE.v


def h2_scalar(e, s, hb, T):
    pe = e.h2pe
    for i in range(4):
        s.wait_ge(e.sPE.h, pe[("mr", i)])
        I(e.sACT, s.copy(out=e.mr_sb[:, i * 512:(i + 1) * 512], in_=e.B[2 + i][:, :]))
    for tg in range(2):
        s.wait_ge(e.sPE.h, pe[("v2", 15, tg)])
        I(e.sACT, s.activation(out=e.rows_sb[0:1, (4 + tg) * 512:(5 + tg) * 512],
                               in_=e.B[2 + tg][0:1, :], func=AF.Ln,
                               scale=1.0 / float(D), bias=EPS))
    for tg in range(2):
        W(s, e.sACT)
        I(e.sACT, s.activation(out=e.rows_sb[0:1, (6 + tg) * 512:(7 + tg) * 512],
                               in_=e.rows_sb[0:1, (4 + tg) * 512:(5 + tg) * 512],
                               func=AF.Exp, scale=-0.5))
    for ch in range(8):
        s.wait_ge(e.sDVE.h, T(ch, 5))
        I(e.sACT, s.activation(out=e.scr[4][:, 5:6], in_=e.scr[4][:, 4:5],
                               func=AF.Sigmoid))
    for i in range(2):
        s.wait_ge(e.sPE.h, pe[("tr", i * 4 + 3)])
        I(e.sACT, s.copy(out=e.wr_sb[:, i * 512:(i + 1) * 512],
                         in_=e.B[5 + i][0:8, :]))


def h2_vector(e, v, hb, A, Bv, T, actmr, actsig, sxb_full):
    pe = e.h2pe
    v.wait_ge(e.sXB.h, sxb_full)
    v.wait_ge(e.sACT.h, actmr)
    for ci in range(DH):
        for tg in range(2):
            xt = e.xbuf2[:, ci * 1024 + tg * 512:ci * 1024 + (tg + 1) * 512]
            W(v, e.sDVE)
            I(e.sDVE, v.tensor_sub(xt, xt, e.mr_sb[:, tg * 512:(tg + 1) * 512]))
            W(v, e.sDVE)
            I(e.sDVE, v.tensor_mul(xt, xt, e.mr_sb[:, (2 + tg) * 512:(3 + tg) * 512]))
            v.wait_ge(e.sPE.h, pe[("wo", ci, tg)])
            W(v, e.sDVE)
            I(e.sDVE, v.tensor_add(xt, xt, e.B[(ci * 2 + tg) % 2][:, :]))
            if ci >= 1:
                v.wait_ge(e.sPE.h, pe[("v2", ci - 1, tg)])
            W(v, e.sDVE)
            I(e.sDVE, v.tensor_mul(e.scr[2 + tg][:, :], xt, xt))
    for ci in range(DH):
        for tg in range(2):
            xt = e.xbuf2[:, ci * 1024 + tg * 512:ci * 1024 + (tg + 1) * 512]
            v.wait_ge(e.sPE.h, pe[("rbc", tg)])
            if ci >= 1:
                v.wait_ge(e.sPE.h, pe[("gate", ci - 1, 7)])
            W(v, e.sDVE)
            I(e.sDVE, v.tensor_mul(e.scr[tg][:, :], xt, e.B[tg][:, :]))
            W(v, e.sDVE)
            I(e.sDVE, v.tensor_scalar(out=e.scr[tg][:, :], in0=e.scr[tg][:, :],
                                      scalar1=e.n2w_c[:, ci:ci + 1], scalar2=None,
                                      op0=ALU.mult, op1=ALU.bypass, accum_out=None))
            if ci >= 2:
                sx = e["sXN%d" % (ci % 2)]
                v.wait_ge(sx.h, hb["sXN%d" % (ci % 2)] + 16 * (ci // 2))
            W(v, e.sDVE)
            I(e.sDVE, v.tensor_copy(out=e.xnt[ci % 2][:, tg * 512:(tg + 1) * 512],
                                    in_=e.scr[tg][:, :]))
        v.wait_ge(e.sPE.h, pe[("gate", ci, 7)])
        W(v, e.sDVE)
        if ci == 0:
            I(e.sDVE, v.tensor_copy(out=e.logacc[:, :], in_=e.B[4][:, 0:64]))
        else:
            I(e.sDVE, v.tensor_add(e.logacc[:, :], e.logacc[:, :], e.B[4][:, 0:64]))
    s4, s5 = e.scr[4], e.scr[5]
    for ch in range(8):
        bch = e.logacc[:, ch * 8:(ch + 1) * 8]
        W(v, e.sDVE)
        I(e.sDVE, v.reduce_max(s4[:, 0:1], bch, axis=AXX))
        W(v, e.sDVE)
        I(e.sDVE, v.tensor_scalar(out=s5[:, 0:8], in0=bch, scalar1=s4[:, 0:1],
                                  scalar2=None, op0=ALU.is_equal, op1=ALU.bypass,
                                  accum_out=None))
        W(v, e.sDVE)
        I(e.sDVE, v.scalar_tensor_tensor(out=s5[:, 8:16], in0=s5[:, 0:8],
                                         scalar=-1e30, in1=bch,
                                         op0=ALU.mult, op1=ALU.add))
        W(v, e.sDVE)
        I(e.sDVE, v.reduce_max(s4[:, 1:2], s5[:, 8:16], axis=AXX))
        W(v, e.sDVE)
        I(e.sDVE, v.tensor_sub(s4[:, 4:5], s4[:, 0:1], s4[:, 1:2]))
        v.wait_ge(e.sACT.h, actsig(ch))
        W(v, e.sDVE)
        I(e.sDVE, v.tensor_scalar(out=s4[:, 6:7], in0=s4[:, 5:6], scalar1=-1.0,
                                  scalar2=1.0, op0=ALU.mult, op1=ALU.add,
                                  accum_out=None))
        W(v, e.sDVE)
        I(e.sDVE, v.tensor_scalar(out=s5[:, 16:24], in0=s5[:, 0:8], scalar1=s4[:, 5:6],
                                  scalar2=None, op0=ALU.mult, op1=ALU.bypass,
                                  accum_out=None))
        W(v, e.sDVE)
        I(e.sDVE, v.tensor_scalar(out=s5[:, 24:32], in0=s5[:, 8:16], scalar1=s4[:, 1:2],
                                  scalar2=None, op0=ALU.is_equal, op1=ALU.bypass,
                                  accum_out=None))
        W(v, e.sDVE)
        I(e.sDVE, v.tensor_scalar(out=s5[:, 32:40], in0=s5[:, 24:32], scalar1=s4[:, 6:7],
                                  scalar2=None, op0=ALU.mult, op1=ALU.bypass,
                                  accum_out=None))
        W(v, e.sDVE)
        I(e.sDVE, v.tensor_add(e.wch[:, ch * 8:(ch + 1) * 8], s5[:, 16:24],
                               s5[:, 32:40]))


def h2_gpsimd(e, g, hb):
    pe = e.h2pe
    for nm in ["sDVE", "sPE", "sACT", "sPOOL"]:
        g.wait_ge(e[nm].h, hb[nm])
    I(e.sPOOL, g.memset(e.id128[:, :], 1.0))
    W(g, e.sPOOL)
    I(e.sPOOL, g.affine_select(out=e.id128[:, :], in_=e.id128[:, :],
                               pattern=[[-1, 128]], compare_op=ALU.is_equal,
                               fill=0.0, base=0, channel_multiplier=1))
    for ci in range(DH):
        if ci >= 2:
            g.wait_ge(e.sPE.h, pe[("wo", ci - 2, 1)])
        for half in range(2):
            I(e["sWO%d" % (ci % 2)], g.dma_start(
                out=e.wo_sb[:, ((ci % 2) * DH + half * 8) * 128:
                            ((ci % 2) * DH + (half + 1) * 8) * 128].rearrange(
                                "p (hq x) -> p hq x", hq=8),
                in_=e.woh[ci, :, half * 8:(half + 1) * 8, :]), 16)


def h2_sync(e, s, hb, A, Bv, actwr):
    I(e.sSP, s.dma_start(
        out=e.gw_sb[:, :].rearrange("p (ci x) -> p ci x", ci=DH),
        in_=e.gwt.rearrange("(ci dl) x -> dl ci x", dl=128)), 16)
    for ci in range(DH):
        s.wait_ge(e.sDVE.h, A(ci, 1, 3))
        I(e.sH2, s.dma_start(out=e.d_h2[ci * 128:(ci + 1) * 128, :],
                             in_=e.xbuf2[:, ci * 1024:(ci + 1) * 1024]), 16)
    for ci in range(DH):
        s.wait_ge(e.sDVE.h, Bv(ci, 1, 3))
        I(e["sXN%d" % (ci % 2)], s.dma_start(
            out=e.d_xn2[ci * 128:(ci + 1) * 128, :], in_=e.xnt[ci % 2][:, :]), 16)
    s.wait_ge(e.sACT.h, actwr)
    I(e.sH2, s.dma_start(out=e.d_w, in_=e.wr_sb[:, :]), 16)
    for c in [e.sH2, e.sXN0, e.sXN1]:
        s.wait_ge(c.h, c.v)


def h2_debug_s(e, s):
    I(e.sST, s.dma_start(out=e.dbg["dbg_h2"], in_=e.d_h2), 16)
    I(e.sST, s.dma_start(out=e.dbg["dbg_w"], in_=e.d_w), 16)
    s.wait_ge(e.sST.h, e.sST.v)


def h2_debug_g(e, g):
    I(e.sGLD, g.dma_start(out=e.dbg["dbg_xn2"], in_=e.d_xn2), 16)
    g.wait_ge(e.sGLD.h, e.sGLD.v)



def emit_moe(e):
    nc = e.nc
    e.moectx = ExitStack()
    e.moe_acc = e.moectx.enter_context(
        nc.sbuf_tensor("moe_acc", [128, DH * 1024], BF16))
    m2 = ExitStack()

    def sb(name, shape, dt):
        return m2.enter_context(nc.sbuf_tensor(name, shape, dt))

    e.xn2b = sb("xn2b", [128, DH * 1024], BF16)
    e.hf_sb = sb("hf_sb", [128, 32 * 1024], BF16)
    e.wf32 = [sb(f"wf32_{i}", [128, 4096], F32) for i in range(2)]
    e.wbf = [sb(f"wbf_{i}", [128, 4096], BF16) for i in range(2)]
    e.wbc = sb("wbc", [128, 1024], F32)
    e.silb = [sb(f"silb{i}", [128, 512], BF16) for i in range(2)]
    e.wrow = sb("wrow", [1, 1024], F32)

    mb = {nm: e[nm].v for nm in
          ["sPE", "sACT", "sDVE", "sPOOL", "sXB", "sMR", "sMW0", "sMW1"]}
    e.mb = mb
    iters = []
    for ee in range(NE):
        iters += [("u", ee, fc) for fc in range(32)] +                  [("d", ee, ci) for ci in range(16)]
    slot_of = {}
    arrive = {}
    cnt = [mb["sMW0"], mb["sMW1"]]
    for gi, it in enumerate(iters):
        s = gi % 2
        slot_of[gi] = s
        cnt[s] += 32 if it[0] == "u" else 16
        arrive[gi] = cnt[s]
    e.m_iters, e.m_slot, e.m_arrive = iters, slot_of, arrive

    pe_wbc = lambda ee, tg: mb["sPE"] + ee * 162 + tg + 1
    pe_u = lambda ee, fc, tg, w: mb["sPE"] + ee * 162 + 2 + fc * 4 + tg * 2 + w + 1
    pe_d = lambda ee, ci, tg: mb["sPE"] + ee * 162 + 130 + ci * 2 + tg + 1
    act_wbc = lambda ee, tg: mb["sACT"] + ee * 66 + tg + 1
    act_sil = lambda ee, fc, tg: mb["sACT"] + ee * 66 + 2 + fc * 2 + tg + 1
    dve_hf = lambda ee, fc, tg: mb["sDVE"] + ee * 128 + fc * 2 + tg + 1
    dve_dn = lambda ee, ci, tg, k: mb["sDVE"] + ee * 128 + 64 + ci * 4 + tg * 2 + k
    pool_cv = lambda ee, fc, w: mb["sPOOL"] + ee * 80 + fc * 2 + w + 1
    pool_cvd = lambda ee, ci: mb["sPOOL"] + ee * 80 + 64 + ci + 1

    def last_pe(gi):
        k, ee, i = iters[gi]
        return pe_u(ee, i, 1, 1) if k == "u" else pe_d(ee, i, 1)

    def last_pool(gi):
        k, ee, i = iters[gi]
        return pool_cv(ee, i, 1) if k == "u" else pool_cvd(ee, i)

    fns = dict(pe_wbc=pe_wbc, pe_u=pe_u, pe_d=pe_d, act_wbc=act_wbc,
               act_sil=act_sil, dve_hf=dve_hf, dve_dn=dve_dn, pool_cv=pool_cv,
               pool_cvd=pool_cvd, last_pe=last_pe, last_pool=last_pool)
    sxb_full = mb["sXB"] + 256

    with nc.Block() as blk:
        blk.tensor(lambda t: moe_tensor(e, t, mb, fns, sxb_full))
        blk.scalar(lambda s: moe_scalar(e, s, mb, fns))
        blk.vector(lambda v: moe_vector(e, v, mb, fns))
        blk.gpsimd(lambda g: moe_gpsimd(e, g, mb, fns))
        blk.sync(lambda s: moe_sync(e, s, mb, fns))

    if DEBUG:
        with nc.Block() as blk2:
            blk2.gpsimd(lambda g: moe_debug_g(e, g))
    m2.close()


def moe_tensor(e, t, mb, f, sxb_full):
    t.wait_ge(e.sDVE.h, mb["sDVE"])
    t.wait_ge(e.sACT.h, mb["sACT"])
    t.wait_ge(e.sPOOL.h, mb["sPOOL"])
    t.wait_ge(e.sXB.h, sxb_full)
    for ee in range(NE):
        t.wait_ge(e.sMR.h, mb["sMR"] + 16 * (ee + 1))
        if ee >= 1:
            t.wait_ge(e.sDVE.h, f["dve_dn"](ee - 1, 15, 1, 2))
        for tg in range(2):
            I(e.sPE, t.matmul(e.B[6 + tg][:, :], e.ones1f[:, :],
                              e.wrow[0:1, tg * 512:(tg + 1) * 512],
                              start=True, stop=True))
        for fc in range(32):
            gi = ee * 48 + fc
            sl = e.m_slot[gi]
            t.wait_ge(e["sMW%d" % sl].h, e.m_arrive[gi])
            t.wait_ge(e.sPOOL.h, f["pool_cv"](ee, fc, 1))
            for tg in range(2):
                for w in range(2):
                    ugi = ee * 32 + fc
                    if ugi >= 2:
                        pe_, pf = divmod(ugi - 2, 32)
                        if w == 0:
                            t.wait_ge(e.sACT.h, f["act_sil"](pe_, pf, tg))
                        else:
                            t.wait_ge(e.sDVE.h, f["dve_hf"](pe_, pf, tg))
                    if fc == 1 and w == 1:
                        t.wait_ge(e.sACT.h, f["act_wbc"](ee, tg))
                    bank = (fc % 2) * 4 + w * 2 + tg
                    for dh in range(DH):
                        ins = t.matmul(
                            e.B[bank][:, :],
                            e.wbf[sl][:, w * 2048 + dh * 128:w * 2048 + (dh + 1) * 128],
                            e.xn2b[:, dh * 1024 + tg * 512:dh * 1024 + (tg + 1) * 512],
                            start=(dh == 0), stop=(dh == DH - 1))
                    I(e.sPE, ins)
        for ci in range(16):
            gi = ee * 48 + 32 + ci
            sl = e.m_slot[gi]
            t.wait_ge(e["sMW%d" % sl].h, e.m_arrive[gi])
            t.wait_ge(e.sPOOL.h, f["pool_cvd"](ee, ci))
            if ci == 0:
                t.wait_ge(e.sDVE.h, f["dve_hf"](ee, 31, 1))
            for tg in range(2):
                bank = 4 + (ci % 2) * 2 + tg
                if ci >= 2:
                    t.wait_ge(e.sDVE.h, f["dve_dn"](ee, ci - 2, tg, 1))
                else:
                    t.wait_ge(e.sACT.h, f["act_sil"](ee, 30 + ci, tg))
                    t.wait_ge(e.sDVE.h, f["dve_hf"](ee, 30 + ci, tg))
                for kc in range(32):
                    ins = t.matmul(
                        e.B[bank][:, :],
                        e.wbf[sl][:, kc * 128:(kc + 1) * 128],
                        e.hf_sb[:, kc * 1024 + tg * 512:kc * 1024 + (tg + 1) * 512],
                        start=(kc == 0), stop=(kc == 31))
                I(e.sPE, ins)


def moe_scalar(e, s, mb, f):
    for ee in range(NE):
        for tg in range(2):
            s.wait_ge(e.sPE.h, f["pe_wbc"](ee, tg))
            if ee >= 1 and tg == 0:
                s.wait_ge(e.sDVE.h, f["dve_dn"](ee - 1, 15, 1, 2))
            I(e.sACT, s.copy(out=e.wbc[:, tg * 512:(tg + 1) * 512],
                             in_=e.B[6 + tg][:, :]))
        for fc in range(32):
            for tg in range(2):
                s.wait_ge(e.sPE.h, f["pe_u"](ee, fc, tg, 0))
                idx = fc * 2 + tg
                if idx >= 2:
                    s.wait_ge(e.sDVE.h, f["dve_hf"](ee, fc - 1, tg))
                I(e.sACT, s.activation(out=e.silb[idx % 2][:, :],
                                       in_=e.B[(fc % 2) * 4 + tg][:, :],
                                       func=AF.Silu))


def moe_vector(e, v, mb, f):
    for ee in range(NE):
        for fc in range(32):
            for tg in range(2):
                v.wait_ge(e.sACT.h, f["act_sil"](ee, fc, tg))
                v.wait_ge(e.sPE.h, f["pe_u"](ee, fc, tg, 1))
                if ee >= 1 and fc == 0 and tg == 0:
                    v.wait_ge(e.sPE.h, f["pe_d"](ee - 1, 15, 1))
                W(v, e.sDVE)
                I(e.sDVE, v.tensor_mul(
                    e.hf_sb[:, fc * 1024 + tg * 512:fc * 1024 + (tg + 1) * 512],
                    e.silb[(fc * 2 + tg) % 2][:, :],
                    e.B[(fc % 2) * 4 + 2 + tg][:, :]))
        for ci in range(16):
            for tg in range(2):
                v.wait_ge(e.sPE.h, f["pe_d"](ee, ci, tg))
                W(v, e.sDVE)
                I(e.sDVE, v.tensor_mul(e.scr[tg][:, :],
                                       e.B[4 + (ci % 2) * 2 + tg][:, :],
                                       e.wbc[:, tg * 512:(tg + 1) * 512]))
                at = e.moe_acc[:, ci * 1024 + tg * 512:ci * 1024 + (tg + 1) * 512]
                W(v, e.sDVE)
                if ee == 0:
                    I(e.sDVE, v.tensor_copy(out=at, in_=e.scr[tg][:, :]))
                else:
                    I(e.sDVE, v.tensor_add(at, at, e.scr[tg][:, :]))


def moe_gpsimd(e, g, mb, f):
    for nm in ["sDVE", "sPE", "sACT", "sPOOL"]:
        g.wait_ge(e[nm].h, mb[nm])
    for gi, (k, ee, i) in enumerate(e.m_iters):
        sl = e.m_slot[gi]
        g.wait_ge(e["sMW%d" % sl].h, e.m_arrive[gi])
        if gi >= 2:
            g.wait_ge(e.sPE.h, f["last_pe"](gi - 2))
        if k == "u":
            for w in range(2):
                I(e.sPOOL, g.tensor_copy(
                    out=e.wbf[sl][:, w * 2048:(w + 1) * 2048],
                    in_=e.wf32[sl][:, w * 2048:(w + 1) * 2048]))
        else:
            I(e.sPOOL, g.tensor_copy(out=e.wbf[sl][:, :], in_=e.wf32[sl][:, :]))


def moe_sync(e, s, mb, f):
    for nm in ["sDVE", "sPE", "sACT", "sPOOL"]:
        s.wait_ge(e[nm].h, mb[nm])
    for half in range(2):
        for q in range(8):
            ci = half * 8 + q
            I(e.sXB, s.dma_start(out=e.xn2b[:, ci * 1024:(ci + 1) * 1024],
                                 in_=e.d_xn2[ci * 128:(ci + 1) * 128, :]), 16)
    for gi, (k, ee, i) in enumerate(e.m_iters):
        if k == "u" and i == 0:
            if ee >= 1:
                s.wait_ge(e.sPE.h, f["pe_wbc"](ee - 1, 1))
            I(e.sMR, s.dma_start(out=e.wrow[:, :], in_=e.d_w[ee:ee + 1, :]), 16)
        sl = e.m_slot[gi]
        if gi >= 2:
            s.wait_ge(e.sPOOL.h, f["last_pool"](gi - 2))
        if k == "u":
            I(e["sMW%d" % sl], s.dma_start(
                out=e.wf32[sl][:, 0:2048].rearrange("p (dh x) -> p dh x", dh=DH),
                in_=e.w1h[ee, :, :, i * 128:(i + 1) * 128]), 16)
            I(e["sMW%d" % sl], s.dma_start(
                out=e.wf32[sl][:, 2048:4096].rearrange("p (dh x) -> p dh x", dh=DH),
                in_=e.w3h[ee, :, :, i * 128:(i + 1) * 128]), 16)
        else:
            I(e["sMW%d" % sl], s.dma_start(
                out=e.wf32[sl][:, :].rearrange("p (kc x) -> p kc x", kc=32),
                in_=e.w2h[ee, :, :, i * 128:(i + 1) * 128]), 16)


def moe_debug_g(e, g):
    g.wait_ge(e.sDVE.h, e.sDVE.v)
    for half in range(2):
        I(e.sGLD, g.dma_start(
            out=e.dbg["dbg_moe"].rearrange("(ci dl) t -> dl ci t", dl=128)[
                :, half * 8:(half + 1) * 8, :],
            in_=e.moe_acc[:, half * 8 * 1024:(half + 1) * 8 * 1024].rearrange(
                "p (ci t) -> p ci t", ci=8)), 16)
    g.wait_ge(e.sGLD.h, e.sGLD.v)



def emit_final(e):
    nc = e.nc
    fctx = ExitStack()

    def sb(name, shape, dt):
        return fctx.enter_context(nc.sbuf_tensor(name, shape, dt))

    e.xbuf3 = sb("xbuf3", [128, DH * 1024], F32)
    e.rows3 = sb("rows3", [1, 4 * 512], F32)
    e.mr3 = sb("mr3", [128, 4 * 512], F32)
    e.ybuf3 = [sb(f"ybuf3_{i}", [128, 1024], F32) for i in range(2)]
    e.hbuf3 = [sb(f"hbuf3_{i}", [128, 1024], F32) for i in range(2)]
    e.obuf = [sb(f"obuf{i}", [128, 1024], F32) for i in range(2)]
    e.gsb = [sb(f"gsb{i}", [128, 512], F32) for i in range(2)]
    e.gob = [sb(f"gob{i}", [128, 512], F32) for i in range(2)]

    fb = {nm: e[nm].v for nm in
          ["sPE", "sACT", "sDVE", "sPOOL", "sXB", "sFT0", "sFT1", "sOU0", "sOU1"]}
    dve_F = lambda ci, tg, k: fb["sDVE"] + (ci * 2 + tg) * 7 + k
    act_F = lambda ci, tg: fb["sACT"] + 4 + ci * 2 + tg + 1
    pe_F = lambda i: fb["sPE"] + i + 1

    seng = nc.sync
    book = SwBook(e, ["sXB"])
    for c in nc.Switch(engines=[seng], index=[e.pid_s], n=8):
        book.enter_body()
        A0, A1 = own_cols(c)
        for nm in ["sDVE", "sPE", "sACT", "sPOOL"]:
            seng.wait_ge(e[nm].h, fb[nm])
        for half in range(2):
            for ib, a in enumerate((A0, A1)):
                I(e.sXB, seng.dma_start(
                    out=e.xbuf3[:, :].rearrange("p (ci t) -> p ci t", ci=DH)[
                        :, half * 8:(half + 1) * 8, ib * 512:(ib + 1) * 512],
                    in_=e.xT.rearrange("(ci dl) t -> dl ci t", dl=128)[
                        :, half * 8:(half + 1) * 8, a:a + 512]), 16)
        for r in range(2):
            for ib, a in enumerate((A0, A1)):
                I(e.sXB, seng.dma_start(
                    out=e.rows3[0:1, (r * 2 + ib) * 512:(r * 2 + ib + 1) * 512],
                    in_=e.d_rows[r:r + 1, a:a + 512]), 16)
    book.finish()
    sxb_full = fb["sXB"] + 128

    with nc.Block() as blk:
        blk.tensor(lambda t: fin_tensor(e, t, fb, sxb_full))
        blk.scalar(lambda s: fin_scalar(e, s, fb, dve_F, pe_F))
        blk.vector(lambda v: fin_vector(e, v, fb, dve_F, act_F, sxb_full))
        blk.sync(lambda s: fin_sync(e, s, fb, dve_F))
    fctx.close()


def fin_tensor(e, t, fb, sxb_full):
    t.wait_ge(e.sDVE.h, fb["sDVE"])
    t.wait_ge(e.sACT.h, fb["sACT"])
    t.wait_ge(e.sPOOL.h, fb["sPOOL"])
    t.wait_ge(e.sXB.h, sxb_full)
    for i in range(4):
        I(e.sPE, t.matmul(e.B[2 + i][:, :], e.ones1f[:, :],
                          e.rows3[0:1, i * 512:(i + 1) * 512],
                          start=True, stop=True))


def fin_scalar(e, s, fb, dve_F, pe_F):
    for i in range(4):
        s.wait_ge(e.sPE.h, pe_F(i))
        I(e.sACT, s.copy(out=e.mr3[:, i * 512:(i + 1) * 512], in_=e.B[2 + i][:, :]))
    for ci in range(DH):
        for tg in range(2):
            s.wait_ge(e.sDVE.h, dve_F(ci, tg, 5))
            if ci >= 1:
                s.wait_ge(e.sDVE.h, dve_F(ci - 1, tg, 6))
            I(e.sACT, s.activation(out=e.gob[tg][:, :], in_=e.gsb[tg][:, :],
                                   func=AF.Silu))


def fin_vector(e, v, fb, dve_F, act_F, sxb_full):
    v.wait_ge(e.sXB.h, sxb_full)
    v.wait_ge(e.sACT.h, fb["sACT"] + 4)
    for ci in range(DH):
        sl = ci % 2
        v.wait_ge(e["sFT%d" % sl].h, fb["sFT%d" % sl] + 32 * (ci // 2 + 1))
        for tg in range(2):
            xt = e.xbuf3[:, ci * 1024 + tg * 512:ci * 1024 + (tg + 1) * 512]
            sc = e.scr[2 + tg]
            W(v, e.sDVE)
            I(e.sDVE, v.tensor_sub(sc[:, :], xt, e.mr3[:, tg * 512:(tg + 1) * 512]))
            W(v, e.sDVE)
            I(e.sDVE, v.tensor_mul(sc[:, :], sc[:, :],
                                   e.mr3[:, (2 + tg) * 512:(3 + tg) * 512]))
            W(v, e.sDVE)
            I(e.sDVE, v.tensor_scalar(out=sc[:, :], in0=sc[:, :],
                                      scalar1=e.D_c[:, ci:ci + 1], scalar2=None,
                                      op0=ALU.mult, op1=ALU.bypass, accum_out=None))
            W(v, e.sDVE)
            I(e.sDVE, v.tensor_add(sc[:, :], sc[:, :],
                                   e.ybuf3[sl][:, tg * 512:(tg + 1) * 512]))
            W(v, e.sDVE)
            I(e.sDVE, v.tensor_add(e.gsb[tg][:, :],
                                   e.hbuf3[sl][:, tg * 512:(tg + 1) * 512],
                                   e.moe_acc[:, ci * 1024 + tg * 512:
                                             ci * 1024 + (tg + 1) * 512]))
            v.wait_ge(e.sACT.h, act_F(ci, tg))
            W(v, e.sDVE)
            I(e.sDVE, v.tensor_mul(sc[:, :], sc[:, :], e.gob[tg][:, :]))
            if ci >= 2 and tg == 0:
                v.wait_ge(e["sOU%d" % sl].h, fb["sOU%d" % sl] + 16 * (ci // 2))
            W(v, e.sDVE)
            I(e.sDVE, v.tensor_add(e.obuf[sl][:, tg * 512:(tg + 1) * 512], xt,
                                   sc[:, :]))


def fin_sync(e, s, fb, dve_F):
    def out_dma(ci):
        s.wait_ge(e.sDVE.h, dve_F(ci, 1, 7))
        I(e["sOU%d" % (ci % 2)], s.dma_start(
            out=e.out[ci * 128:(ci + 1) * 128, :], in_=e.obuf[ci % 2][:, :]), 16)

    for ci in range(DH):
        sl = ci % 2
        if ci >= 2:
            s.wait_ge(e.sDVE.h, dve_F(ci - 2, 1, 5))
        I(e["sFT%d" % sl], s.dma_start(out=e.ybuf3[sl][:, :],
                                       in_=e.d_y[ci * 128:(ci + 1) * 128, :]), 16)
        I(e["sFT%d" % sl], s.dma_start(out=e.hbuf3[sl][:, :],
                                       in_=e.d_h2[ci * 128:(ci + 1) * 128, :]), 16)
        if ci >= 1:
            out_dma(ci - 1)
    out_dma(15)
    for nm in ["sOU0", "sOU1", "sFT0", "sFT1"]:
        s.wait_ge(e[nm].h, e[nm].v)


def prep_inputs(inputs):
    x = np.asarray(inputs["L0_attn_x"])
    fp = {}
    fp["xT"] = np.ascontiguousarray(np.concatenate([x[0], x[1]], axis=0).T)
    fp["wxpT"] = np.ascontiguousarray(np.asarray(inputs["L0_attn_W_xproj"]).T)
    fp["wdtT"] = np.ascontiguousarray(np.asarray(inputs["L0_attn_W_dt"]).T)
    fp["bdt_r"] = np.ascontiguousarray(np.asarray(inputs["L0_attn_b_dt"]).reshape(DH, 128))
    fp["Alog"] = np.ascontiguousarray(np.asarray(inputs["L0_attn_A_log"]))
    fp["D_r"] = np.ascontiguousarray(np.asarray(inputs["L0_attn_D_param"]).reshape(DH, 128))
    fp["n1w_r"] = np.ascontiguousarray(np.asarray(inputs["L0_ffn_norm1_w"]).reshape(DH, 128))
    fp["n2w_r"] = np.ascontiguousarray(np.asarray(inputs["L0_ffn_norm2_w"]).reshape(DH, 128))
    wq = np.asarray(inputs["L0_ffn_wq"])
    fp["wqh"] = np.ascontiguousarray(wq.reshape(DH, 128, NH, 128).transpose(2, 1, 0, 3))
    wk = np.asarray(inputs["L0_ffn_wk"])
    fp["wkh"] = np.ascontiguousarray(wk.reshape(DH, 128, KVH, 128).transpose(2, 1, 0, 3))
    fp["wv"] = np.ascontiguousarray(np.asarray(inputs["L0_ffn_wv"]))
    wo = np.asarray(inputs["L0_ffn_wo"])
    fp["woh"] = np.ascontiguousarray(wo.reshape(DH, 128, DH, 128).transpose(2, 1, 0, 3))
    fp["gwt"] = np.ascontiguousarray(np.asarray(inputs["L0_ffn_gate_w"]))
    w1 = np.asarray(inputs["L0_ffn_w1"])
    fp["w1h"] = np.ascontiguousarray(w1.reshape(NE, DH, 128, FF).transpose(0, 2, 1, 3))
    w3 = np.asarray(inputs["L0_ffn_w3"])
    fp["w3h"] = np.ascontiguousarray(w3.reshape(NE, DH, 128, FF).transpose(0, 2, 1, 3))
    w2 = np.asarray(inputs["L0_ffn_w2"])
    fp["w2h"] = np.ascontiguousarray(w2.reshape(NE, 32, 128, D).transpose(0, 2, 1, 3))
    return fp


def make_cmeta(c):
    m = np.zeros((128, 32), np.float32)
    for j in range(14):
        m[:, j] = 1.0 if j < 2 * c else 0.0
        m[:, 14 + j] = 1.0 if j >= 2 * c else 0.0
    return m


def kernel(**inputs):
    fp = prep_inputs(inputs)
    nc, e = build_nc()
    in_maps = [dict(fp, cmeta=make_cmeta(c)) for c in range(8)]
    res = run_bass_kernel_spmd(nc, in_maps, core_ids=list(range(8)))
    outf = np.zeros((2, L, D), np.float32)
    for c in range(8):
        oc = res.results[c]["out"]
        outf[0, 512 * c:512 * (c + 1), :] = oc[:, 0:512].T
        outf[1, 512 * (7 - c):512 * (8 - c), :] = oc[:, 512:1024].T
    kernel.last_res = res
    return outf


# version 02
# speedup vs baseline: 1.8635x; 1.8635x over previous
import os
import math
import numpy as np
import concourse.bass as bass
import concourse.mybir as mybir
from concourse.bass_utils import run_bass_kernel_spmd
from contextlib import ExitStack

F32 = mybir.dt.float32
BF16 = mybir.dt.bfloat16
I32 = mybir.dt.int32
AF = mybir.ActivationFunctionType
ALU = mybir.AluOpType
AXX = mybir.AxisListType.X

D = 2048
DH = 16
TOK = 8192
L = 4096
G = 16
GW = 512
NH = 16
KVH = 4
NST = 16
R = 64
NE = 8
FF = 4096
EPS = 1e-5
PI = 3.14159265358979
LOG_THETA = math.log(1.0e6)
INV2PI = 1.0 / (2.0 * PI)
TWOPI = 2.0 * PI

DEBUG = os.environ.get("KDBG", "0") == "1"
NPHASE = int(os.environ.get("KPHASES", "9"))

PE_G = 78
ACT_G = 77
DVE_G = 130
POOL_G = 52
SP_G = 45
SX_G = 16

PRO_POOL = 19
PRO_GLD = 13 * 16
PRO_SP = 5 * 16
PRO_PE = 36
PRO_ACT = 38
PRO_DVE = 245


def pe_at(g, off):
    return PRO_PE + g * PE_G + off


def act_at(g, off):
    return PRO_ACT + g * ACT_G + off


def dve_at(g, off):
    return PRO_DVE + g * DVE_G + off


def pool_at(g, off):
    return PRO_POOL + g * POOL_G + off


def sx_at(g, off):
    return 16 * (g * SX_G + off)


def sp_at(g, off):
    return PRO_SP + 16 * (g * SP_G + off)


class Ctr:
    def __init__(self, nc, stack, name):
        self.h = stack.enter_context(nc.semaphore(name))
        self.v = 0
        self.name = name

    def inc(self, instr, n=1):
        instr.then_inc(self.h, n)
        self.v += n
        return self.v


class Env(dict):
    __getattr__ = dict.__getitem__
    __setattr__ = dict.__setitem__


def I(ctr, instr, n=1):
    instr.then_inc(ctr.h, n)
    ctr.v += n
    return instr


def W(eng, ctr, v=None):
    eng.wait_ge(ctr.h, ctr.v if v is None else v)



def build_nc():
    nc = bass.Bass("TRN2", target_bir_lowering=False)
    e = Env()
    e.nc = nc

    e.xT = nc.dram_tensor("xT", [D, TOK], F32, kind="ExternalInput").ap()
    e.wxpT = nc.dram_tensor("wxpT", [D, 96], F32, kind="ExternalInput").ap()
    e.wdtT = nc.dram_tensor("wdtT", [R, D], F32, kind="ExternalInput").ap()
    e.bdt_r = nc.dram_tensor("bdt_r", [DH, 128], F32, kind="ExternalInput").ap()
    e.Alog = nc.dram_tensor("Alog", [D, NST], F32, kind="ExternalInput").ap()
    e.D_r = nc.dram_tensor("D_r", [DH, 128], F32, kind="ExternalInput").ap()
    e.n1w_r = nc.dram_tensor("n1w_r", [DH, 128], F32, kind="ExternalInput").ap()
    e.n2w_r = nc.dram_tensor("n2w_r", [DH, 128], F32, kind="ExternalInput").ap()
    e.wqh = nc.dram_tensor("wqh", [NH, 128, DH, 128], F32, kind="ExternalInput").ap()
    e.wkh = nc.dram_tensor("wkh", [KVH, 128, DH, 128], F32, kind="ExternalInput").ap()
    e.wv = nc.dram_tensor("wv", [D, 512], F32, kind="ExternalInput").ap()
    e.woh = nc.dram_tensor("woh", [DH, 128, DH, 128], F32, kind="ExternalInput").ap()
    e.gwt = nc.dram_tensor("gwt", [D, NE], F32, kind="ExternalInput").ap()
    e.w1h = nc.dram_tensor("w1h", [NE, 128, DH, FF], F32, kind="ExternalInput").ap()
    e.w3h = nc.dram_tensor("w3h", [NE, 128, DH, FF], F32, kind="ExternalInput").ap()
    e.w2h = nc.dram_tensor("w2h", [NE, 128, 32, D], F32, kind="ExternalInput").ap()

    e.d_delta = nc.dram_tensor("d_delta", [D, TOK], F32, kind="Internal").ap()
    e.d_u = nc.dram_tensor("d_u", [D, TOK], F32, kind="Internal").ap()
    e.d_xn1 = nc.dram_tensor("d_xn1", [D, TOK], BF16, kind="Internal").ap()
    e.d_k = nc.dram_tensor("d_k", [KVH, 128, TOK], BF16, kind="Internal").ap()
    e.d_v = nc.dram_tensor("d_v", [TOK, 512], BF16, kind="Internal").ap()
    e.d_bc = nc.dram_tensor("d_bc", [32, TOK], F32, kind="Internal").ap()
    e.d_rows = nc.dram_tensor("d_rows", [2, TOK], F32, kind="Internal").ap()
    e.d_h2 = nc.dram_tensor("d_h2", [D, 1024], F32, kind="Internal").ap()
    e.d_y = nc.dram_tensor("d_y", [D, 1024], F32, kind="Internal").ap()
    e.d_moe = nc.dram_tensor("d_moe", [D, 1024], F32, kind="Internal").ap()
    e.d_ysrc = nc.dram_tensor("d_ysrc", [256, TOK], F32, kind="Internal").ap()
    e.d_ygat = nc.dram_tensor("d_ygat", [D, TOK], F32, kind="Internal").ap()
    e.d_xn2 = nc.dram_tensor("d_xn2", [D, 1024], BF16, kind="Internal").ap()
    e.d_w = nc.dram_tensor("d_w", [NE, 1024], F32, kind="Internal").ap()
    e.out = nc.dram_tensor("out", [D, 1024], F32, kind="ExternalOutput").ap()

    e.dbg = {}
    if DEBUG:
        for nm, shape in [
            ("dbg_delta", [D, 1024]), ("dbg_u", [D, 1024]), ("dbg_bc", [32, 1024]),
            ("dbg_xn1", [D, 1024]), ("dbg_k", [KVH * 128, 1024]), ("dbg_v", [1024, 512]),
            ("dbg_rows", [2, TOK]), ("dbg_cos", [64, 1024]), ("dbg_sin", [64, 1024]),
            ("dbg_y", [D, 1024]), ("dbg_hloc", [D, NST]), ("dbg_hinit", [D, 2 * NST]),
            ("dbg_av", [D, 1024]), ("dbg_q", [D, 1024]),
            ("dbg_h2", [D, 1024]), ("dbg_xn2", [D, 1024]), ("dbg_w", [NE, 1024]),
            ("dbg_moe", [D, 1024]),
        ]:
            e.dbg[nm] = nc.dram_tensor(nm, shape, F32, kind="ExternalOutput").ap()

    with ExitStack() as ctx:
        def sb(name, shape, dt):
            return ctx.enter_context(nc.sbuf_tensor(name, shape, dt))
        e.sb = sb

        e.ones_bf = sb("ones_bf", [128, 1], BF16)
        e.ones1f = sb("ones1f", [1, 128], F32)
        e.ones16 = sb("ones16", [16, 16], F32)
        e.id16 = sb("id16", [16, 16], F32)
        e.Apos = sb("Apos", [128, DH * NST], F32)
        e.bdt_c = sb("bdt_c", [128, DH], F32)
        e.D_c = sb("D_c", [128, DH], F32)
        e.n1w_c = sb("n1w_c", [128, DH], F32)
        e.n2w_c = sb("n2w_c", [128, DH], F32)
        e.invf = sb("invf", [1, 64], F32)
        e.iot = sb("iot", [1, 1024], F32)
        e.kmax2 = sb("kmax2", [1, 8], F32)
        e.cst = sb("cst", [128, 2], F32)
        nc.const_aps.register(F32, EPS, e.cst[:, 0:1])
        nc.const_aps.register(F32, 1.0, e.cst[:, 1:2])
        e.scr = [sb(f"scr{i}", [128, GW], F32) for i in range(6)]
        e.iscr = sb("iscr", [128, GW], I32)

        e.ropectx = ExitStack()
        e.cosr = e.ropectx.enter_context(nc.sbuf_tensor("cosr", [64, TOK], BF16))
        e.sinr = e.ropectx.enter_context(nc.sbuf_tensor("sinr", [64, TOK], BF16))

        p0ctx = ExitStack()

        e.p0ctx = p0ctx

        def sb0(name, shape, dt):
            return p0ctx.enter_context(nc.sbuf_tensor(name, shape, dt))
        e.xbuf = sb0("xbuf", [128, DH * GW], F32)
        e.xnb = sb0("xnb", [128, DH * GW], BF16)
        e.xn1b = sb0("xn1b", [128, DH * GW], BF16)
        e.sqb = [sb0(f"sqb{i}", [128, GW], BF16) for i in range(2)]
        e.draw_b = sb0("draw_b", [64, GW], BF16)
        e.wxp_sb = sb0("wxp_sb", [128, DH * 96], BF16)
        e.wdt_sb = sb0("wdt_sb", [64, D], BF16)
        e.wk_sb = sb0("wk_sb", [128, KVH * DH * 128], BF16)
        e.wv_sb = sb0("wv_sb", [128, DH * 512], BF16)
        e.kvb = [sb0(f"kvb{i}", [128, GW], BF16) for i in range(2)]
        e.vb = [sb0(f"vb{i}", [128, GW], BF16) for i in range(2)]
        e.dsc = [sb0(f"dsc{i}", [128, GW], F32) for i in range(2)]
        e.usc = [sb0(f"usc{i}", [128, GW], F32) for i in range(2)]
        e.rowsB = sb0("rowsB", [1, 4 * GW], F32)
        e.mrow = e.rowsB[0:1, 0:GW]
        e.mrow2 = e.rowsB[0:1, GW:2 * GW]
        e.rrow = e.rowsB[0:1, 2 * GW:3 * GW]
        e.r1row = e.rowsB[0:1, 3 * GW:4 * GW]

        e.B = [ctx.enter_context(nc.psum_tensor(f"bank{i}", [128, 512], F32))
               for i in range(8)]

        for nm in ["sX", "sSP", "sPE", "sACT", "sDVE", "sPOOL", "sGLD", "sST",
                   "sX1", "sBC", "sRW", "sMS", "sQS", "sKV", "sWQ0", "sWQ1",
                   "sXB", "sWO0", "sWO1", "sH2", "sXN0", "sXN1",
                   "sMR", "sMW0", "sMW1", "sFT0", "sFT1", "sOU0", "sOU1",
                   "sBB", "sCB", "sAG"]:
            e[nm] = Ctr(nc, ctx, nm)
        e.sDS = [Ctr(nc, ctx, f"sDS{i}") for i in range(2)]
        e.sUS = [Ctr(nc, ctx, f"sUS{i}") for i in range(2)]
        e.sKB = [Ctr(nc, ctx, f"sKB{i}") for i in range(2)]
        e.sVB = [Ctr(nc, ctx, f"sVB{i}") for i in range(2)]

        emit_program(e)
        verify_sem_totals(e)

    return nc, e


def verify_sem_totals(e):
    import collections
    nc = e.nc
    tot = collections.Counter()
    for ins in nc.all_instructions():
        si = ins.sync_info
        if si is None:
            continue
        for u in si.on_update:
            if u.update_mode in ("sem-inc", "sem-add-imm"):
                tot[u.ant_name] += u.update_value
    names = ["sX", "sSP", "sPE", "sACT", "sDVE", "sPOOL", "sGLD", "sST", "sX1", "sBC", "sRW",
             "sQS", "sKV", "sWQ0", "sWQ1",
             "sXB", "sWO0", "sWO1", "sH2", "sXN0", "sXN1",
             "sMR", "sMW0", "sMW1", "sFT0", "sFT1", "sOU0", "sOU1",
             "sBB", "sCB", "sAG",
             "sDS0", "sDS1", "sUS0", "sUS1", "sKB0", "sKB1", "sVB0", "sVB1"]
    objs = {n: e[n] for n in names[:31]}
    for fam, lst in [("sDS", e.sDS), ("sUS", e.sUS), ("sKB", e.sKB), ("sVB", e.sVB)]:
        for i, c in enumerate(lst):
            objs[f"{fam}{i}"] = c
    excess = e.get("sw_excess", {})
    for name in names:
        want = objs[name].v + excess.get(name, 0)
        got = tot.get(name, 0)
        if got != want:
            raise AssertionError(f"sem {name}: emitted {got} != bookkept {want}")


def emit_program(e):
    nc = e.nc
    with nc.Block() as blk:
        blk.gpsimd(lambda eng: p0_gpsimd(e, eng))
        blk.scalar(lambda eng: p0_scalar(e, eng))
        blk.tensor(lambda eng: p0_tensor(e, eng))
        blk.vector(lambda eng: p0_vector(e, eng))
        blk.sync(lambda eng: p0_sync(e, eng))
    e.p0ctx.close()

    if NPHASE >= 2:
        emit_mamba(e)
    if NPHASE >= 3:
        emit_attention(e)
        if NPHASE >= 4:
            emit_h2(e)
        e.avctx.close()
        if DEBUG:
            with nc.Block() as blkr:
                blkr.gpsimd(lambda g: rope_debug(e, g))
        e.ropectx.close()
        if NPHASE >= 5:
            emit_moe(e)
            if NPHASE >= 6:
                emit_final(e)
            e.moectx.close()

    if DEBUG:
        with nc.Block() as blk2:
            blk2.gpsimd(lambda eng: p0_debug_g(e, eng))
            blk2.sync(lambda eng: p0_debug_s(e, eng))
    elif NPHASE < 6:
        with nc.Block() as blk3:
            blk3.sync(lambda eng: stub_out(e, eng))


def stub_out(e, s):
    s.dma_start(out=e.out[0:128, 0:512], in_=e.scr[0][:, 0:512]).then_inc(e.sST.h, 16)
    e.sST.v += 16
    s.wait_ge(e.sST.h, e.sST.v)



def p0_gpsimd(e, g):
    I(e.sPOOL, g.iota(e.invf[:, :], pattern=[[1, 64]], base=0, channel_multiplier=0,
                      allow_small_or_imprecise_dtypes=True))
    I(e.sPOOL, g.memset(e.ones16[:, :], 1.0))
    W(g, e.sPOOL)
    I(e.sPOOL, g.affine_select(out=e.id16[:, :], in_=e.ones16[:, :], pattern=[[-1, 16]],
                               compare_op=ALU.is_equal, fill=0.0, base=0,
                               channel_multiplier=1))
    for ch in range(16):
        if ch >= 2:
            g.wait_ge(e.sPE.h, 4 + 2 * (ch - 1))
        I(e.sPOOL, g.iota(e.iot[:, (ch % 2) * 512:(ch % 2) * 512 + 512], pattern=[[1, 512]],
                          base=(ch % 8) * 512, channel_multiplier=0,
                          allow_small_or_imprecise_dtypes=True))
    assert e.sPOOL.v == PRO_POOL, e.sPOOL.v
    for half in range(2):
        I(e.sGLD, g.dma_start(out=e.wxp_sb[:, half * 8 * 96:(half + 1) * 8 * 96].rearrange(
                        "p (dh r) -> p dh r", dh=8),
                    in_=e.wxpT.rearrange("(dh dl) r -> dl dh r", dl=128)[:, half * 8:(half + 1) * 8, :]), 16)
    I(e.sGLD, g.dma_start(out=e.wdt_sb[:, :], in_=e.wdtT), 16)
    for kvh in range(KVH):
        for half in range(2):
            I(e.sGLD, g.dma_start(
                out=e.wk_sb[:, (kvh * DH + half * 8) * 128:(kvh * DH + (half + 1) * 8) * 128
                            ].rearrange("p (dh x) -> p dh x", dh=8),
                in_=e.wkh[kvh, :, half * 8:(half + 1) * 8, :]), 16)
    for half in range(2):
        I(e.sGLD, g.dma_start(out=e.wv_sb[:, half * 8 * 512:(half + 1) * 8 * 512].rearrange(
                        "p (dh x) -> p dh x", dh=8),
                    in_=e.wv.rearrange("(dh dl) x -> dl dh x", dl=128)[:, half * 8:(half + 1) * 8, :]), 16)
    assert e.sGLD.v == PRO_GLD, e.sGLD.v

    for gi in range(G):
        for dh in range(DH):
            g.wait_ge(e.sX.h, sx_at(gi, 16))
            if dh >= 2:
                g.wait_ge(e.sPE.h, pe_at(gi, dh - 1))
            elif gi > 0:
                g.wait_ge(e.sPE.h, pe_at(gi - 1, PE_G))
            I(e.sPOOL, g.tensor_tensor(out=e.sqb[dh % 2][:, :], in0=e.xbuf[:, dh * GW:(dh + 1) * GW],
                                       in1=e.xbuf[:, dh * GW:(dh + 1) * GW], op=ALU.mult))
        for dh in range(DH):
            g.wait_ge(e.sDVE.h, dve_at(gi, 21 + 3 * dh))
            g.wait_ge(e.sPE.h, pe_at(gi, 16 + dh) if dh >= 2 else pe_at(gi, 16))
            I(e.sPOOL, g.tensor_tensor(out=e.sqb[dh % 2][:, :], in0=e.xbuf[:, dh * GW:(dh + 1) * GW],
                                       in1=e.xbuf[:, dh * GW:(dh + 1) * GW], op=ALU.mult))
        for dh in range(DH):
            g.wait_ge(e.sDVE.h, dve_at(gi, 83 + dh))
            if dh >= 2:
                g.wait_ge(e.sUS[dh % 2].h, 16 * (gi * 8 + (dh - 2) // 2 + 1))
            elif gi > 0:
                g.wait_ge(e.sUS[dh % 2].h, 16 * (gi * 8))
            I(e.sPOOL, g.tensor_tensor(out=e.usc[dh % 2][:, :], in0=e.dsc[dh % 2][:, :],
                                       in1=e.xbuf[:, dh * GW:(dh + 1) * GW], op=ALU.mult))
        for kvh in range(KVH):
            g.wait_ge(e.sDVE.h, dve_at(gi, 104 + 6 * kvh))
            if kvh >= 2:
                g.wait_ge(e.sPE.h, pe_at(gi, 74 + kvh - 1))
            I(e.sPOOL, g.tensor_tensor(out=e.sqb[kvh % 2][:, :], in0=e.kvb[kvh % 2][:, :],
                                       in1=e.kvb[kvh % 2][:, :], op=ALU.mult))
        assert e.sPOOL.v == pool_at(gi, POOL_G), (gi, e.sPOOL.v)


def p0_scalar(e, s):
    s.wait_ge(e.sSP.h, 16 * 5)
    I(e.sACT, s.activation(out=e.Apos[:, :], in_=e.scr[0][:, 0:DH * NST], func=AF.Exp))
    s.wait_ge(e.sPOOL.h, 1)
    I(e.sACT, s.activation(out=e.invf[:, :], in_=e.invf[:, :], func=AF.Exp,
                           scale=-LOG_THETA / 64.0))
    for i, dst in enumerate([e.bdt_c, e.D_c, e.n1w_c, e.n2w_c]):
        s.wait_ge(e.sPE.h, i + 1)
        I(e.sACT, s.copy(out=dst[:, :], in_=e.B[0][:, i * 16:(i + 1) * 16]))
    for ch in range(16):
        s.wait_ge(e.sDVE.h, 5 + 15 * ch + 7)
        I(e.sACT, s.activation(out=e.sinr[:, ch * 512:(ch + 1) * 512], in_=e.scr[5][0:64, :],
                               func=AF.Sin))
        s.wait_ge(e.sDVE.h, 5 + 15 * ch + 15)
        I(e.sACT, s.activation(out=e.cosr[:, ch * 512:(ch + 1) * 512], in_=e.scr[4][0:64, :],
                               func=AF.Sin))
    assert e.sACT.v == PRO_ACT, e.sACT.v

    for gi in range(G):
        s.wait_ge(e.sPE.h, pe_at(gi, 16))
        if gi > 0:
            s.wait_ge(e.sRW.h, 32 * gi)
            W(s, e.sACT)
        I(e.sACT, s.mul(out=e.mrow[:, :], in_=e.B[0][0:1, :], mul=1.0 / D))
        I(e.sACT, s.mul(out=e.mrow2[:, :], in_=e.B[1][0:1, :], mul=1.0 / D))
        s.wait_ge(e.sDVE.h, dve_at(gi, 18))
        I(e.sACT, s.activation(out=e.scr[2][0:1, :], in_=e.mrow2[:, :], func=AF.Ln, bias=EPS))
        W(s, e.sACT)
        I(e.sACT, s.activation(out=e.rrow[:, :], in_=e.scr[2][0:1, :], func=AF.Exp, scale=-0.5))
        s.wait_ge(e.sPE.h, pe_at(gi, 33))
        W(s, e.sACT)
        I(e.sACT, s.mul(out=e.scr[2][0:1, :], in_=e.B[1][0:1, :], mul=1.0 / D))
        W(s, e.sACT)
        I(e.sACT, s.activation(out=e.scr[2][0:1, :], in_=e.scr[2][0:1, :], func=AF.Ln, bias=EPS))
        W(s, e.sACT)
        I(e.sACT, s.activation(out=e.r1row[:, :], in_=e.scr[2][0:1, :], func=AF.Exp, scale=-0.5))
        s.wait_ge(e.sPE.h, pe_at(gi, 50))
        I(e.sACT, s.copy(out=e.draw_b[:, :], in_=e.B[4][0:64, :]))
        if gi > 0:
            s.wait_ge(e.sBC.h, 16 * gi)
        I(e.sACT, s.copy(out=e.scr[3][0:32, :], in_=e.B[5][0:32, :]))
        for dh in range(DH):
            bank = e.B[6 + dh % 2]
            s.wait_ge(e.sPE.h, pe_at(gi, 50 + dh + 1))
            if dh >= 1:
                s.wait_ge(e.sDVE.h, dve_at(gi, 82 + dh))
            elif gi > 0:
                s.wait_ge(e.sDVE.h, dve_at(gi - 1, 98))
            W(s, e.sACT)
            bcol = e.bdt_c[:, dh:dh + 1]
            I(e.sACT, s.activation(out=e.scr[0][:, :], in_=bank[:, :], func=AF.Abs, bias=bcol))
            W(s, e.sACT)
            I(e.sACT, s.activation(out=e.scr[1][:, :], in_=e.scr[0][:, :], func=AF.Exp, scale=-1.0))
            W(s, e.sACT)
            I(e.sACT, s.activation(out=e.scr[1][:, :], in_=e.scr[1][:, :], func=AF.Ln, bias=1.0))
            W(s, e.sACT)
            I(e.sACT, s.activation(out=e.scr[0][:, :], in_=bank[:, :], func=AF.Relu, bias=bcol))
        for tc in range(4):
            s.wait_ge(e.sPE.h, pe_at(gi, 70 + tc + 1))
            if tc >= 2:
                s.wait_ge(e.sVB[tc % 2].h, 16 * (gi * 2 + 1))
            elif gi > 0:
                s.wait_ge(e.sVB[tc % 2].h, 16 * (gi * 2))
            I(e.sACT, s.copy(out=e.vb[tc % 2][:, :], in_=e.B[2 + tc % 2][:, :]))
        assert e.sACT.v == act_at(gi, ACT_G), (gi, e.sACT.v)


def p0_tensor(e, t):
    t.wait_ge(e.sSP.h, 16 * 5)
    t.wait_ge(e.sPOOL.h, 3)
    for i in range(4):
        if i > 0:
            t.wait_ge(e.sACT.h, 2 + i)
        I(e.sPE, t.transpose(e.B[0][:, i * 16:(i + 1) * 16], e.scr[1][0:16, i * 128:(i + 1) * 128],
                             e.id16[:, :]))
    for ch in range(16):
        t.wait_ge(e.sACT.h, 2)
        t.wait_ge(e.sPOOL.h, 3 + ch + 1)
        if ch >= 1:
            t.wait_ge(e.sDVE.h, 5 + 15 * (ch - 1) + 8)
        sl = e.iot[:, (ch % 2) * 512:(ch % 2) * 512 + 512]
        I(e.sPE, t.matmul(e.B[1][0:64, :], e.invf[:, :], sl, start=True, stop=True))
        I(e.sPE, t.matmul(e.B[2][0:64, :], e.invf[:, :], sl, start=True, stop=True))
    assert e.sPE.v == PRO_PE, e.sPE.v

    for gi in range(G):
        if gi > 0:
            t.wait_ge(e.sACT.h, act_at(gi - 1, ACT_G))
            t.wait_ge(e.sDVE.h, dve_at(gi - 1, DVE_G))
        else:
            t.wait_ge(e.sDVE.h, PRO_DVE)
            t.wait_ge(e.sGLD.h, PRO_GLD)
        for dh in range(DH):
            t.wait_ge(e.sDVE.h, dve_at(gi, dh + 1))
            t.wait_ge(e.sPOOL.h, pool_at(gi, dh + 1))
            t.matmul(e.B[0][0:1, :], e.ones_bf[:, :], e.xn1b[:, dh * GW:(dh + 1) * GW],
                     start=(dh == 0), stop=(dh == DH - 1))
            I(e.sPE, t.matmul(e.B[1][0:1, :], e.ones_bf[:, :], e.sqb[dh % 2][:, :],
                              start=(dh == 0), stop=(dh == DH - 1)))
        t.wait_ge(e.sACT.h, act_at(gi, 4))
        t.matmul(e.B[2][:, :], e.ones1f[:, :], e.mrow[:, :], start=True, stop=True)
        I(e.sPE, t.matmul(e.B[3][:, :], e.ones1f[:, :], e.rrow[:, :], start=True, stop=True))
        for dh in range(DH):
            t.wait_ge(e.sPOOL.h, pool_at(gi, 16 + dh + 1))
            I(e.sPE, t.matmul(e.B[1][0:1, :], e.ones_bf[:, :], e.sqb[dh % 2][:, :],
                              start=(dh == 0), stop=(dh == DH - 1)))
        t.wait_ge(e.sACT.h, act_at(gi, 7))
        t.wait_ge(e.sDVE.h, dve_at(gi, 66))
        I(e.sPE, t.matmul(e.B[3][:, :], e.ones1f[:, :], e.r1row[:, :], start=True, stop=True))
        for dh in range(DH):
            t.wait_ge(e.sDVE.h, dve_at(gi, 21 + 3 * dh))
            if dh == 0 and gi > 0:
                t.wait_ge(e.sACT.h, act_at(gi - 1, 9))
            t.matmul(e.B[4][0:64, :], e.wxp_sb[:, dh * 96:dh * 96 + 64],
                     e.xnb[:, dh * GW:(dh + 1) * GW], start=(dh == 0), stop=(dh == DH - 1))
            I(e.sPE, t.matmul(e.B[5][0:32, :], e.wxp_sb[:, dh * 96 + 64:dh * 96 + 96],
                              e.xnb[:, dh * GW:(dh + 1) * GW], start=(dh == 0),
                              stop=(dh == DH - 1)))
        for dh in range(DH):
            t.wait_ge(e.sACT.h, act_at(gi, 8))
            if dh >= 2:
                t.wait_ge(e.sACT.h, act_at(gi, 13 + 4 * (dh - 2)))
            I(e.sPE, t.matmul(e.B[6 + dh % 2][:, :], e.wdt_sb[:, dh * 128:(dh + 1) * 128],
                              e.draw_b[:, :], start=True, stop=True))
        t.wait_ge(e.sGLD.h, PRO_GLD)
        for kvh in range(KVH):
            t.wait_ge(e.sDVE.h, dve_at(gi, 82))
            if kvh >= 2:
                t.wait_ge(e.sDVE.h, dve_at(gi, 104 + 6 * (kvh - 2)))
            for dh in range(DH):
                ins = t.matmul(e.B[kvh % 2][:, :],
                               e.wk_sb[:, (kvh * DH + dh) * 128:(kvh * DH + dh + 1) * 128],
                               e.xn1b[:, dh * GW:(dh + 1) * GW],
                               start=(dh == 0), stop=(dh == DH - 1))
                if dh == DH - 1:
                    I(e.sPE, ins)
        for tc in range(4):
            if tc >= 2:
                t.wait_ge(e.sACT.h, act_at(gi, 72 + tc))
            for dh in range(DH):
                ins = t.matmul(e.B[2 + tc % 2][:, :],
                               e.xn1b[:, dh * GW + tc * 128:dh * GW + (tc + 1) * 128],
                               e.wv_sb[:, dh * 512:(dh + 1) * 512],
                               start=(dh == 0), stop=(dh == DH - 1))
                if dh == DH - 1:
                    I(e.sPE, ins)
        for kvh in range(KVH):
            t.wait_ge(e.sPOOL.h, pool_at(gi, 48 + kvh + 1))
            if kvh == 0:
                t.wait_ge(e.sACT.h, act_at(gi, 8))
            else:
                t.wait_ge(e.sDVE.h, dve_at(gi, 122 + 2 * kvh))
            I(e.sPE, t.matmul(e.B[4][0:1, :], e.ones_bf[:, :], e.sqb[kvh % 2][:, :],
                              start=True, stop=True))
        assert e.sPE.v == pe_at(gi, PE_G), (gi, e.sPE.v)


def p0_vector(e, v):
    I(e.sDVE, v.memset(e.ones_bf[:, :], 1.0))
    I(e.sDVE, v.memset(e.ones1f[:, :], 1.0))
    I(e.sDVE, v.memset(e.cst[:, 0:1], EPS))
    I(e.sDVE, v.memset(e.cst[:, 1:2], 1.0))
    I(e.sDVE, v.memset(e.kmax2[:, :], 0.0))
    for ch in range(16):
        v.wait_ge(e.sPE.h, 4 + 2 * ch + 2)
        if ch >= 1:
            v.wait_ge(e.sACT.h, 6 + 2 * ch)
        b = e.B[1]
        b2 = e.B[2]

        def redchain(v, srcap, dst, tmp):
            W(v, e.sDVE)
            I(e.sDVE, v.tensor_scalar_mul(dst[0:64, :], srcap, INV2PI))
            W(v, e.sDVE)
            I(e.sDVE, v.tensor_copy(out=e.iscr[0:64, :], in_=dst[0:64, :]))
            W(v, e.sDVE)
            I(e.sDVE, v.tensor_copy(out=tmp[0:64, :], in_=e.iscr[0:64, :]))
            W(v, e.sDVE)
            I(e.sDVE, v.scalar_tensor_tensor(out=dst[0:64, :], in0=tmp[0:64, :], scalar=-TWOPI,
                                             in1=srcap, op0=ALU.mult, op1=ALU.add))
            W(v, e.sDVE)
            I(e.sDVE, v.tensor_scalar(out=tmp[0:64, :], in0=dst[0:64, :], scalar1=PI,
                                      scalar2=None, op0=ALU.is_gt, op1=ALU.bypass,
                                      accum_out=None))
            W(v, e.sDVE)
            I(e.sDVE, v.scalar_tensor_tensor(out=dst[0:64, :], in0=tmp[0:64, :], scalar=-TWOPI,
                                             in1=dst[0:64, :], op0=ALU.mult, op1=ALU.add))
            W(v, e.sDVE)
            I(e.sDVE, v.tensor_scalar(out=dst[0:64, :], in0=dst[0:64, :], scalar1=PI,
                                      scalar2=-PI, op0=ALU.min, op1=ALU.max, accum_out=None))

        redchain(v, b[0:64, :], e.scr[5], e.scr[3])
        W(v, e.sDVE)
        I(e.sDVE, v.tensor_scalar_add(e.scr[2][0:64, :], b2[0:64, :], PI / 2))
        redchain(v, e.scr[2][0:64, :], e.scr[4], e.scr[3])
        assert e.sDVE.v == 5 + 15 * (ch + 1), (ch, e.sDVE.v)
    assert e.sDVE.v == PRO_DVE, e.sDVE.v

    for gi in range(G):
        for dh in range(DH):
            v.wait_ge(e.sX.h, sx_at(gi, 16))
            if gi > 0 and dh == 0:
                v.wait_ge(e.sPE.h, pe_at(gi - 1, PE_G))
                v.wait_ge(e.sX1.h, 32 * gi)
            I(e.sDVE, v.tensor_copy(out=e.xn1b[:, dh * GW:(dh + 1) * GW],
                                    in_=e.xbuf[:, dh * GW:(dh + 1) * GW]))
        v.wait_ge(e.sACT.h, act_at(gi, 2))
        W(v, e.sDVE)
        I(e.sDVE, v.tensor_mul(e.scr[2][0:1, :], e.mrow[:, :], e.mrow[:, :]))
        W(v, e.sDVE)
        I(e.sDVE, v.tensor_sub(e.mrow2[:, :], e.mrow2[:, :], e.scr[2][0:1, :]))
        v.wait_ge(e.sPE.h, pe_at(gi, 17))
        for dh in range(DH):
            v.wait_ge(e.sPOOL.h, pool_at(gi, dh + 1))
            W(v, e.sDVE)
            I(e.sDVE, v.tensor_sub(e.xbuf[:, dh * GW:(dh + 1) * GW],
                                   e.xbuf[:, dh * GW:(dh + 1) * GW], e.B[2][:, :]))
            W(v, e.sDVE)
            I(e.sDVE, v.tensor_mul(e.xbuf[:, dh * GW:(dh + 1) * GW],
                                   e.xbuf[:, dh * GW:(dh + 1) * GW], e.B[3][:, :]))
            W(v, e.sDVE)
            I(e.sDVE, v.tensor_copy(out=e.xnb[:, dh * GW:(dh + 1) * GW],
                                    in_=e.xbuf[:, dh * GW:(dh + 1) * GW]))
        v.wait_ge(e.sPE.h, pe_at(gi, 34))
        W(v, e.sDVE)
        for dh in range(DH):
            I(e.sDVE, v.scalar_tensor_tensor(out=e.xn1b[:, dh * GW:(dh + 1) * GW],
                                             in0=e.xbuf[:, dh * GW:(dh + 1) * GW],
                                             scalar=e.n1w_c[:, dh:dh + 1], in1=e.B[3][:, :],
                                             op0=ALU.mult, op1=ALU.mult))
        for dh in range(DH):
            v.wait_ge(e.sACT.h, act_at(gi, 13 + 4 * dh))
            if dh >= 2:
                v.wait_ge(e.sDS[dh % 2].h, 16 * (gi * 8 + (dh - 2) // 2 + 1))
                v.wait_ge(e.sPOOL.h, pool_at(gi, 32 + dh - 1))
            elif gi > 0:
                v.wait_ge(e.sDS[dh % 2].h, 16 * (gi * 8))
                v.wait_ge(e.sPOOL.h, pool_at(gi - 1, 32 + DH - 1 + dh))
            I(e.sDVE, v.tensor_add(e.dsc[dh % 2][:, :], e.scr[0][:, :], e.scr[1][:, :]))
        for kvh in range(KVH):
            v.wait_ge(e.sPE.h, pe_at(gi, 66 + kvh + 1))
            v.wait_ge(e.sACT.h, PRO_ACT)
            if kvh >= 2:
                v.wait_ge(e.sKB[kvh % 2].h, 16 * (gi * 2 + 1))
                v.wait_ge(e.sPOOL.h, pool_at(gi, 48 + kvh - 1))
            elif gi > 0:
                v.wait_ge(e.sKB[kvh % 2].h, 16 * (gi * 2))
                v.wait_ge(e.sPOOL.h, pool_at(gi - 1, 48 + kvh + 3))
            b = e.B[kvh % 2]
            cs = e.cosr[:, gi * GW:(gi + 1) * GW]
            sn = e.sinr[:, gi * GW:(gi + 1) * GW]
            kb = e.kvb[kvh % 2]
            W(v, e.sDVE)
            I(e.sDVE, v.tensor_mul(e.scr[0][0:64, :], b[0:64, :], cs))
            I(e.sDVE, v.tensor_mul(e.scr[1][0:64, :], b[64:128, :], sn))
            W(v, e.sDVE)
            I(e.sDVE, v.tensor_sub(kb[0:64, :], e.scr[0][0:64, :], e.scr[1][0:64, :]))
            W(v, e.sDVE)
            I(e.sDVE, v.tensor_mul(e.scr[0][0:64, :], b[64:128, :], cs))
            I(e.sDVE, v.tensor_mul(e.scr[1][0:64, :], b[0:64, :], sn))
            W(v, e.sDVE)
            I(e.sDVE, v.tensor_add(kb[64:128, :], e.scr[0][0:64, :], e.scr[1][0:64, :]))
        for kvh in range(KVH):
            v.wait_ge(e.sPE.h, pe_at(gi, 74 + kvh + 1))
            W(v, e.sDVE)
            I(e.sDVE, v.reduce_max(e.scr[2][0:1, 0:1], e.B[4][0:1, :], axis=AXX))
            W(v, e.sDVE)
            idx = (gi // 8) * 4 + kvh
            I(e.sDVE, v.tensor_max(e.kmax2[:, idx:idx + 1], e.kmax2[:, idx:idx + 1],
                                   e.scr[2][0:1, 0:1]))
        assert e.sDVE.v == dve_at(gi, DVE_G), (gi, e.sDVE.v)


def p0_sync(e, s):
    s.dma_start(out=e.scr[0][:, 0:DH * NST].rearrange("p (dh n) -> p dh n", dh=DH),
                in_=e.Alog.rearrange("(dh dl) n -> dl dh n", dl=128)).then_inc(e.sSP.h, 16)
    for i, t_ in enumerate([e.bdt_r, e.D_r, e.n1w_r, e.n2w_r]):
        s.dma_start(out=e.scr[1][0:16, i * 128:(i + 1) * 128], in_=t_).then_inc(e.sSP.h, 16)
    e.sSP.v = PRO_SP

    for gi in range(G):
        if gi > 0:
            s.wait_ge(e.sPOOL.h, pool_at(gi - 1, 48))
            s.wait_ge(e.sDVE.h, dve_at(gi - 1, 82))
        for dh in range(DH):
            s.dma_start(out=e.xbuf[:, dh * GW:(dh + 1) * GW],
                        in_=e.xT[dh * 128:(dh + 1) * 128, gi * GW:(gi + 1) * GW]).then_inc(e.sX.h, 16)
        e.sX.v = sx_at(gi, SX_G)
        s.wait_ge(e.sDVE.h, dve_at(gi, 82))
        for half in range(2):
            I(e.sX1, s.dma_start(
                out=e.d_xn1.rearrange("(dh dl) t -> dl dh t", dl=128)[:, half * 8:(half + 1) * 8,
                                                                      gi * GW:(gi + 1) * GW],
                in_=e.xn1b[:, half * 8 * GW:(half + 1) * 8 * GW].rearrange(
                    "p (dh t) -> p dh t", dh=8)), 16)
        for dh in range(DH):
            s.wait_ge(e.sDVE.h, dve_at(gi, 83 + dh))
            I(e.sDS[dh % 2], s.dma_start(out=e.d_delta[dh * 128:(dh + 1) * 128,
                                                       gi * GW:(gi + 1) * GW],
                                         in_=e.dsc[dh % 2][:, :]), 16)
            s.wait_ge(e.sPOOL.h, pool_at(gi, 32 + dh + 1))
            I(e.sUS[dh % 2], s.dma_start(out=e.d_u[dh * 128:(dh + 1) * 128,
                                                   gi * GW:(gi + 1) * GW],
                                         in_=e.usc[dh % 2][:, :]), 16)
        for kvh in range(KVH):
            s.wait_ge(e.sDVE.h, dve_at(gi, 104 + 6 * kvh))
            I(e.sKB[kvh % 2], s.dma_start(out=e.d_k[kvh, :, gi * GW:(gi + 1) * GW],
                                          in_=e.kvb[kvh % 2][:, :]), 16)
        for tc in range(4):
            s.wait_ge(e.sACT.h, act_at(gi, 73 + tc + 1))
            I(e.sVB[tc % 2], s.dma_start(out=e.d_v[gi * GW + tc * 128:gi * GW + (tc + 1) * 128, :],
                                         in_=e.vb[tc % 2][:, :]), 16)
        s.wait_ge(e.sACT.h, act_at(gi, 9))
        I(e.sBC, s.dma_start(out=e.d_bc[:, gi * GW:(gi + 1) * GW], in_=e.scr[3][0:32, :]), 16)
        s.wait_ge(e.sACT.h, act_at(gi, 4))
        I(e.sRW, s.dma_start(out=e.d_rows[0:1, gi * GW:(gi + 1) * GW], in_=e.mrow), 16)
        I(e.sRW, s.dma_start(out=e.d_rows[1:2, gi * GW:(gi + 1) * GW], in_=e.rrow), 16)
    for c in [e.sX1, e.sBC, e.sRW] + e.sDS + e.sUS + e.sKB + e.sVB:
        s.wait_ge(c.h, c.v)


def rope_debug(e, g):
    g.dma_start(out=e.dbg["dbg_cos"], in_=e.cosr[:, 0:1024]).then_inc(e.sGLD.h, 16)
    g.dma_start(out=e.dbg["dbg_sin"], in_=e.sinr[:, 0:1024]).then_inc(e.sGLD.h, 16)
    e.sGLD.v += 2 * 16
    g.wait_ge(e.sGLD.h, e.sGLD.v)


def p0_debug_g(e, g):
    for half in range(2):
        g.dma_start(out=e.dbg["dbg_xn1"][half * 1024:(half + 1) * 1024, :],
                    in_=e.d_xn1[half * 1024:(half + 1) * 1024, 0:1024]).then_inc(e.sGLD.h, 16)
    g.dma_start(out=e.dbg["dbg_k"].rearrange("(a p) t -> a p t", a=KVH),
                in_=e.d_k[:, :, 0:1024]).then_inc(e.sGLD.h, 16)
    g.dma_start(out=e.dbg["dbg_v"], in_=e.d_v[0:1024, :]).then_inc(e.sGLD.h, 16)
    e.sGLD.v += 4 * 16


def p0_debug_s(e, s):
    s.dma_start(out=e.dbg["dbg_delta"], in_=e.d_delta[:, 0:1024]).then_inc(e.sST.h, 16)
    s.dma_start(out=e.dbg["dbg_u"], in_=e.d_u[:, 0:1024]).then_inc(e.sST.h, 16)
    s.dma_start(out=e.dbg["dbg_bc"], in_=e.d_bc[:, 0:1024]).then_inc(e.sST.h, 16)
    s.dma_start(out=e.dbg["dbg_rows"], in_=e.d_rows).then_inc(e.sST.h, 16)
    s.dma_start(out=e.out[0:128, 0:512], in_=e.scr[0][:, 0:512]).then_inc(e.sST.h, 16)
    e.sST.v += 5 * 16
    s.wait_ge(e.sST.h, e.sST.v)
    s.wait_ge(e.sGLD.h, e.sGLD.v)



TSC = 2048


def own_cols(c):
    return (512 * c, 4096 + 512 * (7 - c))


def emit_mamba(e):
    nc = e.nc
    mctx = ExitStack()

    def sb(name, shape, dt):
        return mctx.enter_context(nc.sbuf_tensor(name, shape, dt))
    if "sw_excess" not in e:
        e.sw_excess = {}
    if "pid_s" not in e:
        e.pid_s = nc.sync.partition_id()
    e.negA = sb("negA", [128, 32], F32)
    e.m_dlt = sb("m_dlt", [128, L], F32)
    e.m_ut = sb("m_ut", [128, L], F32)
    e.m_ysb = sb("m_ysb", [128, L], F32)
    e.m_dab = [sb(f"m_dab{i}", [128, TSC], F32) for i in range(2)]
    e.m_bcb = [sb(f"m_bcb{i}", [128, TSC], F32) for i in range(2)]
    e.m_ccb = [sb(f"m_ccb{i}", [128, TSC], F32) for i in range(2)]
    e.m_hsb = [sb(f"m_hsb{i}", [128, TSC], F32) for i in range(2)]

    dve0 = e.sDVE.v + 1
    act0 = e.sACT.v + 1
    bb0 = e.sBB.v
    cb0 = e.sCB.v
    dvev = lambda git, k: dve0 + git * 4 + k
    actv = lambda git: act0 + git + 1
    bbv = lambda git: bb0 + 16 * (git + 1)
    cbv = lambda git: cb0 + 16 * (git + 1)

    seng = nc.sync
    for blkj in range(4):
        ci2, b = blkj // 2, blkj % 2
        for c in nc.Switch(engines=[seng], index=[e.pid_s], n=8):
            row = 256 * c + 128 * ci2
            if blkj > 0:
                seng.wait_ge(e.sACT.h, actv(blkj * 32 - 1))
                seng.wait_ge(e.sDVE.h, dvev(blkj * 32 - 1, 1))
            I(e.sMS, seng.dma_start(out=e.m_dlt[:, :],
                                    in_=e.d_delta[row:row + 128, b * L:(b + 1) * L]), 16)
            e.sMS.v -= 16
            I(e.sMS, seng.dma_start(out=e.m_ut[:, :],
                                    in_=e.d_u[row:row + 128, b * L:(b + 1) * L]), 16)
            e.sMS.v -= 16
            if blkj == 0:
                I(e.sMS, seng.dma_start(
                    out=e.negA[:, :].rearrange("p (ci n) -> p ci n", ci=2),
                    in_=e.Alog.rearrange("(ci dl) n -> dl ci n", dl=128)[:, 2 * c:2 * c + 2, :]), 16)
                e.sMS.v -= 16
        nadd = 3 if blkj == 0 else 2
        e.sMS.v += 16 * nadd
        e.sw_excess["sMS"] = e.sw_excess.get("sMS", 0) + 7 * 16 * nadd

        ms_now = e.sMS.v
        with nc.Block() as blk:
            def fsy(s, blkj=blkj, ci2=ci2, b=b):
                for n in range(NST):
                    for kk in range(2):
                        git = blkj * 32 + n * 2 + kk
                        if git >= 2:
                            s.wait_ge(e.sDVE.h, dvev(git - 2, 4))
                        I(e.sCB, s.dma_start(
                            out=e.m_ccb[git % 2][:, :],
                            in_=e.d_bc[16 + n:17 + n, b * L + kk * TSC:
                                       b * L + (kk + 1) * TSC].partition_broadcast(128)), 16)
                s.wait_ge(e.sDVE.h, dvev(blkj * 32 + 31, 4))
                I(e.sMS, s.dma_start(out=e.d_ysrc[ci2 * 128:(ci2 + 1) * 128,
                                                  b * L:(b + 1) * L],
                                     in_=e.m_ysb[:, :]), 16)

            def fgp(g, blkj=blkj, b=b):
                if blkj == 0:
                    g.wait_ge(e.sBC.h, e.sBC.v)
                for n in range(NST):
                    for kk in range(2):
                        git = blkj * 32 + n * 2 + kk
                        if git >= 2:
                            g.wait_ge(e.sDVE.h, dvev(git - 2, 2))
                        I(e.sBB, g.dma_start(
                            out=e.m_bcb[git % 2][:, :],
                            in_=e.d_bc[n:n + 1, b * L + kk * TSC:
                                       b * L + (kk + 1) * TSC].partition_broadcast(128)), 16)

            def fsc(s, blkj=blkj, ci2=ci2, ms_now=ms_now):
                s.wait_ge(e.sMS.h, ms_now)
                if blkj == 0:
                    I(e.sACT, s.activation(out=e.negA[:, :], in_=e.negA[:, :],
                                           func=AF.Exp))
                    s.wait_ge(e.sDVE.h, dve0)
                for n in range(NST):
                    for kk in range(2):
                        git = blkj * 32 + n * 2 + kk
                        if git >= 2:
                            s.wait_ge(e.sDVE.h, dvev(git - 2, 2))
                        I(e.sACT, s.activation(
                            out=e.m_dab[git % 2][:, :],
                            in_=e.m_dlt[:, kk * TSC:(kk + 1) * TSC],
                            func=AF.Exp,
                            scale=e.negA[:, ci2 * 16 + n:ci2 * 16 + n + 1]))

            def fv(v, blkj=blkj, ms_now=ms_now):
                v.wait_ge(e.sMS.h, ms_now)
                if blkj == 0:
                    v.wait_ge(e.sACT.h, act0)
                    I(e.sDVE, v.tensor_scalar(out=e.negA[:, :], in0=e.negA[:, :],
                                              scalar1=-1.0, scalar2=None,
                                              op0=ALU.mult, op1=ALU.bypass,
                                              accum_out=None))
                for n in range(NST):
                    for kk in range(2):
                        git = blkj * 32 + n * 2 + kk
                        p = git % 2
                        v.wait_ge(e.sBB.h, bbv(git))
                        I(e.sDVE, v.tensor_mul(e.m_bcb[p][:, :],
                                               e.m_ut[:, kk * TSC:(kk + 1) * TSC],
                                               e.m_bcb[p][:, :]))
                        v.wait_ge(e.sACT.h, actv(git))
                        init = 0.0 if kk == 0 else e.m_hsb[1 - p][:, TSC - 1:TSC]
                        I(e.sDVE, v.tensor_tensor_scan(out=e.m_hsb[p][:, :],
                                                       data0=e.m_dab[p][:, :],
                                                       data1=e.m_bcb[p][:, :],
                                                       initial=init,
                                                       op0=ALU.mult, op1=ALU.add))
                        v.wait_ge(e.sCB.h, cbv(git))
                        I(e.sDVE, v.tensor_mul(e.m_ccb[p][:, :], e.m_hsb[p][:, :],
                                               e.m_ccb[p][:, :]))
                        ysl = e.m_ysb[:, kk * TSC:(kk + 1) * TSC]
                        if n == 0:
                            I(e.sDVE, v.tensor_copy(out=ysl, in_=e.m_ccb[p][:, :]))
                        else:
                            I(e.sDVE, v.tensor_add(ysl, ysl, e.m_ccb[p][:, :]))

            blk.sync(fsy)
            blk.gpsimd(fgp)
            blk.scalar(fsc)
            blk.vector(fv)

    with nc.Block() as blk:
        def fg(g):
            g.wait_ge(e.sMS.h, e.sMS.v)
            I(e.sAG, g.collective_compute(
                "AllGather", ALU.bypass,
                replica_groups=[list(range(8))],
                ins=[e.d_ysrc[:, :]], outs=[e.d_ygat[:, :]],
            ))
        blk.gpsimd(fg)
    mctx.close()



ATT_PAIRS = [(b, k) for b in range(2) for k in range(KVH)]


def att_E(c, beta):
    return 4 * (c + 1) if beta == 0 else 4 * (8 - c)


def act_pairh_base(c, pair, h):
    off = 2
    for p2 in range(pair):
        off += 4 * (att_E(c, ATT_PAIRS[p2][0]) + 4)
    off += h * (att_E(c, ATT_PAIRS[pair][0]) + 4)
    return off


class SwBook:

    def __init__(self, e, names):
        self.e = e
        self.names = names
        self.base = {}

    def enter_body(self):
        for n in self.names:
            c = self.e[n]
            if n not in self.base:
                self.base[n] = c.v
            c.v = self.base[n]

    def finish(self):
        for n in self.names:
            c = self.e[n]
            per = c.v - self.base[n]
            self.e.sw_excess[n] = self.e.sw_excess.get(n, 0) + 7 * per


def emit_attention(e):
    nc = e.nc
    if "sw_excess" not in e:
        e.sw_excess = {}
    e.avctx = ExitStack()
    e.av_sb = e.avctx.enter_context(
        nc.sbuf_tensor("av_sb", [128, NH * 1024], BF16))
    actx = ExitStack()

    def sb(name, shape, dt):
        return actx.enter_context(nc.sbuf_tensor(name, shape, dt))

    e.qin = sb("qin", [128, DH * 1024], BF16)
    e.q_sb = sb("q_sb", [128, NH * 1024], BF16)
    e.kbufA = sb("kbufA", [128, 32 * 128], BF16)
    e.vbufA = sb("vbufA", [128, 32 * 128], BF16)
    e.wq_sb = sb("wq_sb", [128, 2 * DH * 128], BF16)
    e.probs = [sb(f"probs{i}", [128, 512], BF16) for i in range(2)]
    e.avf = sb("avf", [128, 512], F32)
    e.q2b = sb("q2b", [128, 1024], BF16)
    e.bias_col = sb("bias_col", [128, 1], F32)
    e.srow = sb("srow", [1, 512], F32)
    e.trow = sb("trow", [1, 8], F32)

    teng, seng, veng, geng, speng = nc.tensor, nc.scalar, nc.vector, nc.gpsimd, nc.sync
    if "pid_t" not in e:
        e.pid_t = teng.partition_id()
        e.pid_sc = seng.partition_id()
        e.pid_v = veng.partition_id()
        e.pid_g2 = geng.partition_id()
        e.pid_sp2 = speng.partition_id()
    book = SwBook(e, ["sPE", "sACT", "sDVE", "sPOOL", "sQS", "sKV", "sWQ0", "sWQ1"])
    for c in nc.Switch(engines=[teng, seng, veng, geng, speng],
                       index=[e.pid_t, e.pid_sc, e.pid_v, e.pid_g2, e.pid_sp2], n=8):
        book.enter_body()
        e.ab = dict(book.base)
        attn_body(e, c, teng, seng, veng, geng, speng)
    book.finish()

    if DEBUG:
        with nc.Block() as blk:
            blk.gpsimd(lambda s: attn_debug(e, s))
    actx.close()


def attn_body(e, c, t, s, v, g, sp):
    A0, A1 = own_cols(c)
    ab = e.ab
    peq = lambda hq, k: ab["sPE"] + 2 * hq + k
    peqn = lambda hq: ab["sPE"] + 32 + hq + 1
    pebias = ab["sPE"] + 49
    dveq = lambda hq, ib: ab["sDVE"] + 12 * hq + 6 * (ib + 1)
    dveqn = lambda hq: ab["sDVE"] + 192 + 2 * (hq + 1)
    dveu = ab["sDVE"] + 226
    dveneg = ab["sDVE"] + 227
    dveav = lambda pair, h: ab["sDVE"] + 227 + pair * 4 + h + 1
    actexp = lambda pair, h, kc: ab["sACT"] + act_pairh_base(c, pair, h) + kc + 1
    actsum = lambda pair, h: (ab["sACT"] + act_pairh_base(c, pair, h)
                              + att_E(c, ATT_PAIRS[pair][0]) + 1)
    actrec = lambda pair, h: actsum(pair, h) + 2
    actavc = lambda pair, h: actsum(pair, h) + 3
    poolq2 = lambda hq: ab["sPOOL"] + hq + 1
    poolaff = lambda pair, h, j: ab["sPOOL"] + 16 + (pair * 4 + h) * 4 + j + 1

    pe_off = {}

    def emit_sumav(pair, h, kc, beta):
        E = att_E(c, beta)
        t.wait_ge(e.sACT.h, actexp(pair, h, kc))
        if kc >= E - 4:
            t.wait_ge(e.sPOOL.h, poolaff(pair, h, kc - (E - 4)))
        t.matmul(e.B[3][0:1, :], e.ones_bf[:, :], e.probs[kc % 2][:, :],
                 start=(kc == 0), stop=(kc == E - 1))
        ins = t.matmul(e.B[2][:, :], e.vbufA[:, kc * 128:(kc + 1) * 128],
                       e.probs[kc % 2][:, :], start=(kc == 0), stop=(kc == E - 1))
        I(e.sPE, ins)
        pe_off[("av", pair, h, kc)] = e.sPE.v

    def emit_tensor():
        t.wait_ge(e.sQS.h, ab["sQS"] + 16 * 4)
        for hq in range(NH):
            t.wait_ge(e["sWQ%d" % (hq % 2)].h, ab["sWQ%d" % (hq % 2)] + 32 * (hq // 2 + 1))
            for grp in range(2):
                if hq >= 1:
                    t.wait_ge(e.sDVE.h, dveq(hq - 1, grp))
                for dh in range(DH):
                    ins = t.matmul(
                        e.B[grp][:, :],
                        e.wq_sb[:, ((hq % 2) * DH + dh) * 128:((hq % 2) * DH + dh + 1) * 128],
                        e.qin[:, dh * 1024 + grp * 512:dh * 1024 + (grp + 1) * 512],
                        start=(dh == 0), stop=(dh == DH - 1))
                I(e.sPE, ins)
        for hq in range(NH):
            t.wait_ge(e.sPOOL.h, poolq2(hq))
            if hq >= 1:
                t.wait_ge(e.sDVE.h, dveqn(hq - 1))
            t.matmul(e.B[5][0:1, :], e.ones_bf[:, :], e.q2b[:, 0:512],
                     start=True, stop=False)
            I(e.sPE, t.matmul(e.B[5][0:1, :], e.ones_bf[:, :], e.q2b[:, 512:1024],
                              start=False, stop=True))
        t.wait_ge(e.sACT.h, ab["sACT"] + 2)
        I(e.sPE, t.matmul(e.B[4][:, 0:1], e.ones1f[:, :], e.trow[0:1, 3:4],
                          start=True, stop=True))
        t.wait_ge(e.sDVE.h, ab["sDVE"] + 192)
        for pair, (beta, kvh) in enumerate(ATT_PAIRS):
            E = att_E(c, beta)
            t.wait_ge(e.sKV.h, ab["sKV"] + 32 * (pair + 1))
            for h in range(4):
                hglob = kvh * 4 + h
                qsl = e.q_sb[:, hglob * 1024 + beta * 512:hglob * 1024 + (beta + 1) * 512]
                if h == 0 and pair == 0:
                    t.wait_ge(e.sDVE.h, dveneg)
                else:
                    ph, hh = (pair, h - 1) if h > 0 else (pair - 1, 3)
                    t.wait_ge(e.sACT.h, actavc(ph, hh))
                for kc in range(E):
                    if kc >= 2:
                        t.wait_ge(e.sACT.h, actexp(pair, h, kc - 2))
                    ins = t.matmul(e.B[kc % 2][:, :], e.kbufA[:, kc * 128:(kc + 1) * 128],
                                   qsl, start=True, stop=True)
                    I(e.sPE, ins)
                    pe_off[("score", pair, h, kc)] = e.sPE.v
                    if kc >= 1:
                        emit_sumav(pair, h, kc - 1, beta)
                emit_sumav(pair, h, E - 1, beta)
                t.wait_ge(e.sACT.h, actrec(pair, h))
                if pair == 0 and h == 0:
                    t.wait_ge(e.sDVE.h, dveneg)
                else:
                    ph, hh = (pair, h - 1) if h > 0 else (pair - 1, 3)
                    t.wait_ge(e.sDVE.h, dveav(ph, hh))
                I(e.sPE, t.matmul(e.B[4][:, :], e.ones1f[:, :], e.srow[0:1, :],
                                  start=True, stop=True))
                pe_off[("recip", pair, h)] = e.sPE.v

    emit_tensor()

    sp.wait_ge(e.sMS.h, e.sMS.v)
    sp.wait_ge(e.sDVE.h, ab["sDVE"])
    sp.wait_ge(e.sPE.h, ab["sPE"])
    sp.wait_ge(e.sACT.h, ab["sACT"])
    for half in range(2):
        for ib, a in enumerate((A0, A1)):
            I(e.sQS, sp.dma_start(
                out=e.qin[:, :].rearrange("p (dh tk) -> p dh tk", dh=DH)[
                    :, half * 8:(half + 1) * 8, ib * 512:(ib + 1) * 512],
                in_=e.d_xn1.rearrange("(dh dl) t -> dl dh t", dl=128)[
                    :, half * 8:(half + 1) * 8, a:a + 512]), 16)
    for pair, (beta, kvh) in enumerate(ATT_PAIRS):
        E = att_E(c, beta)
        kb = beta * L
        if pair >= 1:
            sp.wait_ge(e.sPE.h, pe_off[("recip", pair - 1, 3)])
        I(e.sKV, sp.dma_start(out=e.kbufA[:, 0:E * 128],
                              in_=e.d_k[kvh, :, kb:kb + E * 128]), 16)
        I(e.sKV, sp.dma_start(
            out=e.vbufA[:, 0:E * 128].rearrange("p (kc x) -> p kc x", kc=E),
            in_=e.d_v[kb:kb + E * 128, kvh * 128:(kvh + 1) * 128].rearrange(
                "(kc p) x -> p kc x", p=128)), 16)

    g.wait_ge(e.sMS.h, e.sMS.v)
    g.wait_ge(e.sDVE.h, ab["sDVE"])
    g.wait_ge(e.sPE.h, ab["sPE"])
    g.wait_ge(e.sACT.h, ab["sACT"])
    for hq in range(NH):
        if hq >= 2:
            g.wait_ge(e.sPE.h, peq(hq - 2, 2))
        for half in range(2):
            I(e["sWQ%d" % (hq % 2)], g.dma_start(
                out=e.wq_sb[:, ((hq % 2) * DH + half * 8) * 128:
                            ((hq % 2) * DH + (half + 1) * 8) * 128].rearrange(
                                "p (dh x) -> p dh x", dh=8),
                in_=e.wqh[hq, :, half * 8:(half + 1) * 8, :]), 16)
    for hq in range(NH):
        g.wait_ge(e.sDVE.h, dveq(hq, 1))
        if hq >= 1:
            g.wait_ge(e.sPE.h, peqn(hq - 1))
        I(e.sPOOL, g.tensor_tensor(out=e.q2b[:, :], in0=e.q_sb[:, hq * 1024:(hq + 1) * 1024],
                                   in1=e.q_sb[:, hq * 1024:(hq + 1) * 1024], op=ALU.mult))
    for pair, (beta, kvh) in enumerate(ATT_PAIRS):
        E = att_E(c, beta)
        qposl = 512 * c if beta == 0 else 512 * (7 - c)
        for h in range(4):
            for j in range(4):
                kc = E - 4 + j
                g.wait_ge(e.sACT.h, actexp(pair, h, kc))
                I(e.sPOOL, g.affine_select(
                    out=e.probs[kc % 2][:, :], in_=e.probs[kc % 2][:, :],
                    pattern=[[1, 512]], compare_op=ALU.is_ge, fill=0.0,
                    base=qposl - kc * 128, channel_multiplier=-1))

    s.wait_ge(e.sDVE.h, dveu)
    I(e.sACT, s.activation(out=e.trow[0:1, 2:3], in_=e.trow[0:1, 0:1], func=AF.Ln))
    W(s, e.sACT)
    I(e.sACT, s.activation(out=e.trow[0:1, 3:4], in_=e.trow[0:1, 2:3], func=AF.Exp,
                           scale=0.5))
    for pair, (beta, kvh) in enumerate(ATT_PAIRS):
        E = att_E(c, beta)
        for h in range(4):
            for kc in range(E):
                s.wait_ge(e.sPE.h, pe_off[("score", pair, h, kc)])
                if kc >= 2:
                    s.wait_ge(e.sPE.h, pe_off[("av", pair, h, kc - 2)])
                if pair == 0 and h == 0 and kc == 0:
                    s.wait_ge(e.sDVE.h, dveneg)
                I(e.sACT, s.activation(out=e.probs[kc % 2][:, :], in_=e.B[kc % 2][:, :],
                                       func=AF.Exp, scale=1.0 / math.sqrt(128.0),
                                       bias=e.bias_col[:, 0:1]))
            s.wait_ge(e.sPE.h, pe_off[("av", pair, h, E - 1)])
            I(e.sACT, s.copy(out=e.srow[:, :], in_=e.B[3][0:1, :]))
            W(s, e.sACT)
            I(e.sACT, s.activation(out=e.scr[2][0:1, 0:512], in_=e.srow[:, :], func=AF.Ln))
            W(s, e.sACT)
            I(e.sACT, s.activation(out=e.srow[:, :], in_=e.scr[2][0:1, 0:512], func=AF.Exp,
                                   scale=-1.0))
            if pair > 0 or h > 0:
                ph, hh = (pair, h - 1) if h > 0 else (pair - 1, 3)
                s.wait_ge(e.sDVE.h, dveav(ph, hh))
            I(e.sACT, s.copy(out=e.avf[:, :], in_=e.B[2][:, :]))

    for hq in range(NH):
        for ib in range(2):
            v.wait_ge(e.sPE.h, peq(hq, ib + 1))
            a = (A0, A1)[ib]
            b = e.B[ib]
            cs = e.cosr[:, a:a + 512]
            sn = e.sinr[:, a:a + 512]
            dst = e.q_sb[:, hq * 1024 + ib * 512:hq * 1024 + (ib + 1) * 512]
            W(v, e.sDVE)
            I(e.sDVE, v.tensor_mul(e.scr[0][0:64, :], b[0:64, :], cs))
            I(e.sDVE, v.tensor_mul(e.scr[1][0:64, :], b[64:128, :], sn))
            W(v, e.sDVE)
            I(e.sDVE, v.tensor_sub(dst[0:64, :], e.scr[0][0:64, :], e.scr[1][0:64, :]))
            W(v, e.sDVE)
            I(e.sDVE, v.tensor_mul(e.scr[0][0:64, :], b[64:128, :], cs))
            I(e.sDVE, v.tensor_mul(e.scr[1][0:64, :], b[0:64, :], sn))
            W(v, e.sDVE)
            I(e.sDVE, v.tensor_add(dst[64:128, :], e.scr[0][0:64, :], e.scr[1][0:64, :]))
    for hq in range(NH):
        v.wait_ge(e.sPE.h, peqn(hq))
        W(v, e.sDVE)
        I(e.sDVE, v.reduce_max(e.scr[2][0:1, 0:1], e.B[5][0:1, :], axis=AXX))
        W(v, e.sDVE)
        if hq == 0:
            I(e.sDVE, v.tensor_copy(out=e.trow[0:1, 4:5], in_=e.scr[2][0:1, 0:1]))
        else:
            I(e.sDVE, v.tensor_max(e.trow[0:1, 4:5], e.trow[0:1, 4:5], e.scr[2][0:1, 0:1]))
    W(v, e.sDVE)
    I(e.sDVE, v.reduce_max(e.scr[2][0:1, 0:1], e.kmax2[:, :], axis=AXX))
    W(v, e.sDVE)
    I(e.sDVE, v.scalar_tensor_tensor(out=e.trow[0:1, 0:1], in0=e.trow[0:1, 4:5],
                                     scalar=1.0 / 128.0, in1=e.scr[2][0:1, 0:1],
                                     op0=ALU.mult, op1=ALU.mult))
    v.wait_ge(e.sPE.h, pebias)
    W(v, e.sDVE)
    I(e.sDVE, v.tensor_scalar_mul(e.bias_col[:, :], e.B[4][:, 0:1], -1.0))
    for pair, (beta, kvh) in enumerate(ATT_PAIRS):
        for h in range(4):
            hglob = kvh * 4 + h
            v.wait_ge(e.sACT.h, actavc(pair, h))
            v.wait_ge(e.sPE.h, pe_off[("recip", pair, h)])
            W(v, e.sDVE)
            I(e.sDVE, v.tensor_mul(
                e.av_sb[:, hglob * 1024 + beta * 512:hglob * 1024 + (beta + 1) * 512],
                e.avf[:, :], e.B[4][:, :]))


def attn_debug(e, s):
    s.wait_ge(e.sDVE.h, e.sDVE.v)
    for half in range(2):
        I(e.sGLD, s.dma_start(
            out=e.dbg["dbg_av"].rearrange("(hq dl) t -> dl hq t", dl=128)[
                :, half * 8:(half + 1) * 8, :],
            in_=e.av_sb[:, half * 8 * 1024:(half + 1) * 8 * 1024].rearrange(
                "p (hq t) -> p hq t", hq=8)), 16)
        I(e.sGLD, s.dma_start(
            out=e.dbg["dbg_q"].rearrange("(hq dl) t -> dl hq t", dl=128)[
                :, half * 8:(half + 1) * 8, :],
            in_=e.q_sb[:, half * 8 * 1024:(half + 1) * 8 * 1024].rearrange(
                "p (hq t) -> p hq t", hq=8)), 16)
    s.wait_ge(e.sGLD.h, e.sGLD.v)



def emit_h2(e):
    nc = e.nc
    h2ctx = ExitStack()

    def sb(name, shape, dt):
        return h2ctx.enter_context(nc.sbuf_tensor(name, shape, dt))

    e.xbuf2 = sb("xbuf2", [128, DH * 1024], F32)
    e.wo_sb = sb("wo_sb", [128, 2 * DH * 128], BF16)
    e.rows_sb = sb("rows_sb", [1, 8 * 512], F32)
    e.mr_sb = sb("mr_sb", [128, 4 * 512], F32)
    e.gw_sb = sb("gw_sb", [128, DH * NE], F32)
    e.id128 = sb("id128", [128, 128], F32)
    e.xnt = [sb(f"xnt{i}", [128, 1024], BF16) for i in range(2)]
    e.wch = sb("wch", [128, 64], F32)
    e.logacc = sb("logacc", [128, 64], F32)
    e.wr_sb = sb("wr_sb", [8, 1024], F32)

    hb = {nm: e[nm].v for nm in
          ["sPE", "sACT", "sDVE", "sPOOL", "sXB", "sWO0", "sWO1", "sH2",
           "sXN0", "sXN1", "sSP"]}
    e.h2b = hb
    A = lambda ci, tg, k: hb["sDVE"] + (ci * 2 + tg) * 4 + k
    Bv = lambda ci, tg, k: hb["sDVE"] + 128 + ci * 7 + tg * 3 + k
    accL = lambda ci: hb["sDVE"] + 128 + ci * 7 + 7
    T = lambda ch, k: hb["sDVE"] + 240 + ch * 10 + k
    actmr = hb["sACT"] + 4
    actexp = lambda tg: hb["sACT"] + 6 + tg + 1
    actsig = lambda ch: hb["sACT"] + 8 + ch + 1
    actwr = hb["sACT"] + 18

    seng = nc.sync
    if "pid_s" not in e:
        e.pid_s = seng.partition_id()
    book = SwBook(e, ["sXB"])
    for c in nc.Switch(engines=[seng], index=[e.pid_s], n=8):
        book.enter_body()
        A0, A1 = own_cols(c)
        for nm in ["sDVE", "sPE", "sACT", "sPOOL"]:
            seng.wait_ge(e[nm].h, hb[nm])
        for half in range(2):
            for ib, a in enumerate((A0, A1)):
                I(e.sXB, seng.dma_start(
                    out=e.xbuf2[:, :].rearrange("p (ci t) -> p ci t", ci=DH)[
                        :, half * 8:(half + 1) * 8, ib * 512:(ib + 1) * 512],
                    in_=e.xT.rearrange("(ci dl) t -> dl ci t", dl=128)[
                        :, half * 8:(half + 1) * 8, a:a + 512]), 16)
        for r in range(2):
            for ib, a in enumerate((A0, A1)):
                I(e.sXB, seng.dma_start(
                    out=e.rows_sb[0:1, (r * 2 + ib) * 512:(r * 2 + ib + 1) * 512],
                    in_=e.d_rows[r:r + 1, a:a + 512]), 16)
    book.finish()
    sxb_full = hb["sXB"] + 128

    e.h2pe = {}

    with nc.Block() as blk:
        blk.tensor(lambda t: h2_tensor(e, t, hb, A, Bv, accL, T, actmr, actexp, sxb_full))
        blk.scalar(lambda s: h2_scalar(e, s, hb, T))
        blk.vector(lambda v: h2_vector(e, v, hb, A, Bv, T, actmr, actsig, sxb_full))
        blk.gpsimd(lambda g: h2_gpsimd(e, g, hb))
        blk.sync(lambda s: h2_sync(e, s, hb, A, Bv, actwr))

    if DEBUG:
        with nc.Block() as blk2:
            blk2.sync(lambda s: h2_debug_s(e, s))
            blk2.gpsimd(lambda g: h2_debug_g(e, g))
    h2ctx.close()


def h2_tensor(e, t, hb, A, Bv, accL, T, actmr, actexp, sxb_full):
    pe = e.h2pe
    t.wait_ge(e.sDVE.h, hb["sDVE"])
    t.wait_ge(e.sACT.h, hb["sACT"])
    t.wait_ge(e.sPOOL.h, hb["sPOOL"])
    t.wait_ge(e.sXB.h, sxb_full)
    for i in range(4):
        I(e.sPE, t.matmul(e.B[2 + i][:, :], e.ones1f[:, :],
                          e.rows_sb[0:1, i * 512:(i + 1) * 512],
                          start=True, stop=True))
        pe[("mr", i)] = e.sPE.v
    for ci in range(DH):
        for tg in range(2):
            j = ci * 2 + tg
            sw = e["sWO%d" % (ci % 2)]
            t.wait_ge(sw.h, hb["sWO%d" % (ci % 2)] + 32 * (ci // 2 + 1))
            if j >= 2:
                pc, pt = (j - 2) // 2, (j - 2) % 2
                t.wait_ge(e.sDVE.h, A(pc, pt, 3))
            for hq in range(DH):
                ins = t.matmul(
                    e.B[j % 2][:, :],
                    e.wo_sb[:, ((ci % 2) * DH + hq) * 128:((ci % 2) * DH + hq + 1) * 128],
                    e.av_sb[:, hq * 1024 + tg * 512:hq * 1024 + (tg + 1) * 512],
                    start=(hq == 0), stop=(hq == DH - 1))
            I(e.sPE, ins)
            pe[("wo", ci, tg)] = e.sPE.v
        if ci >= 1:
            for tg in range(2):
                t.wait_ge(e.sDVE.h, A(ci - 1, tg, 4))
                I(e.sPE, t.matmul(e.B[2 + tg][0:1, :], e.cst[:, 1:2],
                                  e.scr[2 + tg][:, :],
                                  start=(ci - 1 == 0), stop=False))
                pe[("v2", ci - 1, tg)] = e.sPE.v
    for tg in range(2):
        t.wait_ge(e.sDVE.h, A(15, tg, 4))
        I(e.sPE, t.matmul(e.B[2 + tg][0:1, :], e.cst[:, 1:2], e.scr[2 + tg][:, :],
                          start=False, stop=True))
        pe[("v2", 15, tg)] = e.sPE.v
    for tg in range(2):
        t.wait_ge(e.sACT.h, actexp(tg))
        t.wait_ge(e.sDVE.h, A(15, tg, 3))
        I(e.sPE, t.matmul(e.B[tg][:, :], e.ones1f[:, :],
                          e.rows_sb[0:1, (6 + tg) * 512:(7 + tg) * 512],
                          start=True, stop=True))
        pe[("rbc", tg)] = e.sPE.v
    t.wait_ge(e.sSP.h, hb["sSP"] + 16)
    for ci in range(DH):
        for ch in range(8):
            t.wait_ge(e.sDVE.h, Bv(ci, ch // 4, 2))
            if ci >= 1 and ch == 0:
                t.wait_ge(e.sDVE.h, accL(ci - 1))
            I(e.sPE, t.matmul(e.B[4][:, ch * 8:(ch + 1) * 8],
                              e.scr[ch // 4][:, (ch % 4) * 128:(ch % 4 + 1) * 128],
                              e.gw_sb[:, ci * 8:(ci + 1) * 8],
                              start=True, stop=True))
            pe[("gate", ci, ch)] = e.sPE.v
    t.wait_ge(e.sPOOL.h, hb["sPOOL"] + 2)
    for ch in range(8):
        t.wait_ge(e.sDVE.h, T(ch, 10))
        I(e.sPE, t.matmul(e.B[5 + ch // 4][0:8, (ch % 4) * 128:(ch % 4 + 1) * 128],
                          e.wch[:, ch * 8:(ch + 1) * 8], e.id128[:, :],
                          start=True, stop=True))
        pe[("tr", ch)] = e.sPE.v


def h2_scalar(e, s, hb, T):
    pe = e.h2pe
    for i in range(4):
        s.wait_ge(e.sPE.h, pe[("mr", i)])
        I(e.sACT, s.copy(out=e.mr_sb[:, i * 512:(i + 1) * 512], in_=e.B[2 + i][:, :]))
    for tg in range(2):
        s.wait_ge(e.sPE.h, pe[("v2", 15, tg)])
        I(e.sACT, s.activation(out=e.rows_sb[0:1, (4 + tg) * 512:(5 + tg) * 512],
                               in_=e.B[2 + tg][0:1, :], func=AF.Ln,
                               scale=1.0 / float(D), bias=EPS))
    for tg in range(2):
        W(s, e.sACT)
        I(e.sACT, s.activation(out=e.rows_sb[0:1, (6 + tg) * 512:(7 + tg) * 512],
                               in_=e.rows_sb[0:1, (4 + tg) * 512:(5 + tg) * 512],
                               func=AF.Exp, scale=-0.5))
    for ch in range(8):
        s.wait_ge(e.sDVE.h, T(ch, 5))
        I(e.sACT, s.activation(out=e.scr[4][:, 5:6], in_=e.scr[4][:, 4:5],
                               func=AF.Sigmoid))
    for i in range(2):
        s.wait_ge(e.sPE.h, pe[("tr", i * 4 + 3)])
        I(e.sACT, s.copy(out=e.wr_sb[:, i * 512:(i + 1) * 512],
                         in_=e.B[5 + i][0:8, :]))


def h2_vector(e, v, hb, A, Bv, T, actmr, actsig, sxb_full):
    pe = e.h2pe
    v.wait_ge(e.sXB.h, sxb_full)
    v.wait_ge(e.sACT.h, actmr)
    for ci in range(DH):
        for tg in range(2):
            xt = e.xbuf2[:, ci * 1024 + tg * 512:ci * 1024 + (tg + 1) * 512]
            W(v, e.sDVE)
            I(e.sDVE, v.tensor_sub(xt, xt, e.mr_sb[:, tg * 512:(tg + 1) * 512]))
            W(v, e.sDVE)
            I(e.sDVE, v.tensor_mul(xt, xt, e.mr_sb[:, (2 + tg) * 512:(3 + tg) * 512]))
            v.wait_ge(e.sPE.h, pe[("wo", ci, tg)])
            W(v, e.sDVE)
            I(e.sDVE, v.tensor_add(xt, xt, e.B[(ci * 2 + tg) % 2][:, :]))
            if ci >= 1:
                v.wait_ge(e.sPE.h, pe[("v2", ci - 1, tg)])
            W(v, e.sDVE)
            I(e.sDVE, v.tensor_mul(e.scr[2 + tg][:, :], xt, xt))
    for ci in range(DH):
        for tg in range(2):
            xt = e.xbuf2[:, ci * 1024 + tg * 512:ci * 1024 + (tg + 1) * 512]
            v.wait_ge(e.sPE.h, pe[("rbc", tg)])
            if ci >= 1:
                v.wait_ge(e.sPE.h, pe[("gate", ci - 1, 7)])
            W(v, e.sDVE)
            I(e.sDVE, v.tensor_mul(e.scr[tg][:, :], xt, e.B[tg][:, :]))
            W(v, e.sDVE)
            I(e.sDVE, v.tensor_scalar(out=e.scr[tg][:, :], in0=e.scr[tg][:, :],
                                      scalar1=e.n2w_c[:, ci:ci + 1], scalar2=None,
                                      op0=ALU.mult, op1=ALU.bypass, accum_out=None))
            if ci >= 2:
                sx = e["sXN%d" % (ci % 2)]
                v.wait_ge(sx.h, hb["sXN%d" % (ci % 2)] + 16 * (ci // 2))
            W(v, e.sDVE)
            I(e.sDVE, v.tensor_copy(out=e.xnt[ci % 2][:, tg * 512:(tg + 1) * 512],
                                    in_=e.scr[tg][:, :]))
        v.wait_ge(e.sPE.h, pe[("gate", ci, 7)])
        W(v, e.sDVE)
        if ci == 0:
            I(e.sDVE, v.tensor_copy(out=e.logacc[:, :], in_=e.B[4][:, 0:64]))
        else:
            I(e.sDVE, v.tensor_add(e.logacc[:, :], e.logacc[:, :], e.B[4][:, 0:64]))
    s4, s5 = e.scr[4], e.scr[5]
    for ch in range(8):
        bch = e.logacc[:, ch * 8:(ch + 1) * 8]
        W(v, e.sDVE)
        I(e.sDVE, v.reduce_max(s4[:, 0:1], bch, axis=AXX))
        W(v, e.sDVE)
        I(e.sDVE, v.tensor_scalar(out=s5[:, 0:8], in0=bch, scalar1=s4[:, 0:1],
                                  scalar2=None, op0=ALU.is_equal, op1=ALU.bypass,
                                  accum_out=None))
        W(v, e.sDVE)
        I(e.sDVE, v.scalar_tensor_tensor(out=s5[:, 8:16], in0=s5[:, 0:8],
                                         scalar=-1e30, in1=bch,
                                         op0=ALU.mult, op1=ALU.add))
        W(v, e.sDVE)
        I(e.sDVE, v.reduce_max(s4[:, 1:2], s5[:, 8:16], axis=AXX))
        W(v, e.sDVE)
        I(e.sDVE, v.tensor_sub(s4[:, 4:5], s4[:, 0:1], s4[:, 1:2]))
        v.wait_ge(e.sACT.h, actsig(ch))
        W(v, e.sDVE)
        I(e.sDVE, v.tensor_scalar(out=s4[:, 6:7], in0=s4[:, 5:6], scalar1=-1.0,
                                  scalar2=1.0, op0=ALU.mult, op1=ALU.add,
                                  accum_out=None))
        W(v, e.sDVE)
        I(e.sDVE, v.tensor_scalar(out=s5[:, 16:24], in0=s5[:, 0:8], scalar1=s4[:, 5:6],
                                  scalar2=None, op0=ALU.mult, op1=ALU.bypass,
                                  accum_out=None))
        W(v, e.sDVE)
        I(e.sDVE, v.tensor_scalar(out=s5[:, 24:32], in0=s5[:, 8:16], scalar1=s4[:, 1:2],
                                  scalar2=None, op0=ALU.is_equal, op1=ALU.bypass,
                                  accum_out=None))
        W(v, e.sDVE)
        I(e.sDVE, v.tensor_scalar(out=s5[:, 32:40], in0=s5[:, 24:32], scalar1=s4[:, 6:7],
                                  scalar2=None, op0=ALU.mult, op1=ALU.bypass,
                                  accum_out=None))
        W(v, e.sDVE)
        I(e.sDVE, v.tensor_add(e.wch[:, ch * 8:(ch + 1) * 8], s5[:, 16:24],
                               s5[:, 32:40]))


def h2_gpsimd(e, g, hb):
    pe = e.h2pe
    for nm in ["sDVE", "sPE", "sACT", "sPOOL"]:
        g.wait_ge(e[nm].h, hb[nm])
    I(e.sPOOL, g.memset(e.id128[:, :], 1.0))
    W(g, e.sPOOL)
    I(e.sPOOL, g.affine_select(out=e.id128[:, :], in_=e.id128[:, :],
                               pattern=[[-1, 128]], compare_op=ALU.is_equal,
                               fill=0.0, base=0, channel_multiplier=1))
    for ci in range(DH):
        if ci >= 2:
            g.wait_ge(e.sPE.h, pe[("wo", ci - 2, 1)])
        for half in range(2):
            I(e["sWO%d" % (ci % 2)], g.dma_start(
                out=e.wo_sb[:, ((ci % 2) * DH + half * 8) * 128:
                            ((ci % 2) * DH + (half + 1) * 8) * 128].rearrange(
                                "p (hq x) -> p hq x", hq=8),
                in_=e.woh[ci, :, half * 8:(half + 1) * 8, :]), 16)


def h2_sync(e, s, hb, A, Bv, actwr):
    I(e.sSP, s.dma_start(
        out=e.gw_sb[:, :].rearrange("p (ci x) -> p ci x", ci=DH),
        in_=e.gwt.rearrange("(ci dl) x -> dl ci x", dl=128)), 16)
    for ci in range(DH):
        s.wait_ge(e.sDVE.h, A(ci, 1, 3))
        I(e.sH2, s.dma_start(out=e.d_h2[ci * 128:(ci + 1) * 128, :],
                             in_=e.xbuf2[:, ci * 1024:(ci + 1) * 1024]), 16)
    for ci in range(DH):
        s.wait_ge(e.sDVE.h, Bv(ci, 1, 3))
        I(e["sXN%d" % (ci % 2)], s.dma_start(
            out=e.d_xn2[ci * 128:(ci + 1) * 128, :], in_=e.xnt[ci % 2][:, :]), 16)
    s.wait_ge(e.sACT.h, actwr)
    I(e.sH2, s.dma_start(out=e.d_w, in_=e.wr_sb[:, :]), 16)
    for c in [e.sH2, e.sXN0, e.sXN1]:
        s.wait_ge(c.h, c.v)


def h2_debug_s(e, s):
    I(e.sST, s.dma_start(out=e.dbg["dbg_h2"], in_=e.d_h2), 16)
    I(e.sST, s.dma_start(out=e.dbg["dbg_w"], in_=e.d_w), 16)
    s.wait_ge(e.sST.h, e.sST.v)


def h2_debug_g(e, g):
    I(e.sGLD, g.dma_start(out=e.dbg["dbg_xn2"], in_=e.d_xn2), 16)
    g.wait_ge(e.sGLD.h, e.sGLD.v)



def emit_moe(e):
    nc = e.nc
    e.moectx = ExitStack()
    e.moe_acc = e.moectx.enter_context(
        nc.sbuf_tensor("moe_acc", [128, DH * 1024], BF16))
    m2 = ExitStack()

    def sb(name, shape, dt):
        return m2.enter_context(nc.sbuf_tensor(name, shape, dt))

    e.xn2b = sb("xn2b", [128, DH * 1024], BF16)
    e.hf_sb = sb("hf_sb", [128, 32 * 1024], BF16)
    e.wf32 = [sb(f"wf32_{i}", [128, 4096], F32) for i in range(2)]
    e.wbf = [sb(f"wbf_{i}", [128, 4096], BF16) for i in range(2)]
    e.wbc = sb("wbc", [128, 1024], F32)
    e.silb = [sb(f"silb{i}", [128, 512], BF16) for i in range(2)]
    e.wrow = sb("wrow", [1, 1024], F32)

    mb = {nm: e[nm].v for nm in
          ["sPE", "sACT", "sDVE", "sPOOL", "sXB", "sMR", "sMW0", "sMW1"]}
    e.mb = mb
    iters = []
    for ee in range(NE):
        iters += [("u", ee, fc) for fc in range(32)] +                  [("d", ee, ci) for ci in range(16)]
    slot_of = {}
    arrive = {}
    cnt = [mb["sMW0"], mb["sMW1"]]
    for gi, it in enumerate(iters):
        s = gi % 2
        slot_of[gi] = s
        cnt[s] += 32 if it[0] == "u" else 16
        arrive[gi] = cnt[s]
    e.m_iters, e.m_slot, e.m_arrive = iters, slot_of, arrive

    pe_wbc = lambda ee, tg: mb["sPE"] + ee * 162 + tg + 1
    pe_u = lambda ee, fc, tg, w: mb["sPE"] + ee * 162 + 2 + fc * 4 + tg * 2 + w + 1
    pe_d = lambda ee, ci, tg: mb["sPE"] + ee * 162 + 130 + ci * 2 + tg + 1
    act_wbc = lambda ee, tg: mb["sACT"] + ee * 66 + tg + 1
    act_sil = lambda ee, fc, tg: mb["sACT"] + ee * 66 + 2 + fc * 2 + tg + 1
    dve_hf = lambda ee, fc, tg: mb["sDVE"] + ee * 128 + fc * 2 + tg + 1
    dve_dn = lambda ee, ci, tg, k: mb["sDVE"] + ee * 128 + 64 + ci * 4 + tg * 2 + k
    pool_cv = lambda ee, fc, w: mb["sPOOL"] + ee * 80 + fc * 2 + w + 1
    pool_cvd = lambda ee, ci: mb["sPOOL"] + ee * 80 + 64 + ci + 1

    def last_pe(gi):
        k, ee, i = iters[gi]
        return pe_u(ee, i, 1, 1) if k == "u" else pe_d(ee, i, 1)

    def last_pool(gi):
        k, ee, i = iters[gi]
        return pool_cv(ee, i, 1) if k == "u" else pool_cvd(ee, i)

    fns = dict(pe_wbc=pe_wbc, pe_u=pe_u, pe_d=pe_d, act_wbc=act_wbc,
               act_sil=act_sil, dve_hf=dve_hf, dve_dn=dve_dn, pool_cv=pool_cv,
               pool_cvd=pool_cvd, last_pe=last_pe, last_pool=last_pool)
    sxb_full = mb["sXB"] + 256

    with nc.Block() as blk:
        blk.tensor(lambda t: moe_tensor(e, t, mb, fns, sxb_full))
        blk.scalar(lambda s: moe_scalar(e, s, mb, fns))
        blk.vector(lambda v: moe_vector(e, v, mb, fns))
        blk.gpsimd(lambda g: moe_gpsimd(e, g, mb, fns))
        blk.sync(lambda s: moe_sync(e, s, mb, fns))

    if DEBUG:
        with nc.Block() as blk2:
            blk2.gpsimd(lambda g: moe_debug_g(e, g))
    m2.close()


def moe_tensor(e, t, mb, f, sxb_full):
    t.wait_ge(e.sDVE.h, mb["sDVE"])
    t.wait_ge(e.sACT.h, mb["sACT"])
    t.wait_ge(e.sPOOL.h, mb["sPOOL"])
    t.wait_ge(e.sXB.h, sxb_full)
    for ee in range(NE):
        t.wait_ge(e.sMR.h, mb["sMR"] + 16 * (ee + 1))
        if ee >= 1:
            t.wait_ge(e.sDVE.h, f["dve_dn"](ee - 1, 15, 1, 2))
        for tg in range(2):
            I(e.sPE, t.matmul(e.B[6 + tg][:, :], e.ones1f[:, :],
                              e.wrow[0:1, tg * 512:(tg + 1) * 512],
                              start=True, stop=True))
        for fc in range(32):
            gi = ee * 48 + fc
            sl = e.m_slot[gi]
            t.wait_ge(e["sMW%d" % sl].h, e.m_arrive[gi])
            t.wait_ge(e.sPOOL.h, f["pool_cv"](ee, fc, 1))
            for tg in range(2):
                for w in range(2):
                    ugi = ee * 32 + fc
                    if ugi >= 2:
                        pe_, pf = divmod(ugi - 2, 32)
                        if w == 0:
                            t.wait_ge(e.sACT.h, f["act_sil"](pe_, pf, tg))
                        else:
                            t.wait_ge(e.sDVE.h, f["dve_hf"](pe_, pf, tg))
                    if fc == 1 and w == 1:
                        t.wait_ge(e.sACT.h, f["act_wbc"](ee, tg))
                    bank = (fc % 2) * 4 + w * 2 + tg
                    for dh in range(DH):
                        ins = t.matmul(
                            e.B[bank][:, :],
                            e.wbf[sl][:, w * 2048 + dh * 128:w * 2048 + (dh + 1) * 128],
                            e.xn2b[:, dh * 1024 + tg * 512:dh * 1024 + (tg + 1) * 512],
                            start=(dh == 0), stop=(dh == DH - 1))
                    I(e.sPE, ins)
        for ci in range(16):
            gi = ee * 48 + 32 + ci
            sl = e.m_slot[gi]
            t.wait_ge(e["sMW%d" % sl].h, e.m_arrive[gi])
            t.wait_ge(e.sPOOL.h, f["pool_cvd"](ee, ci))
            if ci == 0:
                t.wait_ge(e.sDVE.h, f["dve_hf"](ee, 31, 1))
            for tg in range(2):
                bank = 4 + (ci % 2) * 2 + tg
                if ci >= 2:
                    t.wait_ge(e.sDVE.h, f["dve_dn"](ee, ci - 2, tg, 1))
                else:
                    t.wait_ge(e.sACT.h, f["act_sil"](ee, 30 + ci, tg))
                    t.wait_ge(e.sDVE.h, f["dve_hf"](ee, 30 + ci, tg))
                for kc in range(32):
                    ins = t.matmul(
                        e.B[bank][:, :],
                        e.wbf[sl][:, kc * 128:(kc + 1) * 128],
                        e.hf_sb[:, kc * 1024 + tg * 512:kc * 1024 + (tg + 1) * 512],
                        start=(kc == 0), stop=(kc == 31))
                I(e.sPE, ins)


def moe_scalar(e, s, mb, f):
    for ee in range(NE):
        for tg in range(2):
            s.wait_ge(e.sPE.h, f["pe_wbc"](ee, tg))
            if ee >= 1 and tg == 0:
                s.wait_ge(e.sDVE.h, f["dve_dn"](ee - 1, 15, 1, 2))
            I(e.sACT, s.copy(out=e.wbc[:, tg * 512:(tg + 1) * 512],
                             in_=e.B[6 + tg][:, :]))
        for fc in range(32):
            for tg in range(2):
                s.wait_ge(e.sPE.h, f["pe_u"](ee, fc, tg, 0))
                idx = fc * 2 + tg
                if idx >= 2:
                    s.wait_ge(e.sDVE.h, f["dve_hf"](ee, fc - 1, tg))
                I(e.sACT, s.activation(out=e.silb[idx % 2][:, :],
                                       in_=e.B[(fc % 2) * 4 + tg][:, :],
                                       func=AF.Silu))


def moe_vector(e, v, mb, f):
    for ee in range(NE):
        for fc in range(32):
            for tg in range(2):
                v.wait_ge(e.sACT.h, f["act_sil"](ee, fc, tg))
                v.wait_ge(e.sPE.h, f["pe_u"](ee, fc, tg, 1))
                if ee >= 1 and fc == 0 and tg == 0:
                    v.wait_ge(e.sPE.h, f["pe_d"](ee - 1, 15, 1))
                W(v, e.sDVE)
                I(e.sDVE, v.tensor_mul(
                    e.hf_sb[:, fc * 1024 + tg * 512:fc * 1024 + (tg + 1) * 512],
                    e.silb[(fc * 2 + tg) % 2][:, :],
                    e.B[(fc % 2) * 4 + 2 + tg][:, :]))
        for ci in range(16):
            for tg in range(2):
                v.wait_ge(e.sPE.h, f["pe_d"](ee, ci, tg))
                W(v, e.sDVE)
                I(e.sDVE, v.tensor_mul(e.scr[tg][:, :],
                                       e.B[4 + (ci % 2) * 2 + tg][:, :],
                                       e.wbc[:, tg * 512:(tg + 1) * 512]))
                at = e.moe_acc[:, ci * 1024 + tg * 512:ci * 1024 + (tg + 1) * 512]
                W(v, e.sDVE)
                if ee == 0:
                    I(e.sDVE, v.tensor_copy(out=at, in_=e.scr[tg][:, :]))
                else:
                    I(e.sDVE, v.tensor_add(at, at, e.scr[tg][:, :]))


def moe_gpsimd(e, g, mb, f):
    for nm in ["sDVE", "sPE", "sACT", "sPOOL"]:
        g.wait_ge(e[nm].h, mb[nm])
    for gi, (k, ee, i) in enumerate(e.m_iters):
        sl = e.m_slot[gi]
        g.wait_ge(e["sMW%d" % sl].h, e.m_arrive[gi])
        if gi >= 2:
            g.wait_ge(e.sPE.h, f["last_pe"](gi - 2))
        if k == "u":
            for w in range(2):
                I(e.sPOOL, g.tensor_copy(
                    out=e.wbf[sl][:, w * 2048:(w + 1) * 2048],
                    in_=e.wf32[sl][:, w * 2048:(w + 1) * 2048]))
        else:
            I(e.sPOOL, g.tensor_copy(out=e.wbf[sl][:, :], in_=e.wf32[sl][:, :]))


def moe_sync(e, s, mb, f):
    for nm in ["sDVE", "sPE", "sACT", "sPOOL"]:
        s.wait_ge(e[nm].h, mb[nm])
    for half in range(2):
        for q in range(8):
            ci = half * 8 + q
            I(e.sXB, s.dma_start(out=e.xn2b[:, ci * 1024:(ci + 1) * 1024],
                                 in_=e.d_xn2[ci * 128:(ci + 1) * 128, :]), 16)
    for gi, (k, ee, i) in enumerate(e.m_iters):
        if k == "u" and i == 0:
            if ee >= 1:
                s.wait_ge(e.sPE.h, f["pe_wbc"](ee - 1, 1))
            I(e.sMR, s.dma_start(out=e.wrow[:, :], in_=e.d_w[ee:ee + 1, :]), 16)
        sl = e.m_slot[gi]
        if gi >= 2:
            s.wait_ge(e.sPOOL.h, f["last_pool"](gi - 2))
        if k == "u":
            I(e["sMW%d" % sl], s.dma_start(
                out=e.wf32[sl][:, 0:2048].rearrange("p (dh x) -> p dh x", dh=DH),
                in_=e.w1h[ee, :, :, i * 128:(i + 1) * 128]), 16)
            I(e["sMW%d" % sl], s.dma_start(
                out=e.wf32[sl][:, 2048:4096].rearrange("p (dh x) -> p dh x", dh=DH),
                in_=e.w3h[ee, :, :, i * 128:(i + 1) * 128]), 16)
        else:
            I(e["sMW%d" % sl], s.dma_start(
                out=e.wf32[sl][:, :].rearrange("p (kc x) -> p kc x", kc=32),
                in_=e.w2h[ee, :, :, i * 128:(i + 1) * 128]), 16)


def moe_debug_g(e, g):
    g.wait_ge(e.sDVE.h, e.sDVE.v)
    for half in range(2):
        I(e.sGLD, g.dma_start(
            out=e.dbg["dbg_moe"].rearrange("(ci dl) t -> dl ci t", dl=128)[
                :, half * 8:(half + 1) * 8, :],
            in_=e.moe_acc[:, half * 8 * 1024:(half + 1) * 8 * 1024].rearrange(
                "p (ci t) -> p ci t", ci=8)), 16)
    g.wait_ge(e.sGLD.h, e.sGLD.v)



def emit_final(e):
    nc = e.nc
    fctx = ExitStack()

    def sb(name, shape, dt):
        return fctx.enter_context(nc.sbuf_tensor(name, shape, dt))

    e.xbuf3 = sb("xbuf3", [128, DH * 1024], F32)
    e.rows3 = sb("rows3", [1, 4 * 512], F32)
    e.mr3 = sb("mr3", [128, 4 * 512], F32)
    e.ybuf3 = [sb(f"ybuf3_{i}", [128, 1024], F32) for i in range(2)]
    e.hbuf3 = [sb(f"hbuf3_{i}", [128, 1024], F32) for i in range(2)]
    e.obuf = [sb(f"obuf{i}", [128, 1024], F32) for i in range(2)]
    e.gsb = [sb(f"gsb{i}", [128, 512], F32) for i in range(2)]
    e.gob = [sb(f"gob{i}", [128, 512], F32) for i in range(2)]

    fb = {nm: e[nm].v for nm in
          ["sPE", "sACT", "sDVE", "sPOOL", "sXB", "sFT0", "sFT1", "sOU0", "sOU1"]}
    dve_F = lambda ci, tg, k: fb["sDVE"] + (ci * 2 + tg) * 7 + k
    act_F = lambda ci, tg: fb["sACT"] + 4 + ci * 2 + tg + 1
    pe_F = lambda i: fb["sPE"] + i + 1

    seng = nc.sync
    book = SwBook(e, ["sXB"])
    for c in nc.Switch(engines=[seng], index=[e.pid_s], n=8):
        book.enter_body()
        A0, A1 = own_cols(c)
        for nm in ["sDVE", "sPE", "sACT", "sPOOL"]:
            seng.wait_ge(e[nm].h, fb[nm])
        seng.wait_ge(e.sAG.h, 1)
        I(e.sXB, seng.dma_start(out=e.d_y[:, 0:512],
                                in_=e.d_ygat[:, A0:A0 + 512]), 16)
        I(e.sXB, seng.dma_start(out=e.d_y[:, 512:1024],
                                in_=e.d_ygat[:, A1:A1 + 512]), 16)
        for half in range(2):
            for ib, a in enumerate((A0, A1)):
                I(e.sXB, seng.dma_start(
                    out=e.xbuf3[:, :].rearrange("p (ci t) -> p ci t", ci=DH)[
                        :, half * 8:(half + 1) * 8, ib * 512:(ib + 1) * 512],
                    in_=e.xT.rearrange("(ci dl) t -> dl ci t", dl=128)[
                        :, half * 8:(half + 1) * 8, a:a + 512]), 16)
        for r in range(2):
            for ib, a in enumerate((A0, A1)):
                I(e.sXB, seng.dma_start(
                    out=e.rows3[0:1, (r * 2 + ib) * 512:(r * 2 + ib + 1) * 512],
                    in_=e.d_rows[r:r + 1, a:a + 512]), 16)
    book.finish()
    sxb_full = fb["sXB"] + 160

    with nc.Block() as blk:
        blk.tensor(lambda t: fin_tensor(e, t, fb, sxb_full))
        blk.scalar(lambda s: fin_scalar(e, s, fb, dve_F, pe_F))
        blk.vector(lambda v: fin_vector(e, v, fb, dve_F, act_F, sxb_full))
        blk.sync(lambda s: fin_sync(e, s, fb, dve_F, sxb_full))
    fctx.close()


def fin_tensor(e, t, fb, sxb_full):
    t.wait_ge(e.sDVE.h, fb["sDVE"])
    t.wait_ge(e.sACT.h, fb["sACT"])
    t.wait_ge(e.sPOOL.h, fb["sPOOL"])
    t.wait_ge(e.sXB.h, sxb_full)
    for i in range(4):
        I(e.sPE, t.matmul(e.B[2 + i][:, :], e.ones1f[:, :],
                          e.rows3[0:1, i * 512:(i + 1) * 512],
                          start=True, stop=True))


def fin_scalar(e, s, fb, dve_F, pe_F):
    for i in range(4):
        s.wait_ge(e.sPE.h, pe_F(i))
        I(e.sACT, s.copy(out=e.mr3[:, i * 512:(i + 1) * 512], in_=e.B[2 + i][:, :]))
    for ci in range(DH):
        for tg in range(2):
            s.wait_ge(e.sDVE.h, dve_F(ci, tg, 5))
            if ci >= 1:
                s.wait_ge(e.sDVE.h, dve_F(ci - 1, tg, 6))
            I(e.sACT, s.activation(out=e.gob[tg][:, :], in_=e.gsb[tg][:, :],
                                   func=AF.Silu))


def fin_vector(e, v, fb, dve_F, act_F, sxb_full):
    v.wait_ge(e.sXB.h, sxb_full)
    v.wait_ge(e.sACT.h, fb["sACT"] + 4)
    for ci in range(DH):
        sl = ci % 2
        v.wait_ge(e["sFT%d" % sl].h, fb["sFT%d" % sl] + 32 * (ci // 2 + 1))
        for tg in range(2):
            xt = e.xbuf3[:, ci * 1024 + tg * 512:ci * 1024 + (tg + 1) * 512]
            sc = e.scr[2 + tg]
            W(v, e.sDVE)
            I(e.sDVE, v.tensor_sub(sc[:, :], xt, e.mr3[:, tg * 512:(tg + 1) * 512]))
            W(v, e.sDVE)
            I(e.sDVE, v.tensor_mul(sc[:, :], sc[:, :],
                                   e.mr3[:, (2 + tg) * 512:(3 + tg) * 512]))
            W(v, e.sDVE)
            I(e.sDVE, v.tensor_scalar(out=sc[:, :], in0=sc[:, :],
                                      scalar1=e.D_c[:, ci:ci + 1], scalar2=None,
                                      op0=ALU.mult, op1=ALU.bypass, accum_out=None))
            W(v, e.sDVE)
            I(e.sDVE, v.tensor_add(sc[:, :], sc[:, :],
                                   e.ybuf3[sl][:, tg * 512:(tg + 1) * 512]))
            W(v, e.sDVE)
            I(e.sDVE, v.tensor_add(e.gsb[tg][:, :],
                                   e.hbuf3[sl][:, tg * 512:(tg + 1) * 512],
                                   e.moe_acc[:, ci * 1024 + tg * 512:
                                             ci * 1024 + (tg + 1) * 512]))
            v.wait_ge(e.sACT.h, act_F(ci, tg))
            W(v, e.sDVE)
            I(e.sDVE, v.tensor_mul(sc[:, :], sc[:, :], e.gob[tg][:, :]))
            if ci >= 2 and tg == 0:
                v.wait_ge(e["sOU%d" % sl].h, fb["sOU%d" % sl] + 16 * (ci // 2))
            W(v, e.sDVE)
            I(e.sDVE, v.tensor_add(e.obuf[sl][:, tg * 512:(tg + 1) * 512], xt,
                                   sc[:, :]))


def fin_sync(e, s, fb, dve_F, sxb_full):
    s.wait_ge(e.sXB.h, sxb_full)
    def out_dma(ci):
        s.wait_ge(e.sDVE.h, dve_F(ci, 1, 7))
        I(e["sOU%d" % (ci % 2)], s.dma_start(
            out=e.out[ci * 128:(ci + 1) * 128, :], in_=e.obuf[ci % 2][:, :]), 16)

    for ci in range(DH):
        sl = ci % 2
        if ci >= 2:
            s.wait_ge(e.sDVE.h, dve_F(ci - 2, 1, 5))
        I(e["sFT%d" % sl], s.dma_start(out=e.ybuf3[sl][:, :],
                                       in_=e.d_y[ci * 128:(ci + 1) * 128, :]), 16)
        I(e["sFT%d" % sl], s.dma_start(out=e.hbuf3[sl][:, :],
                                       in_=e.d_h2[ci * 128:(ci + 1) * 128, :]), 16)
        if ci >= 1:
            out_dma(ci - 1)
    out_dma(15)
    for nm in ["sOU0", "sOU1", "sFT0", "sFT1"]:
        s.wait_ge(e[nm].h, e[nm].v)


def prep_inputs(inputs):
    x = np.asarray(inputs["L0_attn_x"])
    fp = {}
    fp["xT"] = np.ascontiguousarray(np.concatenate([x[0], x[1]], axis=0).T)
    fp["wxpT"] = np.ascontiguousarray(np.asarray(inputs["L0_attn_W_xproj"]).T)
    fp["wdtT"] = np.ascontiguousarray(np.asarray(inputs["L0_attn_W_dt"]).T)
    fp["bdt_r"] = np.ascontiguousarray(np.asarray(inputs["L0_attn_b_dt"]).reshape(DH, 128))
    fp["Alog"] = np.ascontiguousarray(np.asarray(inputs["L0_attn_A_log"]))
    fp["D_r"] = np.ascontiguousarray(np.asarray(inputs["L0_attn_D_param"]).reshape(DH, 128))
    fp["n1w_r"] = np.ascontiguousarray(np.asarray(inputs["L0_ffn_norm1_w"]).reshape(DH, 128))
    fp["n2w_r"] = np.ascontiguousarray(np.asarray(inputs["L0_ffn_norm2_w"]).reshape(DH, 128))
    wq = np.asarray(inputs["L0_ffn_wq"])
    fp["wqh"] = np.ascontiguousarray(wq.reshape(DH, 128, NH, 128).transpose(2, 1, 0, 3))
    wk = np.asarray(inputs["L0_ffn_wk"])
    fp["wkh"] = np.ascontiguousarray(wk.reshape(DH, 128, KVH, 128).transpose(2, 1, 0, 3))
    fp["wv"] = np.ascontiguousarray(np.asarray(inputs["L0_ffn_wv"]))
    wo = np.asarray(inputs["L0_ffn_wo"])
    fp["woh"] = np.ascontiguousarray(wo.reshape(DH, 128, DH, 128).transpose(2, 1, 0, 3))
    fp["gwt"] = np.ascontiguousarray(np.asarray(inputs["L0_ffn_gate_w"]))
    w1 = np.asarray(inputs["L0_ffn_w1"])
    fp["w1h"] = np.ascontiguousarray(w1.reshape(NE, DH, 128, FF).transpose(0, 2, 1, 3))
    w3 = np.asarray(inputs["L0_ffn_w3"])
    fp["w3h"] = np.ascontiguousarray(w3.reshape(NE, DH, 128, FF).transpose(0, 2, 1, 3))
    w2 = np.asarray(inputs["L0_ffn_w2"])
    fp["w2h"] = np.ascontiguousarray(w2.reshape(NE, 32, 128, D).transpose(0, 2, 1, 3))
    return fp


def kernel(**inputs):
    fp = prep_inputs(inputs)
    nc, e = build_nc()
    in_maps = [dict(fp) for c in range(8)]
    res = run_bass_kernel_spmd(nc, in_maps, core_ids=list(range(8)))
    outf = np.zeros((2, L, D), np.float32)
    for c in range(8):
        oc = res.results[c]["out"]
        outf[0, 512 * c:512 * (c + 1), :] = oc[:, 0:512].T
        outf[1, 512 * (7 - c):512 * (8 - c), :] = oc[:, 512:1024].T
    kernel.last_res = res
    return outf


# version 03 (final)
# speedup vs baseline: 1.8937x; 1.0162x over previous
import os
import math
import numpy as np
import concourse.bass as bass
import concourse.mybir as mybir
from concourse.bass_utils import run_bass_kernel_spmd
from contextlib import ExitStack

F32 = mybir.dt.float32
BF16 = mybir.dt.bfloat16
I32 = mybir.dt.int32
AF = mybir.ActivationFunctionType
ALU = mybir.AluOpType
AXX = mybir.AxisListType.X

D = 2048
DH = 16
TOK = 8192
L = 4096
G = 16
GW = 512
NH = 16
KVH = 4
NST = 16
R = 64
NE = 8
FF = 4096
EPS = 1e-5
PI = 3.14159265358979
LOG_THETA = math.log(1.0e6)
INV2PI = 1.0 / (2.0 * PI)
TWOPI = 2.0 * PI

DEBUG = os.environ.get("KDBG", "0") == "1"
NPHASE = int(os.environ.get("KPHASES", "9"))

PE_G = 78
ACT_G = 77
DVE_G = 130
POOL_G = 52
SP_G = 45
SX_G = 16

PRO_POOL = 19
PRO_GLD = 13 * 16
PRO_SP = 5 * 16
PRO_PE = 36
PRO_ACT = 38
PRO_DVE = 245


def pe_at(g, off):
    return PRO_PE + g * PE_G + off


def act_at(g, off):
    return PRO_ACT + g * ACT_G + off


def dve_at(g, off):
    return PRO_DVE + g * DVE_G + off


def pool_at(g, off):
    return PRO_POOL + g * POOL_G + off


def sx_at(g, off):
    return 16 * (g * SX_G + off)


def sp_at(g, off):
    return PRO_SP + 16 * (g * SP_G + off)


class Ctr:
    def __init__(self, nc, stack, name):
        self.h = stack.enter_context(nc.semaphore(name))
        self.v = 0
        self.name = name

    def inc(self, instr, n=1):
        instr.then_inc(self.h, n)
        self.v += n
        return self.v


class Env(dict):
    __getattr__ = dict.__getitem__
    __setattr__ = dict.__setitem__


def I(ctr, instr, n=1):
    instr.then_inc(ctr.h, n)
    ctr.v += n
    return instr


def W(eng, ctr, v=None):
    eng.wait_ge(ctr.h, ctr.v if v is None else v)



def build_nc():
    nc = bass.Bass("TRN2", target_bir_lowering=False)
    e = Env()
    e.nc = nc

    e.xT = nc.dram_tensor("xT", [D, TOK], F32, kind="ExternalInput").ap()
    e.wxpT = nc.dram_tensor("wxpT", [D, 96], F32, kind="ExternalInput").ap()
    e.wdtT = nc.dram_tensor("wdtT", [R, D], F32, kind="ExternalInput").ap()
    e.bdt_r = nc.dram_tensor("bdt_r", [DH, 128], F32, kind="ExternalInput").ap()
    e.Alog = nc.dram_tensor("Alog", [D, NST], F32, kind="ExternalInput").ap()
    e.D_r = nc.dram_tensor("D_r", [DH, 128], F32, kind="ExternalInput").ap()
    e.n1w_r = nc.dram_tensor("n1w_r", [DH, 128], F32, kind="ExternalInput").ap()
    e.n2w_r = nc.dram_tensor("n2w_r", [DH, 128], F32, kind="ExternalInput").ap()
    e.wqh = nc.dram_tensor("wqh", [NH, 128, DH, 128], F32, kind="ExternalInput").ap()
    e.wkh = nc.dram_tensor("wkh", [KVH, 128, DH, 128], F32, kind="ExternalInput").ap()
    e.wv = nc.dram_tensor("wv", [D, 512], F32, kind="ExternalInput").ap()
    e.woh = nc.dram_tensor("woh", [DH, 128, DH, 128], F32, kind="ExternalInput").ap()
    e.gwt = nc.dram_tensor("gwt", [D, NE], F32, kind="ExternalInput").ap()
    e.w1h = nc.dram_tensor("w1h", [NE, 128, DH, FF], F32, kind="ExternalInput").ap()
    e.w3h = nc.dram_tensor("w3h", [NE, 128, DH, FF], F32, kind="ExternalInput").ap()
    e.w2h = nc.dram_tensor("w2h", [NE, 128, 32, D], F32, kind="ExternalInput").ap()

    e.d_delta = nc.dram_tensor("d_delta", [D, TOK], F32, kind="Internal").ap()
    e.d_u = nc.dram_tensor("d_u", [D, TOK], F32, kind="Internal").ap()
    e.d_xn1 = nc.dram_tensor("d_xn1", [D, TOK], BF16, kind="Internal").ap()
    e.d_k = nc.dram_tensor("d_k", [KVH, 128, TOK], BF16, kind="Internal").ap()
    e.d_v = nc.dram_tensor("d_v", [TOK, 512], BF16, kind="Internal").ap()
    e.d_bc = nc.dram_tensor("d_bc", [32, TOK], F32, kind="Internal").ap()
    e.d_rows = nc.dram_tensor("d_rows", [2, TOK], F32, kind="Internal").ap()
    e.d_h2 = nc.dram_tensor("d_h2", [D, 1024], F32, kind="Internal").ap()
    e.d_y = nc.dram_tensor("d_y", [D, 1024], F32, kind="Internal").ap()
    e.d_moe = nc.dram_tensor("d_moe", [D, 1024], F32, kind="Internal").ap()
    e.d_ysrc = nc.dram_tensor("d_ysrc", [256, TOK], F32, kind="Internal").ap()
    e.d_ygat = nc.dram_tensor("d_ygat", [D, TOK], F32, kind="Internal").ap()
    e.d_xn2 = nc.dram_tensor("d_xn2", [D, 1024], BF16, kind="Internal").ap()
    e.d_w = nc.dram_tensor("d_w", [NE, 1024], F32, kind="Internal").ap()
    e.out = nc.dram_tensor("out", [D, 1024], F32, kind="ExternalOutput").ap()

    e.dbg = {}
    if DEBUG:
        for nm, shape in [
            ("dbg_delta", [D, 1024]), ("dbg_u", [D, 1024]), ("dbg_bc", [32, 1024]),
            ("dbg_xn1", [D, 1024]), ("dbg_k", [KVH * 128, 1024]), ("dbg_v", [1024, 512]),
            ("dbg_rows", [2, TOK]), ("dbg_cos", [64, 1024]), ("dbg_sin", [64, 1024]),
            ("dbg_y", [D, 1024]), ("dbg_hloc", [D, NST]), ("dbg_hinit", [D, 2 * NST]),
            ("dbg_av", [D, 1024]), ("dbg_q", [D, 1024]),
            ("dbg_h2", [D, 1024]), ("dbg_xn2", [D, 1024]), ("dbg_w", [NE, 1024]),
            ("dbg_moe", [D, 1024]),
        ]:
            e.dbg[nm] = nc.dram_tensor(nm, shape, F32, kind="ExternalOutput").ap()

    with ExitStack() as ctx:
        def sb(name, shape, dt):
            return ctx.enter_context(nc.sbuf_tensor(name, shape, dt))
        e.sb = sb

        e.ones_bf = sb("ones_bf", [128, 1], BF16)
        e.ones1f = sb("ones1f", [1, 128], F32)
        e.ones16 = sb("ones16", [16, 16], F32)
        e.id16 = sb("id16", [16, 16], F32)
        e.Apos = sb("Apos", [128, DH * NST], F32)
        e.bdt_c = sb("bdt_c", [128, DH], F32)
        e.D_c = sb("D_c", [128, DH], F32)
        e.n1w_c = sb("n1w_c", [128, DH], F32)
        e.n2w_c = sb("n2w_c", [128, DH], F32)
        e.invf = sb("invf", [1, 64], F32)
        e.iot = sb("iot", [1, 1024], F32)
        e.kmax2 = sb("kmax2", [1, 8], F32)
        e.cst = sb("cst", [128, 2], F32)
        nc.const_aps.register(F32, EPS, e.cst[:, 0:1])
        nc.const_aps.register(F32, 1.0, e.cst[:, 1:2])
        e.scr = [sb(f"scr{i}", [128, GW], F32) for i in range(6)]
        e.iscr = sb("iscr", [128, GW], I32)

        e.ropectx = ExitStack()
        e.cosr = e.ropectx.enter_context(nc.sbuf_tensor("cosr", [64, TOK], BF16))
        e.sinr = e.ropectx.enter_context(nc.sbuf_tensor("sinr", [64, TOK], BF16))

        p0ctx = ExitStack()

        e.p0ctx = p0ctx

        def sb0(name, shape, dt):
            return p0ctx.enter_context(nc.sbuf_tensor(name, shape, dt))
        e.xbuf = sb0("xbuf", [128, DH * GW], F32)
        e.xnb = sb0("xnb", [128, DH * GW], BF16)
        e.xn1b = sb0("xn1b", [128, DH * GW], BF16)
        e.sqb = [sb0(f"sqb{i}", [128, GW], BF16) for i in range(2)]
        e.draw_b = sb0("draw_b", [64, GW], BF16)
        e.wxp_sb = sb0("wxp_sb", [128, DH * 96], BF16)
        e.wdt_sb = sb0("wdt_sb", [64, D], BF16)
        e.wk_sb = sb0("wk_sb", [128, KVH * DH * 128], BF16)
        e.wv_sb = sb0("wv_sb", [128, DH * 512], BF16)
        e.kvb = [sb0(f"kvb{i}", [128, GW], BF16) for i in range(2)]
        e.vb = [sb0(f"vb{i}", [128, GW], BF16) for i in range(2)]
        e.dsc = [sb0(f"dsc{i}", [128, GW], F32) for i in range(2)]
        e.usc = [sb0(f"usc{i}", [128, GW], F32) for i in range(2)]
        e.rowsB = sb0("rowsB", [1, 4 * GW], F32)
        e.mrow = e.rowsB[0:1, 0:GW]
        e.mrow2 = e.rowsB[0:1, GW:2 * GW]
        e.rrow = e.rowsB[0:1, 2 * GW:3 * GW]
        e.r1row = e.rowsB[0:1, 3 * GW:4 * GW]

        e.B = [ctx.enter_context(nc.psum_tensor(f"bank{i}", [128, 512], F32))
               for i in range(8)]

        for nm in ["sX", "sSP", "sPE", "sACT", "sDVE", "sPOOL", "sGLD", "sST",
                   "sX1", "sBC", "sRW", "sMS", "sQS", "sKV", "sWQ0", "sWQ1",
                   "sXB", "sWO0", "sWO1", "sH2", "sXN0", "sXN1",
                   "sMR", "sMW0", "sMW1", "sFT0", "sFT1", "sOU0", "sOU1",
                   "sBB", "sCB", "sAG"]:
            e[nm] = Ctr(nc, ctx, nm)
        e.sDS = [Ctr(nc, ctx, f"sDS{i}") for i in range(2)]
        e.sUS = [Ctr(nc, ctx, f"sUS{i}") for i in range(2)]
        e.sKB = [Ctr(nc, ctx, f"sKB{i}") for i in range(2)]
        e.sVB = [Ctr(nc, ctx, f"sVB{i}") for i in range(2)]

        emit_program(e)
        verify_sem_totals(e)

    return nc, e


def verify_sem_totals(e):
    import collections
    nc = e.nc
    tot = collections.Counter()
    for ins in nc.all_instructions():
        si = ins.sync_info
        if si is None:
            continue
        for u in si.on_update:
            if u.update_mode in ("sem-inc", "sem-add-imm"):
                tot[u.ant_name] += u.update_value
    names = ["sX", "sSP", "sPE", "sACT", "sDVE", "sPOOL", "sGLD", "sST", "sX1", "sBC", "sRW",
             "sQS", "sKV", "sWQ0", "sWQ1",
             "sXB", "sWO0", "sWO1", "sH2", "sXN0", "sXN1",
             "sMR", "sMW0", "sMW1", "sFT0", "sFT1", "sOU0", "sOU1",
             "sBB", "sCB", "sAG",
             "sDS0", "sDS1", "sUS0", "sUS1", "sKB0", "sKB1", "sVB0", "sVB1"]
    objs = {n: e[n] for n in names[:31]}
    for fam, lst in [("sDS", e.sDS), ("sUS", e.sUS), ("sKB", e.sKB), ("sVB", e.sVB)]:
        for i, c in enumerate(lst):
            objs[f"{fam}{i}"] = c
    excess = e.get("sw_excess", {})
    for name in names:
        want = objs[name].v + excess.get(name, 0)
        got = tot.get(name, 0)
        if got != want:
            raise AssertionError(f"sem {name}: emitted {got} != bookkept {want}")


def emit_program(e):
    nc = e.nc
    with nc.Block() as blk:
        blk.gpsimd(lambda eng: p0_gpsimd(e, eng))
        blk.scalar(lambda eng: p0_scalar(e, eng))
        blk.tensor(lambda eng: p0_tensor(e, eng))
        blk.vector(lambda eng: p0_vector(e, eng))
        blk.sync(lambda eng: p0_sync(e, eng))
    e.p0ctx.close()

    if NPHASE >= 2:
        emit_mamba(e)
    if NPHASE >= 3:
        emit_attention(e)
        if NPHASE >= 4:
            emit_h2(e)
        e.avctx.close()
        if DEBUG:
            with nc.Block() as blkr:
                blkr.gpsimd(lambda g: rope_debug(e, g))
        e.ropectx.close()
        if NPHASE >= 5:
            emit_moe(e)
            if NPHASE >= 6:
                emit_final(e)
            e.moectx.close()

    if DEBUG:
        with nc.Block() as blk2:
            blk2.gpsimd(lambda eng: p0_debug_g(e, eng))
            blk2.sync(lambda eng: p0_debug_s(e, eng))
    elif NPHASE < 6:
        with nc.Block() as blk3:
            blk3.sync(lambda eng: stub_out(e, eng))


def stub_out(e, s):
    s.dma_start(out=e.out[0:128, 0:512], in_=e.scr[0][:, 0:512]).then_inc(e.sST.h, 16)
    e.sST.v += 16
    s.wait_ge(e.sST.h, e.sST.v)



def p0_gpsimd(e, g):
    I(e.sPOOL, g.iota(e.invf[:, :], pattern=[[1, 64]], base=0, channel_multiplier=0,
                      allow_small_or_imprecise_dtypes=True))
    I(e.sPOOL, g.memset(e.ones16[:, :], 1.0))
    W(g, e.sPOOL)
    I(e.sPOOL, g.affine_select(out=e.id16[:, :], in_=e.ones16[:, :], pattern=[[-1, 16]],
                               compare_op=ALU.is_equal, fill=0.0, base=0,
                               channel_multiplier=1))
    for ch in range(16):
        if ch >= 2:
            g.wait_ge(e.sPE.h, 4 + 2 * (ch - 1))
        I(e.sPOOL, g.iota(e.iot[:, (ch % 2) * 512:(ch % 2) * 512 + 512], pattern=[[1, 512]],
                          base=(ch % 8) * 512, channel_multiplier=0,
                          allow_small_or_imprecise_dtypes=True))
    assert e.sPOOL.v == PRO_POOL, e.sPOOL.v
    for half in range(2):
        I(e.sGLD, g.dma_start(out=e.wxp_sb[:, half * 8 * 96:(half + 1) * 8 * 96].rearrange(
                        "p (dh r) -> p dh r", dh=8),
                    in_=e.wxpT.rearrange("(dh dl) r -> dl dh r", dl=128)[:, half * 8:(half + 1) * 8, :]), 16)
    I(e.sGLD, g.dma_start(out=e.wdt_sb[:, :], in_=e.wdtT), 16)
    for kvh in range(KVH):
        for half in range(2):
            I(e.sGLD, g.dma_start(
                out=e.wk_sb[:, (kvh * DH + half * 8) * 128:(kvh * DH + (half + 1) * 8) * 128
                            ].rearrange("p (dh x) -> p dh x", dh=8),
                in_=e.wkh[kvh, :, half * 8:(half + 1) * 8, :]), 16)
    for half in range(2):
        I(e.sGLD, g.dma_start(out=e.wv_sb[:, half * 8 * 512:(half + 1) * 8 * 512].rearrange(
                        "p (dh x) -> p dh x", dh=8),
                    in_=e.wv.rearrange("(dh dl) x -> dl dh x", dl=128)[:, half * 8:(half + 1) * 8, :]), 16)
    assert e.sGLD.v == PRO_GLD, e.sGLD.v

    for gi in range(G):
        for dh in range(DH):
            g.wait_ge(e.sX.h, sx_at(gi, 16))
            if dh >= 2:
                g.wait_ge(e.sPE.h, pe_at(gi, dh - 1))
            elif gi > 0:
                g.wait_ge(e.sPE.h, pe_at(gi - 1, PE_G))
            I(e.sPOOL, g.tensor_tensor(out=e.sqb[dh % 2][:, :], in0=e.xbuf[:, dh * GW:(dh + 1) * GW],
                                       in1=e.xbuf[:, dh * GW:(dh + 1) * GW], op=ALU.mult))
        for dh in range(DH):
            g.wait_ge(e.sDVE.h, dve_at(gi, 21 + 3 * dh))
            g.wait_ge(e.sPE.h, pe_at(gi, 16 + dh) if dh >= 2 else pe_at(gi, 16))
            I(e.sPOOL, g.tensor_tensor(out=e.sqb[dh % 2][:, :], in0=e.xbuf[:, dh * GW:(dh + 1) * GW],
                                       in1=e.xbuf[:, dh * GW:(dh + 1) * GW], op=ALU.mult))
        for dh in range(DH):
            g.wait_ge(e.sDVE.h, dve_at(gi, 83 + dh))
            if dh >= 2:
                g.wait_ge(e.sUS[dh % 2].h, 16 * (gi * 8 + (dh - 2) // 2 + 1))
            elif gi > 0:
                g.wait_ge(e.sUS[dh % 2].h, 16 * (gi * 8))
            I(e.sPOOL, g.tensor_tensor(out=e.usc[dh % 2][:, :], in0=e.dsc[dh % 2][:, :],
                                       in1=e.xbuf[:, dh * GW:(dh + 1) * GW], op=ALU.mult))
        for kvh in range(KVH):
            g.wait_ge(e.sDVE.h, dve_at(gi, 104 + 6 * kvh))
            if kvh >= 2:
                g.wait_ge(e.sPE.h, pe_at(gi, 74 + kvh - 1))
            I(e.sPOOL, g.tensor_tensor(out=e.sqb[kvh % 2][:, :], in0=e.kvb[kvh % 2][:, :],
                                       in1=e.kvb[kvh % 2][:, :], op=ALU.mult))
        assert e.sPOOL.v == pool_at(gi, POOL_G), (gi, e.sPOOL.v)


def p0_scalar(e, s):
    s.wait_ge(e.sSP.h, 16 * 5)
    I(e.sACT, s.activation(out=e.Apos[:, :], in_=e.scr[0][:, 0:DH * NST], func=AF.Exp))
    s.wait_ge(e.sPOOL.h, 1)
    I(e.sACT, s.activation(out=e.invf[:, :], in_=e.invf[:, :], func=AF.Exp,
                           scale=-LOG_THETA / 64.0))
    for i, dst in enumerate([e.bdt_c, e.D_c, e.n1w_c, e.n2w_c]):
        s.wait_ge(e.sPE.h, i + 1)
        I(e.sACT, s.copy(out=dst[:, :], in_=e.B[0][:, i * 16:(i + 1) * 16]))
    for ch in range(16):
        s.wait_ge(e.sDVE.h, 5 + 15 * ch + 7)
        I(e.sACT, s.activation(out=e.sinr[:, ch * 512:(ch + 1) * 512], in_=e.scr[5][0:64, :],
                               func=AF.Sin))
        s.wait_ge(e.sDVE.h, 5 + 15 * ch + 15)
        I(e.sACT, s.activation(out=e.cosr[:, ch * 512:(ch + 1) * 512], in_=e.scr[4][0:64, :],
                               func=AF.Sin))
    assert e.sACT.v == PRO_ACT, e.sACT.v

    for gi in range(G):
        s.wait_ge(e.sPE.h, pe_at(gi, 16))
        if gi > 0:
            s.wait_ge(e.sRW.h, 32 * gi)
            W(s, e.sACT)
        I(e.sACT, s.mul(out=e.mrow[:, :], in_=e.B[0][0:1, :], mul=1.0 / D))
        I(e.sACT, s.mul(out=e.mrow2[:, :], in_=e.B[1][0:1, :], mul=1.0 / D))
        s.wait_ge(e.sDVE.h, dve_at(gi, 18))
        I(e.sACT, s.activation(out=e.scr[2][0:1, :], in_=e.mrow2[:, :], func=AF.Ln, bias=EPS))
        W(s, e.sACT)
        I(e.sACT, s.activation(out=e.rrow[:, :], in_=e.scr[2][0:1, :], func=AF.Exp, scale=-0.5))
        s.wait_ge(e.sPE.h, pe_at(gi, 33))
        W(s, e.sACT)
        I(e.sACT, s.mul(out=e.scr[2][0:1, :], in_=e.B[1][0:1, :], mul=1.0 / D))
        W(s, e.sACT)
        I(e.sACT, s.activation(out=e.scr[2][0:1, :], in_=e.scr[2][0:1, :], func=AF.Ln, bias=EPS))
        W(s, e.sACT)
        I(e.sACT, s.activation(out=e.r1row[:, :], in_=e.scr[2][0:1, :], func=AF.Exp, scale=-0.5))
        s.wait_ge(e.sPE.h, pe_at(gi, 50))
        I(e.sACT, s.copy(out=e.draw_b[:, :], in_=e.B[4][0:64, :]))
        if gi > 0:
            s.wait_ge(e.sBC.h, 16 * gi)
        I(e.sACT, s.copy(out=e.scr[3][0:32, :], in_=e.B[5][0:32, :]))
        for dh in range(DH):
            bank = e.B[6 + dh % 2]
            s.wait_ge(e.sPE.h, pe_at(gi, 50 + dh + 1))
            if dh >= 1:
                s.wait_ge(e.sDVE.h, dve_at(gi, 82 + dh))
            elif gi > 0:
                s.wait_ge(e.sDVE.h, dve_at(gi - 1, 98))
            W(s, e.sACT)
            bcol = e.bdt_c[:, dh:dh + 1]
            I(e.sACT, s.activation(out=e.scr[0][:, :], in_=bank[:, :], func=AF.Abs, bias=bcol))
            W(s, e.sACT)
            I(e.sACT, s.activation(out=e.scr[1][:, :], in_=e.scr[0][:, :], func=AF.Exp, scale=-1.0))
            W(s, e.sACT)
            I(e.sACT, s.activation(out=e.scr[1][:, :], in_=e.scr[1][:, :], func=AF.Ln, bias=1.0))
            W(s, e.sACT)
            I(e.sACT, s.activation(out=e.scr[0][:, :], in_=bank[:, :], func=AF.Relu, bias=bcol))
        for tc in range(4):
            s.wait_ge(e.sPE.h, pe_at(gi, 70 + tc + 1))
            if tc >= 2:
                s.wait_ge(e.sVB[tc % 2].h, 16 * (gi * 2 + 1))
            elif gi > 0:
                s.wait_ge(e.sVB[tc % 2].h, 16 * (gi * 2))
            I(e.sACT, s.copy(out=e.vb[tc % 2][:, :], in_=e.B[2 + tc % 2][:, :]))
        assert e.sACT.v == act_at(gi, ACT_G), (gi, e.sACT.v)


def p0_tensor(e, t):
    t.wait_ge(e.sSP.h, 16 * 5)
    t.wait_ge(e.sPOOL.h, 3)
    for i in range(4):
        if i > 0:
            t.wait_ge(e.sACT.h, 2 + i)
        I(e.sPE, t.transpose(e.B[0][:, i * 16:(i + 1) * 16], e.scr[1][0:16, i * 128:(i + 1) * 128],
                             e.id16[:, :]))
    for ch in range(16):
        t.wait_ge(e.sACT.h, 2)
        t.wait_ge(e.sPOOL.h, 3 + ch + 1)
        if ch >= 1:
            t.wait_ge(e.sDVE.h, 5 + 15 * (ch - 1) + 8)
        sl = e.iot[:, (ch % 2) * 512:(ch % 2) * 512 + 512]
        I(e.sPE, t.matmul(e.B[1][0:64, :], e.invf[:, :], sl, start=True, stop=True))
        I(e.sPE, t.matmul(e.B[2][0:64, :], e.invf[:, :], sl, start=True, stop=True))
    assert e.sPE.v == PRO_PE, e.sPE.v

    for gi in range(G):
        if gi > 0:
            t.wait_ge(e.sACT.h, act_at(gi - 1, ACT_G))
            t.wait_ge(e.sDVE.h, dve_at(gi - 1, DVE_G))
        else:
            t.wait_ge(e.sDVE.h, PRO_DVE)
            t.wait_ge(e.sGLD.h, PRO_GLD)
        for dh in range(DH):
            t.wait_ge(e.sDVE.h, dve_at(gi, dh + 1))
            t.wait_ge(e.sPOOL.h, pool_at(gi, dh + 1))
            t.matmul(e.B[0][0:1, :], e.ones_bf[:, :], e.xn1b[:, dh * GW:(dh + 1) * GW],
                     start=(dh == 0), stop=(dh == DH - 1))
            I(e.sPE, t.matmul(e.B[1][0:1, :], e.ones_bf[:, :], e.sqb[dh % 2][:, :],
                              start=(dh == 0), stop=(dh == DH - 1)))
        t.wait_ge(e.sACT.h, act_at(gi, 4))
        t.matmul(e.B[2][:, :], e.ones1f[:, :], e.mrow[:, :], start=True, stop=True)
        I(e.sPE, t.matmul(e.B[3][:, :], e.ones1f[:, :], e.rrow[:, :], start=True, stop=True))
        for dh in range(DH):
            t.wait_ge(e.sPOOL.h, pool_at(gi, 16 + dh + 1))
            I(e.sPE, t.matmul(e.B[1][0:1, :], e.ones_bf[:, :], e.sqb[dh % 2][:, :],
                              start=(dh == 0), stop=(dh == DH - 1)))
        t.wait_ge(e.sACT.h, act_at(gi, 7))
        t.wait_ge(e.sDVE.h, dve_at(gi, 66))
        I(e.sPE, t.matmul(e.B[3][:, :], e.ones1f[:, :], e.r1row[:, :], start=True, stop=True))
        for dh in range(DH):
            t.wait_ge(e.sDVE.h, dve_at(gi, 21 + 3 * dh))
            if dh == 0 and gi > 0:
                t.wait_ge(e.sACT.h, act_at(gi - 1, 9))
            t.matmul(e.B[4][0:64, :], e.wxp_sb[:, dh * 96:dh * 96 + 64],
                     e.xnb[:, dh * GW:(dh + 1) * GW], start=(dh == 0), stop=(dh == DH - 1))
            I(e.sPE, t.matmul(e.B[5][0:32, :], e.wxp_sb[:, dh * 96 + 64:dh * 96 + 96],
                              e.xnb[:, dh * GW:(dh + 1) * GW], start=(dh == 0),
                              stop=(dh == DH - 1)))
        for dh in range(DH):
            t.wait_ge(e.sACT.h, act_at(gi, 8))
            if dh >= 2:
                t.wait_ge(e.sACT.h, act_at(gi, 13 + 4 * (dh - 2)))
            I(e.sPE, t.matmul(e.B[6 + dh % 2][:, :], e.wdt_sb[:, dh * 128:(dh + 1) * 128],
                              e.draw_b[:, :], start=True, stop=True))
        t.wait_ge(e.sGLD.h, PRO_GLD)
        for kvh in range(KVH):
            t.wait_ge(e.sDVE.h, dve_at(gi, 82))
            if kvh >= 2:
                t.wait_ge(e.sDVE.h, dve_at(gi, 104 + 6 * (kvh - 2)))
            for dh in range(DH):
                ins = t.matmul(e.B[kvh % 2][:, :],
                               e.wk_sb[:, (kvh * DH + dh) * 128:(kvh * DH + dh + 1) * 128],
                               e.xn1b[:, dh * GW:(dh + 1) * GW],
                               start=(dh == 0), stop=(dh == DH - 1))
                if dh == DH - 1:
                    I(e.sPE, ins)
        for tc in range(4):
            if tc >= 2:
                t.wait_ge(e.sACT.h, act_at(gi, 72 + tc))
            for dh in range(DH):
                ins = t.matmul(e.B[2 + tc % 2][:, :],
                               e.xn1b[:, dh * GW + tc * 128:dh * GW + (tc + 1) * 128],
                               e.wv_sb[:, dh * 512:(dh + 1) * 512],
                               start=(dh == 0), stop=(dh == DH - 1))
                if dh == DH - 1:
                    I(e.sPE, ins)
        for kvh in range(KVH):
            t.wait_ge(e.sPOOL.h, pool_at(gi, 48 + kvh + 1))
            if kvh == 0:
                t.wait_ge(e.sACT.h, act_at(gi, 8))
            else:
                t.wait_ge(e.sDVE.h, dve_at(gi, 122 + 2 * kvh))
            I(e.sPE, t.matmul(e.B[4][0:1, :], e.ones_bf[:, :], e.sqb[kvh % 2][:, :],
                              start=True, stop=True))
        assert e.sPE.v == pe_at(gi, PE_G), (gi, e.sPE.v)


def p0_vector(e, v):
    I(e.sDVE, v.memset(e.ones_bf[:, :], 1.0))
    I(e.sDVE, v.memset(e.ones1f[:, :], 1.0))
    I(e.sDVE, v.memset(e.cst[:, 0:1], EPS))
    I(e.sDVE, v.memset(e.cst[:, 1:2], 1.0))
    I(e.sDVE, v.memset(e.kmax2[:, :], 0.0))
    for ch in range(16):
        v.wait_ge(e.sPE.h, 4 + 2 * ch + 2)
        if ch >= 1:
            v.wait_ge(e.sACT.h, 6 + 2 * ch)
        b = e.B[1]
        b2 = e.B[2]

        def redchain(v, srcap, dst, tmp):
            W(v, e.sDVE)
            I(e.sDVE, v.tensor_scalar_mul(dst[0:64, :], srcap, INV2PI))
            W(v, e.sDVE)
            I(e.sDVE, v.tensor_copy(out=e.iscr[0:64, :], in_=dst[0:64, :]))
            W(v, e.sDVE)
            I(e.sDVE, v.tensor_copy(out=tmp[0:64, :], in_=e.iscr[0:64, :]))
            W(v, e.sDVE)
            I(e.sDVE, v.scalar_tensor_tensor(out=dst[0:64, :], in0=tmp[0:64, :], scalar=-TWOPI,
                                             in1=srcap, op0=ALU.mult, op1=ALU.add))
            W(v, e.sDVE)
            I(e.sDVE, v.tensor_scalar(out=tmp[0:64, :], in0=dst[0:64, :], scalar1=PI,
                                      scalar2=None, op0=ALU.is_gt, op1=ALU.bypass,
                                      accum_out=None))
            W(v, e.sDVE)
            I(e.sDVE, v.scalar_tensor_tensor(out=dst[0:64, :], in0=tmp[0:64, :], scalar=-TWOPI,
                                             in1=dst[0:64, :], op0=ALU.mult, op1=ALU.add))
            W(v, e.sDVE)
            I(e.sDVE, v.tensor_scalar(out=dst[0:64, :], in0=dst[0:64, :], scalar1=PI,
                                      scalar2=-PI, op0=ALU.min, op1=ALU.max, accum_out=None))

        redchain(v, b[0:64, :], e.scr[5], e.scr[3])
        W(v, e.sDVE)
        I(e.sDVE, v.tensor_scalar_add(e.scr[2][0:64, :], b2[0:64, :], PI / 2))
        redchain(v, e.scr[2][0:64, :], e.scr[4], e.scr[3])
        assert e.sDVE.v == 5 + 15 * (ch + 1), (ch, e.sDVE.v)
    assert e.sDVE.v == PRO_DVE, e.sDVE.v

    for gi in range(G):
        for dh in range(DH):
            v.wait_ge(e.sX.h, sx_at(gi, 16))
            if gi > 0 and dh == 0:
                v.wait_ge(e.sPE.h, pe_at(gi - 1, PE_G))
                v.wait_ge(e.sX1.h, 32 * gi)
            I(e.sDVE, v.tensor_copy(out=e.xn1b[:, dh * GW:(dh + 1) * GW],
                                    in_=e.xbuf[:, dh * GW:(dh + 1) * GW]))
        v.wait_ge(e.sACT.h, act_at(gi, 2))
        W(v, e.sDVE)
        I(e.sDVE, v.tensor_mul(e.scr[2][0:1, :], e.mrow[:, :], e.mrow[:, :]))
        W(v, e.sDVE)
        I(e.sDVE, v.tensor_sub(e.mrow2[:, :], e.mrow2[:, :], e.scr[2][0:1, :]))
        v.wait_ge(e.sPE.h, pe_at(gi, 17))
        for dh in range(DH):
            v.wait_ge(e.sPOOL.h, pool_at(gi, dh + 1))
            W(v, e.sDVE)
            I(e.sDVE, v.tensor_sub(e.xbuf[:, dh * GW:(dh + 1) * GW],
                                   e.xbuf[:, dh * GW:(dh + 1) * GW], e.B[2][:, :]))
            W(v, e.sDVE)
            I(e.sDVE, v.tensor_mul(e.xbuf[:, dh * GW:(dh + 1) * GW],
                                   e.xbuf[:, dh * GW:(dh + 1) * GW], e.B[3][:, :]))
            W(v, e.sDVE)
            I(e.sDVE, v.tensor_copy(out=e.xnb[:, dh * GW:(dh + 1) * GW],
                                    in_=e.xbuf[:, dh * GW:(dh + 1) * GW]))
        v.wait_ge(e.sPE.h, pe_at(gi, 34))
        W(v, e.sDVE)
        for dh in range(DH):
            I(e.sDVE, v.scalar_tensor_tensor(out=e.xn1b[:, dh * GW:(dh + 1) * GW],
                                             in0=e.xbuf[:, dh * GW:(dh + 1) * GW],
                                             scalar=e.n1w_c[:, dh:dh + 1], in1=e.B[3][:, :],
                                             op0=ALU.mult, op1=ALU.mult))
        for dh in range(DH):
            v.wait_ge(e.sACT.h, act_at(gi, 13 + 4 * dh))
            if dh >= 2:
                v.wait_ge(e.sDS[dh % 2].h, 16 * (gi * 8 + (dh - 2) // 2 + 1))
                v.wait_ge(e.sPOOL.h, pool_at(gi, 32 + dh - 1))
            elif gi > 0:
                v.wait_ge(e.sDS[dh % 2].h, 16 * (gi * 8))
                v.wait_ge(e.sPOOL.h, pool_at(gi - 1, 32 + DH - 1 + dh))
            I(e.sDVE, v.tensor_add(e.dsc[dh % 2][:, :], e.scr[0][:, :], e.scr[1][:, :]))
        for kvh in range(KVH):
            v.wait_ge(e.sPE.h, pe_at(gi, 66 + kvh + 1))
            v.wait_ge(e.sACT.h, PRO_ACT)
            if kvh >= 2:
                v.wait_ge(e.sKB[kvh % 2].h, 16 * (gi * 2 + 1))
                v.wait_ge(e.sPOOL.h, pool_at(gi, 48 + kvh - 1))
            elif gi > 0:
                v.wait_ge(e.sKB[kvh % 2].h, 16 * (gi * 2))
                v.wait_ge(e.sPOOL.h, pool_at(gi - 1, 48 + kvh + 3))
            b = e.B[kvh % 2]
            cs = e.cosr[:, gi * GW:(gi + 1) * GW]
            sn = e.sinr[:, gi * GW:(gi + 1) * GW]
            kb = e.kvb[kvh % 2]
            W(v, e.sDVE)
            I(e.sDVE, v.tensor_mul(e.scr[0][0:64, :], b[0:64, :], cs))
            I(e.sDVE, v.tensor_mul(e.scr[1][0:64, :], b[64:128, :], sn))
            W(v, e.sDVE)
            I(e.sDVE, v.tensor_sub(kb[0:64, :], e.scr[0][0:64, :], e.scr[1][0:64, :]))
            W(v, e.sDVE)
            I(e.sDVE, v.tensor_mul(e.scr[0][0:64, :], b[64:128, :], cs))
            I(e.sDVE, v.tensor_mul(e.scr[1][0:64, :], b[0:64, :], sn))
            W(v, e.sDVE)
            I(e.sDVE, v.tensor_add(kb[64:128, :], e.scr[0][0:64, :], e.scr[1][0:64, :]))
        for kvh in range(KVH):
            v.wait_ge(e.sPE.h, pe_at(gi, 74 + kvh + 1))
            W(v, e.sDVE)
            I(e.sDVE, v.reduce_max(e.scr[2][0:1, 0:1], e.B[4][0:1, :], axis=AXX))
            W(v, e.sDVE)
            idx = (gi // 8) * 4 + kvh
            I(e.sDVE, v.tensor_max(e.kmax2[:, idx:idx + 1], e.kmax2[:, idx:idx + 1],
                                   e.scr[2][0:1, 0:1]))
        assert e.sDVE.v == dve_at(gi, DVE_G), (gi, e.sDVE.v)


def p0_sync(e, s):
    s.dma_start(out=e.scr[0][:, 0:DH * NST].rearrange("p (dh n) -> p dh n", dh=DH),
                in_=e.Alog.rearrange("(dh dl) n -> dl dh n", dl=128)).then_inc(e.sSP.h, 16)
    for i, t_ in enumerate([e.bdt_r, e.D_r, e.n1w_r, e.n2w_r]):
        s.dma_start(out=e.scr[1][0:16, i * 128:(i + 1) * 128], in_=t_).then_inc(e.sSP.h, 16)
    e.sSP.v = PRO_SP

    for gi in range(G):
        if gi > 0:
            s.wait_ge(e.sPOOL.h, pool_at(gi - 1, 48))
            s.wait_ge(e.sDVE.h, dve_at(gi - 1, 82))
        for dh in range(DH):
            s.dma_start(out=e.xbuf[:, dh * GW:(dh + 1) * GW],
                        in_=e.xT[dh * 128:(dh + 1) * 128, gi * GW:(gi + 1) * GW]).then_inc(e.sX.h, 16)
        e.sX.v = sx_at(gi, SX_G)
        s.wait_ge(e.sDVE.h, dve_at(gi, 82))
        for half in range(2):
            I(e.sX1, s.dma_start(
                out=e.d_xn1.rearrange("(dh dl) t -> dl dh t", dl=128)[:, half * 8:(half + 1) * 8,
                                                                      gi * GW:(gi + 1) * GW],
                in_=e.xn1b[:, half * 8 * GW:(half + 1) * 8 * GW].rearrange(
                    "p (dh t) -> p dh t", dh=8)), 16)
        for dh in range(DH):
            s.wait_ge(e.sDVE.h, dve_at(gi, 83 + dh))
            I(e.sDS[dh % 2], s.dma_start(out=e.d_delta[dh * 128:(dh + 1) * 128,
                                                       gi * GW:(gi + 1) * GW],
                                         in_=e.dsc[dh % 2][:, :]), 16)
            s.wait_ge(e.sPOOL.h, pool_at(gi, 32 + dh + 1))
            I(e.sUS[dh % 2], s.dma_start(out=e.d_u[dh * 128:(dh + 1) * 128,
                                                   gi * GW:(gi + 1) * GW],
                                         in_=e.usc[dh % 2][:, :]), 16)
        for kvh in range(KVH):
            s.wait_ge(e.sDVE.h, dve_at(gi, 104 + 6 * kvh))
            I(e.sKB[kvh % 2], s.dma_start(out=e.d_k[kvh, :, gi * GW:(gi + 1) * GW],
                                          in_=e.kvb[kvh % 2][:, :]), 16)
        for tc in range(4):
            s.wait_ge(e.sACT.h, act_at(gi, 73 + tc + 1))
            I(e.sVB[tc % 2], s.dma_start(out=e.d_v[gi * GW + tc * 128:gi * GW + (tc + 1) * 128, :],
                                         in_=e.vb[tc % 2][:, :]), 16)
        s.wait_ge(e.sACT.h, act_at(gi, 9))
        I(e.sBC, s.dma_start(out=e.d_bc[:, gi * GW:(gi + 1) * GW], in_=e.scr[3][0:32, :]), 16)
        s.wait_ge(e.sACT.h, act_at(gi, 4))
        I(e.sRW, s.dma_start(out=e.d_rows[0:1, gi * GW:(gi + 1) * GW], in_=e.mrow), 16)
        I(e.sRW, s.dma_start(out=e.d_rows[1:2, gi * GW:(gi + 1) * GW], in_=e.rrow), 16)
    for c in [e.sX1, e.sBC, e.sRW] + e.sDS + e.sUS + e.sKB + e.sVB:
        s.wait_ge(c.h, c.v)


def rope_debug(e, g):
    g.dma_start(out=e.dbg["dbg_cos"], in_=e.cosr[:, 0:1024]).then_inc(e.sGLD.h, 16)
    g.dma_start(out=e.dbg["dbg_sin"], in_=e.sinr[:, 0:1024]).then_inc(e.sGLD.h, 16)
    e.sGLD.v += 2 * 16
    g.wait_ge(e.sGLD.h, e.sGLD.v)


def p0_debug_g(e, g):
    for half in range(2):
        g.dma_start(out=e.dbg["dbg_xn1"][half * 1024:(half + 1) * 1024, :],
                    in_=e.d_xn1[half * 1024:(half + 1) * 1024, 0:1024]).then_inc(e.sGLD.h, 16)
    g.dma_start(out=e.dbg["dbg_k"].rearrange("(a p) t -> a p t", a=KVH),
                in_=e.d_k[:, :, 0:1024]).then_inc(e.sGLD.h, 16)
    g.dma_start(out=e.dbg["dbg_v"], in_=e.d_v[0:1024, :]).then_inc(e.sGLD.h, 16)
    e.sGLD.v += 4 * 16


def p0_debug_s(e, s):
    s.dma_start(out=e.dbg["dbg_delta"], in_=e.d_delta[:, 0:1024]).then_inc(e.sST.h, 16)
    s.dma_start(out=e.dbg["dbg_u"], in_=e.d_u[:, 0:1024]).then_inc(e.sST.h, 16)
    s.dma_start(out=e.dbg["dbg_bc"], in_=e.d_bc[:, 0:1024]).then_inc(e.sST.h, 16)
    s.dma_start(out=e.dbg["dbg_rows"], in_=e.d_rows).then_inc(e.sST.h, 16)
    s.dma_start(out=e.out[0:128, 0:512], in_=e.scr[0][:, 0:512]).then_inc(e.sST.h, 16)
    e.sST.v += 5 * 16
    s.wait_ge(e.sST.h, e.sST.v)
    s.wait_ge(e.sGLD.h, e.sGLD.v)



TSC = 2048


def own_cols(c):
    return (512 * c, 4096 + 512 * (7 - c))


def emit_mamba(e):
    nc = e.nc
    mctx = ExitStack()

    def sb(name, shape, dt):
        return mctx.enter_context(nc.sbuf_tensor(name, shape, dt))
    if "sw_excess" not in e:
        e.sw_excess = {}
    if "pid_s" not in e:
        e.pid_s = nc.sync.partition_id()
    if "pid_gm" not in e:
        e.pid_gm = nc.gpsimd.partition_id()
    e.negA = sb("negA", [128, 32], F32)
    e.m_dlt = sb("m_dlt", [128, L], F32)
    e.m_ut = sb("m_ut", [128, L], BF16)
    e.m_ysb = sb("m_ysb", [128, L], F32)
    e.m_dab = [sb(f"m_dab{i}", [128, TSC], F32) for i in range(2)]
    e.m_bcb = [sb(f"m_bcb{i}", [128, TSC], BF16) for i in range(4)]
    e.m_ccb = [sb(f"m_ccb{i}", [128, TSC], BF16) for i in range(4)]
    e.m_hsb = [sb(f"m_hsb{i}", [128, TSC], BF16) for i in range(2)]

    dve0 = e.sDVE.v + 1
    act0 = e.sACT.v + 1
    bb0 = e.sBB.v
    cb0 = e.sCB.v
    dvev = lambda git, k: dve0 + git * 4 + k
    actv = lambda git: act0 + git + 1
    bbv = lambda git: bb0 + 16 * (git + 1)
    cbv = lambda git: cb0 + 16 * (git + 1)

    seng = nc.sync
    for blkj in range(4):
        ci2, b = blkj // 2, blkj % 2
        geng = nc.gpsimd
        for c in nc.Switch(engines=[seng, geng], index=[e.pid_s, e.pid_gm], n=8):
            row = 256 * c + 128 * ci2
            if blkj > 0:
                seng.wait_ge(e.sACT.h, actv(blkj * 32 - 1))
                geng.wait_ge(e.sDVE.h, dvev(blkj * 32 - 1, 1))
            else:
                geng.wait_ge(e.sUS[0].h, e.sUS[0].v)
                geng.wait_ge(e.sUS[1].h, e.sUS[1].v)
            I(e.sMS, seng.dma_start(out=e.m_dlt[:, :],
                                    in_=e.d_delta[row:row + 128, b * L:(b + 1) * L]), 16)
            e.sMS.v -= 16
            I(e.sMS, geng.dma_start(out=e.m_ut[:, :],
                                    in_=e.d_u[row:row + 128, b * L:(b + 1) * L]), 16)
            e.sMS.v -= 16
            if blkj == 0:
                I(e.sMS, seng.dma_start(
                    out=e.negA[:, :].rearrange("p (ci n) -> p ci n", ci=2),
                    in_=e.Alog.rearrange("(ci dl) n -> dl ci n", dl=128)[:, 2 * c:2 * c + 2, :]), 16)
                e.sMS.v -= 16
        nadd = 3 if blkj == 0 else 2
        e.sMS.v += 16 * nadd
        e.sw_excess["sMS"] = e.sw_excess.get("sMS", 0) + 7 * 16 * nadd

        ms_now = e.sMS.v
        with nc.Block() as blk:
            def fsy(s, blkj=blkj, ci2=ci2, b=b):
                s.wait_ge(e.sDVE.h, dvev(blkj * 32 + 31, 4))
                I(e.sMS, s.dma_start(out=e.d_ysrc[ci2 * 128:(ci2 + 1) * 128,
                                                  b * L:(b + 1) * L],
                                     in_=e.m_ysb[:, :]), 16)

            def fgp(g, blkj=blkj, b=b):
                if blkj == 0:
                    g.wait_ge(e.sBC.h, e.sBC.v)
                for n in range(NST):
                    for kk in range(2):
                        git = blkj * 32 + n * 2 + kk
                        if git >= 4:
                            g.wait_ge(e.sDVE.h, dvev(git - 4, 4))
                        I(e.sBB, g.dma_start(
                            out=e.m_bcb[git % 4][:, :],
                            in_=e.d_bc[n:n + 1, b * L + kk * TSC:
                                       b * L + (kk + 1) * TSC].partition_broadcast(128)), 16)
                        I(e.sCB, g.dma_start(
                            out=e.m_ccb[git % 4][:, :],
                            in_=e.d_bc[16 + n:17 + n, b * L + kk * TSC:
                                       b * L + (kk + 1) * TSC].partition_broadcast(128)), 16)

            def fsc(s, blkj=blkj, ci2=ci2, ms_now=ms_now):
                s.wait_ge(e.sMS.h, ms_now)
                if blkj == 0:
                    I(e.sACT, s.activation(out=e.negA[:, :], in_=e.negA[:, :],
                                           func=AF.Exp))
                    s.wait_ge(e.sDVE.h, dve0)
                for n in range(NST):
                    for kk in range(2):
                        git = blkj * 32 + n * 2 + kk
                        if git >= 2:
                            s.wait_ge(e.sDVE.h, dvev(git - 2, 2))
                        I(e.sACT, s.activation(
                            out=e.m_dab[git % 2][:, :],
                            in_=e.m_dlt[:, kk * TSC:(kk + 1) * TSC],
                            func=AF.Exp,
                            scale=e.negA[:, ci2 * 16 + n:ci2 * 16 + n + 1]))

            def fv(v, blkj=blkj, ms_now=ms_now):
                v.wait_ge(e.sMS.h, ms_now)
                if blkj == 0:
                    v.wait_ge(e.sACT.h, act0)
                    I(e.sDVE, v.tensor_scalar(out=e.negA[:, :], in0=e.negA[:, :],
                                              scalar1=-1.0, scalar2=None,
                                              op0=ALU.mult, op1=ALU.bypass,
                                              accum_out=None))
                for n in range(NST):
                    for kk in range(2):
                        git = blkj * 32 + n * 2 + kk
                        p = git % 2
                        p4 = git % 4
                        v.wait_ge(e.sBB.h, bbv(git))
                        I(e.sDVE, v.tensor_mul(e.m_bcb[p4][:, :],
                                               e.m_ut[:, kk * TSC:(kk + 1) * TSC],
                                               e.m_bcb[p4][:, :]))
                        v.wait_ge(e.sACT.h, actv(git))
                        init = 0.0 if kk == 0 else e.m_hsb[1 - p][:, TSC - 1:TSC]
                        I(e.sDVE, v.tensor_tensor_scan(out=e.m_hsb[p][:, :],
                                                       data0=e.m_dab[p][:, :],
                                                       data1=e.m_bcb[p4][:, :],
                                                       initial=init,
                                                       op0=ALU.mult, op1=ALU.add))
                        v.wait_ge(e.sCB.h, cbv(git))
                        I(e.sDVE, v.tensor_mul(e.m_ccb[p4][:, :], e.m_hsb[p][:, :],
                                               e.m_ccb[p4][:, :]))
                        ysl = e.m_ysb[:, kk * TSC:(kk + 1) * TSC]
                        if n == 0:
                            I(e.sDVE, v.tensor_copy(out=ysl, in_=e.m_ccb[p4][:, :]))
                        else:
                            I(e.sDVE, v.tensor_add(ysl, ysl, e.m_ccb[p4][:, :]))

            blk.sync(fsy)
            blk.gpsimd(fgp)
            blk.scalar(fsc)
            blk.vector(fv)

    with nc.Block() as blk:
        def fg(g):
            g.wait_ge(e.sMS.h, e.sMS.v)
            I(e.sAG, g.collective_compute(
                "AllGather", ALU.bypass,
                replica_groups=[list(range(8))],
                ins=[e.d_ysrc[:, :]], outs=[e.d_ygat[:, :]],
            ))
        blk.gpsimd(fg)
    mctx.close()



ATT_PAIRS = [(b, k) for b in range(2) for k in range(KVH)]


def att_E(c, beta):
    return 4 * (c + 1) if beta == 0 else 4 * (8 - c)


def act_pairh_base(c, pair, h):
    off = 2
    for p2 in range(pair):
        off += 4 * (att_E(c, ATT_PAIRS[p2][0]) + 4)
    off += h * (att_E(c, ATT_PAIRS[pair][0]) + 4)
    return off


class SwBook:

    def __init__(self, e, names):
        self.e = e
        self.names = names
        self.base = {}

    def enter_body(self):
        for n in self.names:
            c = self.e[n]
            if n not in self.base:
                self.base[n] = c.v
            c.v = self.base[n]

    def finish(self):
        for n in self.names:
            c = self.e[n]
            per = c.v - self.base[n]
            self.e.sw_excess[n] = self.e.sw_excess.get(n, 0) + 7 * per


def emit_attention(e):
    nc = e.nc
    if "sw_excess" not in e:
        e.sw_excess = {}
    e.avctx = ExitStack()
    e.av_sb = e.avctx.enter_context(
        nc.sbuf_tensor("av_sb", [128, NH * 1024], BF16))
    actx = ExitStack()

    def sb(name, shape, dt):
        return actx.enter_context(nc.sbuf_tensor(name, shape, dt))

    e.qin = sb("qin", [128, DH * 1024], BF16)
    e.q_sb = sb("q_sb", [128, NH * 1024], BF16)
    e.kbufA = sb("kbufA", [128, 32 * 128], BF16)
    e.vbufA = sb("vbufA", [128, 32 * 128], BF16)
    e.wq_sb = sb("wq_sb", [128, 2 * DH * 128], BF16)
    e.probs = [sb(f"probs{i}", [128, 512], BF16) for i in range(2)]
    e.avf = sb("avf", [128, 512], F32)
    e.q2b = sb("q2b", [128, 1024], BF16)
    e.bias_col = sb("bias_col", [128, 1], F32)
    e.srow = sb("srow", [1, 512], F32)
    e.trow = sb("trow", [1, 8], F32)

    teng, seng, veng, geng, speng = nc.tensor, nc.scalar, nc.vector, nc.gpsimd, nc.sync
    if "pid_t" not in e:
        e.pid_t = teng.partition_id()
        e.pid_sc = seng.partition_id()
        e.pid_v = veng.partition_id()
        e.pid_g2 = geng.partition_id()
        e.pid_sp2 = speng.partition_id()
    book = SwBook(e, ["sPE", "sACT", "sDVE", "sPOOL", "sQS", "sKV", "sWQ0", "sWQ1"])
    for c in nc.Switch(engines=[teng, seng, veng, geng, speng],
                       index=[e.pid_t, e.pid_sc, e.pid_v, e.pid_g2, e.pid_sp2], n=8):
        book.enter_body()
        e.ab = dict(book.base)
        attn_body(e, c, teng, seng, veng, geng, speng)
    book.finish()

    if DEBUG:
        with nc.Block() as blk:
            blk.gpsimd(lambda s: attn_debug(e, s))
    actx.close()


def attn_body(e, c, t, s, v, g, sp):
    A0, A1 = own_cols(c)
    ab = e.ab
    peq = lambda hq, k: ab["sPE"] + 2 * hq + k
    peqn = lambda hq: ab["sPE"] + 32 + hq + 1
    pebias = ab["sPE"] + 49
    dveq = lambda hq, ib: ab["sDVE"] + 12 * hq + 6 * (ib + 1)
    dveqn = lambda hq: ab["sDVE"] + 192 + 2 * (hq + 1)
    dveu = ab["sDVE"] + 226
    dveneg = ab["sDVE"] + 227
    dveav = lambda pair, h: ab["sDVE"] + 227 + pair * 4 + h + 1
    actexp = lambda pair, h, kc: ab["sACT"] + act_pairh_base(c, pair, h) + kc + 1
    actsum = lambda pair, h: (ab["sACT"] + act_pairh_base(c, pair, h)
                              + att_E(c, ATT_PAIRS[pair][0]) + 1)
    actrec = lambda pair, h: actsum(pair, h) + 2
    actavc = lambda pair, h: actsum(pair, h) + 3
    poolq2 = lambda hq: ab["sPOOL"] + hq + 1
    poolaff = lambda pair, h, j: ab["sPOOL"] + 16 + (pair * 4 + h) * 4 + j + 1

    pe_off = {}

    def emit_sumav(pair, h, kc, beta):
        E = att_E(c, beta)
        t.wait_ge(e.sACT.h, actexp(pair, h, kc))
        if kc >= E - 4:
            t.wait_ge(e.sPOOL.h, poolaff(pair, h, kc - (E - 4)))
        t.matmul(e.B[3][0:1, :], e.ones_bf[:, :], e.probs[kc % 2][:, :],
                 start=(kc == 0), stop=(kc == E - 1))
        ins = t.matmul(e.B[2][:, :], e.vbufA[:, kc * 128:(kc + 1) * 128],
                       e.probs[kc % 2][:, :], start=(kc == 0), stop=(kc == E - 1))
        I(e.sPE, ins)
        pe_off[("av", pair, h, kc)] = e.sPE.v

    def emit_tensor():
        t.wait_ge(e.sQS.h, ab["sQS"] + 16 * 4)
        for hq in range(NH):
            t.wait_ge(e["sWQ%d" % (hq % 2)].h, ab["sWQ%d" % (hq % 2)] + 32 * (hq // 2 + 1))
            for grp in range(2):
                if hq >= 1:
                    t.wait_ge(e.sDVE.h, dveq(hq - 1, grp))
                for dh in range(DH):
                    ins = t.matmul(
                        e.B[grp][:, :],
                        e.wq_sb[:, ((hq % 2) * DH + dh) * 128:((hq % 2) * DH + dh + 1) * 128],
                        e.qin[:, dh * 1024 + grp * 512:dh * 1024 + (grp + 1) * 512],
                        start=(dh == 0), stop=(dh == DH - 1))
                I(e.sPE, ins)
        for hq in range(NH):
            t.wait_ge(e.sPOOL.h, poolq2(hq))
            if hq >= 1:
                t.wait_ge(e.sDVE.h, dveqn(hq - 1))
            t.matmul(e.B[5][0:1, :], e.ones_bf[:, :], e.q2b[:, 0:512],
                     start=True, stop=False)
            I(e.sPE, t.matmul(e.B[5][0:1, :], e.ones_bf[:, :], e.q2b[:, 512:1024],
                              start=False, stop=True))
        t.wait_ge(e.sACT.h, ab["sACT"] + 2)
        I(e.sPE, t.matmul(e.B[4][:, 0:1], e.ones1f[:, :], e.trow[0:1, 3:4],
                          start=True, stop=True))
        t.wait_ge(e.sDVE.h, ab["sDVE"] + 192)
        for pair, (beta, kvh) in enumerate(ATT_PAIRS):
            E = att_E(c, beta)
            t.wait_ge(e.sKV.h, ab["sKV"] + 32 * (pair + 1))
            for h in range(4):
                hglob = kvh * 4 + h
                qsl = e.q_sb[:, hglob * 1024 + beta * 512:hglob * 1024 + (beta + 1) * 512]
                if h == 0 and pair == 0:
                    t.wait_ge(e.sDVE.h, dveneg)
                else:
                    ph, hh = (pair, h - 1) if h > 0 else (pair - 1, 3)
                    t.wait_ge(e.sACT.h, actavc(ph, hh))
                for kc in range(E):
                    if kc >= 2:
                        t.wait_ge(e.sACT.h, actexp(pair, h, kc - 2))
                    ins = t.matmul(e.B[kc % 2][:, :], e.kbufA[:, kc * 128:(kc + 1) * 128],
                                   qsl, start=True, stop=True)
                    I(e.sPE, ins)
                    pe_off[("score", pair, h, kc)] = e.sPE.v
                    if kc >= 1:
                        emit_sumav(pair, h, kc - 1, beta)
                emit_sumav(pair, h, E - 1, beta)
                t.wait_ge(e.sACT.h, actrec(pair, h))
                if pair == 0 and h == 0:
                    t.wait_ge(e.sDVE.h, dveneg)
                else:
                    ph, hh = (pair, h - 1) if h > 0 else (pair - 1, 3)
                    t.wait_ge(e.sDVE.h, dveav(ph, hh))
                I(e.sPE, t.matmul(e.B[4][:, :], e.ones1f[:, :], e.srow[0:1, :],
                                  start=True, stop=True))
                pe_off[("recip", pair, h)] = e.sPE.v

    emit_tensor()

    sp.wait_ge(e.sMS.h, e.sMS.v)
    sp.wait_ge(e.sDVE.h, ab["sDVE"])
    sp.wait_ge(e.sPE.h, ab["sPE"])
    sp.wait_ge(e.sACT.h, ab["sACT"])
    for half in range(2):
        for ib, a in enumerate((A0, A1)):
            I(e.sQS, sp.dma_start(
                out=e.qin[:, :].rearrange("p (dh tk) -> p dh tk", dh=DH)[
                    :, half * 8:(half + 1) * 8, ib * 512:(ib + 1) * 512],
                in_=e.d_xn1.rearrange("(dh dl) t -> dl dh t", dl=128)[
                    :, half * 8:(half + 1) * 8, a:a + 512]), 16)
    for pair, (beta, kvh) in enumerate(ATT_PAIRS):
        E = att_E(c, beta)
        kb = beta * L
        if pair >= 1:
            sp.wait_ge(e.sPE.h, pe_off[("recip", pair - 1, 3)])
        I(e.sKV, sp.dma_start(out=e.kbufA[:, 0:E * 128],
                              in_=e.d_k[kvh, :, kb:kb + E * 128]), 16)
        I(e.sKV, sp.dma_start(
            out=e.vbufA[:, 0:E * 128].rearrange("p (kc x) -> p kc x", kc=E),
            in_=e.d_v[kb:kb + E * 128, kvh * 128:(kvh + 1) * 128].rearrange(
                "(kc p) x -> p kc x", p=128)), 16)

    g.wait_ge(e.sMS.h, e.sMS.v)
    g.wait_ge(e.sDVE.h, ab["sDVE"])
    g.wait_ge(e.sPE.h, ab["sPE"])
    g.wait_ge(e.sACT.h, ab["sACT"])
    for hq in range(NH):
        if hq >= 2:
            g.wait_ge(e.sPE.h, peq(hq - 2, 2))
        for half in range(2):
            I(e["sWQ%d" % (hq % 2)], g.dma_start(
                out=e.wq_sb[:, ((hq % 2) * DH + half * 8) * 128:
                            ((hq % 2) * DH + (half + 1) * 8) * 128].rearrange(
                                "p (dh x) -> p dh x", dh=8),
                in_=e.wqh[hq, :, half * 8:(half + 1) * 8, :]), 16)
    for hq in range(NH):
        g.wait_ge(e.sDVE.h, dveq(hq, 1))
        if hq >= 1:
            g.wait_ge(e.sPE.h, peqn(hq - 1))
        I(e.sPOOL, g.tensor_tensor(out=e.q2b[:, :], in0=e.q_sb[:, hq * 1024:(hq + 1) * 1024],
                                   in1=e.q_sb[:, hq * 1024:(hq + 1) * 1024], op=ALU.mult))
    for pair, (beta, kvh) in enumerate(ATT_PAIRS):
        E = att_E(c, beta)
        qposl = 512 * c if beta == 0 else 512 * (7 - c)
        for h in range(4):
            for j in range(4):
                kc = E - 4 + j
                g.wait_ge(e.sACT.h, actexp(pair, h, kc))
                I(e.sPOOL, g.affine_select(
                    out=e.probs[kc % 2][:, :], in_=e.probs[kc % 2][:, :],
                    pattern=[[1, 512]], compare_op=ALU.is_ge, fill=0.0,
                    base=qposl - kc * 128, channel_multiplier=-1))

    s.wait_ge(e.sDVE.h, dveu)
    I(e.sACT, s.activation(out=e.trow[0:1, 2:3], in_=e.trow[0:1, 0:1], func=AF.Ln))
    W(s, e.sACT)
    I(e.sACT, s.activation(out=e.trow[0:1, 3:4], in_=e.trow[0:1, 2:3], func=AF.Exp,
                           scale=0.5))
    for pair, (beta, kvh) in enumerate(ATT_PAIRS):
        E = att_E(c, beta)
        for h in range(4):
            for kc in range(E):
                s.wait_ge(e.sPE.h, pe_off[("score", pair, h, kc)])
                if kc >= 2:
                    s.wait_ge(e.sPE.h, pe_off[("av", pair, h, kc - 2)])
                if pair == 0 and h == 0 and kc == 0:
                    s.wait_ge(e.sDVE.h, dveneg)
                I(e.sACT, s.activation(out=e.probs[kc % 2][:, :], in_=e.B[kc % 2][:, :],
                                       func=AF.Exp, scale=1.0 / math.sqrt(128.0),
                                       bias=e.bias_col[:, 0:1]))
            s.wait_ge(e.sPE.h, pe_off[("av", pair, h, E - 1)])
            I(e.sACT, s.copy(out=e.srow[:, :], in_=e.B[3][0:1, :]))
            W(s, e.sACT)
            I(e.sACT, s.activation(out=e.scr[2][0:1, 0:512], in_=e.srow[:, :], func=AF.Ln))
            W(s, e.sACT)
            I(e.sACT, s.activation(out=e.srow[:, :], in_=e.scr[2][0:1, 0:512], func=AF.Exp,
                                   scale=-1.0))
            if pair > 0 or h > 0:
                ph, hh = (pair, h - 1) if h > 0 else (pair - 1, 3)
                s.wait_ge(e.sDVE.h, dveav(ph, hh))
            I(e.sACT, s.copy(out=e.avf[:, :], in_=e.B[2][:, :]))

    for hq in range(NH):
        for ib in range(2):
            v.wait_ge(e.sPE.h, peq(hq, ib + 1))
            a = (A0, A1)[ib]
            b = e.B[ib]
            cs = e.cosr[:, a:a + 512]
            sn = e.sinr[:, a:a + 512]
            dst = e.q_sb[:, hq * 1024 + ib * 512:hq * 1024 + (ib + 1) * 512]
            W(v, e.sDVE)
            I(e.sDVE, v.tensor_mul(e.scr[0][0:64, :], b[0:64, :], cs))
            I(e.sDVE, v.tensor_mul(e.scr[1][0:64, :], b[64:128, :], sn))
            W(v, e.sDVE)
            I(e.sDVE, v.tensor_sub(dst[0:64, :], e.scr[0][0:64, :], e.scr[1][0:64, :]))
            W(v, e.sDVE)
            I(e.sDVE, v.tensor_mul(e.scr[0][0:64, :], b[64:128, :], cs))
            I(e.sDVE, v.tensor_mul(e.scr[1][0:64, :], b[0:64, :], sn))
            W(v, e.sDVE)
            I(e.sDVE, v.tensor_add(dst[64:128, :], e.scr[0][0:64, :], e.scr[1][0:64, :]))
    for hq in range(NH):
        v.wait_ge(e.sPE.h, peqn(hq))
        W(v, e.sDVE)
        I(e.sDVE, v.reduce_max(e.scr[2][0:1, 0:1], e.B[5][0:1, :], axis=AXX))
        W(v, e.sDVE)
        if hq == 0:
            I(e.sDVE, v.tensor_copy(out=e.trow[0:1, 4:5], in_=e.scr[2][0:1, 0:1]))
        else:
            I(e.sDVE, v.tensor_max(e.trow[0:1, 4:5], e.trow[0:1, 4:5], e.scr[2][0:1, 0:1]))
    W(v, e.sDVE)
    I(e.sDVE, v.reduce_max(e.scr[2][0:1, 0:1], e.kmax2[:, :], axis=AXX))
    W(v, e.sDVE)
    I(e.sDVE, v.scalar_tensor_tensor(out=e.trow[0:1, 0:1], in0=e.trow[0:1, 4:5],
                                     scalar=1.0 / 128.0, in1=e.scr[2][0:1, 0:1],
                                     op0=ALU.mult, op1=ALU.mult))
    v.wait_ge(e.sPE.h, pebias)
    W(v, e.sDVE)
    I(e.sDVE, v.tensor_scalar_mul(e.bias_col[:, :], e.B[4][:, 0:1], -1.0))
    for pair, (beta, kvh) in enumerate(ATT_PAIRS):
        for h in range(4):
            hglob = kvh * 4 + h
            v.wait_ge(e.sACT.h, actavc(pair, h))
            v.wait_ge(e.sPE.h, pe_off[("recip", pair, h)])
            W(v, e.sDVE)
            I(e.sDVE, v.tensor_mul(
                e.av_sb[:, hglob * 1024 + beta * 512:hglob * 1024 + (beta + 1) * 512],
                e.avf[:, :], e.B[4][:, :]))


def attn_debug(e, s):
    s.wait_ge(e.sDVE.h, e.sDVE.v)
    for half in range(2):
        I(e.sGLD, s.dma_start(
            out=e.dbg["dbg_av"].rearrange("(hq dl) t -> dl hq t", dl=128)[
                :, half * 8:(half + 1) * 8, :],
            in_=e.av_sb[:, half * 8 * 1024:(half + 1) * 8 * 1024].rearrange(
                "p (hq t) -> p hq t", hq=8)), 16)
        I(e.sGLD, s.dma_start(
            out=e.dbg["dbg_q"].rearrange("(hq dl) t -> dl hq t", dl=128)[
                :, half * 8:(half + 1) * 8, :],
            in_=e.q_sb[:, half * 8 * 1024:(half + 1) * 8 * 1024].rearrange(
                "p (hq t) -> p hq t", hq=8)), 16)
    s.wait_ge(e.sGLD.h, e.sGLD.v)



def emit_h2(e):
    nc = e.nc
    h2ctx = ExitStack()

    def sb(name, shape, dt):
        return h2ctx.enter_context(nc.sbuf_tensor(name, shape, dt))

    e.xbuf2 = sb("xbuf2", [128, DH * 1024], F32)
    e.wo_sb = sb("wo_sb", [128, 2 * DH * 128], BF16)
    e.rows_sb = sb("rows_sb", [1, 8 * 512], F32)
    e.mr_sb = sb("mr_sb", [128, 4 * 512], F32)
    e.gw_sb = sb("gw_sb", [128, DH * NE], F32)
    e.id128 = sb("id128", [128, 128], F32)
    e.xnt = [sb(f"xnt{i}", [128, 1024], BF16) for i in range(2)]
    e.wch = sb("wch", [128, 64], F32)
    e.logacc = sb("logacc", [128, 64], F32)
    e.wr_sb = sb("wr_sb", [8, 1024], F32)

    hb = {nm: e[nm].v for nm in
          ["sPE", "sACT", "sDVE", "sPOOL", "sXB", "sWO0", "sWO1", "sH2",
           "sXN0", "sXN1", "sSP"]}
    e.h2b = hb
    A = lambda ci, tg, k: hb["sDVE"] + (ci * 2 + tg) * 4 + k
    Bv = lambda ci, tg, k: hb["sDVE"] + 128 + ci * 7 + tg * 3 + k
    accL = lambda ci: hb["sDVE"] + 128 + ci * 7 + 7
    T = lambda ch, k: hb["sDVE"] + 240 + ch * 10 + k
    actmr = hb["sACT"] + 4
    actexp = lambda tg: hb["sACT"] + 6 + tg + 1
    actsig = lambda ch: hb["sACT"] + 8 + ch + 1
    actwr = hb["sACT"] + 18

    seng = nc.sync
    if "pid_s" not in e:
        e.pid_s = seng.partition_id()
    book = SwBook(e, ["sXB"])
    for c in nc.Switch(engines=[seng], index=[e.pid_s], n=8):
        book.enter_body()
        A0, A1 = own_cols(c)
        for nm in ["sDVE", "sPE", "sACT", "sPOOL"]:
            seng.wait_ge(e[nm].h, hb[nm])
        for half in range(2):
            for ib, a in enumerate((A0, A1)):
                I(e.sXB, seng.dma_start(
                    out=e.xbuf2[:, :].rearrange("p (ci t) -> p ci t", ci=DH)[
                        :, half * 8:(half + 1) * 8, ib * 512:(ib + 1) * 512],
                    in_=e.xT.rearrange("(ci dl) t -> dl ci t", dl=128)[
                        :, half * 8:(half + 1) * 8, a:a + 512]), 16)
        for r in range(2):
            for ib, a in enumerate((A0, A1)):
                I(e.sXB, seng.dma_start(
                    out=e.rows_sb[0:1, (r * 2 + ib) * 512:(r * 2 + ib + 1) * 512],
                    in_=e.d_rows[r:r + 1, a:a + 512]), 16)
    book.finish()
    sxb_full = hb["sXB"] + 128

    e.h2pe = {}

    with nc.Block() as blk:
        blk.tensor(lambda t: h2_tensor(e, t, hb, A, Bv, accL, T, actmr, actexp, sxb_full))
        blk.scalar(lambda s: h2_scalar(e, s, hb, T))
        blk.vector(lambda v: h2_vector(e, v, hb, A, Bv, T, actmr, actsig, sxb_full))
        blk.gpsimd(lambda g: h2_gpsimd(e, g, hb))
        blk.sync(lambda s: h2_sync(e, s, hb, A, Bv, actwr))

    if DEBUG:
        with nc.Block() as blk2:
            blk2.sync(lambda s: h2_debug_s(e, s))
            blk2.gpsimd(lambda g: h2_debug_g(e, g))
    h2ctx.close()


def h2_tensor(e, t, hb, A, Bv, accL, T, actmr, actexp, sxb_full):
    pe = e.h2pe
    t.wait_ge(e.sDVE.h, hb["sDVE"])
    t.wait_ge(e.sACT.h, hb["sACT"])
    t.wait_ge(e.sPOOL.h, hb["sPOOL"])
    t.wait_ge(e.sXB.h, sxb_full)
    for i in range(4):
        I(e.sPE, t.matmul(e.B[2 + i][:, :], e.ones1f[:, :],
                          e.rows_sb[0:1, i * 512:(i + 1) * 512],
                          start=True, stop=True))
        pe[("mr", i)] = e.sPE.v
    for ci in range(DH):
        for tg in range(2):
            j = ci * 2 + tg
            sw = e["sWO%d" % (ci % 2)]
            t.wait_ge(sw.h, hb["sWO%d" % (ci % 2)] + 32 * (ci // 2 + 1))
            if j >= 2:
                pc, pt = (j - 2) // 2, (j - 2) % 2
                t.wait_ge(e.sDVE.h, A(pc, pt, 3))
            for hq in range(DH):
                ins = t.matmul(
                    e.B[j % 2][:, :],
                    e.wo_sb[:, ((ci % 2) * DH + hq) * 128:((ci % 2) * DH + hq + 1) * 128],
                    e.av_sb[:, hq * 1024 + tg * 512:hq * 1024 + (tg + 1) * 512],
                    start=(hq == 0), stop=(hq == DH - 1))
            I(e.sPE, ins)
            pe[("wo", ci, tg)] = e.sPE.v
        if ci >= 1:
            for tg in range(2):
                t.wait_ge(e.sDVE.h, A(ci - 1, tg, 4))
                I(e.sPE, t.matmul(e.B[2 + tg][0:1, :], e.cst[:, 1:2],
                                  e.scr[2 + tg][:, :],
                                  start=(ci - 1 == 0), stop=False))
                pe[("v2", ci - 1, tg)] = e.sPE.v
    for tg in range(2):
        t.wait_ge(e.sDVE.h, A(15, tg, 4))
        I(e.sPE, t.matmul(e.B[2 + tg][0:1, :], e.cst[:, 1:2], e.scr[2 + tg][:, :],
                          start=False, stop=True))
        pe[("v2", 15, tg)] = e.sPE.v
    for tg in range(2):
        t.wait_ge(e.sACT.h, actexp(tg))
        t.wait_ge(e.sDVE.h, A(15, tg, 3))
        I(e.sPE, t.matmul(e.B[tg][:, :], e.ones1f[:, :],
                          e.rows_sb[0:1, (6 + tg) * 512:(7 + tg) * 512],
                          start=True, stop=True))
        pe[("rbc", tg)] = e.sPE.v
    t.wait_ge(e.sSP.h, hb["sSP"] + 16)
    for ci in range(DH):
        for ch in range(8):
            t.wait_ge(e.sDVE.h, Bv(ci, ch // 4, 2))
            if ci >= 1 and ch == 0:
                t.wait_ge(e.sDVE.h, accL(ci - 1))
            I(e.sPE, t.matmul(e.B[4][:, ch * 8:(ch + 1) * 8],
                              e.scr[ch // 4][:, (ch % 4) * 128:(ch % 4 + 1) * 128],
                              e.gw_sb[:, ci * 8:(ci + 1) * 8],
                              start=True, stop=True))
            pe[("gate", ci, ch)] = e.sPE.v
    t.wait_ge(e.sPOOL.h, hb["sPOOL"] + 2)
    for ch in range(8):
        t.wait_ge(e.sDVE.h, T(ch, 10))
        I(e.sPE, t.matmul(e.B[5 + ch // 4][0:8, (ch % 4) * 128:(ch % 4 + 1) * 128],
                          e.wch[:, ch * 8:(ch + 1) * 8], e.id128[:, :],
                          start=True, stop=True))
        pe[("tr", ch)] = e.sPE.v


def h2_scalar(e, s, hb, T):
    pe = e.h2pe
    for i in range(4):
        s.wait_ge(e.sPE.h, pe[("mr", i)])
        I(e.sACT, s.copy(out=e.mr_sb[:, i * 512:(i + 1) * 512], in_=e.B[2 + i][:, :]))
    for tg in range(2):
        s.wait_ge(e.sPE.h, pe[("v2", 15, tg)])
        I(e.sACT, s.activation(out=e.rows_sb[0:1, (4 + tg) * 512:(5 + tg) * 512],
                               in_=e.B[2 + tg][0:1, :], func=AF.Ln,
                               scale=1.0 / float(D), bias=EPS))
    for tg in range(2):
        W(s, e.sACT)
        I(e.sACT, s.activation(out=e.rows_sb[0:1, (6 + tg) * 512:(7 + tg) * 512],
                               in_=e.rows_sb[0:1, (4 + tg) * 512:(5 + tg) * 512],
                               func=AF.Exp, scale=-0.5))
    for ch in range(8):
        s.wait_ge(e.sDVE.h, T(ch, 5))
        I(e.sACT, s.activation(out=e.scr[4][:, 5:6], in_=e.scr[4][:, 4:5],
                               func=AF.Sigmoid))
    for i in range(2):
        s.wait_ge(e.sPE.h, pe[("tr", i * 4 + 3)])
        I(e.sACT, s.copy(out=e.wr_sb[:, i * 512:(i + 1) * 512],
                         in_=e.B[5 + i][0:8, :]))


def h2_vector(e, v, hb, A, Bv, T, actmr, actsig, sxb_full):
    pe = e.h2pe
    v.wait_ge(e.sXB.h, sxb_full)
    v.wait_ge(e.sACT.h, actmr)
    for ci in range(DH):
        for tg in range(2):
            xt = e.xbuf2[:, ci * 1024 + tg * 512:ci * 1024 + (tg + 1) * 512]
            W(v, e.sDVE)
            I(e.sDVE, v.tensor_sub(xt, xt, e.mr_sb[:, tg * 512:(tg + 1) * 512]))
            W(v, e.sDVE)
            I(e.sDVE, v.tensor_mul(xt, xt, e.mr_sb[:, (2 + tg) * 512:(3 + tg) * 512]))
            v.wait_ge(e.sPE.h, pe[("wo", ci, tg)])
            W(v, e.sDVE)
            I(e.sDVE, v.tensor_add(xt, xt, e.B[(ci * 2 + tg) % 2][:, :]))
            if ci >= 1:
                v.wait_ge(e.sPE.h, pe[("v2", ci - 1, tg)])
            W(v, e.sDVE)
            I(e.sDVE, v.tensor_mul(e.scr[2 + tg][:, :], xt, xt))
    for ci in range(DH):
        for tg in range(2):
            xt = e.xbuf2[:, ci * 1024 + tg * 512:ci * 1024 + (tg + 1) * 512]
            v.wait_ge(e.sPE.h, pe[("rbc", tg)])
            if ci >= 1:
                v.wait_ge(e.sPE.h, pe[("gate", ci - 1, 7)])
            W(v, e.sDVE)
            I(e.sDVE, v.tensor_mul(e.scr[tg][:, :], xt, e.B[tg][:, :]))
            W(v, e.sDVE)
            I(e.sDVE, v.tensor_scalar(out=e.scr[tg][:, :], in0=e.scr[tg][:, :],
                                      scalar1=e.n2w_c[:, ci:ci + 1], scalar2=None,
                                      op0=ALU.mult, op1=ALU.bypass, accum_out=None))
            if ci >= 2:
                sx = e["sXN%d" % (ci % 2)]
                v.wait_ge(sx.h, hb["sXN%d" % (ci % 2)] + 16 * (ci // 2))
            W(v, e.sDVE)
            I(e.sDVE, v.tensor_copy(out=e.xnt[ci % 2][:, tg * 512:(tg + 1) * 512],
                                    in_=e.scr[tg][:, :]))
        v.wait_ge(e.sPE.h, pe[("gate", ci, 7)])
        W(v, e.sDVE)
        if ci == 0:
            I(e.sDVE, v.tensor_copy(out=e.logacc[:, :], in_=e.B[4][:, 0:64]))
        else:
            I(e.sDVE, v.tensor_add(e.logacc[:, :], e.logacc[:, :], e.B[4][:, 0:64]))
    s4, s5 = e.scr[4], e.scr[5]
    for ch in range(8):
        bch = e.logacc[:, ch * 8:(ch + 1) * 8]
        W(v, e.sDVE)
        I(e.sDVE, v.reduce_max(s4[:, 0:1], bch, axis=AXX))
        W(v, e.sDVE)
        I(e.sDVE, v.tensor_scalar(out=s5[:, 0:8], in0=bch, scalar1=s4[:, 0:1],
                                  scalar2=None, op0=ALU.is_equal, op1=ALU.bypass,
                                  accum_out=None))
        W(v, e.sDVE)
        I(e.sDVE, v.scalar_tensor_tensor(out=s5[:, 8:16], in0=s5[:, 0:8],
                                         scalar=-1e30, in1=bch,
                                         op0=ALU.mult, op1=ALU.add))
        W(v, e.sDVE)
        I(e.sDVE, v.reduce_max(s4[:, 1:2], s5[:, 8:16], axis=AXX))
        W(v, e.sDVE)
        I(e.sDVE, v.tensor_sub(s4[:, 4:5], s4[:, 0:1], s4[:, 1:2]))
        v.wait_ge(e.sACT.h, actsig(ch))
        W(v, e.sDVE)
        I(e.sDVE, v.tensor_scalar(out=s4[:, 6:7], in0=s4[:, 5:6], scalar1=-1.0,
                                  scalar2=1.0, op0=ALU.mult, op1=ALU.add,
                                  accum_out=None))
        W(v, e.sDVE)
        I(e.sDVE, v.tensor_scalar(out=s5[:, 16:24], in0=s5[:, 0:8], scalar1=s4[:, 5:6],
                                  scalar2=None, op0=ALU.mult, op1=ALU.bypass,
                                  accum_out=None))
        W(v, e.sDVE)
        I(e.sDVE, v.tensor_scalar(out=s5[:, 24:32], in0=s5[:, 8:16], scalar1=s4[:, 1:2],
                                  scalar2=None, op0=ALU.is_equal, op1=ALU.bypass,
                                  accum_out=None))
        W(v, e.sDVE)
        I(e.sDVE, v.tensor_scalar(out=s5[:, 32:40], in0=s5[:, 24:32], scalar1=s4[:, 6:7],
                                  scalar2=None, op0=ALU.mult, op1=ALU.bypass,
                                  accum_out=None))
        W(v, e.sDVE)
        I(e.sDVE, v.tensor_add(e.wch[:, ch * 8:(ch + 1) * 8], s5[:, 16:24],
                               s5[:, 32:40]))


def h2_gpsimd(e, g, hb):
    pe = e.h2pe
    for nm in ["sDVE", "sPE", "sACT", "sPOOL"]:
        g.wait_ge(e[nm].h, hb[nm])
    I(e.sPOOL, g.memset(e.id128[:, :], 1.0))
    W(g, e.sPOOL)
    I(e.sPOOL, g.affine_select(out=e.id128[:, :], in_=e.id128[:, :],
                               pattern=[[-1, 128]], compare_op=ALU.is_equal,
                               fill=0.0, base=0, channel_multiplier=1))
    for ci in range(DH):
        if ci >= 2:
            g.wait_ge(e.sPE.h, pe[("wo", ci - 2, 1)])
        for half in range(2):
            I(e["sWO%d" % (ci % 2)], g.dma_start(
                out=e.wo_sb[:, ((ci % 2) * DH + half * 8) * 128:
                            ((ci % 2) * DH + (half + 1) * 8) * 128].rearrange(
                                "p (hq x) -> p hq x", hq=8),
                in_=e.woh[ci, :, half * 8:(half + 1) * 8, :]), 16)


def h2_sync(e, s, hb, A, Bv, actwr):
    I(e.sSP, s.dma_start(
        out=e.gw_sb[:, :].rearrange("p (ci x) -> p ci x", ci=DH),
        in_=e.gwt.rearrange("(ci dl) x -> dl ci x", dl=128)), 16)
    for ci in range(DH):
        s.wait_ge(e.sDVE.h, A(ci, 1, 3))
        I(e.sH2, s.dma_start(out=e.d_h2[ci * 128:(ci + 1) * 128, :],
                             in_=e.xbuf2[:, ci * 1024:(ci + 1) * 1024]), 16)
    for ci in range(DH):
        s.wait_ge(e.sDVE.h, Bv(ci, 1, 3))
        I(e["sXN%d" % (ci % 2)], s.dma_start(
            out=e.d_xn2[ci * 128:(ci + 1) * 128, :], in_=e.xnt[ci % 2][:, :]), 16)
    s.wait_ge(e.sACT.h, actwr)
    I(e.sH2, s.dma_start(out=e.d_w, in_=e.wr_sb[:, :]), 16)
    for c in [e.sH2, e.sXN0, e.sXN1]:
        s.wait_ge(c.h, c.v)


def h2_debug_s(e, s):
    I(e.sST, s.dma_start(out=e.dbg["dbg_h2"], in_=e.d_h2), 16)
    I(e.sST, s.dma_start(out=e.dbg["dbg_w"], in_=e.d_w), 16)
    s.wait_ge(e.sST.h, e.sST.v)


def h2_debug_g(e, g):
    I(e.sGLD, g.dma_start(out=e.dbg["dbg_xn2"], in_=e.d_xn2), 16)
    g.wait_ge(e.sGLD.h, e.sGLD.v)



def emit_moe(e):
    nc = e.nc
    e.moectx = ExitStack()
    e.moe_acc = e.moectx.enter_context(
        nc.sbuf_tensor("moe_acc", [128, DH * 1024], BF16))
    m2 = ExitStack()

    def sb(name, shape, dt):
        return m2.enter_context(nc.sbuf_tensor(name, shape, dt))

    e.xn2b = sb("xn2b", [128, DH * 1024], BF16)
    e.hf_sb = sb("hf_sb", [128, 32 * 1024], BF16)
    e.wf32 = [sb(f"wf32_{i}", [128, 4096], F32) for i in range(2)]
    e.wbf = [sb(f"wbf_{i}", [128, 4096], BF16) for i in range(2)]
    e.wbc = sb("wbc", [128, 1024], F32)
    e.silb = [sb(f"silb{i}", [128, 512], BF16) for i in range(2)]
    e.wrow = sb("wrow", [1, 1024], F32)

    mb = {nm: e[nm].v for nm in
          ["sPE", "sACT", "sDVE", "sPOOL", "sXB", "sMR", "sMW0", "sMW1"]}
    e.mb = mb
    iters = []
    for ee in range(NE):
        iters += [("u", ee, fc) for fc in range(32)] +                  [("d", ee, ci) for ci in range(16)]
    slot_of = {}
    arrive = {}
    cnt = [mb["sMW0"], mb["sMW1"]]
    for gi, it in enumerate(iters):
        s = gi % 2
        slot_of[gi] = s
        cnt[s] += 32 if it[0] == "u" else 16
        arrive[gi] = cnt[s]
    e.m_iters, e.m_slot, e.m_arrive = iters, slot_of, arrive

    pe_wbc = lambda ee, tg: mb["sPE"] + ee * 162 + tg + 1
    pe_u = lambda ee, fc, tg, w: mb["sPE"] + ee * 162 + 2 + fc * 4 + tg * 2 + w + 1
    pe_d = lambda ee, ci, tg: mb["sPE"] + ee * 162 + 130 + ci * 2 + tg + 1
    act_wbc = lambda ee, tg: mb["sACT"] + ee * 66 + tg + 1
    act_sil = lambda ee, fc, tg: mb["sACT"] + ee * 66 + 2 + fc * 2 + tg + 1
    dve_hf = lambda ee, fc, tg: mb["sDVE"] + ee * 128 + fc * 2 + tg + 1
    dve_dn = lambda ee, ci, tg, k: mb["sDVE"] + ee * 128 + 64 + ci * 4 + tg * 2 + k
    pool_cv = lambda ee, fc, w: mb["sPOOL"] + ee * 80 + fc * 2 + w + 1
    pool_cvd = lambda ee, ci: mb["sPOOL"] + ee * 80 + 64 + ci + 1

    def last_pe(gi):
        k, ee, i = iters[gi]
        return pe_u(ee, i, 1, 1) if k == "u" else pe_d(ee, i, 1)

    def last_pool(gi):
        k, ee, i = iters[gi]
        return pool_cv(ee, i, 1) if k == "u" else pool_cvd(ee, i)

    fns = dict(pe_wbc=pe_wbc, pe_u=pe_u, pe_d=pe_d, act_wbc=act_wbc,
               act_sil=act_sil, dve_hf=dve_hf, dve_dn=dve_dn, pool_cv=pool_cv,
               pool_cvd=pool_cvd, last_pe=last_pe, last_pool=last_pool)
    sxb_full = mb["sXB"] + 256

    with nc.Block() as blk:
        blk.tensor(lambda t: moe_tensor(e, t, mb, fns, sxb_full))
        blk.scalar(lambda s: moe_scalar(e, s, mb, fns))
        blk.vector(lambda v: moe_vector(e, v, mb, fns))
        blk.gpsimd(lambda g: moe_gpsimd(e, g, mb, fns))
        blk.sync(lambda s: moe_sync(e, s, mb, fns))

    if DEBUG:
        with nc.Block() as blk2:
            blk2.gpsimd(lambda g: moe_debug_g(e, g))
    m2.close()


def moe_tensor(e, t, mb, f, sxb_full):
    t.wait_ge(e.sDVE.h, mb["sDVE"])
    t.wait_ge(e.sACT.h, mb["sACT"])
    t.wait_ge(e.sPOOL.h, mb["sPOOL"])
    t.wait_ge(e.sXB.h, sxb_full)
    for ee in range(NE):
        t.wait_ge(e.sMR.h, mb["sMR"] + 16 * (ee + 1))
        if ee >= 1:
            t.wait_ge(e.sDVE.h, f["dve_dn"](ee - 1, 15, 1, 2))
        for tg in range(2):
            I(e.sPE, t.matmul(e.B[6 + tg][:, :], e.ones1f[:, :],
                              e.wrow[0:1, tg * 512:(tg + 1) * 512],
                              start=True, stop=True))
        for fc in range(32):
            gi = ee * 48 + fc
            sl = e.m_slot[gi]
            t.wait_ge(e["sMW%d" % sl].h, e.m_arrive[gi])
            t.wait_ge(e.sPOOL.h, f["pool_cv"](ee, fc, 1))
            for tg in range(2):
                for w in range(2):
                    ugi = ee * 32 + fc
                    if ugi >= 2:
                        pe_, pf = divmod(ugi - 2, 32)
                        if w == 0:
                            t.wait_ge(e.sACT.h, f["act_sil"](pe_, pf, tg))
                        else:
                            t.wait_ge(e.sDVE.h, f["dve_hf"](pe_, pf, tg))
                    if fc == 1 and w == 1:
                        t.wait_ge(e.sACT.h, f["act_wbc"](ee, tg))
                    bank = (fc % 2) * 4 + w * 2 + tg
                    for dh in range(DH):
                        ins = t.matmul(
                            e.B[bank][:, :],
                            e.wbf[sl][:, w * 2048 + dh * 128:w * 2048 + (dh + 1) * 128],
                            e.xn2b[:, dh * 1024 + tg * 512:dh * 1024 + (tg + 1) * 512],
                            start=(dh == 0), stop=(dh == DH - 1))
                    I(e.sPE, ins)
        for ci in range(16):
            gi = ee * 48 + 32 + ci
            sl = e.m_slot[gi]
            t.wait_ge(e["sMW%d" % sl].h, e.m_arrive[gi])
            t.wait_ge(e.sPOOL.h, f["pool_cvd"](ee, ci))
            if ci == 0:
                t.wait_ge(e.sDVE.h, f["dve_hf"](ee, 31, 1))
            for tg in range(2):
                bank = 4 + (ci % 2) * 2 + tg
                if ci >= 2:
                    t.wait_ge(e.sDVE.h, f["dve_dn"](ee, ci - 2, tg, 1))
                else:
                    t.wait_ge(e.sACT.h, f["act_sil"](ee, 30 + ci, tg))
                    t.wait_ge(e.sDVE.h, f["dve_hf"](ee, 30 + ci, tg))
                for kc in range(32):
                    ins = t.matmul(
                        e.B[bank][:, :],
                        e.wbf[sl][:, kc * 128:(kc + 1) * 128],
                        e.hf_sb[:, kc * 1024 + tg * 512:kc * 1024 + (tg + 1) * 512],
                        start=(kc == 0), stop=(kc == 31))
                I(e.sPE, ins)


def moe_scalar(e, s, mb, f):
    for ee in range(NE):
        for tg in range(2):
            s.wait_ge(e.sPE.h, f["pe_wbc"](ee, tg))
            if ee >= 1 and tg == 0:
                s.wait_ge(e.sDVE.h, f["dve_dn"](ee - 1, 15, 1, 2))
            I(e.sACT, s.copy(out=e.wbc[:, tg * 512:(tg + 1) * 512],
                             in_=e.B[6 + tg][:, :]))
        for fc in range(32):
            for tg in range(2):
                s.wait_ge(e.sPE.h, f["pe_u"](ee, fc, tg, 0))
                idx = fc * 2 + tg
                if idx >= 2:
                    s.wait_ge(e.sDVE.h, f["dve_hf"](ee, fc - 1, tg))
                I(e.sACT, s.activation(out=e.silb[idx % 2][:, :],
                                       in_=e.B[(fc % 2) * 4 + tg][:, :],
                                       func=AF.Silu))


def moe_vector(e, v, mb, f):
    for ee in range(NE):
        for fc in range(32):
            for tg in range(2):
                v.wait_ge(e.sACT.h, f["act_sil"](ee, fc, tg))
                v.wait_ge(e.sPE.h, f["pe_u"](ee, fc, tg, 1))
                if ee >= 1 and fc == 0 and tg == 0:
                    v.wait_ge(e.sPE.h, f["pe_d"](ee - 1, 15, 1))
                W(v, e.sDVE)
                I(e.sDVE, v.tensor_mul(
                    e.hf_sb[:, fc * 1024 + tg * 512:fc * 1024 + (tg + 1) * 512],
                    e.silb[(fc * 2 + tg) % 2][:, :],
                    e.B[(fc % 2) * 4 + 2 + tg][:, :]))
        for ci in range(16):
            for tg in range(2):
                v.wait_ge(e.sPE.h, f["pe_d"](ee, ci, tg))
                W(v, e.sDVE)
                I(e.sDVE, v.tensor_mul(e.scr[tg][:, :],
                                       e.B[4 + (ci % 2) * 2 + tg][:, :],
                                       e.wbc[:, tg * 512:(tg + 1) * 512]))
                at = e.moe_acc[:, ci * 1024 + tg * 512:ci * 1024 + (tg + 1) * 512]
                W(v, e.sDVE)
                if ee == 0:
                    I(e.sDVE, v.tensor_copy(out=at, in_=e.scr[tg][:, :]))
                else:
                    I(e.sDVE, v.tensor_add(at, at, e.scr[tg][:, :]))


def moe_gpsimd(e, g, mb, f):
    for nm in ["sDVE", "sPE", "sACT", "sPOOL"]:
        g.wait_ge(e[nm].h, mb[nm])
    for gi, (k, ee, i) in enumerate(e.m_iters):
        sl = e.m_slot[gi]
        g.wait_ge(e["sMW%d" % sl].h, e.m_arrive[gi])
        if gi >= 2:
            g.wait_ge(e.sPE.h, f["last_pe"](gi - 2))
        if k == "u":
            for w in range(2):
                I(e.sPOOL, g.tensor_copy(
                    out=e.wbf[sl][:, w * 2048:(w + 1) * 2048],
                    in_=e.wf32[sl][:, w * 2048:(w + 1) * 2048]))
        else:
            I(e.sPOOL, g.tensor_copy(out=e.wbf[sl][:, :], in_=e.wf32[sl][:, :]))


def moe_sync(e, s, mb, f):
    for nm in ["sDVE", "sPE", "sACT", "sPOOL"]:
        s.wait_ge(e[nm].h, mb[nm])
    for half in range(2):
        for q in range(8):
            ci = half * 8 + q
            I(e.sXB, s.dma_start(out=e.xn2b[:, ci * 1024:(ci + 1) * 1024],
                                 in_=e.d_xn2[ci * 128:(ci + 1) * 128, :]), 16)
    for gi, (k, ee, i) in enumerate(e.m_iters):
        if k == "u" and i == 0:
            if ee >= 1:
                s.wait_ge(e.sPE.h, f["pe_wbc"](ee - 1, 1))
            I(e.sMR, s.dma_start(out=e.wrow[:, :], in_=e.d_w[ee:ee + 1, :]), 16)
        sl = e.m_slot[gi]
        if gi >= 2:
            s.wait_ge(e.sPOOL.h, f["last_pool"](gi - 2))
        if k == "u":
            I(e["sMW%d" % sl], s.dma_start(
                out=e.wf32[sl][:, 0:2048].rearrange("p (dh x) -> p dh x", dh=DH),
                in_=e.w1h[ee, :, :, i * 128:(i + 1) * 128]), 16)
            I(e["sMW%d" % sl], s.dma_start(
                out=e.wf32[sl][:, 2048:4096].rearrange("p (dh x) -> p dh x", dh=DH),
                in_=e.w3h[ee, :, :, i * 128:(i + 1) * 128]), 16)
        else:
            I(e["sMW%d" % sl], s.dma_start(
                out=e.wf32[sl][:, :].rearrange("p (kc x) -> p kc x", kc=32),
                in_=e.w2h[ee, :, :, i * 128:(i + 1) * 128]), 16)


def moe_debug_g(e, g):
    g.wait_ge(e.sDVE.h, e.sDVE.v)
    for half in range(2):
        I(e.sGLD, g.dma_start(
            out=e.dbg["dbg_moe"].rearrange("(ci dl) t -> dl ci t", dl=128)[
                :, half * 8:(half + 1) * 8, :],
            in_=e.moe_acc[:, half * 8 * 1024:(half + 1) * 8 * 1024].rearrange(
                "p (ci t) -> p ci t", ci=8)), 16)
    g.wait_ge(e.sGLD.h, e.sGLD.v)



def emit_final(e):
    nc = e.nc
    fctx = ExitStack()

    def sb(name, shape, dt):
        return fctx.enter_context(nc.sbuf_tensor(name, shape, dt))

    e.xbuf3 = sb("xbuf3", [128, DH * 1024], F32)
    e.rows3 = sb("rows3", [1, 4 * 512], F32)
    e.mr3 = sb("mr3", [128, 4 * 512], F32)
    e.ybuf3 = [sb(f"ybuf3_{i}", [128, 1024], F32) for i in range(2)]
    e.hbuf3 = [sb(f"hbuf3_{i}", [128, 1024], F32) for i in range(2)]
    e.obuf = [sb(f"obuf{i}", [128, 1024], F32) for i in range(2)]
    e.gsb = [sb(f"gsb{i}", [128, 512], F32) for i in range(2)]
    e.gob = [sb(f"gob{i}", [128, 512], F32) for i in range(2)]

    fb = {nm: e[nm].v for nm in
          ["sPE", "sACT", "sDVE", "sPOOL", "sXB", "sFT0", "sFT1", "sOU0", "sOU1"]}
    dve_F = lambda ci, tg, k: fb["sDVE"] + (ci * 2 + tg) * 7 + k
    act_F = lambda ci, tg: fb["sACT"] + 4 + ci * 2 + tg + 1
    pe_F = lambda i: fb["sPE"] + i + 1

    seng = nc.sync
    book = SwBook(e, ["sXB"])
    for c in nc.Switch(engines=[seng], index=[e.pid_s], n=8):
        book.enter_body()
        A0, A1 = own_cols(c)
        for nm in ["sDVE", "sPE", "sACT", "sPOOL"]:
            seng.wait_ge(e[nm].h, fb[nm])
        seng.wait_ge(e.sAG.h, 1)
        I(e.sXB, seng.dma_start(out=e.d_y[:, 0:512],
                                in_=e.d_ygat[:, A0:A0 + 512]), 16)
        I(e.sXB, seng.dma_start(out=e.d_y[:, 512:1024],
                                in_=e.d_ygat[:, A1:A1 + 512]), 16)
        for half in range(2):
            for ib, a in enumerate((A0, A1)):
                I(e.sXB, seng.dma_start(
                    out=e.xbuf3[:, :].rearrange("p (ci t) -> p ci t", ci=DH)[
                        :, half * 8:(half + 1) * 8, ib * 512:(ib + 1) * 512],
                    in_=e.xT.rearrange("(ci dl) t -> dl ci t", dl=128)[
                        :, half * 8:(half + 1) * 8, a:a + 512]), 16)
        for r in range(2):
            for ib, a in enumerate((A0, A1)):
                I(e.sXB, seng.dma_start(
                    out=e.rows3[0:1, (r * 2 + ib) * 512:(r * 2 + ib + 1) * 512],
                    in_=e.d_rows[r:r + 1, a:a + 512]), 16)
    book.finish()
    sxb_full = fb["sXB"] + 160

    with nc.Block() as blk:
        blk.tensor(lambda t: fin_tensor(e, t, fb, sxb_full))
        blk.scalar(lambda s: fin_scalar(e, s, fb, dve_F, pe_F))
        blk.vector(lambda v: fin_vector(e, v, fb, dve_F, act_F, sxb_full))
        blk.sync(lambda s: fin_sync(e, s, fb, dve_F, sxb_full))
    fctx.close()


def fin_tensor(e, t, fb, sxb_full):
    t.wait_ge(e.sDVE.h, fb["sDVE"])
    t.wait_ge(e.sACT.h, fb["sACT"])
    t.wait_ge(e.sPOOL.h, fb["sPOOL"])
    t.wait_ge(e.sXB.h, sxb_full)
    for i in range(4):
        I(e.sPE, t.matmul(e.B[2 + i][:, :], e.ones1f[:, :],
                          e.rows3[0:1, i * 512:(i + 1) * 512],
                          start=True, stop=True))


def fin_scalar(e, s, fb, dve_F, pe_F):
    for i in range(4):
        s.wait_ge(e.sPE.h, pe_F(i))
        I(e.sACT, s.copy(out=e.mr3[:, i * 512:(i + 1) * 512], in_=e.B[2 + i][:, :]))
    for ci in range(DH):
        for tg in range(2):
            s.wait_ge(e.sDVE.h, dve_F(ci, tg, 5))
            if ci >= 1:
                s.wait_ge(e.sDVE.h, dve_F(ci - 1, tg, 6))
            I(e.sACT, s.activation(out=e.gob[tg][:, :], in_=e.gsb[tg][:, :],
                                   func=AF.Silu))


def fin_vector(e, v, fb, dve_F, act_F, sxb_full):
    v.wait_ge(e.sXB.h, sxb_full)
    v.wait_ge(e.sACT.h, fb["sACT"] + 4)
    for ci in range(DH):
        sl = ci % 2
        v.wait_ge(e["sFT%d" % sl].h, fb["sFT%d" % sl] + 32 * (ci // 2 + 1))
        for tg in range(2):
            xt = e.xbuf3[:, ci * 1024 + tg * 512:ci * 1024 + (tg + 1) * 512]
            sc = e.scr[2 + tg]
            W(v, e.sDVE)
            I(e.sDVE, v.tensor_sub(sc[:, :], xt, e.mr3[:, tg * 512:(tg + 1) * 512]))
            W(v, e.sDVE)
            I(e.sDVE, v.tensor_mul(sc[:, :], sc[:, :],
                                   e.mr3[:, (2 + tg) * 512:(3 + tg) * 512]))
            W(v, e.sDVE)
            I(e.sDVE, v.tensor_scalar(out=sc[:, :], in0=sc[:, :],
                                      scalar1=e.D_c[:, ci:ci + 1], scalar2=None,
                                      op0=ALU.mult, op1=ALU.bypass, accum_out=None))
            W(v, e.sDVE)
            I(e.sDVE, v.tensor_add(sc[:, :], sc[:, :],
                                   e.ybuf3[sl][:, tg * 512:(tg + 1) * 512]))
            W(v, e.sDVE)
            I(e.sDVE, v.tensor_add(e.gsb[tg][:, :],
                                   e.hbuf3[sl][:, tg * 512:(tg + 1) * 512],
                                   e.moe_acc[:, ci * 1024 + tg * 512:
                                             ci * 1024 + (tg + 1) * 512]))
            v.wait_ge(e.sACT.h, act_F(ci, tg))
            W(v, e.sDVE)
            I(e.sDVE, v.tensor_mul(sc[:, :], sc[:, :], e.gob[tg][:, :]))
            if ci >= 2 and tg == 0:
                v.wait_ge(e["sOU%d" % sl].h, fb["sOU%d" % sl] + 16 * (ci // 2))
            W(v, e.sDVE)
            I(e.sDVE, v.tensor_add(e.obuf[sl][:, tg * 512:(tg + 1) * 512], xt,
                                   sc[:, :]))


def fin_sync(e, s, fb, dve_F, sxb_full):
    s.wait_ge(e.sXB.h, sxb_full)
    def out_dma(ci):
        s.wait_ge(e.sDVE.h, dve_F(ci, 1, 7))
        I(e["sOU%d" % (ci % 2)], s.dma_start(
            out=e.out[ci * 128:(ci + 1) * 128, :], in_=e.obuf[ci % 2][:, :]), 16)

    for ci in range(DH):
        sl = ci % 2
        if ci >= 2:
            s.wait_ge(e.sDVE.h, dve_F(ci - 2, 1, 5))
        I(e["sFT%d" % sl], s.dma_start(out=e.ybuf3[sl][:, :],
                                       in_=e.d_y[ci * 128:(ci + 1) * 128, :]), 16)
        I(e["sFT%d" % sl], s.dma_start(out=e.hbuf3[sl][:, :],
                                       in_=e.d_h2[ci * 128:(ci + 1) * 128, :]), 16)
        if ci >= 1:
            out_dma(ci - 1)
    out_dma(15)
    for nm in ["sOU0", "sOU1", "sFT0", "sFT1"]:
        s.wait_ge(e[nm].h, e[nm].v)


def prep_inputs(inputs):
    x = np.asarray(inputs["L0_attn_x"])
    fp = {}
    fp["xT"] = np.ascontiguousarray(np.concatenate([x[0], x[1]], axis=0).T)
    fp["wxpT"] = np.ascontiguousarray(np.asarray(inputs["L0_attn_W_xproj"]).T)
    fp["wdtT"] = np.ascontiguousarray(np.asarray(inputs["L0_attn_W_dt"]).T)
    fp["bdt_r"] = np.ascontiguousarray(np.asarray(inputs["L0_attn_b_dt"]).reshape(DH, 128))
    fp["Alog"] = np.ascontiguousarray(np.asarray(inputs["L0_attn_A_log"]))
    fp["D_r"] = np.ascontiguousarray(np.asarray(inputs["L0_attn_D_param"]).reshape(DH, 128))
    fp["n1w_r"] = np.ascontiguousarray(np.asarray(inputs["L0_ffn_norm1_w"]).reshape(DH, 128))
    fp["n2w_r"] = np.ascontiguousarray(np.asarray(inputs["L0_ffn_norm2_w"]).reshape(DH, 128))
    wq = np.asarray(inputs["L0_ffn_wq"])
    fp["wqh"] = np.ascontiguousarray(wq.reshape(DH, 128, NH, 128).transpose(2, 1, 0, 3))
    wk = np.asarray(inputs["L0_ffn_wk"])
    fp["wkh"] = np.ascontiguousarray(wk.reshape(DH, 128, KVH, 128).transpose(2, 1, 0, 3))
    fp["wv"] = np.ascontiguousarray(np.asarray(inputs["L0_ffn_wv"]))
    wo = np.asarray(inputs["L0_ffn_wo"])
    fp["woh"] = np.ascontiguousarray(wo.reshape(DH, 128, DH, 128).transpose(2, 1, 0, 3))
    fp["gwt"] = np.ascontiguousarray(np.asarray(inputs["L0_ffn_gate_w"]))
    w1 = np.asarray(inputs["L0_ffn_w1"])
    fp["w1h"] = np.ascontiguousarray(w1.reshape(NE, DH, 128, FF).transpose(0, 2, 1, 3))
    w3 = np.asarray(inputs["L0_ffn_w3"])
    fp["w3h"] = np.ascontiguousarray(w3.reshape(NE, DH, 128, FF).transpose(0, 2, 1, 3))
    w2 = np.asarray(inputs["L0_ffn_w2"])
    fp["w2h"] = np.ascontiguousarray(w2.reshape(NE, 32, 128, D).transpose(0, 2, 1, 3))
    return fp


def kernel(**inputs):
    fp = prep_inputs(inputs)
    nc, e = build_nc()
    in_maps = [dict(fp) for c in range(8)]
    res = run_bass_kernel_spmd(nc, in_maps, core_ids=list(range(8)))
    outf = np.zeros((2, L, D), np.float32)
    for c in range(8):
        oc = res.results[c]["out"]
        outf[0, 512 * c:512 * (c + 1), :] = oc[:, 0:512].T
        outf[1, 512 * (7 - c):512 * (8 - c), :] = oc[:, 512:1024].T
    kernel.last_res = res
    return outf


# version 04
# speedup vs baseline: 1.9137x; 1.0105x over previous
import os
import math
import numpy as np
import concourse.bass as bass
import concourse.mybir as mybir
from concourse.bass_utils import run_bass_kernel_spmd
from contextlib import ExitStack

F32 = mybir.dt.float32
BF16 = mybir.dt.bfloat16
I32 = mybir.dt.int32
AF = mybir.ActivationFunctionType
ALU = mybir.AluOpType
AXX = mybir.AxisListType.X

D = 2048
DH = 16
TOK = 8192
L = 4096
G = 16
GW = 512
NH = 16
KVH = 4
NST = 16
R = 64
NE = 8
FF = 4096
EPS = 1e-5
PI = 3.14159265358979
LOG_THETA = math.log(1.0e6)
INV2PI = 1.0 / (2.0 * PI)
TWOPI = 2.0 * PI

DEBUG = os.environ.get("KDBG", "0") == "1"
NPHASE = int(os.environ.get("KPHASES", "9"))

PE_G = 78
ACT_G = 77
DVE_G = 130
POOL_G = 52
SP_G = 45
SX_G = 16

PRO_POOL = 19
PRO_GLD = 13 * 16
PRO_SP = 5 * 16
PRO_PE = 36
PRO_ACT = 38
PRO_DVE = 245


def pe_at(g, off):
    return PRO_PE + g * PE_G + off


def act_at(g, off):
    return PRO_ACT + g * ACT_G + off


def dve_at(g, off):
    return PRO_DVE + g * DVE_G + off


def pool_at(g, off):
    return PRO_POOL + g * POOL_G + off


def sx_at(g, off):
    return 16 * (g * SX_G + off)


def sp_at(g, off):
    return PRO_SP + 16 * (g * SP_G + off)


class Ctr:
    def __init__(self, nc, stack, name):
        self.h = stack.enter_context(nc.semaphore(name))
        self.v = 0
        self.name = name

    def inc(self, instr, n=1):
        instr.then_inc(self.h, n)
        self.v += n
        return self.v


class Env(dict):
    __getattr__ = dict.__getitem__
    __setattr__ = dict.__setitem__


def I(ctr, instr, n=1):
    instr.then_inc(ctr.h, n)
    ctr.v += n
    return instr


def W(eng, ctr, v=None):
    eng.wait_ge(ctr.h, ctr.v if v is None else v)



def build_nc():
    nc = bass.Bass("TRN2", target_bir_lowering=False)
    e = Env()
    e.nc = nc

    e.xT = nc.dram_tensor("xT", [D, TOK], F32, kind="ExternalInput").ap()
    e.wxpT = nc.dram_tensor("wxpT", [D, 96], F32, kind="ExternalInput").ap()
    e.wdtT = nc.dram_tensor("wdtT", [R, D], F32, kind="ExternalInput").ap()
    e.bdt_r = nc.dram_tensor("bdt_r", [DH, 128], F32, kind="ExternalInput").ap()
    e.Alog = nc.dram_tensor("Alog", [D, NST], F32, kind="ExternalInput").ap()
    e.D_r = nc.dram_tensor("D_r", [DH, 128], F32, kind="ExternalInput").ap()
    e.n1w_r = nc.dram_tensor("n1w_r", [DH, 128], F32, kind="ExternalInput").ap()
    e.n2w_r = nc.dram_tensor("n2w_r", [DH, 128], F32, kind="ExternalInput").ap()
    e.wqh = nc.dram_tensor("wqh", [NH, 128, DH, 128], F32, kind="ExternalInput").ap()
    e.wkh = nc.dram_tensor("wkh", [KVH, 128, DH, 128], F32, kind="ExternalInput").ap()
    e.wv = nc.dram_tensor("wv", [D, 512], F32, kind="ExternalInput").ap()
    e.woh = nc.dram_tensor("woh", [DH, 128, DH, 128], F32, kind="ExternalInput").ap()
    e.gwt = nc.dram_tensor("gwt", [D, NE], F32, kind="ExternalInput").ap()
    e.w1h = nc.dram_tensor("w1h", [NE, 128, DH, FF], F32, kind="ExternalInput").ap()
    e.w3h = nc.dram_tensor("w3h", [NE, 128, DH, FF], F32, kind="ExternalInput").ap()
    e.w2h = nc.dram_tensor("w2h", [NE, 128, 32, D], F32, kind="ExternalInput").ap()

    e.d_delta = nc.dram_tensor("d_delta", [D, TOK], F32, kind="Internal").ap()
    e.d_u = nc.dram_tensor("d_u", [D, TOK], F32, kind="Internal").ap()
    e.d_xn1 = nc.dram_tensor("d_xn1", [D, TOK], BF16, kind="Internal").ap()
    e.d_k = nc.dram_tensor("d_k", [KVH, 128, TOK], BF16, kind="Internal").ap()
    e.d_v = nc.dram_tensor("d_v", [TOK, 512], BF16, kind="Internal").ap()
    e.d_bc = nc.dram_tensor("d_bc", [32, TOK], F32, kind="Internal").ap()
    e.d_rows = nc.dram_tensor("d_rows", [2, TOK], F32, kind="Internal").ap()
    e.d_h2 = nc.dram_tensor("d_h2", [D, 1024], F32, kind="Internal").ap()
    e.d_y = nc.dram_tensor("d_y", [D, 1024], F32, kind="Internal").ap()
    e.d_moe = nc.dram_tensor("d_moe", [D, 1024], F32, kind="Internal").ap()
    e.d_ysrc = nc.dram_tensor("d_ysrc", [256, TOK], F32, kind="Internal").ap()
    e.d_ygat = nc.dram_tensor("d_ygat", [D, TOK], F32, kind="Internal").ap()
    e.d_xn2 = nc.dram_tensor("d_xn2", [D, 1024], BF16, kind="Internal").ap()
    e.d_w = nc.dram_tensor("d_w", [NE, 1024], F32, kind="Internal").ap()
    e.out = nc.dram_tensor("out", [D, 1024], F32, kind="ExternalOutput").ap()

    e.dbg = {}
    if DEBUG:
        for nm, shape in [
            ("dbg_delta", [D, 1024]), ("dbg_u", [D, 1024]), ("dbg_bc", [32, 1024]),
            ("dbg_xn1", [D, 1024]), ("dbg_k", [KVH * 128, 1024]), ("dbg_v", [1024, 512]),
            ("dbg_rows", [2, TOK]), ("dbg_cos", [64, 1024]), ("dbg_sin", [64, 1024]),
            ("dbg_y", [D, 1024]), ("dbg_hloc", [D, NST]), ("dbg_hinit", [D, 2 * NST]),
            ("dbg_av", [D, 1024]), ("dbg_q", [D, 1024]),
            ("dbg_h2", [D, 1024]), ("dbg_xn2", [D, 1024]), ("dbg_w", [NE, 1024]),
            ("dbg_moe", [D, 1024]),
        ]:
            e.dbg[nm] = nc.dram_tensor(nm, shape, F32, kind="ExternalOutput").ap()

    with ExitStack() as ctx:
        def sb(name, shape, dt):
            return ctx.enter_context(nc.sbuf_tensor(name, shape, dt))
        e.sb = sb

        e.ones_bf = sb("ones_bf", [128, 1], BF16)
        e.ones1f = sb("ones1f", [1, 128], F32)
        e.ones16 = sb("ones16", [16, 16], F32)
        e.id16 = sb("id16", [16, 16], F32)
        e.Apos = sb("Apos", [128, DH * NST], F32)
        e.bdt_c = sb("bdt_c", [128, DH], F32)
        e.D_c = sb("D_c", [128, DH], F32)
        e.n1w_c = sb("n1w_c", [128, DH], F32)
        e.n2w_c = sb("n2w_c", [128, DH], F32)
        e.invf = sb("invf", [1, 64], F32)
        e.iot = sb("iot", [1, 1024], F32)
        e.kmax2 = sb("kmax2", [1, 8], F32)
        e.cst = sb("cst", [128, 2], F32)
        nc.const_aps.register(F32, EPS, e.cst[:, 0:1])
        nc.const_aps.register(F32, 1.0, e.cst[:, 1:2])
        e.scr = [sb(f"scr{i}", [128, GW], F32) for i in range(6)]
        e.iscr = sb("iscr", [128, GW], I32)

        e.ropectx = ExitStack()
        e.cosr = e.ropectx.enter_context(nc.sbuf_tensor("cosr", [64, TOK], BF16))
        e.sinr = e.ropectx.enter_context(nc.sbuf_tensor("sinr", [64, TOK], BF16))

        p0ctx = ExitStack()

        e.p0ctx = p0ctx

        def sb0(name, shape, dt):
            return p0ctx.enter_context(nc.sbuf_tensor(name, shape, dt))
        e.xbuf = sb0("xbuf", [128, DH * GW], F32)
        e.xnb = sb0("xnb", [128, DH * GW], BF16)
        e.xn1b = sb0("xn1b", [128, DH * GW], BF16)
        e.sqb = [sb0(f"sqb{i}", [128, GW], BF16) for i in range(2)]
        e.draw_b = sb0("draw_b", [64, GW], BF16)
        e.wxp_sb = sb0("wxp_sb", [128, DH * 96], BF16)
        e.wdt_sb = sb0("wdt_sb", [64, D], BF16)
        e.wk_sb = sb0("wk_sb", [128, KVH * DH * 128], BF16)
        e.wv_sb = sb0("wv_sb", [128, DH * 512], BF16)
        e.kvb = [sb0(f"kvb{i}", [128, GW], BF16) for i in range(2)]
        e.vb = [sb0(f"vb{i}", [128, GW], BF16) for i in range(2)]
        e.dsc = [sb0(f"dsc{i}", [128, GW], F32) for i in range(2)]
        e.usc = [sb0(f"usc{i}", [128, GW], F32) for i in range(2)]
        e.rowsB = sb0("rowsB", [1, 4 * GW], F32)
        e.mrow = e.rowsB[0:1, 0:GW]
        e.mrow2 = e.rowsB[0:1, GW:2 * GW]
        e.rrow = e.rowsB[0:1, 2 * GW:3 * GW]
        e.r1row = e.rowsB[0:1, 3 * GW:4 * GW]

        e.B = [ctx.enter_context(nc.psum_tensor(f"bank{i}", [128, 512], F32))
               for i in range(8)]

        for nm in ["sX", "sSP", "sPE", "sACT", "sDVE", "sPOOL", "sGLD", "sST",
                   "sX1", "sBC", "sRW", "sMS", "sQS", "sKV", "sWQ0", "sWQ1",
                   "sXB", "sWO0", "sWO1", "sH2", "sXN0", "sXN1",
                   "sMR", "sMW0", "sMW1", "sFT0", "sFT1", "sOU0", "sOU1",
                   "sBB", "sCB", "sAG"]:
            e[nm] = Ctr(nc, ctx, nm)
        e.sDS = [Ctr(nc, ctx, f"sDS{i}") for i in range(2)]
        e.sUS = [Ctr(nc, ctx, f"sUS{i}") for i in range(2)]
        e.sKB = [Ctr(nc, ctx, f"sKB{i}") for i in range(2)]
        e.sVB = [Ctr(nc, ctx, f"sVB{i}") for i in range(2)]

        emit_program(e)
        verify_sem_totals(e)

    return nc, e


def verify_sem_totals(e):
    import collections
    nc = e.nc
    tot = collections.Counter()
    for ins in nc.all_instructions():
        si = ins.sync_info
        if si is None:
            continue
        for u in si.on_update:
            if u.update_mode in ("sem-inc", "sem-add-imm"):
                tot[u.ant_name] += u.update_value
    names = ["sX", "sSP", "sPE", "sACT", "sDVE", "sPOOL", "sGLD", "sST", "sX1", "sBC", "sRW",
             "sQS", "sKV", "sWQ0", "sWQ1",
             "sXB", "sWO0", "sWO1", "sH2", "sXN0", "sXN1",
             "sMR", "sMW0", "sMW1", "sFT0", "sFT1", "sOU0", "sOU1",
             "sBB", "sCB", "sAG",
             "sDS0", "sDS1", "sUS0", "sUS1", "sKB0", "sKB1", "sVB0", "sVB1"]
    objs = {n: e[n] for n in names[:31]}
    for fam, lst in [("sDS", e.sDS), ("sUS", e.sUS), ("sKB", e.sKB), ("sVB", e.sVB)]:
        for i, c in enumerate(lst):
            objs[f"{fam}{i}"] = c
    excess = e.get("sw_excess", {})
    for name in names:
        want = objs[name].v + excess.get(name, 0)
        got = tot.get(name, 0)
        if got != want:
            raise AssertionError(f"sem {name}: emitted {got} != bookkept {want}")


def emit_program(e):
    nc = e.nc
    with nc.Block() as blk:
        blk.gpsimd(lambda eng: p0_gpsimd(e, eng))
        blk.scalar(lambda eng: p0_scalar(e, eng))
        blk.tensor(lambda eng: p0_tensor(e, eng))
        blk.vector(lambda eng: p0_vector(e, eng))
        blk.sync(lambda eng: p0_sync(e, eng))
    e.p0ctx.close()

    if NPHASE >= 3:
        emit_attention(e)
        if NPHASE >= 4:
            emit_h2(e)
        e.avctx.close()
        if DEBUG:
            with nc.Block() as blkr:
                blkr.gpsimd(lambda g: rope_debug(e, g))
        e.ropectx.close()
        if NPHASE >= 2:
            emit_mamba(e)
        if NPHASE >= 5:
            emit_moe(e)
            if NPHASE >= 6:
                emit_final(e)
            e.moectx.close()

    if DEBUG:
        with nc.Block() as blk2:
            blk2.gpsimd(lambda eng: p0_debug_g(e, eng))
            blk2.sync(lambda eng: p0_debug_s(e, eng))
    elif NPHASE < 6:
        with nc.Block() as blk3:
            blk3.sync(lambda eng: stub_out(e, eng))


def stub_out(e, s):
    s.dma_start(out=e.out[0:128, 0:512], in_=e.scr[0][:, 0:512]).then_inc(e.sST.h, 16)
    e.sST.v += 16
    s.wait_ge(e.sST.h, e.sST.v)



def p0_gpsimd(e, g):
    I(e.sPOOL, g.iota(e.invf[:, :], pattern=[[1, 64]], base=0, channel_multiplier=0,
                      allow_small_or_imprecise_dtypes=True))
    I(e.sPOOL, g.memset(e.ones16[:, :], 1.0))
    W(g, e.sPOOL)
    I(e.sPOOL, g.affine_select(out=e.id16[:, :], in_=e.ones16[:, :], pattern=[[-1, 16]],
                               compare_op=ALU.is_equal, fill=0.0, base=0,
                               channel_multiplier=1))
    for ch in range(16):
        if ch >= 2:
            g.wait_ge(e.sPE.h, 4 + 2 * (ch - 1))
        I(e.sPOOL, g.iota(e.iot[:, (ch % 2) * 512:(ch % 2) * 512 + 512], pattern=[[1, 512]],
                          base=(ch % 8) * 512, channel_multiplier=0,
                          allow_small_or_imprecise_dtypes=True))
    assert e.sPOOL.v == PRO_POOL, e.sPOOL.v
    for half in range(2):
        I(e.sGLD, g.dma_start(out=e.wxp_sb[:, half * 8 * 96:(half + 1) * 8 * 96].rearrange(
                        "p (dh r) -> p dh r", dh=8),
                    in_=e.wxpT.rearrange("(dh dl) r -> dl dh r", dl=128)[:, half * 8:(half + 1) * 8, :]), 16)
    I(e.sGLD, g.dma_start(out=e.wdt_sb[:, :], in_=e.wdtT), 16)
    for kvh in range(KVH):
        for half in range(2):
            I(e.sGLD, g.dma_start(
                out=e.wk_sb[:, (kvh * DH + half * 8) * 128:(kvh * DH + (half + 1) * 8) * 128
                            ].rearrange("p (dh x) -> p dh x", dh=8),
                in_=e.wkh[kvh, :, half * 8:(half + 1) * 8, :]), 16)
    for half in range(2):
        I(e.sGLD, g.dma_start(out=e.wv_sb[:, half * 8 * 512:(half + 1) * 8 * 512].rearrange(
                        "p (dh x) -> p dh x", dh=8),
                    in_=e.wv.rearrange("(dh dl) x -> dl dh x", dl=128)[:, half * 8:(half + 1) * 8, :]), 16)
    assert e.sGLD.v == PRO_GLD, e.sGLD.v

    for gi in range(G):
        for dh in range(DH):
            g.wait_ge(e.sX.h, sx_at(gi, 16))
            if dh >= 2:
                g.wait_ge(e.sPE.h, pe_at(gi, dh - 1))
            elif gi > 0:
                g.wait_ge(e.sPE.h, pe_at(gi - 1, PE_G))
            I(e.sPOOL, g.tensor_tensor(out=e.sqb[dh % 2][:, :], in0=e.xbuf[:, dh * GW:(dh + 1) * GW],
                                       in1=e.xbuf[:, dh * GW:(dh + 1) * GW], op=ALU.mult))
        for dh in range(DH):
            g.wait_ge(e.sDVE.h, dve_at(gi, 21 + 3 * dh))
            g.wait_ge(e.sPE.h, pe_at(gi, 16 + dh) if dh >= 2 else pe_at(gi, 16))
            I(e.sPOOL, g.tensor_tensor(out=e.sqb[dh % 2][:, :], in0=e.xbuf[:, dh * GW:(dh + 1) * GW],
                                       in1=e.xbuf[:, dh * GW:(dh + 1) * GW], op=ALU.mult))
        for dh in range(DH):
            g.wait_ge(e.sDVE.h, dve_at(gi, 83 + dh))
            if dh >= 2:
                g.wait_ge(e.sUS[dh % 2].h, 16 * (gi * 8 + (dh - 2) // 2 + 1))
            elif gi > 0:
                g.wait_ge(e.sUS[dh % 2].h, 16 * (gi * 8))
            I(e.sPOOL, g.tensor_tensor(out=e.usc[dh % 2][:, :], in0=e.dsc[dh % 2][:, :],
                                       in1=e.xbuf[:, dh * GW:(dh + 1) * GW], op=ALU.mult))
        for kvh in range(KVH):
            g.wait_ge(e.sDVE.h, dve_at(gi, 104 + 6 * kvh))
            if kvh >= 2:
                g.wait_ge(e.sPE.h, pe_at(gi, 74 + kvh - 1))
            I(e.sPOOL, g.tensor_tensor(out=e.sqb[kvh % 2][:, :], in0=e.kvb[kvh % 2][:, :],
                                       in1=e.kvb[kvh % 2][:, :], op=ALU.mult))
        assert e.sPOOL.v == pool_at(gi, POOL_G), (gi, e.sPOOL.v)


def p0_scalar(e, s):
    s.wait_ge(e.sSP.h, 16 * 5)
    I(e.sACT, s.activation(out=e.Apos[:, :], in_=e.scr[0][:, 0:DH * NST], func=AF.Exp))
    s.wait_ge(e.sPOOL.h, 1)
    I(e.sACT, s.activation(out=e.invf[:, :], in_=e.invf[:, :], func=AF.Exp,
                           scale=-LOG_THETA / 64.0))
    for i, dst in enumerate([e.bdt_c, e.D_c, e.n1w_c, e.n2w_c]):
        s.wait_ge(e.sPE.h, i + 1)
        I(e.sACT, s.copy(out=dst[:, :], in_=e.B[0][:, i * 16:(i + 1) * 16]))
    for ch in range(16):
        s.wait_ge(e.sDVE.h, 5 + 15 * ch + 7)
        I(e.sACT, s.activation(out=e.sinr[:, ch * 512:(ch + 1) * 512], in_=e.scr[5][0:64, :],
                               func=AF.Sin))
        s.wait_ge(e.sDVE.h, 5 + 15 * ch + 15)
        I(e.sACT, s.activation(out=e.cosr[:, ch * 512:(ch + 1) * 512], in_=e.scr[4][0:64, :],
                               func=AF.Sin))
    assert e.sACT.v == PRO_ACT, e.sACT.v

    for gi in range(G):
        s.wait_ge(e.sPE.h, pe_at(gi, 16))
        if gi > 0:
            s.wait_ge(e.sRW.h, 32 * gi)
            W(s, e.sACT)
        I(e.sACT, s.mul(out=e.mrow[:, :], in_=e.B[0][0:1, :], mul=1.0 / D))
        I(e.sACT, s.mul(out=e.mrow2[:, :], in_=e.B[1][0:1, :], mul=1.0 / D))
        s.wait_ge(e.sDVE.h, dve_at(gi, 18))
        I(e.sACT, s.activation(out=e.scr[2][0:1, :], in_=e.mrow2[:, :], func=AF.Ln, bias=EPS))
        W(s, e.sACT)
        I(e.sACT, s.activation(out=e.rrow[:, :], in_=e.scr[2][0:1, :], func=AF.Exp, scale=-0.5))
        s.wait_ge(e.sPE.h, pe_at(gi, 33))
        W(s, e.sACT)
        I(e.sACT, s.mul(out=e.scr[2][0:1, :], in_=e.B[1][0:1, :], mul=1.0 / D))
        W(s, e.sACT)
        I(e.sACT, s.activation(out=e.scr[2][0:1, :], in_=e.scr[2][0:1, :], func=AF.Ln, bias=EPS))
        W(s, e.sACT)
        I(e.sACT, s.activation(out=e.r1row[:, :], in_=e.scr[2][0:1, :], func=AF.Exp, scale=-0.5))
        s.wait_ge(e.sPE.h, pe_at(gi, 50))
        I(e.sACT, s.copy(out=e.draw_b[:, :], in_=e.B[4][0:64, :]))
        if gi > 0:
            s.wait_ge(e.sBC.h, 16 * gi)
        I(e.sACT, s.copy(out=e.scr[3][0:32, :], in_=e.B[5][0:32, :]))
        for dh in range(DH):
            bank = e.B[6 + dh % 2]
            s.wait_ge(e.sPE.h, pe_at(gi, 50 + dh + 1))
            if dh >= 1:
                s.wait_ge(e.sDVE.h, dve_at(gi, 82 + dh))
            elif gi > 0:
                s.wait_ge(e.sDVE.h, dve_at(gi - 1, 98))
            W(s, e.sACT)
            bcol = e.bdt_c[:, dh:dh + 1]
            I(e.sACT, s.activation(out=e.scr[0][:, :], in_=bank[:, :], func=AF.Abs, bias=bcol))
            W(s, e.sACT)
            I(e.sACT, s.activation(out=e.scr[1][:, :], in_=e.scr[0][:, :], func=AF.Exp, scale=-1.0))
            W(s, e.sACT)
            I(e.sACT, s.activation(out=e.scr[1][:, :], in_=e.scr[1][:, :], func=AF.Ln, bias=1.0))
            W(s, e.sACT)
            I(e.sACT, s.activation(out=e.scr[0][:, :], in_=bank[:, :], func=AF.Relu, bias=bcol))
        for tc in range(4):
            s.wait_ge(e.sPE.h, pe_at(gi, 70 + tc + 1))
            if tc >= 2:
                s.wait_ge(e.sVB[tc % 2].h, 16 * (gi * 2 + 1))
            elif gi > 0:
                s.wait_ge(e.sVB[tc % 2].h, 16 * (gi * 2))
            I(e.sACT, s.copy(out=e.vb[tc % 2][:, :], in_=e.B[2 + tc % 2][:, :]))
        assert e.sACT.v == act_at(gi, ACT_G), (gi, e.sACT.v)


def p0_tensor(e, t):
    t.wait_ge(e.sSP.h, 16 * 5)
    t.wait_ge(e.sPOOL.h, 3)
    for i in range(4):
        if i > 0:
            t.wait_ge(e.sACT.h, 2 + i)
        I(e.sPE, t.transpose(e.B[0][:, i * 16:(i + 1) * 16], e.scr[1][0:16, i * 128:(i + 1) * 128],
                             e.id16[:, :]))
    for ch in range(16):
        t.wait_ge(e.sACT.h, 2)
        t.wait_ge(e.sPOOL.h, 3 + ch + 1)
        if ch >= 1:
            t.wait_ge(e.sDVE.h, 5 + 15 * (ch - 1) + 8)
        sl = e.iot[:, (ch % 2) * 512:(ch % 2) * 512 + 512]
        I(e.sPE, t.matmul(e.B[1][0:64, :], e.invf[:, :], sl, start=True, stop=True))
        I(e.sPE, t.matmul(e.B[2][0:64, :], e.invf[:, :], sl, start=True, stop=True))
    assert e.sPE.v == PRO_PE, e.sPE.v

    for gi in range(G):
        if gi > 0:
            t.wait_ge(e.sACT.h, act_at(gi - 1, ACT_G))
            t.wait_ge(e.sDVE.h, dve_at(gi - 1, DVE_G))
        else:
            t.wait_ge(e.sDVE.h, PRO_DVE)
            t.wait_ge(e.sGLD.h, PRO_GLD)
        for dh in range(DH):
            t.wait_ge(e.sDVE.h, dve_at(gi, dh + 1))
            t.wait_ge(e.sPOOL.h, pool_at(gi, dh + 1))
            t.matmul(e.B[0][0:1, :], e.ones_bf[:, :], e.xn1b[:, dh * GW:(dh + 1) * GW],
                     start=(dh == 0), stop=(dh == DH - 1))
            I(e.sPE, t.matmul(e.B[1][0:1, :], e.ones_bf[:, :], e.sqb[dh % 2][:, :],
                              start=(dh == 0), stop=(dh == DH - 1)))
        t.wait_ge(e.sACT.h, act_at(gi, 4))
        t.matmul(e.B[2][:, :], e.ones1f[:, :], e.mrow[:, :], start=True, stop=True)
        I(e.sPE, t.matmul(e.B[3][:, :], e.ones1f[:, :], e.rrow[:, :], start=True, stop=True))
        for dh in range(DH):
            t.wait_ge(e.sPOOL.h, pool_at(gi, 16 + dh + 1))
            I(e.sPE, t.matmul(e.B[1][0:1, :], e.ones_bf[:, :], e.sqb[dh % 2][:, :],
                              start=(dh == 0), stop=(dh == DH - 1)))
        t.wait_ge(e.sACT.h, act_at(gi, 7))
        t.wait_ge(e.sDVE.h, dve_at(gi, 66))
        I(e.sPE, t.matmul(e.B[3][:, :], e.ones1f[:, :], e.r1row[:, :], start=True, stop=True))
        for dh in range(DH):
            t.wait_ge(e.sDVE.h, dve_at(gi, 21 + 3 * dh))
            if dh == 0 and gi > 0:
                t.wait_ge(e.sACT.h, act_at(gi - 1, 9))
            t.matmul(e.B[4][0:64, :], e.wxp_sb[:, dh * 96:dh * 96 + 64],
                     e.xnb[:, dh * GW:(dh + 1) * GW], start=(dh == 0), stop=(dh == DH - 1))
            I(e.sPE, t.matmul(e.B[5][0:32, :], e.wxp_sb[:, dh * 96 + 64:dh * 96 + 96],
                              e.xnb[:, dh * GW:(dh + 1) * GW], start=(dh == 0),
                              stop=(dh == DH - 1)))
        for dh in range(DH):
            t.wait_ge(e.sACT.h, act_at(gi, 8))
            if dh >= 2:
                t.wait_ge(e.sACT.h, act_at(gi, 13 + 4 * (dh - 2)))
            I(e.sPE, t.matmul(e.B[6 + dh % 2][:, :], e.wdt_sb[:, dh * 128:(dh + 1) * 128],
                              e.draw_b[:, :], start=True, stop=True))
        t.wait_ge(e.sGLD.h, PRO_GLD)
        for kvh in range(KVH):
            t.wait_ge(e.sDVE.h, dve_at(gi, 82))
            if kvh >= 2:
                t.wait_ge(e.sDVE.h, dve_at(gi, 104 + 6 * (kvh - 2)))
            for dh in range(DH):
                ins = t.matmul(e.B[kvh % 2][:, :],
                               e.wk_sb[:, (kvh * DH + dh) * 128:(kvh * DH + dh + 1) * 128],
                               e.xn1b[:, dh * GW:(dh + 1) * GW],
                               start=(dh == 0), stop=(dh == DH - 1))
                if dh == DH - 1:
                    I(e.sPE, ins)
        for tc in range(4):
            if tc >= 2:
                t.wait_ge(e.sACT.h, act_at(gi, 72 + tc))
            for dh in range(DH):
                ins = t.matmul(e.B[2 + tc % 2][:, :],
                               e.xn1b[:, dh * GW + tc * 128:dh * GW + (tc + 1) * 128],
                               e.wv_sb[:, dh * 512:(dh + 1) * 512],
                               start=(dh == 0), stop=(dh == DH - 1))
                if dh == DH - 1:
                    I(e.sPE, ins)
        for kvh in range(KVH):
            t.wait_ge(e.sPOOL.h, pool_at(gi, 48 + kvh + 1))
            if kvh == 0:
                t.wait_ge(e.sACT.h, act_at(gi, 8))
            else:
                t.wait_ge(e.sDVE.h, dve_at(gi, 122 + 2 * kvh))
            I(e.sPE, t.matmul(e.B[4][0:1, :], e.ones_bf[:, :], e.sqb[kvh % 2][:, :],
                              start=True, stop=True))
        assert e.sPE.v == pe_at(gi, PE_G), (gi, e.sPE.v)


def p0_vector(e, v):
    I(e.sDVE, v.memset(e.ones_bf[:, :], 1.0))
    I(e.sDVE, v.memset(e.ones1f[:, :], 1.0))
    I(e.sDVE, v.memset(e.cst[:, 0:1], EPS))
    I(e.sDVE, v.memset(e.cst[:, 1:2], 1.0))
    I(e.sDVE, v.memset(e.kmax2[:, :], 0.0))
    for ch in range(16):
        v.wait_ge(e.sPE.h, 4 + 2 * ch + 2)
        if ch >= 1:
            v.wait_ge(e.sACT.h, 6 + 2 * ch)
        b = e.B[1]
        b2 = e.B[2]

        def redchain(v, srcap, dst, tmp):
            W(v, e.sDVE)
            I(e.sDVE, v.tensor_scalar_mul(dst[0:64, :], srcap, INV2PI))
            W(v, e.sDVE)
            I(e.sDVE, v.tensor_copy(out=e.iscr[0:64, :], in_=dst[0:64, :]))
            W(v, e.sDVE)
            I(e.sDVE, v.tensor_copy(out=tmp[0:64, :], in_=e.iscr[0:64, :]))
            W(v, e.sDVE)
            I(e.sDVE, v.scalar_tensor_tensor(out=dst[0:64, :], in0=tmp[0:64, :], scalar=-TWOPI,
                                             in1=srcap, op0=ALU.mult, op1=ALU.add))
            W(v, e.sDVE)
            I(e.sDVE, v.tensor_scalar(out=tmp[0:64, :], in0=dst[0:64, :], scalar1=PI,
                                      scalar2=None, op0=ALU.is_gt, op1=ALU.bypass,
                                      accum_out=None))
            W(v, e.sDVE)
            I(e.sDVE, v.scalar_tensor_tensor(out=dst[0:64, :], in0=tmp[0:64, :], scalar=-TWOPI,
                                             in1=dst[0:64, :], op0=ALU.mult, op1=ALU.add))
            W(v, e.sDVE)
            I(e.sDVE, v.tensor_scalar(out=dst[0:64, :], in0=dst[0:64, :], scalar1=PI,
                                      scalar2=-PI, op0=ALU.min, op1=ALU.max, accum_out=None))

        redchain(v, b[0:64, :], e.scr[5], e.scr[3])
        W(v, e.sDVE)
        I(e.sDVE, v.tensor_scalar_add(e.scr[2][0:64, :], b2[0:64, :], PI / 2))
        redchain(v, e.scr[2][0:64, :], e.scr[4], e.scr[3])
        assert e.sDVE.v == 5 + 15 * (ch + 1), (ch, e.sDVE.v)
    assert e.sDVE.v == PRO_DVE, e.sDVE.v

    for gi in range(G):
        for dh in range(DH):
            v.wait_ge(e.sX.h, sx_at(gi, 16))
            if gi > 0 and dh == 0:
                v.wait_ge(e.sPE.h, pe_at(gi - 1, PE_G))
                v.wait_ge(e.sX1.h, 32 * gi)
            I(e.sDVE, v.tensor_copy(out=e.xn1b[:, dh * GW:(dh + 1) * GW],
                                    in_=e.xbuf[:, dh * GW:(dh + 1) * GW]))
        v.wait_ge(e.sACT.h, act_at(gi, 2))
        W(v, e.sDVE)
        I(e.sDVE, v.tensor_mul(e.scr[2][0:1, :], e.mrow[:, :], e.mrow[:, :]))
        W(v, e.sDVE)
        I(e.sDVE, v.tensor_sub(e.mrow2[:, :], e.mrow2[:, :], e.scr[2][0:1, :]))
        v.wait_ge(e.sPE.h, pe_at(gi, 17))
        for dh in range(DH):
            v.wait_ge(e.sPOOL.h, pool_at(gi, dh + 1))
            W(v, e.sDVE)
            I(e.sDVE, v.tensor_sub(e.xbuf[:, dh * GW:(dh + 1) * GW],
                                   e.xbuf[:, dh * GW:(dh + 1) * GW], e.B[2][:, :]))
            W(v, e.sDVE)
            I(e.sDVE, v.tensor_mul(e.xbuf[:, dh * GW:(dh + 1) * GW],
                                   e.xbuf[:, dh * GW:(dh + 1) * GW], e.B[3][:, :]))
            W(v, e.sDVE)
            I(e.sDVE, v.tensor_copy(out=e.xnb[:, dh * GW:(dh + 1) * GW],
                                    in_=e.xbuf[:, dh * GW:(dh + 1) * GW]))
        v.wait_ge(e.sPE.h, pe_at(gi, 34))
        W(v, e.sDVE)
        for dh in range(DH):
            I(e.sDVE, v.scalar_tensor_tensor(out=e.xn1b[:, dh * GW:(dh + 1) * GW],
                                             in0=e.xbuf[:, dh * GW:(dh + 1) * GW],
                                             scalar=e.n1w_c[:, dh:dh + 1], in1=e.B[3][:, :],
                                             op0=ALU.mult, op1=ALU.mult))
        for dh in range(DH):
            v.wait_ge(e.sACT.h, act_at(gi, 13 + 4 * dh))
            if dh >= 2:
                v.wait_ge(e.sDS[dh % 2].h, 16 * (gi * 8 + (dh - 2) // 2 + 1))
                v.wait_ge(e.sPOOL.h, pool_at(gi, 32 + dh - 1))
            elif gi > 0:
                v.wait_ge(e.sDS[dh % 2].h, 16 * (gi * 8))
                v.wait_ge(e.sPOOL.h, pool_at(gi - 1, 32 + DH - 1 + dh))
            I(e.sDVE, v.tensor_add(e.dsc[dh % 2][:, :], e.scr[0][:, :], e.scr[1][:, :]))
        for kvh in range(KVH):
            v.wait_ge(e.sPE.h, pe_at(gi, 66 + kvh + 1))
            v.wait_ge(e.sACT.h, PRO_ACT)
            if kvh >= 2:
                v.wait_ge(e.sKB[kvh % 2].h, 16 * (gi * 2 + 1))
                v.wait_ge(e.sPOOL.h, pool_at(gi, 48 + kvh - 1))
            elif gi > 0:
                v.wait_ge(e.sKB[kvh % 2].h, 16 * (gi * 2))
                v.wait_ge(e.sPOOL.h, pool_at(gi - 1, 48 + kvh + 3))
            b = e.B[kvh % 2]
            cs = e.cosr[:, gi * GW:(gi + 1) * GW]
            sn = e.sinr[:, gi * GW:(gi + 1) * GW]
            kb = e.kvb[kvh % 2]
            W(v, e.sDVE)
            I(e.sDVE, v.tensor_mul(e.scr[0][0:64, :], b[0:64, :], cs))
            I(e.sDVE, v.tensor_mul(e.scr[1][0:64, :], b[64:128, :], sn))
            W(v, e.sDVE)
            I(e.sDVE, v.tensor_sub(kb[0:64, :], e.scr[0][0:64, :], e.scr[1][0:64, :]))
            W(v, e.sDVE)
            I(e.sDVE, v.tensor_mul(e.scr[0][0:64, :], b[64:128, :], cs))
            I(e.sDVE, v.tensor_mul(e.scr[1][0:64, :], b[0:64, :], sn))
            W(v, e.sDVE)
            I(e.sDVE, v.tensor_add(kb[64:128, :], e.scr[0][0:64, :], e.scr[1][0:64, :]))
        for kvh in range(KVH):
            v.wait_ge(e.sPE.h, pe_at(gi, 74 + kvh + 1))
            W(v, e.sDVE)
            I(e.sDVE, v.reduce_max(e.scr[2][0:1, 0:1], e.B[4][0:1, :], axis=AXX))
            W(v, e.sDVE)
            idx = (gi // 8) * 4 + kvh
            I(e.sDVE, v.tensor_max(e.kmax2[:, idx:idx + 1], e.kmax2[:, idx:idx + 1],
                                   e.scr[2][0:1, 0:1]))
        assert e.sDVE.v == dve_at(gi, DVE_G), (gi, e.sDVE.v)


def p0_sync(e, s):
    s.dma_start(out=e.scr[0][:, 0:DH * NST].rearrange("p (dh n) -> p dh n", dh=DH),
                in_=e.Alog.rearrange("(dh dl) n -> dl dh n", dl=128)).then_inc(e.sSP.h, 16)
    for i, t_ in enumerate([e.bdt_r, e.D_r, e.n1w_r, e.n2w_r]):
        s.dma_start(out=e.scr[1][0:16, i * 128:(i + 1) * 128], in_=t_).then_inc(e.sSP.h, 16)
    e.sSP.v = PRO_SP

    for gi in range(G):
        if gi > 0:
            s.wait_ge(e.sPOOL.h, pool_at(gi - 1, 48))
            s.wait_ge(e.sDVE.h, dve_at(gi - 1, 82))
        for dh in range(DH):
            s.dma_start(out=e.xbuf[:, dh * GW:(dh + 1) * GW],
                        in_=e.xT[dh * 128:(dh + 1) * 128, gi * GW:(gi + 1) * GW]).then_inc(e.sX.h, 16)
        e.sX.v = sx_at(gi, SX_G)
        s.wait_ge(e.sDVE.h, dve_at(gi, 82))
        for half in range(2):
            I(e.sX1, s.dma_start(
                out=e.d_xn1.rearrange("(dh dl) t -> dl dh t", dl=128)[:, half * 8:(half + 1) * 8,
                                                                      gi * GW:(gi + 1) * GW],
                in_=e.xn1b[:, half * 8 * GW:(half + 1) * 8 * GW].rearrange(
                    "p (dh t) -> p dh t", dh=8)), 16)
        for dh in range(DH):
            s.wait_ge(e.sDVE.h, dve_at(gi, 83 + dh))
            I(e.sDS[dh % 2], s.dma_start(out=e.d_delta[dh * 128:(dh + 1) * 128,
                                                       gi * GW:(gi + 1) * GW],
                                         in_=e.dsc[dh % 2][:, :]), 16)
            s.wait_ge(e.sPOOL.h, pool_at(gi, 32 + dh + 1))
            I(e.sUS[dh % 2], s.dma_start(out=e.d_u[dh * 128:(dh + 1) * 128,
                                                   gi * GW:(gi + 1) * GW],
                                         in_=e.usc[dh % 2][:, :]), 16)
        for kvh in range(KVH):
            s.wait_ge(e.sDVE.h, dve_at(gi, 104 + 6 * kvh))
            I(e.sKB[kvh % 2], s.dma_start(out=e.d_k[kvh, :, gi * GW:(gi + 1) * GW],
                                          in_=e.kvb[kvh % 2][:, :]), 16)
        for tc in range(4):
            s.wait_ge(e.sACT.h, act_at(gi, 73 + tc + 1))
            I(e.sVB[tc % 2], s.dma_start(out=e.d_v[gi * GW + tc * 128:gi * GW + (tc + 1) * 128, :],
                                         in_=e.vb[tc % 2][:, :]), 16)
        s.wait_ge(e.sACT.h, act_at(gi, 9))
        I(e.sBC, s.dma_start(out=e.d_bc[:, gi * GW:(gi + 1) * GW], in_=e.scr[3][0:32, :]), 16)
        s.wait_ge(e.sACT.h, act_at(gi, 4))
        I(e.sRW, s.dma_start(out=e.d_rows[0:1, gi * GW:(gi + 1) * GW], in_=e.mrow), 16)
        I(e.sRW, s.dma_start(out=e.d_rows[1:2, gi * GW:(gi + 1) * GW], in_=e.rrow), 16)
    for c in [e.sX1, e.sBC, e.sRW] + e.sDS + e.sUS + e.sKB + e.sVB:
        s.wait_ge(c.h, c.v)


def rope_debug(e, g):
    g.dma_start(out=e.dbg["dbg_cos"], in_=e.cosr[:, 0:1024]).then_inc(e.sGLD.h, 16)
    g.dma_start(out=e.dbg["dbg_sin"], in_=e.sinr[:, 0:1024]).then_inc(e.sGLD.h, 16)
    e.sGLD.v += 2 * 16
    g.wait_ge(e.sGLD.h, e.sGLD.v)


def p0_debug_g(e, g):
    for half in range(2):
        g.dma_start(out=e.dbg["dbg_xn1"][half * 1024:(half + 1) * 1024, :],
                    in_=e.d_xn1[half * 1024:(half + 1) * 1024, 0:1024]).then_inc(e.sGLD.h, 16)
    g.dma_start(out=e.dbg["dbg_k"].rearrange("(a p) t -> a p t", a=KVH),
                in_=e.d_k[:, :, 0:1024]).then_inc(e.sGLD.h, 16)
    g.dma_start(out=e.dbg["dbg_v"], in_=e.d_v[0:1024, :]).then_inc(e.sGLD.h, 16)
    e.sGLD.v += 4 * 16


def p0_debug_s(e, s):
    s.dma_start(out=e.dbg["dbg_delta"], in_=e.d_delta[:, 0:1024]).then_inc(e.sST.h, 16)
    s.dma_start(out=e.dbg["dbg_u"], in_=e.d_u[:, 0:1024]).then_inc(e.sST.h, 16)
    s.dma_start(out=e.dbg["dbg_bc"], in_=e.d_bc[:, 0:1024]).then_inc(e.sST.h, 16)
    s.dma_start(out=e.dbg["dbg_rows"], in_=e.d_rows).then_inc(e.sST.h, 16)
    s.dma_start(out=e.out[0:128, 0:512], in_=e.scr[0][:, 0:512]).then_inc(e.sST.h, 16)
    e.sST.v += 5 * 16
    s.wait_ge(e.sST.h, e.sST.v)
    s.wait_ge(e.sGLD.h, e.sGLD.v)



TSC = 2048


def own_cols(c):
    return (512 * c, 4096 + 512 * (7 - c))


def emit_mamba(e):
    nc = e.nc
    mctx = ExitStack()

    def sb(name, shape, dt):
        return mctx.enter_context(nc.sbuf_tensor(name, shape, dt))
    if "sw_excess" not in e:
        e.sw_excess = {}
    if "pid_s" not in e:
        e.pid_s = nc.sync.partition_id()
    if "pid_gm" not in e:
        e.pid_gm = nc.gpsimd.partition_id()
    e.negA = sb("negA", [128, 32], F32)
    e.m_dlt = sb("m_dlt", [128, L], F32)
    e.m_ut = sb("m_ut", [128, L], BF16)
    e.m_ysb = sb("m_ysb", [128, L], F32)
    e.m_dab = [sb(f"m_dab{i}", [128, TSC], F32) for i in range(2)]
    e.m_bcb = [sb(f"m_bcb{i}", [128, TSC], BF16) for i in range(4)]
    e.m_ccb = [sb(f"m_ccb{i}", [128, TSC], BF16) for i in range(4)]
    e.m_hsb = [sb(f"m_hsb{i}", [128, TSC], BF16) for i in range(2)]

    dve0 = e.sDVE.v + 1
    act0 = e.sACT.v + 1
    bb0 = e.sBB.v
    cb0 = e.sCB.v
    dvev = lambda git, k: dve0 + git * 4 + k
    actv = lambda git: act0 + git + 1
    bbv = lambda git: bb0 + 16 * (git + 1)
    cbv = lambda git: cb0 + 16 * (git + 1)

    entry = {nm: e[nm].v for nm in ["sDVE", "sPE", "sACT", "sPOOL"]}

    seng = nc.sync
    for blkj in range(4):
        ci2, b = blkj // 2, blkj % 2
        geng = nc.gpsimd
        for c in nc.Switch(engines=[seng, geng], index=[e.pid_s, e.pid_gm], n=8):
            row = 256 * c + 128 * ci2
            if blkj > 0:
                seng.wait_ge(e.sACT.h, actv(blkj * 32 - 1))
                geng.wait_ge(e.sDVE.h, dvev(blkj * 32 - 1, 1))
            else:
                for nm, val in entry.items():
                    seng.wait_ge(e[nm].h, val)
                    geng.wait_ge(e[nm].h, val)
                geng.wait_ge(e.sUS[0].h, e.sUS[0].v)
                geng.wait_ge(e.sUS[1].h, e.sUS[1].v)
            I(e.sMS, seng.dma_start(out=e.m_dlt[:, :],
                                    in_=e.d_delta[row:row + 128, b * L:(b + 1) * L]), 16)
            e.sMS.v -= 16
            I(e.sMS, geng.dma_start(out=e.m_ut[:, :],
                                    in_=e.d_u[row:row + 128, b * L:(b + 1) * L]), 16)
            e.sMS.v -= 16
            if blkj == 0:
                I(e.sMS, seng.dma_start(
                    out=e.negA[:, :].rearrange("p (ci n) -> p ci n", ci=2),
                    in_=e.Alog.rearrange("(ci dl) n -> dl ci n", dl=128)[:, 2 * c:2 * c + 2, :]), 16)
                e.sMS.v -= 16
        nadd = 3 if blkj == 0 else 2
        e.sMS.v += 16 * nadd
        e.sw_excess["sMS"] = e.sw_excess.get("sMS", 0) + 7 * 16 * nadd

        ms_now = e.sMS.v
        with nc.Block() as blk:
            def fsy(s, blkj=blkj, ci2=ci2, b=b):
                s.wait_ge(e.sDVE.h, dvev(blkj * 32 + 31, 4))
                I(e.sMS, s.dma_start(out=e.d_ysrc[ci2 * 128:(ci2 + 1) * 128,
                                                  b * L:(b + 1) * L],
                                     in_=e.m_ysb[:, :]), 16)

            def fgp(g, blkj=blkj, b=b):
                if blkj == 0:
                    g.wait_ge(e.sBC.h, e.sBC.v)
                for n in range(NST):
                    for kk in range(2):
                        git = blkj * 32 + n * 2 + kk
                        if git >= 4:
                            g.wait_ge(e.sDVE.h, dvev(git - 4, 4))
                        I(e.sBB, g.dma_start(
                            out=e.m_bcb[git % 4][:, :],
                            in_=e.d_bc[n:n + 1, b * L + kk * TSC:
                                       b * L + (kk + 1) * TSC].partition_broadcast(128)), 16)
                        I(e.sCB, g.dma_start(
                            out=e.m_ccb[git % 4][:, :],
                            in_=e.d_bc[16 + n:17 + n, b * L + kk * TSC:
                                       b * L + (kk + 1) * TSC].partition_broadcast(128)), 16)

            def fsc(s, blkj=blkj, ci2=ci2, ms_now=ms_now):
                s.wait_ge(e.sMS.h, ms_now)
                if blkj == 0:
                    I(e.sACT, s.activation(out=e.negA[:, :], in_=e.negA[:, :],
                                           func=AF.Exp))
                    s.wait_ge(e.sDVE.h, dve0)
                for n in range(NST):
                    for kk in range(2):
                        git = blkj * 32 + n * 2 + kk
                        if git >= 2:
                            s.wait_ge(e.sDVE.h, dvev(git - 2, 2))
                        I(e.sACT, s.activation(
                            out=e.m_dab[git % 2][:, :],
                            in_=e.m_dlt[:, kk * TSC:(kk + 1) * TSC],
                            func=AF.Exp,
                            scale=e.negA[:, ci2 * 16 + n:ci2 * 16 + n + 1]))

            def fv(v, blkj=blkj, ms_now=ms_now):
                v.wait_ge(e.sMS.h, ms_now)
                if blkj == 0:
                    v.wait_ge(e.sACT.h, act0)
                    I(e.sDVE, v.tensor_scalar(out=e.negA[:, :], in0=e.negA[:, :],
                                              scalar1=-1.0, scalar2=None,
                                              op0=ALU.mult, op1=ALU.bypass,
                                              accum_out=None))
                for n in range(NST):
                    for kk in range(2):
                        git = blkj * 32 + n * 2 + kk
                        p = git % 2
                        p4 = git % 4
                        v.wait_ge(e.sBB.h, bbv(git))
                        I(e.sDVE, v.tensor_mul(e.m_bcb[p4][:, :],
                                               e.m_ut[:, kk * TSC:(kk + 1) * TSC],
                                               e.m_bcb[p4][:, :]))
                        v.wait_ge(e.sACT.h, actv(git))
                        init = 0.0 if kk == 0 else e.m_hsb[1 - p][:, TSC - 1:TSC]
                        I(e.sDVE, v.tensor_tensor_scan(out=e.m_hsb[p][:, :],
                                                       data0=e.m_dab[p][:, :],
                                                       data1=e.m_bcb[p4][:, :],
                                                       initial=init,
                                                       op0=ALU.mult, op1=ALU.add))
                        v.wait_ge(e.sCB.h, cbv(git))
                        I(e.sDVE, v.tensor_mul(e.m_ccb[p4][:, :], e.m_hsb[p][:, :],
                                               e.m_ccb[p4][:, :]))
                        ysl = e.m_ysb[:, kk * TSC:(kk + 1) * TSC]
                        if n == 0:
                            I(e.sDVE, v.tensor_copy(out=ysl, in_=e.m_ccb[p4][:, :]))
                        else:
                            I(e.sDVE, v.tensor_add(ysl, ysl, e.m_ccb[p4][:, :]))

            blk.sync(fsy)
            blk.gpsimd(fgp)
            blk.scalar(fsc)
            blk.vector(fv)

    with nc.Block() as blk:
        def fg(g):
            g.wait_ge(e.sMS.h, e.sMS.v)
            I(e.sAG, g.collective_compute(
                "AllGather", ALU.bypass,
                replica_groups=[list(range(8))],
                ins=[e.d_ysrc[:, :]], outs=[e.d_ygat[:, :]],
            ))
        blk.gpsimd(fg)
    mctx.close()



ATT_PAIRS = [(b, k) for b in range(2) for k in range(KVH)]


def att_E(c, beta):
    return 4 * (c + 1) if beta == 0 else 4 * (8 - c)


def act_pairh_base(c, pair, h):
    off = 2
    for p2 in range(pair):
        off += 4 * (att_E(c, ATT_PAIRS[p2][0]) + 4)
    off += h * (att_E(c, ATT_PAIRS[pair][0]) + 4)
    return off


class SwBook:

    def __init__(self, e, names):
        self.e = e
        self.names = names
        self.base = {}

    def enter_body(self):
        for n in self.names:
            c = self.e[n]
            if n not in self.base:
                self.base[n] = c.v
            c.v = self.base[n]

    def finish(self):
        for n in self.names:
            c = self.e[n]
            per = c.v - self.base[n]
            self.e.sw_excess[n] = self.e.sw_excess.get(n, 0) + 7 * per


def emit_attention(e):
    nc = e.nc
    if "sw_excess" not in e:
        e.sw_excess = {}
    e.avctx = ExitStack()
    e.av_sb = e.avctx.enter_context(
        nc.sbuf_tensor("av_sb", [128, NH * 1024], BF16))
    actx = ExitStack()

    def sb(name, shape, dt):
        return actx.enter_context(nc.sbuf_tensor(name, shape, dt))

    e.qin = sb("qin", [128, DH * 1024], BF16)
    e.q_sb = sb("q_sb", [128, NH * 1024], BF16)
    e.kbufA = sb("kbufA", [128, 32 * 128], BF16)
    e.vbufA = sb("vbufA", [128, 32 * 128], BF16)
    e.wq_sb = sb("wq_sb", [128, 2 * DH * 128], BF16)
    e.probs = [sb(f"probs{i}", [128, 512], BF16) for i in range(2)]
    e.avf = sb("avf", [128, 512], F32)
    e.q2b = sb("q2b", [128, 1024], BF16)
    e.bias_col = sb("bias_col", [128, 1], F32)
    e.srow = sb("srow", [1, 512], F32)
    e.trow = sb("trow", [1, 8], F32)

    teng, seng, veng, geng, speng = nc.tensor, nc.scalar, nc.vector, nc.gpsimd, nc.sync
    if "pid_t" not in e:
        e.pid_t = teng.partition_id()
        e.pid_sc = seng.partition_id()
        e.pid_v = veng.partition_id()
        e.pid_g2 = geng.partition_id()
        e.pid_sp2 = speng.partition_id()
    book = SwBook(e, ["sPE", "sACT", "sDVE", "sPOOL", "sQS", "sKV", "sWQ0", "sWQ1"])
    for c in nc.Switch(engines=[teng, seng, veng, geng, speng],
                       index=[e.pid_t, e.pid_sc, e.pid_v, e.pid_g2, e.pid_sp2], n=8):
        book.enter_body()
        e.ab = dict(book.base)
        attn_body(e, c, teng, seng, veng, geng, speng)
    book.finish()

    if DEBUG:
        with nc.Block() as blk:
            blk.gpsimd(lambda s: attn_debug(e, s))
    actx.close()


def attn_body(e, c, t, s, v, g, sp):
    A0, A1 = own_cols(c)
    ab = e.ab
    peq = lambda hq, k: ab["sPE"] + 2 * hq + k
    peqn = lambda hq: ab["sPE"] + 32 + hq + 1
    pebias = ab["sPE"] + 49
    dveq = lambda hq, ib: ab["sDVE"] + 12 * hq + 6 * (ib + 1)
    dveqn = lambda hq: ab["sDVE"] + 192 + 2 * (hq + 1)
    dveu = ab["sDVE"] + 226
    dveneg = ab["sDVE"] + 227
    dveav = lambda pair, h: ab["sDVE"] + 227 + pair * 4 + h + 1
    actexp = lambda pair, h, kc: ab["sACT"] + act_pairh_base(c, pair, h) + kc + 1
    actsum = lambda pair, h: (ab["sACT"] + act_pairh_base(c, pair, h)
                              + att_E(c, ATT_PAIRS[pair][0]) + 1)
    actrec = lambda pair, h: actsum(pair, h) + 2
    actavc = lambda pair, h: actsum(pair, h) + 3
    poolq2 = lambda hq: ab["sPOOL"] + hq + 1
    poolaff = lambda pair, h, j: ab["sPOOL"] + 16 + (pair * 4 + h) * 4 + j + 1

    pe_off = {}

    def emit_sumav(pair, h, kc, beta):
        E = att_E(c, beta)
        t.wait_ge(e.sACT.h, actexp(pair, h, kc))
        if kc >= E - 4:
            t.wait_ge(e.sPOOL.h, poolaff(pair, h, kc - (E - 4)))
        t.matmul(e.B[3][0:1, :], e.ones_bf[:, :], e.probs[kc % 2][:, :],
                 start=(kc == 0), stop=(kc == E - 1))
        ins = t.matmul(e.B[2][:, :], e.vbufA[:, kc * 128:(kc + 1) * 128],
                       e.probs[kc % 2][:, :], start=(kc == 0), stop=(kc == E - 1))
        I(e.sPE, ins)
        pe_off[("av", pair, h, kc)] = e.sPE.v

    def emit_tensor():
        t.wait_ge(e.sQS.h, ab["sQS"] + 16 * 4)
        for hq in range(NH):
            t.wait_ge(e["sWQ%d" % (hq % 2)].h, ab["sWQ%d" % (hq % 2)] + 32 * (hq // 2 + 1))
            for grp in range(2):
                if hq >= 1:
                    t.wait_ge(e.sDVE.h, dveq(hq - 1, grp))
                for dh in range(DH):
                    ins = t.matmul(
                        e.B[grp][:, :],
                        e.wq_sb[:, ((hq % 2) * DH + dh) * 128:((hq % 2) * DH + dh + 1) * 128],
                        e.qin[:, dh * 1024 + grp * 512:dh * 1024 + (grp + 1) * 512],
                        start=(dh == 0), stop=(dh == DH - 1))
                I(e.sPE, ins)
        for hq in range(NH):
            t.wait_ge(e.sPOOL.h, poolq2(hq))
            if hq >= 1:
                t.wait_ge(e.sDVE.h, dveqn(hq - 1))
            t.matmul(e.B[5][0:1, :], e.ones_bf[:, :], e.q2b[:, 0:512],
                     start=True, stop=False)
            I(e.sPE, t.matmul(e.B[5][0:1, :], e.ones_bf[:, :], e.q2b[:, 512:1024],
                              start=False, stop=True))
        t.wait_ge(e.sACT.h, ab["sACT"] + 2)
        I(e.sPE, t.matmul(e.B[4][:, 0:1], e.ones1f[:, :], e.trow[0:1, 3:4],
                          start=True, stop=True))
        t.wait_ge(e.sDVE.h, ab["sDVE"] + 192)
        for pair, (beta, kvh) in enumerate(ATT_PAIRS):
            E = att_E(c, beta)
            t.wait_ge(e.sKV.h, ab["sKV"] + 32 * (pair + 1))
            for h in range(4):
                hglob = kvh * 4 + h
                qsl = e.q_sb[:, hglob * 1024 + beta * 512:hglob * 1024 + (beta + 1) * 512]
                if h == 0 and pair == 0:
                    t.wait_ge(e.sDVE.h, dveneg)
                else:
                    ph, hh = (pair, h - 1) if h > 0 else (pair - 1, 3)
                    t.wait_ge(e.sACT.h, actavc(ph, hh))
                for kc in range(E):
                    if kc >= 2:
                        t.wait_ge(e.sACT.h, actexp(pair, h, kc - 2))
                    ins = t.matmul(e.B[kc % 2][:, :], e.kbufA[:, kc * 128:(kc + 1) * 128],
                                   qsl, start=True, stop=True)
                    I(e.sPE, ins)
                    pe_off[("score", pair, h, kc)] = e.sPE.v
                    if kc >= 1:
                        emit_sumav(pair, h, kc - 1, beta)
                emit_sumav(pair, h, E - 1, beta)
                t.wait_ge(e.sACT.h, actrec(pair, h))
                if pair == 0 and h == 0:
                    t.wait_ge(e.sDVE.h, dveneg)
                else:
                    ph, hh = (pair, h - 1) if h > 0 else (pair - 1, 3)
                    t.wait_ge(e.sDVE.h, dveav(ph, hh))
                I(e.sPE, t.matmul(e.B[4][:, :], e.ones1f[:, :], e.srow[0:1, :],
                                  start=True, stop=True))
                pe_off[("recip", pair, h)] = e.sPE.v

    emit_tensor()

    sp.wait_ge(e.sMS.h, e.sMS.v)
    sp.wait_ge(e.sDVE.h, ab["sDVE"])
    sp.wait_ge(e.sPE.h, ab["sPE"])
    sp.wait_ge(e.sACT.h, ab["sACT"])
    for half in range(2):
        for ib, a in enumerate((A0, A1)):
            I(e.sQS, sp.dma_start(
                out=e.qin[:, :].rearrange("p (dh tk) -> p dh tk", dh=DH)[
                    :, half * 8:(half + 1) * 8, ib * 512:(ib + 1) * 512],
                in_=e.d_xn1.rearrange("(dh dl) t -> dl dh t", dl=128)[
                    :, half * 8:(half + 1) * 8, a:a + 512]), 16)
    for pair, (beta, kvh) in enumerate(ATT_PAIRS):
        E = att_E(c, beta)
        kb = beta * L
        if pair >= 1:
            sp.wait_ge(e.sPE.h, pe_off[("recip", pair - 1, 3)])
        I(e.sKV, sp.dma_start(out=e.kbufA[:, 0:E * 128],
                              in_=e.d_k[kvh, :, kb:kb + E * 128]), 16)
        I(e.sKV, sp.dma_start(
            out=e.vbufA[:, 0:E * 128].rearrange("p (kc x) -> p kc x", kc=E),
            in_=e.d_v[kb:kb + E * 128, kvh * 128:(kvh + 1) * 128].rearrange(
                "(kc p) x -> p kc x", p=128)), 16)

    g.wait_ge(e.sMS.h, e.sMS.v)
    g.wait_ge(e.sDVE.h, ab["sDVE"])
    g.wait_ge(e.sPE.h, ab["sPE"])
    g.wait_ge(e.sACT.h, ab["sACT"])
    for hq in range(NH):
        if hq >= 2:
            g.wait_ge(e.sPE.h, peq(hq - 2, 2))
        for half in range(2):
            I(e["sWQ%d" % (hq % 2)], g.dma_start(
                out=e.wq_sb[:, ((hq % 2) * DH + half * 8) * 128:
                            ((hq % 2) * DH + (half + 1) * 8) * 128].rearrange(
                                "p (dh x) -> p dh x", dh=8),
                in_=e.wqh[hq, :, half * 8:(half + 1) * 8, :]), 16)
    for hq in range(NH):
        g.wait_ge(e.sDVE.h, dveq(hq, 1))
        if hq >= 1:
            g.wait_ge(e.sPE.h, peqn(hq - 1))
        I(e.sPOOL, g.tensor_tensor(out=e.q2b[:, :], in0=e.q_sb[:, hq * 1024:(hq + 1) * 1024],
                                   in1=e.q_sb[:, hq * 1024:(hq + 1) * 1024], op=ALU.mult))
    for pair, (beta, kvh) in enumerate(ATT_PAIRS):
        E = att_E(c, beta)
        qposl = 512 * c if beta == 0 else 512 * (7 - c)
        for h in range(4):
            for j in range(4):
                kc = E - 4 + j
                g.wait_ge(e.sACT.h, actexp(pair, h, kc))
                I(e.sPOOL, g.affine_select(
                    out=e.probs[kc % 2][:, :], in_=e.probs[kc % 2][:, :],
                    pattern=[[1, 512]], compare_op=ALU.is_ge, fill=0.0,
                    base=qposl - kc * 128, channel_multiplier=-1))

    s.wait_ge(e.sDVE.h, dveu)
    I(e.sACT, s.activation(out=e.trow[0:1, 2:3], in_=e.trow[0:1, 0:1], func=AF.Ln))
    W(s, e.sACT)
    I(e.sACT, s.activation(out=e.trow[0:1, 3:4], in_=e.trow[0:1, 2:3], func=AF.Exp,
                           scale=0.5))
    for pair, (beta, kvh) in enumerate(ATT_PAIRS):
        E = att_E(c, beta)
        for h in range(4):
            for kc in range(E):
                s.wait_ge(e.sPE.h, pe_off[("score", pair, h, kc)])
                if kc >= 2:
                    s.wait_ge(e.sPE.h, pe_off[("av", pair, h, kc - 2)])
                if pair == 0 and h == 0 and kc == 0:
                    s.wait_ge(e.sDVE.h, dveneg)
                I(e.sACT, s.activation(out=e.probs[kc % 2][:, :], in_=e.B[kc % 2][:, :],
                                       func=AF.Exp, scale=1.0 / math.sqrt(128.0),
                                       bias=e.bias_col[:, 0:1]))
            s.wait_ge(e.sPE.h, pe_off[("av", pair, h, E - 1)])
            I(e.sACT, s.copy(out=e.srow[:, :], in_=e.B[3][0:1, :]))
            W(s, e.sACT)
            I(e.sACT, s.activation(out=e.scr[2][0:1, 0:512], in_=e.srow[:, :], func=AF.Ln))
            W(s, e.sACT)
            I(e.sACT, s.activation(out=e.srow[:, :], in_=e.scr[2][0:1, 0:512], func=AF.Exp,
                                   scale=-1.0))
            if pair > 0 or h > 0:
                ph, hh = (pair, h - 1) if h > 0 else (pair - 1, 3)
                s.wait_ge(e.sDVE.h, dveav(ph, hh))
            I(e.sACT, s.copy(out=e.avf[:, :], in_=e.B[2][:, :]))

    for hq in range(NH):
        for ib in range(2):
            v.wait_ge(e.sPE.h, peq(hq, ib + 1))
            a = (A0, A1)[ib]
            b = e.B[ib]
            cs = e.cosr[:, a:a + 512]
            sn = e.sinr[:, a:a + 512]
            dst = e.q_sb[:, hq * 1024 + ib * 512:hq * 1024 + (ib + 1) * 512]
            W(v, e.sDVE)
            I(e.sDVE, v.tensor_mul(e.scr[0][0:64, :], b[0:64, :], cs))
            I(e.sDVE, v.tensor_mul(e.scr[1][0:64, :], b[64:128, :], sn))
            W(v, e.sDVE)
            I(e.sDVE, v.tensor_sub(dst[0:64, :], e.scr[0][0:64, :], e.scr[1][0:64, :]))
            W(v, e.sDVE)
            I(e.sDVE, v.tensor_mul(e.scr[0][0:64, :], b[64:128, :], cs))
            I(e.sDVE, v.tensor_mul(e.scr[1][0:64, :], b[0:64, :], sn))
            W(v, e.sDVE)
            I(e.sDVE, v.tensor_add(dst[64:128, :], e.scr[0][0:64, :], e.scr[1][0:64, :]))
    for hq in range(NH):
        v.wait_ge(e.sPE.h, peqn(hq))
        W(v, e.sDVE)
        I(e.sDVE, v.reduce_max(e.scr[2][0:1, 0:1], e.B[5][0:1, :], axis=AXX))
        W(v, e.sDVE)
        if hq == 0:
            I(e.sDVE, v.tensor_copy(out=e.trow[0:1, 4:5], in_=e.scr[2][0:1, 0:1]))
        else:
            I(e.sDVE, v.tensor_max(e.trow[0:1, 4:5], e.trow[0:1, 4:5], e.scr[2][0:1, 0:1]))
    W(v, e.sDVE)
    I(e.sDVE, v.reduce_max(e.scr[2][0:1, 0:1], e.kmax2[:, :], axis=AXX))
    W(v, e.sDVE)
    I(e.sDVE, v.scalar_tensor_tensor(out=e.trow[0:1, 0:1], in0=e.trow[0:1, 4:5],
                                     scalar=1.0 / 128.0, in1=e.scr[2][0:1, 0:1],
                                     op0=ALU.mult, op1=ALU.mult))
    v.wait_ge(e.sPE.h, pebias)
    W(v, e.sDVE)
    I(e.sDVE, v.tensor_scalar_mul(e.bias_col[:, :], e.B[4][:, 0:1], -1.0))
    for pair, (beta, kvh) in enumerate(ATT_PAIRS):
        for h in range(4):
            hglob = kvh * 4 + h
            v.wait_ge(e.sACT.h, actavc(pair, h))
            v.wait_ge(e.sPE.h, pe_off[("recip", pair, h)])
            W(v, e.sDVE)
            I(e.sDVE, v.tensor_mul(
                e.av_sb[:, hglob * 1024 + beta * 512:hglob * 1024 + (beta + 1) * 512],
                e.avf[:, :], e.B[4][:, :]))


def attn_debug(e, s):
    s.wait_ge(e.sDVE.h, e.sDVE.v)
    for half in range(2):
        I(e.sGLD, s.dma_start(
            out=e.dbg["dbg_av"].rearrange("(hq dl) t -> dl hq t", dl=128)[
                :, half * 8:(half + 1) * 8, :],
            in_=e.av_sb[:, half * 8 * 1024:(half + 1) * 8 * 1024].rearrange(
                "p (hq t) -> p hq t", hq=8)), 16)
        I(e.sGLD, s.dma_start(
            out=e.dbg["dbg_q"].rearrange("(hq dl) t -> dl hq t", dl=128)[
                :, half * 8:(half + 1) * 8, :],
            in_=e.q_sb[:, half * 8 * 1024:(half + 1) * 8 * 1024].rearrange(
                "p (hq t) -> p hq t", hq=8)), 16)
    s.wait_ge(e.sGLD.h, e.sGLD.v)



def emit_h2(e):
    nc = e.nc
    h2ctx = ExitStack()

    def sb(name, shape, dt):
        return h2ctx.enter_context(nc.sbuf_tensor(name, shape, dt))

    e.xbuf2 = sb("xbuf2", [128, DH * 1024], F32)
    e.wo_sb = sb("wo_sb", [128, 2 * DH * 128], BF16)
    e.rows_sb = sb("rows_sb", [1, 8 * 512], F32)
    e.mr_sb = sb("mr_sb", [128, 4 * 512], F32)
    e.gw_sb = sb("gw_sb", [128, DH * NE], F32)
    e.id128 = sb("id128", [128, 128], F32)
    e.xnt = [sb(f"xnt{i}", [128, 1024], BF16) for i in range(2)]
    e.wch = sb("wch", [128, 64], F32)
    e.logacc = sb("logacc", [128, 64], F32)
    e.wr_sb = sb("wr_sb", [8, 1024], F32)

    hb = {nm: e[nm].v for nm in
          ["sPE", "sACT", "sDVE", "sPOOL", "sXB", "sWO0", "sWO1", "sH2",
           "sXN0", "sXN1", "sSP"]}
    e.h2b = hb
    A = lambda ci, tg, k: hb["sDVE"] + (ci * 2 + tg) * 4 + k
    Bv = lambda ci, tg, k: hb["sDVE"] + 128 + ci * 7 + tg * 3 + k
    accL = lambda ci: hb["sDVE"] + 128 + ci * 7 + 7
    T = lambda ch, k: hb["sDVE"] + 240 + ch * 10 + k
    actmr = hb["sACT"] + 4
    actexp = lambda tg: hb["sACT"] + 6 + tg + 1
    actsig = lambda ch: hb["sACT"] + 8 + ch + 1
    actwr = hb["sACT"] + 18

    seng = nc.sync
    if "pid_s" not in e:
        e.pid_s = seng.partition_id()
    book = SwBook(e, ["sXB"])
    for c in nc.Switch(engines=[seng], index=[e.pid_s], n=8):
        book.enter_body()
        A0, A1 = own_cols(c)
        for nm in ["sDVE", "sPE", "sACT", "sPOOL"]:
            seng.wait_ge(e[nm].h, hb[nm])
        for half in range(2):
            for ib, a in enumerate((A0, A1)):
                I(e.sXB, seng.dma_start(
                    out=e.xbuf2[:, :].rearrange("p (ci t) -> p ci t", ci=DH)[
                        :, half * 8:(half + 1) * 8, ib * 512:(ib + 1) * 512],
                    in_=e.xT.rearrange("(ci dl) t -> dl ci t", dl=128)[
                        :, half * 8:(half + 1) * 8, a:a + 512]), 16)
        for r in range(2):
            for ib, a in enumerate((A0, A1)):
                I(e.sXB, seng.dma_start(
                    out=e.rows_sb[0:1, (r * 2 + ib) * 512:(r * 2 + ib + 1) * 512],
                    in_=e.d_rows[r:r + 1, a:a + 512]), 16)
    book.finish()
    sxb_full = hb["sXB"] + 128

    e.h2pe = {}

    with nc.Block() as blk:
        blk.tensor(lambda t: h2_tensor(e, t, hb, A, Bv, accL, T, actmr, actexp, sxb_full))
        blk.scalar(lambda s: h2_scalar(e, s, hb, T))
        blk.vector(lambda v: h2_vector(e, v, hb, A, Bv, T, actmr, actsig, sxb_full))
        blk.gpsimd(lambda g: h2_gpsimd(e, g, hb))
        blk.sync(lambda s: h2_sync(e, s, hb, A, Bv, actwr))

    if DEBUG:
        with nc.Block() as blk2:
            blk2.sync(lambda s: h2_debug_s(e, s))
            blk2.gpsimd(lambda g: h2_debug_g(e, g))
    h2ctx.close()


def h2_tensor(e, t, hb, A, Bv, accL, T, actmr, actexp, sxb_full):
    pe = e.h2pe
    t.wait_ge(e.sDVE.h, hb["sDVE"])
    t.wait_ge(e.sACT.h, hb["sACT"])
    t.wait_ge(e.sPOOL.h, hb["sPOOL"])
    t.wait_ge(e.sXB.h, sxb_full)
    for i in range(4):
        I(e.sPE, t.matmul(e.B[2 + i][:, :], e.ones1f[:, :],
                          e.rows_sb[0:1, i * 512:(i + 1) * 512],
                          start=True, stop=True))
        pe[("mr", i)] = e.sPE.v
    for ci in range(DH):
        for tg in range(2):
            j = ci * 2 + tg
            sw = e["sWO%d" % (ci % 2)]
            t.wait_ge(sw.h, hb["sWO%d" % (ci % 2)] + 32 * (ci // 2 + 1))
            if j >= 2:
                pc, pt = (j - 2) // 2, (j - 2) % 2
                t.wait_ge(e.sDVE.h, A(pc, pt, 3))
            for hq in range(DH):
                ins = t.matmul(
                    e.B[j % 2][:, :],
                    e.wo_sb[:, ((ci % 2) * DH + hq) * 128:((ci % 2) * DH + hq + 1) * 128],
                    e.av_sb[:, hq * 1024 + tg * 512:hq * 1024 + (tg + 1) * 512],
                    start=(hq == 0), stop=(hq == DH - 1))
            I(e.sPE, ins)
            pe[("wo", ci, tg)] = e.sPE.v
        if ci >= 1:
            for tg in range(2):
                t.wait_ge(e.sDVE.h, A(ci - 1, tg, 4))
                I(e.sPE, t.matmul(e.B[2 + tg][0:1, :], e.cst[:, 1:2],
                                  e.scr[2 + tg][:, :],
                                  start=(ci - 1 == 0), stop=False))
                pe[("v2", ci - 1, tg)] = e.sPE.v
    for tg in range(2):
        t.wait_ge(e.sDVE.h, A(15, tg, 4))
        I(e.sPE, t.matmul(e.B[2 + tg][0:1, :], e.cst[:, 1:2], e.scr[2 + tg][:, :],
                          start=False, stop=True))
        pe[("v2", 15, tg)] = e.sPE.v
    for tg in range(2):
        t.wait_ge(e.sACT.h, actexp(tg))
        t.wait_ge(e.sDVE.h, A(15, tg, 3))
        I(e.sPE, t.matmul(e.B[tg][:, :], e.ones1f[:, :],
                          e.rows_sb[0:1, (6 + tg) * 512:(7 + tg) * 512],
                          start=True, stop=True))
        pe[("rbc", tg)] = e.sPE.v
    t.wait_ge(e.sSP.h, hb["sSP"] + 16)
    for ci in range(DH):
        for ch in range(8):
            t.wait_ge(e.sDVE.h, Bv(ci, ch // 4, 2))
            if ci >= 1 and ch == 0:
                t.wait_ge(e.sDVE.h, accL(ci - 1))
            I(e.sPE, t.matmul(e.B[4][:, ch * 8:(ch + 1) * 8],
                              e.scr[ch // 4][:, (ch % 4) * 128:(ch % 4 + 1) * 128],
                              e.gw_sb[:, ci * 8:(ci + 1) * 8],
                              start=True, stop=True))
            pe[("gate", ci, ch)] = e.sPE.v
    t.wait_ge(e.sPOOL.h, hb["sPOOL"] + 2)
    for ch in range(8):
        t.wait_ge(e.sDVE.h, T(ch, 10))
        I(e.sPE, t.matmul(e.B[5 + ch // 4][0:8, (ch % 4) * 128:(ch % 4 + 1) * 128],
                          e.wch[:, ch * 8:(ch + 1) * 8], e.id128[:, :],
                          start=True, stop=True))
        pe[("tr", ch)] = e.sPE.v


def h2_scalar(e, s, hb, T):
    pe = e.h2pe
    for i in range(4):
        s.wait_ge(e.sPE.h, pe[("mr", i)])
        I(e.sACT, s.copy(out=e.mr_sb[:, i * 512:(i + 1) * 512], in_=e.B[2 + i][:, :]))
    for tg in range(2):
        s.wait_ge(e.sPE.h, pe[("v2", 15, tg)])
        I(e.sACT, s.activation(out=e.rows_sb[0:1, (4 + tg) * 512:(5 + tg) * 512],
                               in_=e.B[2 + tg][0:1, :], func=AF.Ln,
                               scale=1.0 / float(D), bias=EPS))
    for tg in range(2):
        W(s, e.sACT)
        I(e.sACT, s.activation(out=e.rows_sb[0:1, (6 + tg) * 512:(7 + tg) * 512],
                               in_=e.rows_sb[0:1, (4 + tg) * 512:(5 + tg) * 512],
                               func=AF.Exp, scale=-0.5))
    for ch in range(8):
        s.wait_ge(e.sDVE.h, T(ch, 5))
        I(e.sACT, s.activation(out=e.scr[4][:, 5:6], in_=e.scr[4][:, 4:5],
                               func=AF.Sigmoid))
    for i in range(2):
        s.wait_ge(e.sPE.h, pe[("tr", i * 4 + 3)])
        I(e.sACT, s.copy(out=e.wr_sb[:, i * 512:(i + 1) * 512],
                         in_=e.B[5 + i][0:8, :]))


def h2_vector(e, v, hb, A, Bv, T, actmr, actsig, sxb_full):
    pe = e.h2pe
    v.wait_ge(e.sXB.h, sxb_full)
    v.wait_ge(e.sACT.h, actmr)
    for ci in range(DH):
        for tg in range(2):
            xt = e.xbuf2[:, ci * 1024 + tg * 512:ci * 1024 + (tg + 1) * 512]
            W(v, e.sDVE)
            I(e.sDVE, v.tensor_sub(xt, xt, e.mr_sb[:, tg * 512:(tg + 1) * 512]))
            W(v, e.sDVE)
            I(e.sDVE, v.tensor_mul(xt, xt, e.mr_sb[:, (2 + tg) * 512:(3 + tg) * 512]))
            v.wait_ge(e.sPE.h, pe[("wo", ci, tg)])
            W(v, e.sDVE)
            I(e.sDVE, v.tensor_add(xt, xt, e.B[(ci * 2 + tg) % 2][:, :]))
            if ci >= 1:
                v.wait_ge(e.sPE.h, pe[("v2", ci - 1, tg)])
            W(v, e.sDVE)
            I(e.sDVE, v.tensor_mul(e.scr[2 + tg][:, :], xt, xt))
    for ci in range(DH):
        for tg in range(2):
            xt = e.xbuf2[:, ci * 1024 + tg * 512:ci * 1024 + (tg + 1) * 512]
            v.wait_ge(e.sPE.h, pe[("rbc", tg)])
            if ci >= 1:
                v.wait_ge(e.sPE.h, pe[("gate", ci - 1, 7)])
            W(v, e.sDVE)
            I(e.sDVE, v.tensor_mul(e.scr[tg][:, :], xt, e.B[tg][:, :]))
            W(v, e.sDVE)
            I(e.sDVE, v.tensor_scalar(out=e.scr[tg][:, :], in0=e.scr[tg][:, :],
                                      scalar1=e.n2w_c[:, ci:ci + 1], scalar2=None,
                                      op0=ALU.mult, op1=ALU.bypass, accum_out=None))
            if ci >= 2:
                sx = e["sXN%d" % (ci % 2)]
                v.wait_ge(sx.h, hb["sXN%d" % (ci % 2)] + 16 * (ci // 2))
            W(v, e.sDVE)
            I(e.sDVE, v.tensor_copy(out=e.xnt[ci % 2][:, tg * 512:(tg + 1) * 512],
                                    in_=e.scr[tg][:, :]))
        v.wait_ge(e.sPE.h, pe[("gate", ci, 7)])
        W(v, e.sDVE)
        if ci == 0:
            I(e.sDVE, v.tensor_copy(out=e.logacc[:, :], in_=e.B[4][:, 0:64]))
        else:
            I(e.sDVE, v.tensor_add(e.logacc[:, :], e.logacc[:, :], e.B[4][:, 0:64]))
    s4, s5 = e.scr[4], e.scr[5]
    for ch in range(8):
        bch = e.logacc[:, ch * 8:(ch + 1) * 8]
        W(v, e.sDVE)
        I(e.sDVE, v.reduce_max(s4[:, 0:1], bch, axis=AXX))
        W(v, e.sDVE)
        I(e.sDVE, v.tensor_scalar(out=s5[:, 0:8], in0=bch, scalar1=s4[:, 0:1],
                                  scalar2=None, op0=ALU.is_equal, op1=ALU.bypass,
                                  accum_out=None))
        W(v, e.sDVE)
        I(e.sDVE, v.scalar_tensor_tensor(out=s5[:, 8:16], in0=s5[:, 0:8],
                                         scalar=-1e30, in1=bch,
                                         op0=ALU.mult, op1=ALU.add))
        W(v, e.sDVE)
        I(e.sDVE, v.reduce_max(s4[:, 1:2], s5[:, 8:16], axis=AXX))
        W(v, e.sDVE)
        I(e.sDVE, v.tensor_sub(s4[:, 4:5], s4[:, 0:1], s4[:, 1:2]))
        v.wait_ge(e.sACT.h, actsig(ch))
        W(v, e.sDVE)
        I(e.sDVE, v.tensor_scalar(out=s4[:, 6:7], in0=s4[:, 5:6], scalar1=-1.0,
                                  scalar2=1.0, op0=ALU.mult, op1=ALU.add,
                                  accum_out=None))
        W(v, e.sDVE)
        I(e.sDVE, v.tensor_scalar(out=s5[:, 16:24], in0=s5[:, 0:8], scalar1=s4[:, 5:6],
                                  scalar2=None, op0=ALU.mult, op1=ALU.bypass,
                                  accum_out=None))
        W(v, e.sDVE)
        I(e.sDVE, v.tensor_scalar(out=s5[:, 24:32], in0=s5[:, 8:16], scalar1=s4[:, 1:2],
                                  scalar2=None, op0=ALU.is_equal, op1=ALU.bypass,
                                  accum_out=None))
        W(v, e.sDVE)
        I(e.sDVE, v.tensor_scalar(out=s5[:, 32:40], in0=s5[:, 24:32], scalar1=s4[:, 6:7],
                                  scalar2=None, op0=ALU.mult, op1=ALU.bypass,
                                  accum_out=None))
        W(v, e.sDVE)
        I(e.sDVE, v.tensor_add(e.wch[:, ch * 8:(ch + 1) * 8], s5[:, 16:24],
                               s5[:, 32:40]))


def h2_gpsimd(e, g, hb):
    pe = e.h2pe
    for nm in ["sDVE", "sPE", "sACT", "sPOOL"]:
        g.wait_ge(e[nm].h, hb[nm])
    I(e.sPOOL, g.memset(e.id128[:, :], 1.0))
    W(g, e.sPOOL)
    I(e.sPOOL, g.affine_select(out=e.id128[:, :], in_=e.id128[:, :],
                               pattern=[[-1, 128]], compare_op=ALU.is_equal,
                               fill=0.0, base=0, channel_multiplier=1))
    for ci in range(DH):
        if ci >= 2:
            g.wait_ge(e.sPE.h, pe[("wo", ci - 2, 1)])
        for half in range(2):
            I(e["sWO%d" % (ci % 2)], g.dma_start(
                out=e.wo_sb[:, ((ci % 2) * DH + half * 8) * 128:
                            ((ci % 2) * DH + (half + 1) * 8) * 128].rearrange(
                                "p (hq x) -> p hq x", hq=8),
                in_=e.woh[ci, :, half * 8:(half + 1) * 8, :]), 16)


def h2_sync(e, s, hb, A, Bv, actwr):
    I(e.sSP, s.dma_start(
        out=e.gw_sb[:, :].rearrange("p (ci x) -> p ci x", ci=DH),
        in_=e.gwt.rearrange("(ci dl) x -> dl ci x", dl=128)), 16)
    for ci in range(DH):
        s.wait_ge(e.sDVE.h, A(ci, 1, 3))
        I(e.sH2, s.dma_start(out=e.d_h2[ci * 128:(ci + 1) * 128, :],
                             in_=e.xbuf2[:, ci * 1024:(ci + 1) * 1024]), 16)
    for ci in range(DH):
        s.wait_ge(e.sDVE.h, Bv(ci, 1, 3))
        I(e["sXN%d" % (ci % 2)], s.dma_start(
            out=e.d_xn2[ci * 128:(ci + 1) * 128, :], in_=e.xnt[ci % 2][:, :]), 16)
    s.wait_ge(e.sACT.h, actwr)
    I(e.sH2, s.dma_start(out=e.d_w, in_=e.wr_sb[:, :]), 16)
    for c in [e.sH2, e.sXN0, e.sXN1]:
        s.wait_ge(c.h, c.v)


def h2_debug_s(e, s):
    I(e.sST, s.dma_start(out=e.dbg["dbg_h2"], in_=e.d_h2), 16)
    I(e.sST, s.dma_start(out=e.dbg["dbg_w"], in_=e.d_w), 16)
    s.wait_ge(e.sST.h, e.sST.v)


def h2_debug_g(e, g):
    I(e.sGLD, g.dma_start(out=e.dbg["dbg_xn2"], in_=e.d_xn2), 16)
    g.wait_ge(e.sGLD.h, e.sGLD.v)



def emit_moe(e):
    nc = e.nc
    e.moectx = ExitStack()
    e.moe_acc = e.moectx.enter_context(
        nc.sbuf_tensor("moe_acc", [128, DH * 1024], BF16))
    m2 = ExitStack()

    def sb(name, shape, dt):
        return m2.enter_context(nc.sbuf_tensor(name, shape, dt))

    e.xn2b = sb("xn2b", [128, DH * 1024], BF16)
    e.hf_sb = sb("hf_sb", [128, 32 * 1024], BF16)
    e.wf32 = [sb(f"wf32_{i}", [128, 4096], F32) for i in range(2)]
    e.wbf = [sb(f"wbf_{i}", [128, 4096], BF16) for i in range(2)]
    e.wbc = sb("wbc", [128, 1024], F32)
    e.silb = [sb(f"silb{i}", [128, 512], BF16) for i in range(2)]
    e.wrow = sb("wrow", [1, 1024], F32)

    mb = {nm: e[nm].v for nm in
          ["sPE", "sACT", "sDVE", "sPOOL", "sXB", "sMR", "sMW0", "sMW1"]}
    e.mb = mb
    iters = []
    for ee in range(NE):
        iters += [("u", ee, fc) for fc in range(32)] +                  [("d", ee, ci) for ci in range(16)]
    slot_of = {}
    arrive = {}
    cnt = [mb["sMW0"], mb["sMW1"]]
    for gi, it in enumerate(iters):
        s = gi % 2
        slot_of[gi] = s
        cnt[s] += 32 if it[0] == "u" else 16
        arrive[gi] = cnt[s]
    e.m_iters, e.m_slot, e.m_arrive = iters, slot_of, arrive

    pe_wbc = lambda ee, tg: mb["sPE"] + ee * 162 + tg + 1
    pe_u = lambda ee, fc, tg, w: mb["sPE"] + ee * 162 + 2 + fc * 4 + tg * 2 + w + 1
    pe_d = lambda ee, ci, tg: mb["sPE"] + ee * 162 + 130 + ci * 2 + tg + 1
    act_wbc = lambda ee, tg: mb["sACT"] + ee * 66 + tg + 1
    act_sil = lambda ee, fc, tg: mb["sACT"] + ee * 66 + 2 + fc * 2 + tg + 1
    dve_hf = lambda ee, fc, tg: mb["sDVE"] + ee * 128 + fc * 2 + tg + 1
    dve_dn = lambda ee, ci, tg, k: mb["sDVE"] + ee * 128 + 64 + ci * 4 + tg * 2 + k
    pool_cv = lambda ee, fc, w: mb["sPOOL"] + ee * 80 + fc * 2 + w + 1
    pool_cvd = lambda ee, ci: mb["sPOOL"] + ee * 80 + 64 + ci + 1

    def last_pe(gi):
        k, ee, i = iters[gi]
        return pe_u(ee, i, 1, 1) if k == "u" else pe_d(ee, i, 1)

    def last_pool(gi):
        k, ee, i = iters[gi]
        return pool_cv(ee, i, 1) if k == "u" else pool_cvd(ee, i)

    fns = dict(pe_wbc=pe_wbc, pe_u=pe_u, pe_d=pe_d, act_wbc=act_wbc,
               act_sil=act_sil, dve_hf=dve_hf, dve_dn=dve_dn, pool_cv=pool_cv,
               pool_cvd=pool_cvd, last_pe=last_pe, last_pool=last_pool)
    sxb_full = mb["sXB"] + 256

    with nc.Block() as blk:
        blk.tensor(lambda t: moe_tensor(e, t, mb, fns, sxb_full))
        blk.scalar(lambda s: moe_scalar(e, s, mb, fns))
        blk.vector(lambda v: moe_vector(e, v, mb, fns))
        blk.gpsimd(lambda g: moe_gpsimd(e, g, mb, fns))
        blk.sync(lambda s: moe_sync(e, s, mb, fns))

    if DEBUG:
        with nc.Block() as blk2:
            blk2.gpsimd(lambda g: moe_debug_g(e, g))
    m2.close()


def moe_tensor(e, t, mb, f, sxb_full):
    t.wait_ge(e.sDVE.h, mb["sDVE"])
    t.wait_ge(e.sACT.h, mb["sACT"])
    t.wait_ge(e.sPOOL.h, mb["sPOOL"])
    t.wait_ge(e.sXB.h, sxb_full)
    for ee in range(NE):
        t.wait_ge(e.sMR.h, mb["sMR"] + 16 * (ee + 1))
        if ee >= 1:
            t.wait_ge(e.sDVE.h, f["dve_dn"](ee - 1, 15, 1, 2))
        for tg in range(2):
            I(e.sPE, t.matmul(e.B[6 + tg][:, :], e.ones1f[:, :],
                              e.wrow[0:1, tg * 512:(tg + 1) * 512],
                              start=True, stop=True))
        for fc in range(32):
            gi = ee * 48 + fc
            sl = e.m_slot[gi]
            t.wait_ge(e["sMW%d" % sl].h, e.m_arrive[gi])
            t.wait_ge(e.sPOOL.h, f["pool_cv"](ee, fc, 1))
            for tg in range(2):
                for w in range(2):
                    ugi = ee * 32 + fc
                    if ugi >= 2:
                        pe_, pf = divmod(ugi - 2, 32)
                        if w == 0:
                            t.wait_ge(e.sACT.h, f["act_sil"](pe_, pf, tg))
                        else:
                            t.wait_ge(e.sDVE.h, f["dve_hf"](pe_, pf, tg))
                    if fc == 1 and w == 1:
                        t.wait_ge(e.sACT.h, f["act_wbc"](ee, tg))
                    bank = (fc % 2) * 4 + w * 2 + tg
                    for dh in range(DH):
                        ins = t.matmul(
                            e.B[bank][:, :],
                            e.wbf[sl][:, w * 2048 + dh * 128:w * 2048 + (dh + 1) * 128],
                            e.xn2b[:, dh * 1024 + tg * 512:dh * 1024 + (tg + 1) * 512],
                            start=(dh == 0), stop=(dh == DH - 1))
                    I(e.sPE, ins)
        for ci in range(16):
            gi = ee * 48 + 32 + ci
            sl = e.m_slot[gi]
            t.wait_ge(e["sMW%d" % sl].h, e.m_arrive[gi])
            t.wait_ge(e.sPOOL.h, f["pool_cvd"](ee, ci))
            if ci == 0:
                t.wait_ge(e.sDVE.h, f["dve_hf"](ee, 31, 1))
            for tg in range(2):
                bank = 4 + (ci % 2) * 2 + tg
                if ci >= 2:
                    t.wait_ge(e.sDVE.h, f["dve_dn"](ee, ci - 2, tg, 1))
                else:
                    t.wait_ge(e.sACT.h, f["act_sil"](ee, 30 + ci, tg))
                    t.wait_ge(e.sDVE.h, f["dve_hf"](ee, 30 + ci, tg))
                for kc in range(32):
                    ins = t.matmul(
                        e.B[bank][:, :],
                        e.wbf[sl][:, kc * 128:(kc + 1) * 128],
                        e.hf_sb[:, kc * 1024 + tg * 512:kc * 1024 + (tg + 1) * 512],
                        start=(kc == 0), stop=(kc == 31))
                I(e.sPE, ins)


def moe_scalar(e, s, mb, f):
    for ee in range(NE):
        for tg in range(2):
            s.wait_ge(e.sPE.h, f["pe_wbc"](ee, tg))
            if ee >= 1 and tg == 0:
                s.wait_ge(e.sDVE.h, f["dve_dn"](ee - 1, 15, 1, 2))
            I(e.sACT, s.copy(out=e.wbc[:, tg * 512:(tg + 1) * 512],
                             in_=e.B[6 + tg][:, :]))
        for fc in range(32):
            for tg in range(2):
                s.wait_ge(e.sPE.h, f["pe_u"](ee, fc, tg, 0))
                idx = fc * 2 + tg
                if idx >= 2:
                    s.wait_ge(e.sDVE.h, f["dve_hf"](ee, fc - 1, tg))
                I(e.sACT, s.activation(out=e.silb[idx % 2][:, :],
                                       in_=e.B[(fc % 2) * 4 + tg][:, :],
                                       func=AF.Silu))


def moe_vector(e, v, mb, f):
    for ee in range(NE):
        for fc in range(32):
            for tg in range(2):
                v.wait_ge(e.sACT.h, f["act_sil"](ee, fc, tg))
                v.wait_ge(e.sPE.h, f["pe_u"](ee, fc, tg, 1))
                if ee >= 1 and fc == 0 and tg == 0:
                    v.wait_ge(e.sPE.h, f["pe_d"](ee - 1, 15, 1))
                W(v, e.sDVE)
                I(e.sDVE, v.tensor_mul(
                    e.hf_sb[:, fc * 1024 + tg * 512:fc * 1024 + (tg + 1) * 512],
                    e.silb[(fc * 2 + tg) % 2][:, :],
                    e.B[(fc % 2) * 4 + 2 + tg][:, :]))
        for ci in range(16):
            for tg in range(2):
                v.wait_ge(e.sPE.h, f["pe_d"](ee, ci, tg))
                W(v, e.sDVE)
                I(e.sDVE, v.tensor_mul(e.scr[tg][:, :],
                                       e.B[4 + (ci % 2) * 2 + tg][:, :],
                                       e.wbc[:, tg * 512:(tg + 1) * 512]))
                at = e.moe_acc[:, ci * 1024 + tg * 512:ci * 1024 + (tg + 1) * 512]
                W(v, e.sDVE)
                if ee == 0:
                    I(e.sDVE, v.tensor_copy(out=at, in_=e.scr[tg][:, :]))
                else:
                    I(e.sDVE, v.tensor_add(at, at, e.scr[tg][:, :]))


def moe_gpsimd(e, g, mb, f):
    for nm in ["sDVE", "sPE", "sACT", "sPOOL"]:
        g.wait_ge(e[nm].h, mb[nm])
    for gi, (k, ee, i) in enumerate(e.m_iters):
        sl = e.m_slot[gi]
        g.wait_ge(e["sMW%d" % sl].h, e.m_arrive[gi])
        if gi >= 2:
            g.wait_ge(e.sPE.h, f["last_pe"](gi - 2))
        if k == "u":
            for w in range(2):
                I(e.sPOOL, g.tensor_copy(
                    out=e.wbf[sl][:, w * 2048:(w + 1) * 2048],
                    in_=e.wf32[sl][:, w * 2048:(w + 1) * 2048]))
        else:
            I(e.sPOOL, g.tensor_copy(out=e.wbf[sl][:, :], in_=e.wf32[sl][:, :]))


def moe_sync(e, s, mb, f):
    for nm in ["sDVE", "sPE", "sACT", "sPOOL"]:
        s.wait_ge(e[nm].h, mb[nm])
    for half in range(2):
        for q in range(8):
            ci = half * 8 + q
            I(e.sXB, s.dma_start(out=e.xn2b[:, ci * 1024:(ci + 1) * 1024],
                                 in_=e.d_xn2[ci * 128:(ci + 1) * 128, :]), 16)
    for gi, (k, ee, i) in enumerate(e.m_iters):
        if k == "u" and i == 0:
            if ee >= 1:
                s.wait_ge(e.sPE.h, f["pe_wbc"](ee - 1, 1))
            I(e.sMR, s.dma_start(out=e.wrow[:, :], in_=e.d_w[ee:ee + 1, :]), 16)
        sl = e.m_slot[gi]
        if gi >= 2:
            s.wait_ge(e.sPOOL.h, f["last_pool"](gi - 2))
        if k == "u":
            I(e["sMW%d" % sl], s.dma_start(
                out=e.wf32[sl][:, 0:2048].rearrange("p (dh x) -> p dh x", dh=DH),
                in_=e.w1h[ee, :, :, i * 128:(i + 1) * 128]), 16)
            I(e["sMW%d" % sl], s.dma_start(
                out=e.wf32[sl][:, 2048:4096].rearrange("p (dh x) -> p dh x", dh=DH),
                in_=e.w3h[ee, :, :, i * 128:(i + 1) * 128]), 16)
        else:
            I(e["sMW%d" % sl], s.dma_start(
                out=e.wf32[sl][:, :].rearrange("p (kc x) -> p kc x", kc=32),
                in_=e.w2h[ee, :, :, i * 128:(i + 1) * 128]), 16)


def moe_debug_g(e, g):
    g.wait_ge(e.sDVE.h, e.sDVE.v)
    for half in range(2):
        I(e.sGLD, g.dma_start(
            out=e.dbg["dbg_moe"].rearrange("(ci dl) t -> dl ci t", dl=128)[
                :, half * 8:(half + 1) * 8, :],
            in_=e.moe_acc[:, half * 8 * 1024:(half + 1) * 8 * 1024].rearrange(
                "p (ci t) -> p ci t", ci=8)), 16)
    g.wait_ge(e.sGLD.h, e.sGLD.v)



def emit_final(e):
    nc = e.nc
    fctx = ExitStack()

    def sb(name, shape, dt):
        return fctx.enter_context(nc.sbuf_tensor(name, shape, dt))

    e.xbuf3 = sb("xbuf3", [128, DH * 1024], F32)
    e.rows3 = sb("rows3", [1, 4 * 512], F32)
    e.mr3 = sb("mr3", [128, 4 * 512], F32)
    e.ybuf3 = [sb(f"ybuf3_{i}", [128, 1024], F32) for i in range(2)]
    e.hbuf3 = [sb(f"hbuf3_{i}", [128, 1024], F32) for i in range(2)]
    e.obuf = [sb(f"obuf{i}", [128, 1024], F32) for i in range(2)]
    e.gsb = [sb(f"gsb{i}", [128, 512], F32) for i in range(2)]
    e.gob = [sb(f"gob{i}", [128, 512], F32) for i in range(2)]

    fb = {nm: e[nm].v for nm in
          ["sPE", "sACT", "sDVE", "sPOOL", "sXB", "sFT0", "sFT1", "sOU0", "sOU1"]}
    dve_F = lambda ci, tg, k: fb["sDVE"] + (ci * 2 + tg) * 7 + k
    act_F = lambda ci, tg: fb["sACT"] + 4 + ci * 2 + tg + 1
    pe_F = lambda i: fb["sPE"] + i + 1

    seng = nc.sync
    book = SwBook(e, ["sXB"])
    for c in nc.Switch(engines=[seng], index=[e.pid_s], n=8):
        book.enter_body()
        A0, A1 = own_cols(c)
        for nm in ["sDVE", "sPE", "sACT", "sPOOL"]:
            seng.wait_ge(e[nm].h, fb[nm])
        seng.wait_ge(e.sAG.h, 1)
        I(e.sXB, seng.dma_start(out=e.d_y[:, 0:512],
                                in_=e.d_ygat[:, A0:A0 + 512]), 16)
        I(e.sXB, seng.dma_start(out=e.d_y[:, 512:1024],
                                in_=e.d_ygat[:, A1:A1 + 512]), 16)
        for half in range(2):
            for ib, a in enumerate((A0, A1)):
                I(e.sXB, seng.dma_start(
                    out=e.xbuf3[:, :].rearrange("p (ci t) -> p ci t", ci=DH)[
                        :, half * 8:(half + 1) * 8, ib * 512:(ib + 1) * 512],
                    in_=e.xT.rearrange("(ci dl) t -> dl ci t", dl=128)[
                        :, half * 8:(half + 1) * 8, a:a + 512]), 16)
        for r in range(2):
            for ib, a in enumerate((A0, A1)):
                I(e.sXB, seng.dma_start(
                    out=e.rows3[0:1, (r * 2 + ib) * 512:(r * 2 + ib + 1) * 512],
                    in_=e.d_rows[r:r + 1, a:a + 512]), 16)
    book.finish()
    sxb_full = fb["sXB"] + 160

    with nc.Block() as blk:
        blk.tensor(lambda t: fin_tensor(e, t, fb, sxb_full))
        blk.scalar(lambda s: fin_scalar(e, s, fb, dve_F, pe_F))
        blk.vector(lambda v: fin_vector(e, v, fb, dve_F, act_F, sxb_full))
        blk.sync(lambda s: fin_sync(e, s, fb, dve_F, sxb_full))
    fctx.close()


def fin_tensor(e, t, fb, sxb_full):
    t.wait_ge(e.sDVE.h, fb["sDVE"])
    t.wait_ge(e.sACT.h, fb["sACT"])
    t.wait_ge(e.sPOOL.h, fb["sPOOL"])
    t.wait_ge(e.sXB.h, sxb_full)
    for i in range(4):
        I(e.sPE, t.matmul(e.B[2 + i][:, :], e.ones1f[:, :],
                          e.rows3[0:1, i * 512:(i + 1) * 512],
                          start=True, stop=True))


def fin_scalar(e, s, fb, dve_F, pe_F):
    for i in range(4):
        s.wait_ge(e.sPE.h, pe_F(i))
        I(e.sACT, s.copy(out=e.mr3[:, i * 512:(i + 1) * 512], in_=e.B[2 + i][:, :]))
    for ci in range(DH):
        for tg in range(2):
            s.wait_ge(e.sDVE.h, dve_F(ci, tg, 5))
            if ci >= 1:
                s.wait_ge(e.sDVE.h, dve_F(ci - 1, tg, 6))
            I(e.sACT, s.activation(out=e.gob[tg][:, :], in_=e.gsb[tg][:, :],
                                   func=AF.Silu))


def fin_vector(e, v, fb, dve_F, act_F, sxb_full):
    v.wait_ge(e.sXB.h, sxb_full)
    v.wait_ge(e.sACT.h, fb["sACT"] + 4)
    for ci in range(DH):
        sl = ci % 2
        v.wait_ge(e["sFT%d" % sl].h, fb["sFT%d" % sl] + 32 * (ci // 2 + 1))
        for tg in range(2):
            xt = e.xbuf3[:, ci * 1024 + tg * 512:ci * 1024 + (tg + 1) * 512]
            sc = e.scr[2 + tg]
            W(v, e.sDVE)
            I(e.sDVE, v.tensor_sub(sc[:, :], xt, e.mr3[:, tg * 512:(tg + 1) * 512]))
            W(v, e.sDVE)
            I(e.sDVE, v.tensor_mul(sc[:, :], sc[:, :],
                                   e.mr3[:, (2 + tg) * 512:(3 + tg) * 512]))
            W(v, e.sDVE)
            I(e.sDVE, v.tensor_scalar(out=sc[:, :], in0=sc[:, :],
                                      scalar1=e.D_c[:, ci:ci + 1], scalar2=None,
                                      op0=ALU.mult, op1=ALU.bypass, accum_out=None))
            W(v, e.sDVE)
            I(e.sDVE, v.tensor_add(sc[:, :], sc[:, :],
                                   e.ybuf3[sl][:, tg * 512:(tg + 1) * 512]))
            W(v, e.sDVE)
            I(e.sDVE, v.tensor_add(e.gsb[tg][:, :],
                                   e.hbuf3[sl][:, tg * 512:(tg + 1) * 512],
                                   e.moe_acc[:, ci * 1024 + tg * 512:
                                             ci * 1024 + (tg + 1) * 512]))
            v.wait_ge(e.sACT.h, act_F(ci, tg))
            W(v, e.sDVE)
            I(e.sDVE, v.tensor_mul(sc[:, :], sc[:, :], e.gob[tg][:, :]))
            if ci >= 2 and tg == 0:
                v.wait_ge(e["sOU%d" % sl].h, fb["sOU%d" % sl] + 16 * (ci // 2))
            W(v, e.sDVE)
            I(e.sDVE, v.tensor_add(e.obuf[sl][:, tg * 512:(tg + 1) * 512], xt,
                                   sc[:, :]))


def fin_sync(e, s, fb, dve_F, sxb_full):
    s.wait_ge(e.sXB.h, sxb_full)
    def out_dma(ci):
        s.wait_ge(e.sDVE.h, dve_F(ci, 1, 7))
        I(e["sOU%d" % (ci % 2)], s.dma_start(
            out=e.out[ci * 128:(ci + 1) * 128, :], in_=e.obuf[ci % 2][:, :]), 16)

    for ci in range(DH):
        sl = ci % 2
        if ci >= 2:
            s.wait_ge(e.sDVE.h, dve_F(ci - 2, 1, 5))
        I(e["sFT%d" % sl], s.dma_start(out=e.ybuf3[sl][:, :],
                                       in_=e.d_y[ci * 128:(ci + 1) * 128, :]), 16)
        I(e["sFT%d" % sl], s.dma_start(out=e.hbuf3[sl][:, :],
                                       in_=e.d_h2[ci * 128:(ci + 1) * 128, :]), 16)
        if ci >= 1:
            out_dma(ci - 1)
    out_dma(15)
    for nm in ["sOU0", "sOU1", "sFT0", "sFT1"]:
        s.wait_ge(e[nm].h, e[nm].v)


def prep_inputs(inputs):
    x = np.asarray(inputs["L0_attn_x"])
    fp = {}
    fp["xT"] = np.ascontiguousarray(np.concatenate([x[0], x[1]], axis=0).T)
    fp["wxpT"] = np.ascontiguousarray(np.asarray(inputs["L0_attn_W_xproj"]).T)
    fp["wdtT"] = np.ascontiguousarray(np.asarray(inputs["L0_attn_W_dt"]).T)
    fp["bdt_r"] = np.ascontiguousarray(np.asarray(inputs["L0_attn_b_dt"]).reshape(DH, 128))
    fp["Alog"] = np.ascontiguousarray(np.asarray(inputs["L0_attn_A_log"]))
    fp["D_r"] = np.ascontiguousarray(np.asarray(inputs["L0_attn_D_param"]).reshape(DH, 128))
    fp["n1w_r"] = np.ascontiguousarray(np.asarray(inputs["L0_ffn_norm1_w"]).reshape(DH, 128))
    fp["n2w_r"] = np.ascontiguousarray(np.asarray(inputs["L0_ffn_norm2_w"]).reshape(DH, 128))
    wq = np.asarray(inputs["L0_ffn_wq"])
    fp["wqh"] = np.ascontiguousarray(wq.reshape(DH, 128, NH, 128).transpose(2, 1, 0, 3))
    wk = np.asarray(inputs["L0_ffn_wk"])
    fp["wkh"] = np.ascontiguousarray(wk.reshape(DH, 128, KVH, 128).transpose(2, 1, 0, 3))
    fp["wv"] = np.ascontiguousarray(np.asarray(inputs["L0_ffn_wv"]))
    wo = np.asarray(inputs["L0_ffn_wo"])
    fp["woh"] = np.ascontiguousarray(wo.reshape(DH, 128, DH, 128).transpose(2, 1, 0, 3))
    fp["gwt"] = np.ascontiguousarray(np.asarray(inputs["L0_ffn_gate_w"]))
    w1 = np.asarray(inputs["L0_ffn_w1"])
    fp["w1h"] = np.ascontiguousarray(w1.reshape(NE, DH, 128, FF).transpose(0, 2, 1, 3))
    w3 = np.asarray(inputs["L0_ffn_w3"])
    fp["w3h"] = np.ascontiguousarray(w3.reshape(NE, DH, 128, FF).transpose(0, 2, 1, 3))
    w2 = np.asarray(inputs["L0_ffn_w2"])
    fp["w2h"] = np.ascontiguousarray(w2.reshape(NE, 32, 128, D).transpose(0, 2, 1, 3))
    return fp


def kernel(**inputs):
    fp = prep_inputs(inputs)
    nc, e = build_nc()
    in_maps = [dict(fp) for c in range(8)]
    res = run_bass_kernel_spmd(nc, in_maps, core_ids=list(range(8)))
    outf = np.zeros((2, L, D), np.float32)
    for c in range(8):
        oc = res.results[c]["out"]
        outf[0, 512 * c:512 * (c + 1), :] = oc[:, 0:512].T
        outf[1, 512 * (7 - c):512 * (8 - c), :] = oc[:, 512:1024].T
    kernel.last_res = res
    return outf
